# version 12
# baseline (speedup 1.0000x reference)
import contextlib
import numpy as np
import ml_dtypes
import concourse.bass as bass
import concourse.mybir as mybir
from concourse import bass_utils

F32 = mybir.dt.float32
BF16 = mybir.dt.bfloat16
AF = mybir.ActivationFunctionType
ALU = mybir.AluOpType
AX = mybir.AxisListType
BF = ml_dtypes.bfloat16

D = 1024
S = 2048
NT = D // 128
NC = 4

ENGS = ("pe", "dve", "act", "sp", "gp")


class Res:
    __slots__ = ("name", "w", "r")

    def __init__(self, name=""):
        self.name = name
        self.w = None
        self.r = []


class Reg:
    __slots__ = ("res", "deps")

    def __init__(self, res, deps):
        self.res = res
        self.deps = list(deps)

    def wdeps(self):
        d, self.deps = self.deps, []
        return d


NDMA = 8


class Op:
    __slots__ = ("eng", "fn", "deps", "dma", "idx", "sem_val", "needed", "slot")


class Prog:
    def __init__(self, nc):
        self.nc = nc
        self.items = []

    def op(self, eng, fn, reads=(), writes=(), deps=(), dma=False):
        idx = len(self.items)
        dd = set(d for d in deps if d is not None)
        for r in reads:
            if r.w is not None:
                dd.add(r.w)
        for w in writes:
            if w.w is not None:
                dd.add(w.w)
            dd.update(w.r)
        o = Op()
        o.eng, o.fn, o.deps, o.dma, o.idx = eng, fn, sorted(dd), dma, idx
        o.sem_val, o.needed = None, False
        self.items.append(o)
        for r in reads:
            r.r.append(idx)
        for w in writes:
            w.w = idx
            w.r = []
        return idx

    def barrier(self):
        last = {}
        for it in self.items:
            last[it.eng] = it.idx
        toks = list(last.values())
        for e in ENGS:
            if e in ("dve", "act", "gp"):
                def fn(eng):
                    eng.drain()
                    return eng.nop()
            else:
                def fn(eng):
                    return eng.nop()
            self.op(e, fn, deps=toks)

    def emit(self, block, sems):
        items = self.items
        for it in items:
            for d in it.deps:
                p = items[d]
                if p.eng != it.eng or p.dma:
                    p.needed = True
        cnt = {e: 0 for e in ENGS}
        dcnt = {e: 0 for e in ENGS}
        for it in items:
            if it.dma:
                i = dcnt[it.eng]
                dcnt[it.eng] += 1
                it.slot = i % NDMA
                it.sem_val = 16 * (i // NDMA + 1)
            elif it.needed:
                cnt[it.eng] += 1
                it.sem_val = cnt[it.eng]

        def run_engine(ename, eng):
            waited = {}
            drained_upto = -1
            own_done = -1
            for it in items:
                if it.eng != ename:
                    continue
                need_drain = False
                for d in it.deps:
                    p = items[d]
                    if p.eng == ename and not p.dma:
                        if ename != "pe" and d > drained_upto:
                            need_drain = True
                    else:
                        if p.dma:
                            key = (p.eng, "d", p.slot)
                        else:
                            key = (p.eng, "c")
                        v = p.sem_val
                        assert v is not None, (ename, it.idx, d, p.eng)
                        if waited.get(key, 0) < v:
                            eng.wait_ge(sems[key], v)
                            waited[key] = v
                if it.dma and it.sem_val > 16:
                    key = (ename, "d", it.slot)
                    v = it.sem_val - 16
                    if waited.get(key, 0) < v:
                        eng.wait_ge(sems[key], v)
                        waited[key] = v
                if need_drain:
                    eng.drain()
                    drained_upto = own_done
                inst = it.fn(eng)
                own_done = it.idx
                if it.sem_val is not None:
                    if inst is None:
                        inst = eng.nop()
                    key = (ename, "d", it.slot) if it.dma else (ename, "c")
                    inst.then_inc(sems[key], 16 if it.dma else 1)

        nc = self.nc

        @block.tensor
        def _(_e):
            run_engine("pe", nc.tensor)

        @block.vector
        def _(_e):
            run_engine("dve", nc.vector)

        @block.scalar
        def _(_e):
            run_engine("act", nc.scalar)

        @block.sync
        def _(_e):
            run_engine("sp", nc.sync)

        @block.gpsimd
        def _(_e):
            run_engine("gp", nc.gpsimd)


class RegTracker:
    def __init__(self, name):
        self.name = name
        self.regs = []

    def carve(self, start, end, tag=""):
        deps = []
        keep = []
        for (s, e, r) in self.regs:
            if s < end and start < e:
                if r.w is not None:
                    deps.append(r.w)
                deps.extend(r.r)
            else:
                keep.append((s, e, r))
        res = Res(self.name + ":" + tag)
        keep.append((start, end, res))
        self.regs = keep
        return Reg(res, deps)


class MMGroup:
    def __init__(self, k, out_ap, bank, n):
        self.k = k
        self.out_ap = out_ap
        self.bank = bank
        self.n = n
        self.i = 0
        self.prev = None

    def part(self, lhsT, rhs, reads, tp=None, extra_deps=(), out_ap=None):
        i, n = self.i, self.n
        assert i < n
        deps = list(extra_deps)
        if i == 0:
            if self.bank.w is not None:
                deps.append(self.bank.w)
            deps += self.bank.r
        else:
            deps.append(self.prev)
        fn = (lambda eng, o=(out_ap if out_ap is not None else self.out_ap),
              l=lhsT, r=rhs, st=(i == 0), sp=(i == n - 1),
              tp=tp: eng.matmul(o, l, r, start=st, stop=sp, tile_position=tp))
        self.prev = self.k.p.op("pe", fn, reads=list(reads), deps=deps)
        self.i += 1
        if self.i == self.n:
            self.bank.w = self.prev
            self.bank.r = []
        return self.prev


def mm(k, out_ap, bank, parts, tp=None):
    g = MMGroup(k, out_ap, bank, len(parts))
    for (l, r, rd) in parts:
        g.part(l, r, rd, tp=tp)
    return g.prev



SCR_COLS = 55296
F32S_COLS = 3072
LNT = 2048
BAND_W = 1280


class K:
    def __init__(self, nc, es):
        self.nc = nc
        self.p = Prog(nc)
        self.es = es
        self.inp = {}
        sb = lambda n, sh, dt: es.enter_context(nc.sbuf_tensor(n, sh, dt))
        pst = lambda n, sh: es.enter_context(nc.psum_tensor(n, sh, F32))
        self.acc = sb("acc", [128, NT * S], F32)
        self.acc_r = [Res("acc%d" % j) for j in range(NT)]
        self.rows = sb("rows", [1, 4096], F32)
        self.scr = sb("scr", [128, SCR_COLS], BF16)
        self.f32s = sb("f32s", [128, F32S_COLS], F32)
        self.id_f32 = sb("id_f32", [128, 128], F32)
        self.ones_col_bf = sb("ones_col_bf", [128, 1], BF16)
        self.ones_row_f32 = sb("ones_row_f32", [1, 128], F32)
        self.band = sb("band", [128, BAND_W], BF16)
        self.onehot8 = sb("onehot8", [8, 1024], BF16)
        self.misc = sb("misc", [128, 128], F32)
        self.PA = pst("PA", [128, 1024])
        self.PB = pst("PB", [128, 1024])
        self.PC = pst("PC", [128, 512])
        self.PD = pst("PD", [128, 512])
        self.PEb = pst("PEb", [128, 512])
        self.PF = pst("PF", [128, 512])
        self.bPA0, self.bPA1 = Res("PA0"), Res("PA1")
        self.bPB0, self.bPB1 = Res("PB0"), Res("PB1")
        self.bPC, self.bPD = Res("PC"), Res("PD")
        self.bPE, self.bPF = Res("PEb"), Res("PF")
        for n in ("id_f32", "ones_col_bf", "ones_row_f32", "band", "misc", "onehot8"):
            setattr(self, n + "_r", Res(n))
        self.scr_t = RegTracker("scr")
        self.f32_t = RegTracker("f32s")
        self.row_t = RegTracker("rows")
        self.misc_used = 0
        self.cvals = sb("cvals", [128, 32], F32)
        self.cvals_r = Res("cvals")
        self.cval_map = {}
        self.cast_rr = 0
        self.cload("c_cvals", [128, 32], self.cvals[:], self.cvals_r)
        self.cload("c_id_f32", [128, 128], self.id_f32[:], self.id_f32_r)
        self.cload("c_ones_col_bf", [128, 1], self.ones_col_bf[:], self.ones_col_bf_r, dt=BF16)
        self.cload("c_ones_row_f32", [1, 128], self.ones_row_f32[:], self.ones_row_f32_r)
        self.cload("c_band", [128, BAND_W], self.band[:], self.band_r, dt=BF16)
        self.cload("c_onehot8", [8, 1024], self.onehot8[:], self.onehot8_r, dt=BF16)

    def dram(self, name, shape, dt=F32, kind="ExternalInput"):
        if name not in self.inp:
            self.inp[name] = self.nc.dram_tensor(name, list(shape), dt, kind=kind).ap()
        return self.inp[name]

    def cv(self, val, p=128):
        val = float(val)
        if val not in self.cval_map:
            self.cval_map[val] = len(self.cval_map)
            assert len(self.cval_map) <= 32
        c = self.cval_map[val]
        return self.cvals[0:p, c:c + 1]

    def cload(self, name, shape, dst_ap, dst_res, dt=F32):
        ap = self.dram(name, shape, dt)
        return self.p.op("sp", lambda eng, o=dst_ap, i=ap: eng.dma_start(out=o, in_=i),
                         writes=[dst_res], dma=True)


def param_cols(k, name, n=D):
    nt = n // 128
    ap = k.dram(name, [n], F32)
    st = k.f32_t.carve(2824, 2952, name)
    k.p.op("sp", lambda eng, i=ap, nt=nt: eng.dma_start(
        out=k.f32s[0:nt, 2824:2824 + 128], in_=i.rearrange("(t p) -> t p", p=128)),
        writes=[st.res], deps=st.wdeps(), dma=True)
    k.p.op("pe", lambda eng, nt=nt: eng.transpose(
        k.PF[0:128, 0:nt], k.f32s[0:nt, 2824:2824 + 128], k.id_f32[0:nt, 0:nt]),
        reads=[st.res, k.id_f32_r], writes=[k.bPF])
    c0 = k.misc_used
    k.misc_used += nt
    assert k.misc_used <= 128
    col_res = Res(name + "_col")
    k.p.op("dve", lambda eng, c0=c0, nt=nt: eng.tensor_copy(
        k.misc[:, c0:c0 + nt], k.PF[0:128, 0:nt]),
        reads=[k.bPF], writes=[col_res])
    return c0, col_res


STAT_XC = 52224
STAT_SQ = 52736


def stats(k, eps):
    p, scr, rows = k.p, k.scr, k.rows
    mu = k.row_t.carve(0, 2048, "mu")
    rstd = k.row_t.carve(2048, 4096, "rstd")
    xc = k.scr_t.carve(STAT_XC, STAT_XC + 512, "statxc")
    sq = k.scr_t.carve(STAT_SQ, STAT_SQ + 512, "statsq")
    for c in range(NC):
        g1 = MMGroup(k, k.PC[0:1, 0:512], k.bPC, NT)
        g2 = MMGroup(k, k.PD[0:1, 0:512], k.bPD, NT)
        for j in range(NT):
            p.op("dve", lambda eng, j=j, c=c: eng.tensor_copy(
                scr[:, STAT_XC:STAT_XC + 512],
                k.acc[:, j * S + c * 512: j * S + (c + 1) * 512]),
                reads=[k.acc_r[j]], writes=[xc.res], deps=xc.wdeps())
            p.op("act", lambda eng: eng.activation(
                scr[:, STAT_SQ:STAT_SQ + 512], scr[:, STAT_XC:STAT_XC + 512], AF.Square),
                reads=[xc.res], writes=[sq.res], deps=sq.wdeps())
            g1.part(k.ones_col_bf[:], scr[:, STAT_XC:STAT_XC + 512], [xc.res, k.ones_col_bf_r])
            g2.part(k.ones_col_bf[:], scr[:, STAT_SQ:STAT_SQ + 512], [sq.res, k.ones_col_bf_r])
        p.op("dve", lambda eng, c=c: eng.tensor_scalar(
            rows[0:1, c * 512:(c + 1) * 512], k.PC[0:1, 0:512], 1.0 / D, None, ALU.mult),
            reads=[k.bPC], writes=[mu.res], deps=mu.wdeps())
        p.op("dve", lambda eng, c=c: eng.tensor_scalar(
            rows[0:1, 2048 + c * 512:2048 + (c + 1) * 512], k.PD[0:1, 0:512], 1.0 / D,
            None, ALU.mult),
            reads=[k.bPD], writes=[rstd.res], deps=rstd.wdeps())
    for c in range(NC):
        p.op("dve", lambda eng, c=c: eng.tensor_tensor(
            k.PC[0:1, 0:512], rows[0:1, c * 512:(c + 1) * 512],
            rows[0:1, c * 512:(c + 1) * 512], ALU.mult),
            reads=[mu.res], writes=[k.bPC])
        p.op("dve", lambda eng, c=c: eng.tensor_tensor(
            rows[0:1, 2048 + c * 512:2048 + (c + 1) * 512],
            rows[0:1, 2048 + c * 512:2048 + (c + 1) * 512], k.PC[0:1, 0:512], ALU.subtract),
            reads=[k.bPC, rstd.res], writes=[rstd.res])
    p.op("act", lambda eng, b=k.cv(eps, 1): eng.activation(
        rows[0:1, 2048:4096], rows[0:1, 2048:4096], AF.Ln, bias=b),
        reads=[rstd.res, k.cvals_r], writes=[rstd.res])
    p.op("act", lambda eng: eng.activation(
        rows[0:1, 2048:4096], rows[0:1, 2048:4096], AF.Exp, scale=-0.5),
        reads=[rstd.res], writes=[rstd.res])
    return mu.res, rstd.res


def ln_apply(k, mu_res, rstd_res, affine=None, out_reg=None, out_region=None):
    p, rows, scr = k.p, k.rows, k.scr
    lnt = k.f32_t.carve(LNT, LNT + 512, "lnt")
    for c in range(NC):
        mm(k, k.PEb[:, 0:512], k.bPE,
           [(k.ones_row_f32[:], rows[0:1, c * 512:(c + 1) * 512], [mu_res, k.ones_row_f32_r])])
        mm(k, k.PF[:, 0:512], k.bPF,
           [(k.ones_row_f32[:], rows[0:1, 2048 + c * 512:2048 + (c + 1) * 512],
             [rstd_res, k.ones_row_f32_r])])
        for j in range(NT):
            a_sl = (slice(None), slice(j * S + c * 512, j * S + (c + 1) * 512))
            p.op("dve", lambda eng, a=a_sl: eng.tensor_tensor(
                k.f32s[:, LNT:LNT + 512], k.acc[a], k.PEb[:, 0:512], ALU.subtract),
                reads=[k.acc_r[j], k.bPE], writes=[lnt.res], deps=lnt.wdeps())
            if affine is None:
                p.op("dve", lambda eng, a=a_sl: eng.tensor_tensor(
                    k.acc[a], k.f32s[:, LNT:LNT + 512], k.PF[:, 0:512], ALU.mult),
                    reads=[lnt.res, k.bPF], writes=[k.acc_r[j]])
            elif out_region is None:
                gc0, g_res, bc0, b_res = affine
                p.op("dve", lambda eng: eng.tensor_tensor(
                    k.f32s[:, LNT:LNT + 512], k.f32s[:, LNT:LNT + 512], k.PF[:, 0:512],
                    ALU.mult),
                    reads=[lnt.res, k.bPF], writes=[lnt.res])
                p.op("dve", lambda eng, a=a_sl, j=j, gc0=gc0, bc0=bc0: eng.tensor_scalar(
                    k.acc[a], k.f32s[:, LNT:LNT + 512], k.misc[:, gc0 + j:gc0 + j + 1],
                    k.misc[:, bc0 + j:bc0 + j + 1], ALU.mult, ALU.add),
                    reads=[lnt.res, g_res, b_res], writes=[k.acc_r[j]])
            else:
                gc0, g_res, bc0, b_res = affine
                p.op("dve", lambda eng: eng.tensor_tensor(
                    k.f32s[:, LNT:LNT + 512], k.f32s[:, LNT:LNT + 512], k.PF[:, 0:512],
                    ALU.mult),
                    reads=[lnt.res, k.bPF], writes=[lnt.res])
                o_sl = (slice(None),
                        slice(out_region + j * S + c * 512, out_region + j * S + (c + 1) * 512))
                p.op("dve", lambda eng, o=o_sl, j=j, gc0=gc0, bc0=bc0: eng.tensor_scalar(
                    scr[o], k.f32s[:, LNT:LNT + 512], k.misc[:, gc0 + j:gc0 + j + 1],
                    k.misc[:, bc0 + j:bc0 + j + 1], ALU.mult, ALU.add),
                    reads=[lnt.res, g_res, b_res], writes=[out_reg.res], deps=out_reg.wdeps())


def stream_cast(k, slot, f32_off, dram_ap, ncols, bf_off, bf_reg, crit=True):
    p = k.p
    p.op("sp", lambda eng, i=dram_ap, fo=f32_off, n=ncols: eng.dma_start(
        out=k.f32s[:, fo:fo + n], in_=i), writes=[slot.res], deps=slot.wdeps(), dma=True)
    if crit:
        eng_name = ("dve", "act")[k.cast_rr % 2]
        k.cast_rr += 1
    else:
        eng_name = "gp"
    if eng_name == "act":
        p.op("act", lambda eng, fo=f32_off, n=ncols, bo=bf_off: eng.activation(
            k.scr[:, bo:bo + n], k.f32s[:, fo:fo + n], AF.Copy),
            reads=[slot.res], writes=[bf_reg.res], deps=bf_reg.wdeps())
    else:
        p.op(eng_name, lambda eng, fo=f32_off, n=ncols, bo=bf_off: eng.tensor_copy(
            k.scr[:, bo:bo + n], k.f32s[:, fo:fo + n]),
            reads=[slot.res], writes=[bf_reg.res], deps=bf_reg.wdeps())


L1A_XN = 0
L1A_VG = 16384
L1A_QK = 24576
L1A_EXP = 52224
L1A_OT = 29696
L1A_VW = 46080
L1A_QKW = 50176


def emit_l1_attn(k):
    p, scr, rows = k.p, k.scr, k.rows
    g_c0, g_res = param_cols(k, "L1_attn_ln_gamma")
    b_c0, b_res = param_cols(k, "L1_attn_ln_beta")
    bo_c0, bo_res = param_cols(k, "L1_attn_b_out")
    qkw = k.dram("l1a_qkw", [8, 2, 128, 1024])
    vw = k.dram("l1a_vw", [2, 8, 128, 512])
    wo = k.dram("l1a_wo", [8, 128, 1024])

    mu_res, rstd_res = stats(k, 1e-5)
    xn = k.scr_t.carve(L1A_XN, L1A_XN + 16384, "xn")
    ln_apply(k, mu_res, rstd_res, affine=(g_c0, g_res, b_c0, b_res),
             out_reg=xn, out_region=L1A_XN)

    vg = k.scr_t.carve(L1A_VG, L1A_VG + 8192, "vg")
    ot = k.scr_t.carve(L1A_OT, L1A_OT + 16384, "ot")
    fslot = [k.f32_t.carve(0, 1024, "f0"), k.f32_t.carve(1024, 2048, "f1")]
    vwb = [k.scr_t.carve(L1A_VW + i * 512, L1A_VW + (i + 1) * 512, "vwb%d" % i)
           for i in range(8)]
    qkwb = [k.scr_t.carve(L1A_QKW + i * 1024, L1A_QKW + (i + 1) * 1024, "qkwb%d" % i)
            for i in range(2)]
    expb = [k.scr_t.carve(L1A_EXP + i * 512, L1A_EXP + (i + 1) * 512, "expb%d" % i)
            for i in range(4)]
    qk = k.scr_t.carve(L1A_QK, L1A_QK + 4096, "qk")

    for pair in range(8):
        g = pair // 4
        if pair % 4 == 0:
            for kt in range(NT):
                stream_cast(k, fslot[kt % 2], (kt % 2) * 1024, vw[g, kt], 512,
                            L1A_VW + kt * 512, vwb[kt])
            for t in range(16):
                gmm = MMGroup(k, k.PB[:, 512:1024], k.bPB1, NT)
                for kt in range(NT):
                    gmm.part(scr[:, L1A_XN + kt * S + t * 128: L1A_XN + kt * S + t * 128 + 128],
                             scr[:, L1A_VW + kt * 512: L1A_VW + (kt + 1) * 512],
                             [xn.res, vwb[kt].res])
                p.op("act", lambda eng, t=t: eng.activation(
                    scr[:, L1A_VG + t * 512: L1A_VG + (t + 1) * 512],
                    k.PB[:, 512:1024], AF.Copy),
                    reads=[k.bPB1], writes=[vg.res], deps=vg.wdeps())
        for qki in range(2):
            stream_cast(k, fslot[qki], qki * 1024, qkw[pair, qki], 1024,
                        L1A_QKW + qki * 1024, qkwb[qki])
            bank_ap = k.PEb[:, 0:512] if qki == 0 else k.PF[:, 0:512]
            bank = k.bPE if qki == 0 else k.bPF
            for c in range(NC):
                gmm = MMGroup(k, bank_ap, bank, NT)
                for kt in range(NT):
                    gmm.part(
                        scr[:, L1A_QKW + qki * 1024 + kt * 128: L1A_QKW + qki * 1024 + (kt + 1) * 128],
                        scr[:, L1A_XN + kt * S + c * 512: L1A_XN + kt * S + (c + 1) * 512],
                        [qkwb[qki].res, xn.res])
                p.op("act", lambda eng, qki=qki, c=c, b=bank_ap: eng.activation(
                    scr[:, L1A_QK + qki * 2048 + c * 512: L1A_QK + qki * 2048 + (c + 1) * 512],
                    b, AF.Copy),
                    reads=[bank], writes=[qk.res], deps=qk.wdeps())
        for qc in range(NC):
            nkb = 4 * qc + 4
            deng = []
            for h in range(2):
                denb = k.bPC if h == 0 else k.bPD
                denp = (k.PC if h == 0 else k.PD)[0:1, 0:512]
                deng.append(MMGroup(k, denp, denb, nkb))
            ot0 = MMGroup(k, k.PB[0:64, 0:512], k.bPB0, nkb)
            ot1 = MMGroup(k, k.PB[64:128, 512:1024], k.bPB1, nkb)

            def sc_bank(kb, h):
                if kb % 2 == 0:
                    return (k.PA[:, h * 512:(h + 1) * 512], k.bPA0 if h == 0 else k.bPA1)
                return ((k.PEb if h == 0 else k.PF)[:, 0:512], k.bPE if h == 0 else k.bPF)

            def emit_scores(kb):
                for h in range(2):
                    ei = (kb * 2 + h) % 4
                    eo = L1A_EXP + ei * 512
                    sb_ap, sb = sc_bank(kb, h)
                    mm(k, sb_ap, sb,
                       [(scr[64 * h:64 * h + 64, L1A_QK + 2048 + kb * 128: L1A_QK + 2048 + (kb + 1) * 128],
                         scr[64 * h:64 * h + 64, L1A_QK + qc * 512: L1A_QK + (qc + 1) * 512],
                         [qk.res])],
                       tp=(64 * h, 0))
                    p.op("act", lambda eng, eo=eo, sp=sb_ap, b=k.cv(-20.0): eng.activation(
                        scr[:, eo:eo + 512], sp, AF.Exp, bias=b, scale=0.125),
                        reads=[sb, k.cvals_r], writes=[expb[ei].res], deps=expb[ei].wdeps())
                    off = qc * 512 - kb * 128
                    if off < 512:
                        cb = off + 384
                        assert 0 <= cb and cb + 512 <= BAND_W
                        p.op("dve", lambda eng, eo=eo, cb=cb: eng.tensor_tensor(
                            scr[:, eo:eo + 512], scr[:, eo:eo + 512],
                            k.band[:, cb:cb + 512], ALU.mult),
                            reads=[expb[ei].res, k.band_r], writes=[expb[ei].res])

            emit_scores(0)
            for kb in range(nkb):
                if kb + 1 < nkb:
                    emit_scores(kb + 1)
                for h in range(2):
                    ei = (kb * 2 + h) % 4
                    eo = L1A_EXP + ei * 512
                    deng[h].part(k.ones_col_bf[:], scr[:, eo:eo + 512],
                                 [expb[ei].res, k.ones_col_bf_r])
                    hw = (pair % 4) * 2 + h
                    (ot0 if h == 0 else ot1).part(
                        scr[:, L1A_VG + kb * 512 + hw * 64: L1A_VG + kb * 512 + hw * 64 + 64],
                        scr[:, eo:eo + 512],
                        [vg.res, expb[ei].res], tp=(0, 64 * h))
            for h in range(2):
                den = k.row_t.carve(h * 512, (h + 1) * 512, "den%d" % h)
                bank = k.bPC if h == 0 else k.bPD
                p.op("dve", lambda eng, h=h, ps=(k.PC if h == 0 else k.PD): eng.tensor_copy(
                    rows[0:1, h * 512:(h + 1) * 512], ps[0:1, 0:512]),
                    reads=[bank], writes=[den.res], deps=den.wdeps())
                p.op("act", lambda eng, h=h: eng.activation(
                    rows[0:1, h * 512:(h + 1) * 512], rows[0:1, h * 512:(h + 1) * 512], AF.Ln),
                    reads=[den.res], writes=[den.res])
                p.op("act", lambda eng, h=h: eng.activation(
                    rows[0:1, h * 512:(h + 1) * 512], rows[0:1, h * 512:(h + 1) * 512],
                    AF.Exp, scale=-1.0),
                    reads=[den.res], writes=[den.res])
                rb_ap = k.PA[64 * h:64 * h + 64, h * 512:(h + 1) * 512]
                rb = k.bPA0 if h == 0 else k.bPA1
                mm(k, rb_ap, rb,
                   [(k.ones_row_f32[0:1, 0:64], rows[0:1, h * 512:(h + 1) * 512],
                     [den.res, k.ones_row_f32_r])], tp=(0, 64 * h))
                p.op("act", lambda eng, h=h: eng.activation(
                    scr[64 * h:64 * h + 64, L1A_EXP + h * 512: L1A_EXP + h * 512 + 512],
                    k.PB[64 * h:64 * h + 64, 512 * h:512 * h + 512], AF.Copy),
                    reads=[k.bPB0 if h == 0 else k.bPB1], writes=[expb[h].res])
                o_col = L1A_OT + pair * 2048 + qc * 512
                p.op("dve", lambda eng, h=h, o=o_col, rba=rb_ap: eng.tensor_tensor(
                    scr[64 * h:64 * h + 64, o:o + 512],
                    scr[64 * h:64 * h + 64, L1A_EXP + h * 512: L1A_EXP + h * 512 + 512],
                    rba, ALU.mult),
                    reads=[expb[h].res, rb], writes=[ot.res], deps=ot.wdeps())

    p.barrier()
    wob = [k.scr_t.carve(L1A_XN + kt * 1024, L1A_XN + (kt + 1) * 1024, "wob%d" % kt)
           for kt in range(NT)]
    for kt in range(NT):
        stream_cast(k, fslot[kt % 2], (kt % 2) * 1024, wo[kt], 1024,
                    L1A_XN + kt * 1024, wob[kt])
    for m in range(NT):
        for c in range(NC):
            par = (m * NC + c) % 2
            bank_ap = k.PA[:, par * 512:(par + 1) * 512]
            bank = k.bPA0 if par == 0 else k.bPA1
            gmm = MMGroup(k, bank_ap, bank, NT)
            for kt in range(NT):
                gmm.part(scr[:, L1A_XN + kt * 1024 + m * 128: L1A_XN + kt * 1024 + (m + 1) * 128],
                         scr[:, L1A_OT + kt * 2048 + c * 512: L1A_OT + kt * 2048 + (c + 1) * 512],
                         [wob[kt].res, ot.res])
            a_sl = (slice(None), slice(m * S + c * 512, m * S + (c + 1) * 512))
            p.op("dve", lambda eng, a=a_sl, m=m, b=bank_ap: eng.scalar_tensor_tensor(
                k.acc[a], b, k.misc[:, bo_c0 + m:bo_c0 + m + 1], k.acc[a], ALU.add, ALU.add),
                reads=[bank, bo_res, k.acc_r[m]], writes=[k.acc_r[m]])
    p.barrier()


def emit_outer_ln(k, eps=1e-5):
    mu_res, rstd_res = stats(k, eps)
    ln_apply(k, mu_res, rstd_res)
    k.p.barrier()



BLOCK_EMITTERS = {}
PREPPERS = {}


def build_program(blocks):
    nc = bass.Bass("TRN2", target_bir_lowering=False)
    with contextlib.ExitStack() as es:
        k = K(nc, es)
        xio = k.dram("xio", [D, S], F32)
        yio = k.dram("yio", [D, S], F32, kind="ExternalOutput")
        for j in range(NT):
            k.p.op("sp", lambda eng, j=j: eng.dma_start(
                out=k.acc[:, j * S:(j + 1) * S], in_=xio[j * 128:(j + 1) * 128, :]),
                writes=[k.acc_r[j]], dma=True)
        for b in blocks:
            BLOCK_EMITTERS[b](k)
            emit_outer_ln(k)
        out_toks = []
        for j in range(NT):
            t = k.p.op("sp", lambda eng, j=j: eng.dma_start(
                out=yio[j * 128:(j + 1) * 128, :], in_=k.acc[:, j * S:(j + 1) * S]),
                reads=[k.acc_r[j]], dma=True)
            out_toks.append(t)
        k.p.op("sp", lambda eng: eng.nop(), deps=out_toks)
        sems = {}
        with contextlib.ExitStack() as es2:
            for e in ENGS:
                sems[(e, "c")] = es2.enter_context(nc.semaphore("s_" + e))
                for i in range(NDMA):
                    sems[(e, "d", i)] = es2.enter_context(
                        nc.semaphore("sd_%s_%d" % (e, i)))
            block = es2.enter_context(nc.Block())
            k.p.emit(block, sems)
    return nc, k


BLOCK_EMITTERS["l1_attn"] = emit_l1_attn





def emit_top2_gate(k, lhsT_tile_fn, lhsT_res_fn, wg_sb, wg_res, bg_row, bg_res,
                   pexp_off, pexp_reg, scale_col_fn=None, scale_col_res=None):
    p, scr, rows = k.p, k.scr, k.rows
    gt = k.f32_t.carve(2560, 2760, "gate")
    GT = 2560
    for t in range(16):
        g = MMGroup(k, k.PC[0:128, 0:8], k.bPC, NT + (1 if bg_row is not None else 0))
        for kt in range(NT):
            g.part(lhsT_tile_fn(kt, t), wg_sb[:, kt * 8:(kt + 1) * 8],
                   [lhsT_res_fn(kt), wg_res])
        if bg_row is not None:
            g.part(k.ones_row_f32[:], bg_row, [bg_res, k.ones_row_f32_r])
        d = gt.wdeps()
        p.op("dve", lambda eng: eng.tensor_copy(k.f32s[:, GT:GT + 8], k.PC[0:128, 0:8]),
             reads=[k.bPC], writes=[gt.res], deps=d)
        if scale_col_fn is not None:
            p.op("dve", lambda eng, sc=scale_col_fn(t): eng.tensor_scalar(
                k.f32s[:, GT:GT + 8], k.f32s[:, GT:GT + 8], sc, None, ALU.mult),
                reads=[gt.res, scale_col_res], writes=[gt.res])
        p.op("dve", lambda eng: eng.tensor_reduce(
            k.f32s[:, GT + 8:GT + 9], k.f32s[:, GT:GT + 8], AX.X, ALU.max),
            reads=[gt.res], writes=[gt.res])
        p.op("dve", lambda eng: eng.tensor_scalar(
            k.f32s[:, GT + 16:GT + 24], k.f32s[:, GT:GT + 8],
            k.f32s[:, GT + 8:GT + 9], None, ALU.is_equal),
            reads=[gt.res], writes=[gt.res])
        p.op("dve", lambda eng: eng.scalar_tensor_tensor(
            k.f32s[:, GT + 32:GT + 40], k.f32s[:, GT + 16:GT + 24], -1.0e30,
            k.f32s[:, GT:GT + 8], ALU.mult, ALU.add),
            reads=[gt.res], writes=[gt.res])
        p.op("dve", lambda eng: eng.tensor_reduce(
            k.f32s[:, GT + 9:GT + 10], k.f32s[:, GT + 32:GT + 40], AX.X, ALU.max),
            reads=[gt.res], writes=[gt.res])
        p.op("dve", lambda eng: eng.tensor_scalar(
            k.f32s[:, GT + 24:GT + 32], k.f32s[:, GT + 32:GT + 40],
            k.f32s[:, GT + 9:GT + 10], None, ALU.is_equal),
            reads=[gt.res], writes=[gt.res])
        p.op("dve", lambda eng: eng.tensor_tensor(
            k.f32s[:, GT + 10:GT + 11], k.f32s[:, GT + 8:GT + 9],
            k.f32s[:, GT + 9:GT + 10], ALU.subtract),
            reads=[gt.res], writes=[gt.res])
        p.op("act", lambda eng: eng.activation(
            k.f32s[:, GT + 11:GT + 12], k.f32s[:, GT + 10:GT + 11], AF.Sigmoid),
            reads=[gt.res], writes=[gt.res])
        p.op("act", lambda eng: eng.activation(
            k.f32s[:, GT + 12:GT + 13], k.f32s[:, GT + 10:GT + 11], AF.Sigmoid,
            scale=-1.0),
            reads=[gt.res], writes=[gt.res])
        p.op("dve", lambda eng: eng.tensor_scalar(
            k.f32s[:, GT + 16:GT + 24], k.f32s[:, GT + 16:GT + 24],
            k.f32s[:, GT + 11:GT + 12], None, ALU.mult),
            reads=[gt.res], writes=[gt.res])
        p.op("dve", lambda eng: eng.tensor_scalar(
            k.f32s[:, GT + 24:GT + 32], k.f32s[:, GT + 24:GT + 32],
            k.f32s[:, GT + 12:GT + 13], None, ALU.mult),
            reads=[gt.res], writes=[gt.res])
        p.op("dve", lambda eng: eng.tensor_tensor(
            k.f32s[:, GT + 40:GT + 48], k.f32s[:, GT + 16:GT + 24],
            k.f32s[:, GT + 24:GT + 32], ALU.add),
            reads=[gt.res], writes=[gt.res])
        p.op("pe", lambda eng: eng.transpose(
            k.PF[0:8, 0:128], k.f32s[:, GT + 40:GT + 48], k.id_f32[:]),
            reads=[gt.res, k.id_f32_r], writes=[k.bPF])
        p.op("dve", lambda eng, t=t: eng.tensor_copy(
            scr[0:8, pexp_off + t * 128: pexp_off + (t + 1) * 128], k.PF[0:8, 0:128]),
            reads=[k.bPF], writes=[pexp_reg.res], deps=pexp_reg.wdeps())


L0F_XB = 0
L0F_XP = 8192
L0F_H = 16384
L0F_PEXP = 49152
L0F_PB = 51200
L0F_W13 = 53248


def emit_l0_ffn(k):
    import os
    dbg_v = os.environ.get("L0F_DEBUG", "0")
    dbg = dbg_v == "1"
    halves = [1] if dbg_v in ("1", "2", "3") else range(2)
    experts = ([7] if dbg_v in ("1", "4") else [6, 7] if dbg_v == "3" else range(8))
    p, scr, rows = k.p, k.scr, k.rows
    w1d = k.dram("l0f_w1", [8, 32, 128, 1024])
    w3d = k.dram("l0f_w3", [8, 32, 128, 1024])
    w2d = k.dram("l0f_w2", [8, 8, 128, 4096])
    wgd = k.dram("l0f_wg", [128, 64])
    bgd = k.dram("L0_ffn_bg", [8])

    wgr = k.f32_t.carve(2760, 2824, "wg")
    p.op("sp", lambda eng: eng.dma_start(out=k.f32s[:, 2760:2824], in_=wgd[:]),
         writes=[wgr.res], deps=wgr.wdeps(), dma=True)
    bgr = k.row_t.carve(3072, 3080, "bg")
    p.op("sp", lambda eng: eng.dma_start(
        out=rows[0:1, 3072:3080], in_=bgd.rearrange("(o e) -> o e", o=1)),
        writes=[bgr.res], deps=bgr.wdeps(), dma=True)

    pexp = k.scr_t.carve(L0F_PEXP, L0F_PEXP + 2048, "pexp")
    emit_top2_gate(k,
                   lambda kt, t: k.acc[:, kt * S + t * 128: kt * S + (t + 1) * 128],
                   lambda kt: k.acc_r[kt],
                   k.f32s[:, 2760:2824], wgr.res,
                   rows[0:1, 3072:3080], bgr.res,
                   L0F_PEXP, pexp)

    pb = k.scr_t.carve(L0F_PB, L0F_PB + 2048, "pb")
    for half in halves:
        hb = half * 1024
        xb = k.scr_t.carve(L0F_XB, L0F_XB + 8192, "xb%d" % half)
        for kt in range(NT):
            p.op("dve", lambda eng, kt=kt, hb=hb: eng.tensor_copy(
                scr[:, L0F_XB + kt * 1024: L0F_XB + (kt + 1) * 1024],
                k.acc[:, kt * S + hb: kt * S + hb + 1024]),
                reads=[k.acc_r[kt]], writes=[xb.res], deps=xb.wdeps())
            p.op("dve", lambda eng, kt=kt, hb=hb: eng.tensor_scalar(
                k.acc[:, kt * S + hb: kt * S + hb + 1024],
                k.acc[:, kt * S + hb: kt * S + hb + 1024], 2.0, None, ALU.mult),
                reads=[xb.res], writes=[k.acc_r[kt]])
        for e in experts:
            for c in range(2):
                mm(k, k.PEb[:, 0:512], k.bPE,
                   [(k.onehot8[0:8, e * 128:(e + 1) * 128],
                     scr[0:8, L0F_PEXP + hb + c * 512: L0F_PEXP + hb + (c + 1) * 512],
                     [pexp.res, k.onehot8_r])])
                p.op("dve", lambda eng, c=c: eng.tensor_copy(
                    scr[:, L0F_PB + c * 512: L0F_PB + (c + 1) * 512], k.PEb[:, 0:512]),
                    reads=[k.bPE], writes=[pb.res], deps=pb.wdeps())
            xp = [k.scr_t.carve(L0F_XP + kt * 1024, L0F_XP + (kt + 1) * 1024,
                                "xp%d_%d_%d" % (half, e, kt)) for kt in range(NT)]
            for kt in range(NT):
                p.op("dve", lambda eng, kt=kt: eng.tensor_tensor(
                    scr[:, L0F_XP + kt * 1024: L0F_XP + (kt + 1) * 1024],
                    scr[:, L0F_XB + kt * 1024: L0F_XB + (kt + 1) * 1024],
                    scr[:, L0F_PB: L0F_PB + 1024], ALU.mult),
                    reads=[xb.res, pb.res], writes=[xp[kt].res], deps=xp[kt].wdeps())
            w1r = [k.scr_t.carve(L0F_W13 + i * 512, L0F_W13 + (i + 1) * 512, "w1r%d" % i)
                   for i in range(2)]
            w3r = [k.scr_t.carve(L0F_W13 + 1024 + i * 512, L0F_W13 + 1024 + (i + 1) * 512,
                                 "w3r%d" % i) for i in range(2)]
            fs4 = [k.f32_t.carve(i * 512, (i + 1) * 512, "fs%d" % i) for i in range(4)]
            hreg = k.scr_t.carve(L0F_H, L0F_H + 32768, "h%d_%d" % (half, e))
            for ht in range(32):
                for hc in range(2):
                    stream_cast(k, fs4[hc], hc * 512, w1d[e, ht, :, hc * 512:(hc + 1) * 512],
                                512, L0F_W13 + hc * 512, w1r[hc])
                    stream_cast(k, fs4[2 + hc], 1024 + hc * 512,
                                w3d[e, ht, :, hc * 512:(hc + 1) * 512],
                                512, L0F_W13 + 1024 + hc * 512, w3r[hc])
                par = ht % 2
                if par == 0:
                    ub = [(k.PA[:, 0:512], k.bPA0), (k.PA[:, 512:1024], k.bPA1)]
                    vb = [(k.PB[:, 0:512], k.bPB0), (k.PB[:, 512:1024], k.bPB1)]
                else:
                    ub = [(k.PC[0:128, 0:512], k.bPC), (k.PD[0:128, 0:512], k.bPD)]
                    vb = [(k.PEb[0:128, 0:512], k.bPE), (k.PF[0:128, 0:512], k.bPF)]
                for c in range(2):
                    gu = MMGroup(k, ub[c][0], ub[c][1], NT)
                    for kt in range(NT):
                        gu.part(scr[:, L0F_W13 + kt * 128: L0F_W13 + (kt + 1) * 128],
                                scr[:, L0F_XP + kt * 1024 + c * 512: L0F_XP + kt * 1024 + (c + 1) * 512],
                                [w1r[kt // 4].res, xp[kt].res])
                    gv = MMGroup(k, vb[c][0], vb[c][1], NT)
                    for kt in range(NT):
                        gv.part(scr[:, L0F_W13 + 1024 + kt * 128: L0F_W13 + 1024 + (kt + 1) * 128],
                                scr[:, L0F_XP + kt * 1024 + c * 512: L0F_XP + kt * 1024 + (c + 1) * 512],
                                [w3r[kt // 4].res, xp[kt].res])
                hcol = L0F_H + ht * 1024
                for c in range(2):
                    p.op("act", lambda eng, hcol=hcol, c=c, ua=ub[c][0]: eng.activation(
                        scr[:, hcol + c * 512:hcol + (c + 1) * 512], ua, AF.Silu),
                        reads=[ub[c][1]], writes=[hreg.res], deps=hreg.wdeps())
                    p.op("dve", lambda eng, hcol=hcol, c=c, va=vb[c][0]: eng.tensor_tensor(
                        scr[:, hcol + c * 512:hcol + (c + 1) * 512],
                        scr[:, hcol + c * 512:hcol + (c + 1) * 512], va, ALU.mult),
                        reads=[vb[c][1], hreg.res], writes=[hreg.res])
            if dbg:
                continue
            w2b = [k.scr_t.carve(L0F_XP + i * 1024, L0F_XP + (i + 1) * 1024, "w2b%d" % i)
                   for i in range(4)]
            w2f = [k.f32_t.carve(i * 1024, (i + 1) * 1024, "w2f%d" % i) for i in range(2)]
            for d2 in range(NT):
                for q in range(4):
                    stream_cast(k, w2f[q % 2], (q % 2) * 1024,
                                w2d[e, d2, :, q * 1024:(q + 1) * 1024],
                                1024, L0F_XP + q * 1024, w2b[q], crit=False)
                if d2 % 2 == 0:
                    yb = [(k.PA[:, 0:512], k.bPA0), (k.PA[:, 512:1024], k.bPA1)]
                else:
                    yb = [(k.PB[:, 0:512], k.bPB0), (k.PB[:, 512:1024], k.bPB1)]
                for c in range(2):
                    gy = MMGroup(k, yb[c][0], yb[c][1], 32)
                    for kt in range(32):
                        gy.part(scr[:, L0F_XP + kt * 128: L0F_XP + (kt + 1) * 128],
                                scr[:, L0F_H + kt * 1024 + c * 512: L0F_H + kt * 1024 + (c + 1) * 512],
                                [w2b[kt // 8].res, hreg.res])
                    p.op("dve", lambda eng, d2=d2, c=c, hb=hb, ps=yb[c][0]:
                         eng.tensor_tensor(
                        k.acc[:, d2 * S + hb + c * 512: d2 * S + hb + (c + 1) * 512],
                        k.acc[:, d2 * S + hb + c * 512: d2 * S + hb + (c + 1) * 512],
                        ps, ALU.add),
                        reads=[yb[c][1], k.acc_r[d2]],
                        writes=[k.acc_r[d2]])
    p.barrier()


BLOCK_EMITTERS["l0_ffn"] = emit_l0_ffn


def prep_l0_ffn(inputs):
    W1 = np.asarray(inputs["L0_ffn_W1"], np.float32)
    W3 = np.asarray(inputs["L0_ffn_W3"], np.float32)
    W2 = np.asarray(inputs["L0_ffn_W2"], np.float32)
    w1t = np.ascontiguousarray(
        W1.reshape(8, 8, 128, 32, 128).transpose(0, 3, 2, 1, 4).reshape(8, 32, 128, 1024))
    w3t = np.ascontiguousarray(
        W3.reshape(8, 8, 128, 32, 128).transpose(0, 3, 2, 1, 4).reshape(8, 32, 128, 1024))
    w2t = np.ascontiguousarray(
        W2.reshape(8, 32, 128, 8, 128).transpose(0, 3, 2, 1, 4).reshape(8, 8, 128, 4096))
    Wg = np.asarray(inputs["L0_ffn_Wg"], np.float32)
    wgt = np.ascontiguousarray(Wg.reshape(8, 128, 8).transpose(1, 0, 2).reshape(128, 64))
    return {"l0f_w1": w1t, "l0f_w3": w3t, "l0f_w2": w2t, "l0f_wg": wgt,
            "L0_ffn_bg": np.asarray(inputs["L0_ffn_bg"], np.float32)}


PREPPERS["l0_ffn"] = prep_l0_ffn


def prep_l1_ffn(inputs):
    wd = np.asarray(inputs["L1_ffn_w_down"], np.float32)
    wq = np.asarray(inputs["L1_ffn_wq"], np.float32)
    wk = np.asarray(inputs["L1_ffn_wk"], np.float32)
    wv = np.asarray(inputs["L1_ffn_wv"], np.float32)
    wu = np.asarray(inputs["L1_ffn_w_up"], np.float32)
    rw = np.asarray(inputs["L1_ffn_router_w"], np.float32)
    w1 = np.asarray(inputs["L1_ffn_w1"], np.float32)
    w3 = np.asarray(inputs["L1_ffn_w3"], np.float32)
    w2 = np.asarray(inputs["L1_ffn_w2"], np.float32)
    perm = np.empty(512, np.int64)
    for h in range(16):
        for f in range(16):
            perm[16 * h + f] = 32 * h + 2 * f
            perm[256 + 16 * h + f] = 32 * h + 2 * f + 1
    wq_p = wq[:, perm]
    wk_p = wk[:, perm]
    tile4 = lambda w, M: np.ascontiguousarray(
        w.reshape(w.shape[0] // 128, 128, M, 128).transpose(2, 1, 0, 3)
        .reshape(M, 128, (w.shape[0] // 128) * 128))
    out = {
        "l1f_wd": tile4(wd, 4),
        "l1f_wq": tile4(wq_p, 4),
        "l1f_wk": tile4(wk_p, 4),
        "L1_ffn_wv": wv,
        "l1f_wu": tile4(wu, 8),
        "l1f_rw": np.ascontiguousarray(rw.reshape(8, 128, 8).transpose(1, 0, 2).reshape(128, 64)),
        "l1f_w1": np.ascontiguousarray(
            w1.reshape(8, 8, 128, 16, 128).transpose(0, 3, 2, 1, 4).reshape(8, 16, 128, 1024)),
        "l1f_w3": np.ascontiguousarray(
            w3.reshape(8, 8, 128, 16, 128).transpose(0, 3, 2, 1, 4).reshape(8, 16, 128, 1024)),
        "l1f_w2": np.ascontiguousarray(
            w2.reshape(8, 16, 128, 8, 128).transpose(0, 3, 2, 1, 4).reshape(8, 8, 128, 2048)),
        "L1_ffn_g1": np.asarray(inputs["L1_ffn_g1"], np.float32),
        "L1_ffn_g2": np.asarray(inputs["L1_ffn_g2"], np.float32),
    }
    f = np.arange(16)
    inv = 10000.0 ** (-(2.0 * f) / 32.0)
    ang = np.arange(2048)[None, :] * np.tile(inv, 8)[:, None]
    out["l1f_cos"] = np.cos(ang).astype(BF)
    out["l1f_sin"] = np.sin(ang).astype(BF)
    sel = np.zeros((128, 512), BF)
    for par in range(2):
        for j in range(4):
            for t in range(16):
                sel[64 * par + 16 * j + t, par * 128 + 32 * j + t] = 1
                sel[64 * par + 16 * j + t, (2 + par) * 128 + 32 * j + 16 + t] = 1
    out["l1f_sel"] = sel
    return out


PREPPERS["l1_ffn"] = prep_l1_ffn



L1F_H1 = 0
L1F_CT = 16384
L1F_Q = 24576
L1F_K = 32768
L1F_V = 40960
L1F_OT = 16384
L1F_COS = 0
L1F_SIN = 2048
L1F_SEL = 4096
L1F_RTMP = 4608
L1F_RR = 6656
L1F_WST = 53248


def emit_rms_rows(k, eps, src_fn=None, src_res=None, bf_src=False):
    p, scr, rows = k.p, k.scr, k.rows
    rstd = k.row_t.carve(2048, 4096, "rmsr")
    xc = k.scr_t.carve(STAT_XC, STAT_XC + 512, "rmsxc")
    sq = k.scr_t.carve(STAT_SQ, STAT_SQ + 512, "rmssq")
    for c in range(NC):
        g2 = MMGroup(k, k.PD[0:1, 0:512], k.bPD, NT)
        for j in range(NT):
            if not bf_src:
                p.op("dve", lambda eng, j=j, c=c: eng.tensor_copy(
                    scr[:, STAT_XC:STAT_XC + 512],
                    src_fn(j)[:, c * 512:(c + 1) * 512]),
                    reads=[src_res] if src_res else [k.acc_r[j]], writes=[xc.res],
                    deps=xc.wdeps())
                sqsrc = scr[:, STAT_XC:STAT_XC + 512]
                sqread = xc.res
            else:
                sqsrc = src_fn(j)[:, c * 512:(c + 1) * 512]
                sqread = src_res
            p.op("act", lambda eng, s=sqsrc: eng.activation(
                scr[:, STAT_SQ:STAT_SQ + 512], s, AF.Square),
                reads=[sqread], writes=[sq.res], deps=sq.wdeps())
            g2.part(k.ones_col_bf[:], scr[:, STAT_SQ:STAT_SQ + 512],
                    [sq.res, k.ones_col_bf_r])
        p.op("dve", lambda eng, c=c: eng.tensor_scalar(
            rows[0:1, 2048 + c * 512:2048 + (c + 1) * 512], k.PD[0:1, 0:512],
            1.0 / D, None, ALU.mult),
            reads=[k.bPD], writes=[rstd.res], deps=rstd.wdeps())
    p.op("act", lambda eng, b=k.cv(eps, 1): eng.activation(
        rows[0:1, 2048:4096], rows[0:1, 2048:4096], AF.Ln, bias=b),
        reads=[rstd.res, k.cvals_r], writes=[rstd.res])
    p.op("act", lambda eng: eng.activation(
        rows[0:1, 2048:4096], rows[0:1, 2048:4096], AF.Exp, scale=-0.5),
        reads=[rstd.res], writes=[rstd.res])
    return rstd


def emit_l1_ffn(k):
    import os
    stop = os.environ.get("L1F_STOP", "")
    p, scr, rows = k.p, k.scr, k.rows
    g1_c0, g1_res = param_cols(k, "L1_ffn_g1")
    g2_c0, g2_res = param_cols(k, "L1_ffn_g2")
    wdd = k.dram("l1f_wd", [4, 128, 1024])
    wqd = k.dram("l1f_wq", [4, 128, 512])
    wkd = k.dram("l1f_wk", [4, 128, 512])
    wvd = k.dram("L1_ffn_wv", [512, 512])
    wud = k.dram("l1f_wu", [8, 128, 512])
    rwd = k.dram("l1f_rw", [128, 64])
    w1d = k.dram("l1f_w1", [8, 16, 128, 1024])
    w3d = k.dram("l1f_w3", [8, 16, 128, 1024])
    w2d = k.dram("l1f_w2", [8, 8, 128, 2048])
    cosd = k.dram("l1f_cos", [128, 2048], BF16)
    sind = k.dram("l1f_sin", [128, 2048], BF16)
    seld = k.dram("l1f_sel", [128, 512], BF16)

    fsl = [k.f32_t.carve(0, 1024, "lf0"), k.f32_t.carve(1024, 2048, "lf1")]

    rstd = emit_rms_rows(k, 1e-5, src_fn=lambda j: k.acc[:, j * S:(j + 1) * S])
    h1 = k.scr_t.carve(L1F_H1, L1F_H1 + 16384, "h1")
    for c in range(NC):
        mm(k, k.PF[:, 0:512], k.bPF,
           [(k.ones_row_f32[:], rows[0:1, 2048 + c * 512:2048 + (c + 1) * 512],
             [rstd.res, k.ones_row_f32_r])])
        for j in range(NT):
            p.op("dve", lambda eng, j=j, c=c, gc=g1_c0: eng.scalar_tensor_tensor(
                scr[:, L1F_H1 + j * S + c * 512: L1F_H1 + j * S + (c + 1) * 512],
                k.acc[:, j * S + c * 512: j * S + (c + 1) * 512],
                k.misc[:, gc + j:gc + j + 1], k.PF[:, 0:512], ALU.mult, ALU.mult),
                reads=[k.acc_r[j], g1_res, k.bPF], writes=[h1.res], deps=h1.wdeps())

    ct = k.scr_t.carve(L1F_CT, L1F_CT + 4096, "ct")
    wst = k.scr_t.carve(L1F_WST, L1F_WST + 1024, "wst")
    for m in range(4):
        stream_cast(k, fsl[m % 2], (m % 2) * 1024, wdd[m], 1024, L1F_WST, wst)
        for c in range(NC):
            g = MMGroup(k, k.PA[:, 0:512], k.bPA0, NT)
            for kt in range(NT):
                g.part(scr[:, L1F_WST + kt * 128: L1F_WST + (kt + 1) * 128],
                       scr[:, L1F_H1 + kt * S + c * 512: L1F_H1 + kt * S + (c + 1) * 512],
                       [wst.res, h1.res])
            p.op("act", lambda eng, m=m, c=c: eng.activation(
                scr[:, L1F_CT + m * S + c * 512: L1F_CT + m * S + (c + 1) * 512],
                k.PA[:, 0:512], AF.Copy),
                reads=[k.bPA0], writes=[ct.res], deps=ct.wdeps())

    qraw = k.scr_t.carve(L1F_Q, L1F_Q + 8192, "qraw")
    kraw = k.scr_t.carve(L1F_K, L1F_K + 8192, "kraw")
    for qk, (dram_w, reg, base) in enumerate([(wqd, qraw, L1F_Q), (wkd, kraw, L1F_K)]):
        for m in range(4):
            wst = k.scr_t.carve(L1F_WST, L1F_WST + 512, "wqk%d_%d" % (qk, m))
            stream_cast(k, fsl[m % 2], (m % 2) * 1024, dram_w[m], 512, L1F_WST, wst)
            for c in range(NC):
                g = MMGroup(k, k.PA[:, 0:512], k.bPA0, 4)
                for kt in range(4):
                    g.part(scr[:, L1F_WST + kt * 128: L1F_WST + (kt + 1) * 128],
                           scr[:, L1F_CT + kt * S + c * 512: L1F_CT + kt * S + (c + 1) * 512],
                           [wst.res, ct.res])
                p.op("act", lambda eng, b=base, m=m, c=c: eng.activation(
                    scr[:, b + m * S + c * 512: b + m * S + (c + 1) * 512],
                    k.PA[:, 0:512], AF.Copy),
                    reads=[k.bPA0], writes=[reg.res], deps=reg.wdeps())
    vreg = k.scr_t.carve(L1F_V, L1F_V + 8192, "v")
    wstv = [k.scr_t.carve(L1F_WST + i * 512, L1F_WST + (i + 1) * 512, "wv%d" % i)
            for i in range(4)]
    for kt in range(4):
        stream_cast(k, fsl[kt % 2], (kt % 2) * 1024, wvd[kt * 128:(kt + 1) * 128, :],
                    512, L1F_WST + kt * 512, wstv[kt])
    for t in range(16):
        g = MMGroup(k, k.PB[:, 0:512], k.bPB0, 4)
        for kt in range(4):
            g.part(scr[:, L1F_CT + kt * S + t * 128: L1F_CT + kt * S + t * 128 + 128],
                   scr[:, L1F_WST + kt * 512: L1F_WST + (kt + 1) * 512],
                   [ct.res, wstv[kt].res])
        p.op("act", lambda eng, t=t: eng.activation(
            scr[:, L1F_V + t * 512: L1F_V + (t + 1) * 512], k.PB[:, 0:512], AF.Copy),
            reads=[k.bPB0], writes=[vreg.res], deps=vreg.wdeps())

    if stop == "p3":
        return
    cos = k.scr_t.carve(L1F_COS, L1F_COS + 2048, "cos")
    sin = k.scr_t.carve(L1F_SIN, L1F_SIN + 2048, "sin")
    sel = k.scr_t.carve(L1F_SEL, L1F_SEL + 512, "sel")
    p.op("sp", lambda eng: eng.dma_start(out=scr[:, L1F_COS:L1F_COS + 2048], in_=cosd[:]),
         writes=[cos.res], deps=cos.wdeps(), dma=True)
    p.op("sp", lambda eng: eng.dma_start(out=scr[:, L1F_SIN:L1F_SIN + 2048], in_=sind[:]),
         writes=[sin.res], deps=sin.wdeps(), dma=True)
    p.op("sp", lambda eng: eng.dma_start(out=scr[:, L1F_SEL:L1F_SEL + 512], in_=seld[:]),
         writes=[sel.res], deps=sel.wdeps(), dma=True)
    rtmp = k.scr_t.carve(L1F_RTMP, L1F_RTMP + 2048, "rtmp")
    hc_regs = []
    for qk, (raw_base, raw_reg) in enumerate([(L1F_Q, qraw), (L1F_K, kraw)]):
        rr1 = [k.scr_t.carve(L1F_RR + i * S, L1F_RR + (i + 1) * S, "rr1_%d_%d" % (qk, i))
               for i in range(2)]
        rr2 = [k.scr_t.carve(L1F_RR + (2 + i) * S, L1F_RR + (3 + i) * S, "rr2_%d_%d" % (qk, i))
               for i in range(2)]
        rtm0 = k.scr_t.carve(L1F_RTMP, L1F_RTMP + S, "rtm%d" % qk)
        rtm = [rtm0, rtm0]
        for i in range(2):
            q1 = (slice(None), slice(raw_base + i * S, raw_base + (i + 1) * S))
            q2 = (slice(None), slice(raw_base + (2 + i) * S, raw_base + (3 + i) * S))
            r1 = (slice(None), slice(L1F_RR + i * S, L1F_RR + (i + 1) * S))
            r2 = (slice(None), slice(L1F_RR + (2 + i) * S, L1F_RR + (3 + i) * S))
            tm = (slice(None), slice(L1F_RTMP, L1F_RTMP + S))
            p.op("dve", lambda eng, a=q1, o=r1: eng.tensor_tensor(
                scr[o], scr[a], scr[:, L1F_COS:L1F_COS + 2048], ALU.mult),
                reads=[raw_reg.res, cos.res], writes=[rr1[i].res], deps=rr1[i].wdeps())
            p.op("dve", lambda eng, a=q1, o=tm: eng.tensor_tensor(
                scr[o], scr[a], scr[:, L1F_SIN:L1F_SIN + 2048], ALU.mult),
                reads=[raw_reg.res, sin.res], writes=[rtm[i].res], deps=rtm[i].wdeps())
            p.op("dve", lambda eng, a=q2, o=r2: eng.tensor_tensor(
                scr[o], scr[a], scr[:, L1F_SIN:L1F_SIN + 2048], ALU.mult),
                reads=[raw_reg.res, sin.res], writes=[rr2[i].res], deps=rr2[i].wdeps())
            p.op("dve", lambda eng, o=r1, b=r2: eng.tensor_tensor(
                scr[o], scr[o], scr[b], ALU.subtract),
                reads=[rr1[i].res, rr2[i].res], writes=[rr1[i].res])
            p.op("dve", lambda eng, a=q2, o=r2: eng.tensor_tensor(
                scr[o], scr[a], scr[:, L1F_COS:L1F_COS + 2048], ALU.mult),
                reads=[raw_reg.res, rr2[i].res], writes=[rr2[i].res])
            p.op("dve", lambda eng, o=r2, b=tm: eng.tensor_tensor(
                scr[o], scr[o], scr[b], ALU.add),
                reads=[rr2[i].res, rtm[i].res], writes=[rr2[i].res])
        hc = k.scr_t.carve(raw_base, raw_base + 8192, "hc%d" % qk)
        for i in range(4):
            par = i % 2
            for c in range(NC):
                g = MMGroup(k, k.PA[:, 0:512], k.bPA0, 2)
                g.part(scr[:, L1F_SEL + par * 128: L1F_SEL + (par + 1) * 128],
                       scr[:, L1F_RR + (i // 2) * S + c * 512: L1F_RR + (i // 2) * S + (c + 1) * 512],
                       [sel.res, rr1[i // 2].res])
                g.part(scr[:, L1F_SEL + (2 + par) * 128: L1F_SEL + (3 + par) * 128],
                       scr[:, L1F_RR + (2 + i // 2) * S + c * 512: L1F_RR + (2 + i // 2) * S + (c + 1) * 512],
                       [sel.res, rr2[i // 2].res])
                p.op("act", lambda eng, b=raw_base, i=i, c=c: eng.activation(
                    scr[:, b + i * S + c * 512: b + i * S + (c + 1) * 512],
                    k.PA[:, 0:512], AF.Copy),
                    reads=[k.bPA0], writes=[hc.res], deps=hc.wdeps())
        hc_regs.append(hc)
    qr, kr = hc_regs

    ot = k.scr_t.carve(L1F_OT, L1F_OT + 8192, "ot")
    expb = [k.scr_t.carve(L1F_RR + i * 512, L1F_RR + (i + 1) * 512, "l1fexp%d" % i)
            for i in range(4)]
    SC32 = float(1.0 / np.sqrt(32.0))
    for pr in range(8):
        tile = pr // 2
        for qc in range(NC):
            nkb = 4 * qc + 4
            deng = []
            otg = []
            for h in range(2):
                H = 2 * pr + h
                base = 32 * (H % 4)
                denb = k.bPC if h == 0 else k.bPD
                deng.append(MMGroup(k, (k.PC if h == 0 else k.PD)[0:1, 0:512],
                                    denb, nkb))
                og = MMGroup(k, (k.PEb if h == 0 else k.PF)[base:base + 32, 0:512],
                             k.bPE if h == 0 else k.bPF, nkb)
                otg.append(og)
            def sc_bank2(kb, h):
                if kb % 2 == 0:
                    return (k.PA[:, h * 512:(h + 1) * 512], k.bPA0 if h == 0 else k.bPA1)
                return (k.PB[:, h * 512:(h + 1) * 512], k.bPB0 if h == 0 else k.bPB1)

            def emit_scores2(kb):
                for h in range(2):
                    H = 2 * pr + h
                    base = 32 * (H % 4)
                    ei = (kb * 2 + h) % 4
                    eo = L1F_RR + ei * 512
                    sb_ap, sb = sc_bank2(kb, h)
                    mm(k, sb_ap, sb,
                       [(scr[base:base + 32, L1F_K + tile * S + kb * 128: L1F_K + tile * S + (kb + 1) * 128],
                         scr[base:base + 32, L1F_Q + tile * S + qc * 512: L1F_Q + tile * S + (qc + 1) * 512],
                         [qr.res, kr.res])],
                       tp=(base, 0))
                    p.op("act", lambda eng, eo=eo, sp_=sb_ap, b=k.cv(-12.0): eng.activation(
                        scr[:, eo:eo + 512], sp_, AF.Exp, bias=b, scale=SC32),
                        reads=[sb, k.cvals_r], writes=[expb[ei].res], deps=expb[ei].wdeps())
                    off = qc * 512 - kb * 128
                    if off < 512:
                        cb = off + 384
                        p.op("dve", lambda eng, eo=eo, cb=cb: eng.tensor_tensor(
                            scr[:, eo:eo + 512], scr[:, eo:eo + 512],
                            k.band[:, cb:cb + 512], ALU.mult),
                            reads=[expb[ei].res, k.band_r], writes=[expb[ei].res])

            emit_scores2(0)
            for kb in range(nkb):
                if kb + 1 < nkb:
                    emit_scores2(kb + 1)
                for h in range(2):
                    H = 2 * pr + h
                    base = 32 * (H % 4)
                    ei = (kb * 2 + h) % 4
                    eo = L1F_RR + ei * 512
                    deng[h].part(k.ones_col_bf[:], scr[:, eo:eo + 512],
                                 [expb[ei].res, k.ones_col_bf_r])
                    otg[h].part(
                        scr[:, L1F_V + kb * 512 + 32 * H: L1F_V + kb * 512 + 32 * H + 32],
                        scr[:, eo:eo + 512],
                        [vreg.res, expb[ei].res], tp=(0, base))
            for h in range(2):
                H = 2 * pr + h
                base = 32 * (H % 4)
                den = k.row_t.carve(h * 512, (h + 1) * 512, "l1fden%d" % h)
                bank = k.bPC if h == 0 else k.bPD
                p.op("dve", lambda eng, h=h, ps=(k.PC if h == 0 else k.PD): eng.tensor_copy(
                    rows[0:1, h * 512:(h + 1) * 512], ps[0:1, 0:512]),
                    reads=[bank], writes=[den.res], deps=den.wdeps())
                p.op("act", lambda eng, h=h: eng.activation(
                    rows[0:1, h * 512:(h + 1) * 512], rows[0:1, h * 512:(h + 1) * 512],
                    AF.Ln),
                    reads=[den.res], writes=[den.res])
                p.op("act", lambda eng, h=h: eng.activation(
                    rows[0:1, h * 512:(h + 1) * 512], rows[0:1, h * 512:(h + 1) * 512],
                    AF.Exp, scale=-1.0),
                    reads=[den.res], writes=[den.res])
                rb_ap = k.PB[base:base + 32, 0:512] if h == 0 else k.PB[base:base + 32, 512:1024]
                rb = k.bPB0 if h == 0 else k.bPB1
                mm(k, rb_ap, rb,
                   [(k.ones_row_f32[0:1, 0:32], rows[0:1, h * 512:(h + 1) * 512],
                     [den.res, k.ones_row_f32_r])], tp=(0, base))
                p.op("act", lambda eng, h=h, base=base,
                     ps=(k.PEb if h == 0 else k.PF): eng.activation(
                    scr[base:base + 32, L1F_RR + h * 512: L1F_RR + h * 512 + 512],
                    ps[base:base + 32, 0:512], AF.Copy),
                    reads=[k.bPE if h == 0 else k.bPF], writes=[expb[h].res])
                o_col = L1F_OT + tile * S + qc * 512
                p.op("dve", lambda eng, h=h, base=base, o=o_col, rba=rb_ap: eng.tensor_tensor(
                    scr[base:base + 32, o:o + 512],
                    scr[base:base + 32, L1F_RR + h * 512: L1F_RR + h * 512 + 512],
                    rba, ALU.mult),
                    reads=[expb[h].res, rb], writes=[ot.res], deps=ot.wdeps())

    if stop == "p5":
        return
    p.barrier()
    resbf = k.scr_t.carve(L1F_Q, L1F_Q + 16384, "resbf")
    for m in range(NT):
        wst = k.scr_t.carve(L1F_WST, L1F_WST + 512, "wu%d" % m)
        stream_cast(k, fsl[m % 2], (m % 2) * 1024, wud[m], 512, L1F_WST, wst)
        for c in range(NC):
            par = c % 2
            bank_ap = k.PA[:, par * 512:(par + 1) * 512]
            bank = k.bPA0 if par == 0 else k.bPA1
            g = MMGroup(k, bank_ap, bank, 4)
            for kt in range(4):
                g.part(scr[:, L1F_WST + kt * 128: L1F_WST + (kt + 1) * 128],
                       scr[:, L1F_OT + kt * S + c * 512: L1F_OT + kt * S + (c + 1) * 512],
                       [wst.res, ot.res])
            a_sl = (slice(None), slice(m * S + c * 512, m * S + (c + 1) * 512))
            r_sl = (slice(None), slice(L1F_Q + m * S + c * 512, L1F_Q + m * S + (c + 1) * 512))
            p.op("dve", lambda eng, a=a_sl, r=r_sl, b=bank_ap: eng.tensor_tensor(
                scr[r], k.acc[a], b, ALU.add),
                reads=[k.acc_r[m], bank], writes=[resbf.res], deps=resbf.wdeps())
            p.op("dve", lambda eng, a=a_sl, b=bank_ap: eng.scalar_tensor_tensor(
                k.acc[a], k.acc[a], 2.0, b, ALU.mult, ALU.add),
                reads=[bank, k.acc_r[m]], writes=[k.acc_r[m]])

    rstd2 = emit_rms_rows(
        k, 1e-5, src_fn=lambda j: scr[:, L1F_Q + j * S: L1F_Q + (j + 1) * S],
        src_res=resbf.res, bf_src=True)
    rc0 = k.misc_used
    k.misc_used += 16
    assert k.misc_used <= 128
    rcol_res = Res("rstdcols")
    for t in range(16):
        p.op("pe", lambda eng, t=t: eng.transpose(
            k.PF[0:128, 0:1], rows[0:1, 2048 + t * 128: 2048 + (t + 1) * 128],
            k.id_f32[0:1, 0:1]),
            reads=[rstd2.res, k.id_f32_r], writes=[k.bPF])
        p.op("dve", lambda eng, t=t, rc0=rc0: eng.tensor_copy(
            k.misc[:, rc0 + t:rc0 + t + 1], k.PF[0:128, 0:1]),
            reads=[k.bPF], writes=[rcol_res])
    rwf = k.f32_t.carve(2760, 2824, "rw")
    p.op("sp", lambda eng: eng.dma_start(out=k.f32s[:, 2760:2824], in_=rwd[:]),
         writes=[rwf.res], deps=rwf.wdeps(), dma=True)
    RWP = 43008
    rwp = k.scr_t.carve(RWP, RWP + 64, "rwp")
    for kt in range(NT):
        p.op("dve", lambda eng, kt=kt, gc=g2_c0: eng.tensor_scalar(
            scr[:, RWP + kt * 8: RWP + (kt + 1) * 8],
            k.f32s[:, 2760 + kt * 8: 2760 + (kt + 1) * 8],
            k.misc[:, gc + kt:gc + kt + 1], None, ALU.mult),
            reads=[rwf.res, g2_res], writes=[rwp.res], deps=rwp.wdeps())
    PEXP2 = 40960
    pexp = k.scr_t.carve(PEXP2, PEXP2 + 2048, "l1fpexp")
    emit_top2_gate(k,
                   lambda kt, t: scr[:, L1F_Q + kt * S + t * 128: L1F_Q + kt * S + (t + 1) * 128],
                   lambda kt: resbf.res,
                   scr[:, RWP:RWP + 64], rwp.res, None, None,
                   PEXP2, pexp,
                   scale_col_fn=lambda t: k.misc[:, rc0 + t:rc0 + t + 1],
                   scale_col_res=rcol_res)

    treg = k.scr_t.carve(0, 16384, "T")
    for c in range(NC):
        mm(k, k.PF[:, 0:512], k.bPF,
           [(k.ones_row_f32[:], rows[0:1, 2048 + c * 512:2048 + (c + 1) * 512],
             [rstd2.res, k.ones_row_f32_r])])
        for j in range(NT):
            p.op("dve", lambda eng, j=j, c=c, gc=g2_c0: eng.scalar_tensor_tensor(
                scr[:, j * S + c * 512: j * S + (c + 1) * 512],
                scr[:, L1F_Q + j * S + c * 512: L1F_Q + j * S + (c + 1) * 512],
                k.misc[:, gc + j:gc + j + 1], k.PF[:, 0:512], ALU.mult, ALU.mult),
                reads=[resbf.res, g2_res, k.bPF], writes=[treg.res], deps=treg.wdeps())
    TG, HID, PBR, W13R, W2BR = 16384, 24576, 43072, 45120, 47168
    pb = k.scr_t.carve(PBR, PBR + 2048, "l1fpb")
    for half in range(2):
        hb = half * 1024
        for e in range(8):
            for c in range(2):
                mm(k, k.PEb[:, 0:512], k.bPE,
                   [(k.onehot8[0:8, e * 128:(e + 1) * 128],
                     scr[0:8, PEXP2 + hb + c * 512: PEXP2 + hb + (c + 1) * 512],
                     [pexp.res, k.onehot8_r])])
                p.op("dve", lambda eng, c=c: eng.tensor_copy(
                    scr[:, PBR + c * 512: PBR + (c + 1) * 512], k.PEb[:, 0:512]),
                    reads=[k.bPE], writes=[pb.res], deps=pb.wdeps())
            tg = [k.scr_t.carve(TG + kt * 1024, TG + (kt + 1) * 1024,
                                "tg%d_%d_%d" % (half, e, kt)) for kt in range(NT)]
            for kt in range(NT):
                p.op("dve", lambda eng, kt=kt, hb=hb: eng.tensor_tensor(
                    scr[:, TG + kt * 1024: TG + (kt + 1) * 1024],
                    scr[:, kt * S + hb: kt * S + hb + 1024],
                    scr[:, PBR: PBR + 1024], ALU.mult),
                    reads=[treg.res, pb.res], writes=[tg[kt].res], deps=tg[kt].wdeps())
            w1r = [k.scr_t.carve(W13R + i * 512, W13R + (i + 1) * 512, "l1fw1r%d" % i)
                   for i in range(2)]
            w3r = [k.scr_t.carve(W13R + 1024 + i * 512, W13R + 1024 + (i + 1) * 512,
                                 "l1fw3r%d" % i) for i in range(2)]
            fs4 = [k.f32_t.carve(i * 512, (i + 1) * 512, "lfs%d" % i) for i in range(4)]
            hreg = k.scr_t.carve(HID, HID + 16384, "l1fh%d_%d" % (half, e))
            for ht in range(16):
                for hc in range(2):
                    stream_cast(k, fs4[hc], hc * 512, w1d[e, ht, :, hc * 512:(hc + 1) * 512],
                                512, W13R + hc * 512, w1r[hc])
                    stream_cast(k, fs4[2 + hc], 1024 + hc * 512,
                                w3d[e, ht, :, hc * 512:(hc + 1) * 512],
                                512, W13R + 1024 + hc * 512, w3r[hc])
                par = ht % 2
                if par == 0:
                    ub = [(k.PA[:, 0:512], k.bPA0), (k.PA[:, 512:1024], k.bPA1)]
                    vb = [(k.PB[:, 0:512], k.bPB0), (k.PB[:, 512:1024], k.bPB1)]
                else:
                    ub = [(k.PC[0:128, 0:512], k.bPC), (k.PD[0:128, 0:512], k.bPD)]
                    vb = [(k.PEb[0:128, 0:512], k.bPE), (k.PF[0:128, 0:512], k.bPF)]
                for c in range(2):
                    gu = MMGroup(k, ub[c][0], ub[c][1], NT)
                    for kt in range(NT):
                        gu.part(scr[:, W13R + kt * 128: W13R + (kt + 1) * 128],
                                scr[:, kt * S + hb + c * 512: kt * S + hb + (c + 1) * 512],
                                [w1r[kt // 4].res, treg.res])
                    gv = MMGroup(k, vb[c][0], vb[c][1], NT)
                    for kt in range(NT):
                        gv.part(scr[:, W13R + 1024 + kt * 128: W13R + 1024 + (kt + 1) * 128],
                                scr[:, TG + kt * 1024 + c * 512: TG + kt * 1024 + (c + 1) * 512],
                                [w3r[kt // 4].res, tg[kt].res])
                hcol = HID + ht * 1024
                for c in range(2):
                    p.op("act", lambda eng, hcol=hcol, c=c, ua=ub[c][0]: eng.activation(
                        scr[:, hcol + c * 512:hcol + (c + 1) * 512], ua, AF.Silu),
                        reads=[ub[c][1]], writes=[hreg.res], deps=hreg.wdeps())
                    p.op("dve", lambda eng, hcol=hcol, c=c, va=vb[c][0]: eng.tensor_tensor(
                        scr[:, hcol + c * 512:hcol + (c + 1) * 512],
                        scr[:, hcol + c * 512:hcol + (c + 1) * 512], va, ALU.mult),
                        reads=[vb[c][1], hreg.res], writes=[hreg.res])
            w2b = [k.scr_t.carve(W2BR + i * 1024, W2BR + (i + 1) * 1024, "l1fw2b%d" % i)
                   for i in range(2)]
            w2f = [k.f32_t.carve(i * 1024, (i + 1) * 1024, "lw2f%d" % i) for i in range(2)]
            for d2 in range(NT):
                for q in range(2):
                    stream_cast(k, w2f[q], q * 1024,
                                w2d[e, d2, :, q * 1024:(q + 1) * 1024],
                                1024, W2BR + q * 1024, w2b[q], crit=False)
                if d2 % 2 == 0:
                    yb = [(k.PA[:, 0:512], k.bPA0), (k.PA[:, 512:1024], k.bPA1)]
                else:
                    yb = [(k.PB[:, 0:512], k.bPB0), (k.PB[:, 512:1024], k.bPB1)]
                for c in range(2):
                    gy = MMGroup(k, yb[c][0], yb[c][1], 16)
                    for kt in range(16):
                        gy.part(scr[:, W2BR + kt * 128: W2BR + (kt + 1) * 128],
                                scr[:, HID + kt * 1024 + c * 512: HID + kt * 1024 + (c + 1) * 512],
                                [w2b[kt // 8].res, hreg.res])
                    p.op("dve", lambda eng, d2=d2, c=c, hb=hb, ps=yb[c][0]:
                         eng.tensor_tensor(
                        k.acc[:, d2 * S + hb + c * 512: d2 * S + hb + (c + 1) * 512],
                        k.acc[:, d2 * S + hb + c * 512: d2 * S + hb + (c + 1) * 512],
                        ps, ALU.add),
                        reads=[yb[c][1], k.acc_r[d2]],
                        writes=[k.acc_r[d2]])
    p.barrier()


BLOCK_EMITTERS["l1_ffn"] = emit_l1_ffn


L0A_XB = 0
L0A_OT = 16384
L0A_VP = 32768
L0A_QP = 34816
L0A_KP = 36864
L0A_QPT = 38912
L0A_KPT = 43008
L0A_KTOK = 47104
L0A_KVS = 51200
L0A_NUM = 51456
L0A_WST = 53504
L0A_ASB = 53504
L0A_PROJ = 54528
L0A_KSP = 54784
LN16 = float(np.log(16.0))


def emit_l0_attn(k):
    import os
    dbg = os.environ.get("L0A_DEBUG", "")
    pairs = [7] if dbg else range(8)
    heads = [0] if dbg == "1" else range(2)
    p, scr, rows = k.p, k.scr, k.rows
    qw = k.dram("l0a_qw", [8, 128, 1024])
    kw = k.dram("l0a_kw", [8, 128, 1024])
    vw = k.dram("l0a_vw", [8, 128, 1024])
    ow = k.dram("l0a_ow", [8, 128, 1024])
    projd = k.dram("l0a_projT2", [128, 256])
    vbd = k.dram("L0_attn_v_b", [D])
    qb_c0, qb_res = param_cols(k, "L0_attn_q_b")
    kb_c0, kb_res = param_cols(k, "L0_attn_k_b")
    ga_c0, ga_res = param_cols(k, "L0_attn_gamma")
    be_c0, be_res = param_cols(k, "L0_attn_beta")
    qb8_c0 = k.misc_used
    k.misc_used += 8
    kb8_c0 = k.misc_used
    k.misc_used += 8
    assert k.misc_used <= 128
    qb8_res, kb8_res = Res("qb8"), Res("kb8")
    p.op("dve", lambda eng: eng.tensor_scalar(
        k.misc[:, qb8_c0:qb8_c0 + 8], k.misc[:, qb_c0:qb_c0 + 8], 0.125, None, ALU.mult),
        reads=[qb_res], writes=[qb8_res])
    p.op("dve", lambda eng: eng.tensor_scalar(
        k.misc[:, kb8_c0:kb8_c0 + 8], k.misc[:, kb_c0:kb_c0 + 8], 0.125, None, ALU.mult),
        reads=[kb_res], writes=[kb8_res])
    vb = k.row_t.carve(2560, 3584, "l0avb")
    p.op("sp", lambda eng: eng.dma_start(
        out=rows[0:1, 2560:3584], in_=vbd.rearrange("(o e) -> o e", o=1)),
        writes=[vb.res], deps=vb.wdeps(), dma=True)
    prj = k.scr_t.carve(L0A_PROJ, L0A_PROJ + 256, "prj")
    pf32 = k.f32_t.carve(2760, 3016, "prjf")
    p.op("sp", lambda eng: eng.dma_start(out=k.f32s[:, 2760:3016], in_=projd[:]),
         writes=[pf32.res], deps=pf32.wdeps(), dma=True)
    p.op("dve", lambda eng: eng.tensor_copy(
        scr[:, L0A_PROJ:L0A_PROJ + 256], k.f32s[:, 2760:3016]),
        reads=[pf32.res], writes=[prj.res], deps=prj.wdeps())
    xb = k.scr_t.carve(L0A_XB, L0A_XB + 16384, "l0axb")
    for j in range(NT):
        p.op("dve", lambda eng, j=j: eng.tensor_copy(
            scr[:, L0A_XB + j * S: L0A_XB + (j + 1) * S], k.acc[:, j * S:(j + 1) * S]),
            reads=[k.acc_r[j]], writes=[xb.res], deps=xb.wdeps())

    ot = k.scr_t.carve(L0A_OT, L0A_OT + 16384, "l0aot")
    fsl = [k.f32_t.carve(0, 1024, "l0af0"), k.f32_t.carve(1024, 2048, "l0af1")]
    kvs = k.scr_t.carve(L0A_KVS, L0A_KVS + 256, "kvs")
    num = k.scr_t.carve(L0A_NUM, L0A_NUM + 2048, "num")

    for pr in pairs:
        qp_ = k.scr_t.carve(L0A_QP, L0A_QP + 2048, "qp%d" % pr)
        kp_ = k.scr_t.carve(L0A_KP, L0A_KP + 2048, "kp%d" % pr)
        vp_ = k.scr_t.carve(L0A_VP, L0A_VP + 2048, "vp%d" % pr)
        for qki, (wd_, reg, base, b8c0, b8res) in enumerate(
                [(qw, qp_, L0A_QP, qb8_c0, qb8_res), (kw, kp_, L0A_KP, kb8_c0, kb8_res)]):
            wst = k.scr_t.carve(L0A_WST, L0A_WST + 1024, "l0awq%d" % qki)
            stream_cast(k, fsl[qki], qki * 1024, wd_[pr], 1024, L0A_WST, wst)
            for c in range(NC):
                g = MMGroup(k, k.PA[:, 0:512], k.bPA0, NT)
                for kt in range(NT):
                    g.part(scr[:, L0A_WST + kt * 128: L0A_WST + (kt + 1) * 128],
                           scr[:, L0A_XB + kt * S + c * 512: L0A_XB + kt * S + (c + 1) * 512],
                           [wst.res, xb.res])
                p.op("dve", lambda eng, b=base, c=c, bc=(b8c0, pr): eng.tensor_scalar(
                    scr[:, b + c * 512: b + (c + 1) * 512], k.PA[:, 0:512], 0.125,
                    k.misc[:, bc[0] + bc[1]:bc[0] + bc[1] + 1], ALU.mult, ALU.add),
                    reads=[k.bPA0, b8res], writes=[reg.res], deps=reg.wdeps())
        wst = k.scr_t.carve(L0A_WST, L0A_WST + 1024, "l0awv")
        stream_cast(k, fsl[0], 0, vw[pr], 1024, L0A_WST, wst)
        for t in range(16):
            g = MMGroup(k, k.PA[0:128, 0:128], k.bPA0, NT + 1)
            for kt in range(NT):
                g.part(scr[:, L0A_XB + kt * S + t * 128: L0A_XB + kt * S + t * 128 + 128],
                       scr[:, L0A_WST + kt * 128: L0A_WST + (kt + 1) * 128],
                       [xb.res, wst.res])
            g.part(k.ones_row_f32[:], rows[0:1, 2560 + pr * 128: 2560 + (pr + 1) * 128],
                   [vb.res, k.ones_row_f32_r])
            p.op("act", lambda eng, t=t: eng.activation(
                scr[:, L0A_VP + t * 128: L0A_VP + (t + 1) * 128],
                k.PA[0:128, 0:128], AF.Copy),
                reads=[k.bPA0], writes=[vp_.res], deps=vp_.wdeps())
        for h01 in heads:
            hb = 64 * h01
            skip_scan = (dbg in ("3", "4", "5", "6", "7") and h01 == 1)
            qpt = k.scr_t.carve(L0A_QPT, L0A_QPT + 4096, "qpt")
            kpt = k.scr_t.carve(L0A_KPT, L0A_KPT + 4096, "kpt")
            ktok = k.scr_t.carve(L0A_KTOK, L0A_KTOK + 4096, "ktok")
            feat_list = [(L0A_QP, qp_), (L0A_KP, kp_)]
            if skip_scan and dbg in ("4", "6", "7"):
                feat_list = feat_list[:1]
            for qki, (base, reg) in enumerate(feat_list):
                crow = k.row_t.carve(0, 2048, "corr%d" % qki)
                skip_corr = skip_scan and dbg == "7"
                skip_map = skip_scan and dbg == "6"
                for c in range(NC if not skip_corr else 0):
                    sq = k.scr_t.carve(STAT_XC, STAT_XC + 512, "l0asq")
                    p.op("act", lambda eng, b=base, c=c, hb=hb: eng.activation(
                        scr[hb:hb + 64, STAT_XC:STAT_XC + 512],
                        scr[hb:hb + 64, b + c * 512: b + (c + 1) * 512], AF.Square),
                        reads=[reg.res], writes=[sq.res], deps=sq.wdeps())
                    mm(k, k.PC[0:1, 0:512], k.bPC,
                       [(k.ones_col_bf[hb:hb + 64, :], scr[hb:hb + 64, STAT_XC:STAT_XC + 512],
                         [sq.res, k.ones_col_bf_r])], tp=(hb, 0))
                    p.op("dve", lambda eng, c=c: eng.tensor_scalar(
                        rows[0:1, c * 512: (c + 1) * 512],
                        k.PC[0:1, 0:512], -0.5, -LN16, ALU.mult, ALU.add),
                        reads=[k.bPC], writes=[crow.res], deps=crow.wdeps())
                dst = L0A_QPT if qki == 0 else L0A_KPT
                dreg = qpt if qki == 0 else kpt
                for mt in range(2 if not skip_map else 0):
                    for c in range(NC):
                        ba = k.PB[:, 0:512]
                        bb = k.bPB0
                        g = MMGroup(k, ba, bb, 2)
                        g.part(scr[hb:hb + 64, L0A_PROJ + mt * 128: L0A_PROJ + (mt + 1) * 128],
                               scr[hb:hb + 64, base + c * 512: base + (c + 1) * 512],
                               [prj.res, reg.res], tp=(hb, 0))
                        g.part(k.ones_row_f32[:],
                               rows[0:1, c * 512: (c + 1) * 512],
                               [crow.res, k.ones_row_f32_r])
                        p.op("act", lambda eng, d=dst, mt=mt, c=c, ba=ba: eng.activation(
                            scr[:, d + mt * S + c * 512: d + mt * S + (c + 1) * 512],
                            ba, AF.Exp),
                            reads=[bb], writes=[dreg.res], deps=dreg.wdeps())
                if qki == 1 and dbg == "5" and skip_scan:
                    continue
                if qki == 1:
                    for t in range(16):
                        ba = k.PB[:, 512:768]
                        bb = k.bPB1
                        g = MMGroup(k, ba, bb, 3)
                        g.part(scr[hb:hb + 64, L0A_KP + t * 128: L0A_KP + (t + 1) * 128],
                               scr[hb:hb + 64, L0A_PROJ:L0A_PROJ + 256],
                               [kp_.res, prj.res], tp=(hb, 0))
                        g.part(rows[0:1, t * 128: (t + 1) * 128],
                               k.ones_row_f32[:], [crow.res, k.ones_row_f32_r],
                               out_ap=ba[:, 0:128])
                        g.part(rows[0:1, t * 128: (t + 1) * 128],
                               k.ones_row_f32[:], [crow.res, k.ones_row_f32_r],
                               out_ap=ba[:, 128:256])
                        p.op("act", lambda eng, t=t, ba=ba: eng.activation(
                            scr[:, L0A_KTOK + t * 256: L0A_KTOK + (t + 1) * 256],
                            ba, AF.Exp),
                            reads=[bb], writes=[ktok.res], deps=ktok.wdeps())
                    gks = MMGroup(k, k.PC[0:128, 0:16], k.bPC, 32)
                    for ch8 in range(8):
                        for mt in range(2):
                            for tt in range(2):
                                gks.part(
                                    scr[:, L0A_KTOK + (2 * ch8 + tt) * 256 + mt * 128:
                                        L0A_KTOK + (2 * ch8 + tt) * 256 + (mt + 1) * 128],
                                    k.ones_col_bf[:], [ktok.res, k.ones_col_bf_r],
                                    out_ap=k.PC[0:128, mt * 8 + ch8: mt * 8 + ch8 + 1])
                    ksf = k.f32_t.carve(2560, 2608, "ksf")
                    SS, PP, QQ = 2560, 2576, 2592
                    p.op("dve", lambda eng: eng.tensor_copy(
                        k.f32s[:, SS:SS + 16], k.PC[0:128, 0:16]),
                        reads=[k.bPC], writes=[ksf.res], deps=ksf.wdeps())
                    p.op("dve", lambda eng: eng.tensor_copy(
                        k.f32s[:, PP:PP + 16], k.f32s[:, SS:SS + 16]),
                        reads=[ksf.res], writes=[ksf.res])
                    cur, oth = PP, QQ
                    for sh in (1, 2, 4):
                        for mt in range(2):
                            a, b = cur + mt * 8, oth + mt * 8
                            p.op("dve", lambda eng, a=a, b=b, sh=sh: eng.tensor_copy(
                                k.f32s[:, b:b + sh], k.f32s[:, a:a + sh]),
                                reads=[ksf.res], writes=[ksf.res])
                            p.op("dve", lambda eng, a=a, b=b, sh=sh: eng.tensor_tensor(
                                k.f32s[:, b + sh:b + 8], k.f32s[:, a + sh:a + 8],
                                k.f32s[:, a:a + 8 - sh], ALU.add),
                                reads=[ksf.res], writes=[ksf.res])
                        cur, oth = oth, cur
                    ksp = k.scr_t.carve(L0A_KSP, L0A_KSP + 16, "ksp")
                    p.op("dve", lambda eng, a=cur: eng.tensor_tensor(
                        k.f32s[:, a:a + 16], k.f32s[:, a:a + 16],
                        k.f32s[:, SS:SS + 16], ALU.subtract),
                        reads=[ksf.res], writes=[ksf.res])
                    p.op("dve", lambda eng, a=cur: eng.tensor_copy(
                        scr[:, L0A_KSP:L0A_KSP + 16], k.f32s[:, a:a + 16]),
                        reads=[ksf.res], writes=[ksp.res], deps=ksp.wdeps())
            dreg_den = k.row_t.carve(0, 2048, "l0aden")
            asbs = [k.scr_t.carve(L0A_ASB, L0A_ASB + 512, "asb0"),
                    k.scr_t.carve(L0A_ASB + 512, L0A_ASB + 1024, "asb1")]

            def emit_at2(ch):
                bank_ap = k.PA[:, 0:512] if ch % 2 == 0 else k.PA[:, 512:1024]
                bank = k.bPA0 if ch % 2 == 0 else k.bPA1
                for ks in range(2):
                    g = MMGroup(k, bank_ap[:, ks * 256:(ks + 1) * 256], bank, 2)
                    for mt in range(2):
                        g.part(
                            scr[:, L0A_KPT + mt * S + ch * 256 + ks * 128:
                                L0A_KPT + mt * S + ch * 256 + (ks + 1) * 128],
                            scr[:, L0A_QPT + mt * S + ch * 256: L0A_QPT + mt * S + (ch + 1) * 256],
                            [kpt.res, qpt.res])
                ao = L0A_ASB + (ch % 2) * 512
                for ks in range(2):
                    cb = 384 - 128 * ks
                    p.op("dve", lambda eng, b=bank_ap, ao=ao, ks=ks, cb=cb: eng.tensor_tensor(
                        scr[:, ao + ks * 256: ao + (ks + 1) * 256],
                        b[:, ks * 256:(ks + 1) * 256],
                        k.band[:, cb:cb + 256], ALU.mult),
                        reads=[bank, k.band_r], writes=[asbs[ch % 2].res],
                        deps=asbs[ch % 2].wdeps())

            emit_at2(0)
            for ch in range(8):
                if ch + 1 < 8:
                    emit_at2(ch + 1)
                ao = L0A_ASB + (ch % 2) * 512
                ob_ap = (k.PB[:, 0:512] if ch % 2 == 0 else k.PB[:, 512:1024])[hb:hb + 64, 0:256]
                ob_bank = k.bPB0 if ch % 2 == 0 else k.bPB1
                nparts = 2 if ch == 0 else 4
                g = MMGroup(k, ob_ap, ob_bank, nparts)
                for ks in range(2):
                    g.part(scr[:, L0A_VP + (2 * ch + ks) * 128 + hb:
                               L0A_VP + (2 * ch + ks) * 128 + hb + 64],
                           scr[:, ao + ks * 256: ao + (ks + 1) * 256],
                           [vp_.res, asbs[ch % 2].res], tp=(0, hb))
                if ch > 0:
                    for mt in range(2):
                        g.part(scr[:, L0A_KVS + mt * 128: L0A_KVS + mt * 128 + 64],
                               scr[:, L0A_QPT + mt * S + ch * 256: L0A_QPT + mt * S + (ch + 1) * 256],
                               [kvs.res, qpt.res], tp=(0, hb))
                dn_ap = (k.PC if ch % 2 == 0 else k.PF)[0:1, 0:256]
                dn_bank = k.bPC if ch % 2 == 0 else k.bPF
                gd = MMGroup(k, dn_ap, dn_bank, nparts)
                for ks in range(2):
                    gd.part(k.ones_col_bf[:], scr[:, ao + ks * 256: ao + (ks + 1) * 256],
                            [asbs[ch % 2].res, k.ones_col_bf_r])
                if ch > 0:
                    for mt in range(2):
                        gd.part(scr[:, L0A_KSP + mt * 8 + ch: L0A_KSP + mt * 8 + ch + 1],
                                scr[:, L0A_QPT + mt * S + ch * 256: L0A_QPT + mt * S + (ch + 1) * 256],
                                [ksp.res, qpt.res])
                p.op("act", lambda eng, ch=ch, hb=hb, b=ob_ap: eng.activation(
                    scr[hb:hb + 64, L0A_NUM + ch * 256: L0A_NUM + (ch + 1) * 256],
                    b, AF.Copy),
                    reads=[ob_bank], writes=[num.res], deps=num.wdeps())
                p.op("dve", lambda eng, ch=ch, b=dn_ap: eng.tensor_copy(
                    rows[0:1, ch * 256:(ch + 1) * 256], b),
                    reads=[dn_bank], writes=[dreg_den.res], deps=dreg_den.wdeps())
                for mt in range(2):
                    kvp = k.PD if mt == 0 else k.PEb
                    kvb = k.bPD if mt == 0 else k.bPE
                    for tt in range(2):
                        p.op("pe", lambda eng, mt=mt, ch=ch, tt=tt, hb=hb, kvp=kvp:
                             eng.matmul(
                            kvp[0:128, 0:64],
                            scr[:, L0A_KTOK + (2 * ch + tt) * 256 + mt * 128:
                                L0A_KTOK + (2 * ch + tt) * 256 + (mt + 1) * 128],
                            scr[:, L0A_VP + (2 * ch + tt) * 128 + hb:
                                L0A_VP + (2 * ch + tt) * 128 + hb + 64],
                            start=(ch == 0 and tt == 0), stop=True,
                            skip_group_check=True),
                            reads=[ktok.res, vp_.res],
                            deps=[kvb.w] + kvb.r)
                        kvb.w = p.items[-1].idx
                        kvb.r = []
                if ch < 7:
                    for mt in range(2):
                        kvp = k.PD if mt == 0 else k.PEb
                        kvb = k.bPD if mt == 0 else k.bPE
                        p.op("dve", lambda eng, mt=mt, kvp=kvp: eng.tensor_copy(
                            scr[:, L0A_KVS + mt * 128: L0A_KVS + mt * 128 + 64],
                            kvp[0:128, 0:64]),
                            reads=[kvb], writes=[kvs.res], deps=kvs.wdeps())
            p.op("act", lambda eng: eng.activation(
                rows[0:1, 0:2048], rows[0:1, 0:2048], AF.Ln),
                reads=[dreg_den.res], writes=[dreg_den.res])
            p.op("act", lambda eng: eng.activation(
                rows[0:1, 0:2048], rows[0:1, 0:2048], AF.Exp, scale=-1.0),
                reads=[dreg_den.res], writes=[dreg_den.res])
            for c in range(NC):
                mm(k, k.PF[hb:hb + 64, 0:512], k.bPF,
                   [(k.ones_row_f32[0:1, 0:64], rows[0:1, c * 512:(c + 1) * 512],
                     [dreg_den.res, k.ones_row_f32_r])], tp=(0, hb))
                o_col = L0A_OT + pr * 2048 + c * 512
                p.op("dve", lambda eng, c=c, o=o_col, hb=hb: eng.tensor_tensor(
                    scr[hb:hb + 64, o:o + 512],
                    scr[hb:hb + 64, L0A_NUM + c * 512: L0A_NUM + (c + 1) * 512],
                    k.PF[hb:hb + 64, 0:512], ALU.mult),
                    reads=[num.res, k.bPF], writes=[ot.res], deps=ot.wdeps())

    if dbg:
        return
    p.barrier()
    ob_c0, ob_res = param_cols(k, "L0_attn_out_b")
    for m in range(NT):
        wst = k.scr_t.carve(L0A_WST, L0A_WST + 1024, "l0awo%d" % m)
        stream_cast(k, fsl[m % 2], (m % 2) * 1024, ow[m], 1024, L0A_WST, wst)
        for c in range(NC):
            par = c % 2
            bank_ap = k.PA[:, par * 512:(par + 1) * 512]
            bank = k.bPA0 if par == 0 else k.bPA1
            g = MMGroup(k, bank_ap, bank, NT)
            for kt in range(NT):
                g.part(scr[:, L0A_WST + kt * 128: L0A_WST + (kt + 1) * 128],
                       scr[:, L0A_OT + kt * S + c * 512: L0A_OT + kt * S + (c + 1) * 512],
                       [wst.res, ot.res])
            a_sl = (slice(None), slice(m * S + c * 512, m * S + (c + 1) * 512))
            p.op("dve", lambda eng, a=a_sl, m=m, b=bank_ap: eng.scalar_tensor_tensor(
                k.acc[a], b, k.misc[:, ob_c0 + m:ob_c0 + m + 1], k.acc[a],
                ALU.add, ALU.add),
                reads=[bank, ob_res, k.acc_r[m]], writes=[k.acc_r[m]])
    mu_res, rstd_res = stats(k, 1e-6)
    ln_apply(k, mu_res, rstd_res, affine=(ga_c0, ga_res, be_c0, be_res))
    for j in range(NT):
        for c in range(NC):
            a_sl = (slice(None), slice(j * S + c * 512, j * S + (c + 1) * 512))
            p.op("dve", lambda eng, a=a_sl, j=j, c=c: eng.tensor_tensor(
                k.acc[a], k.acc[a],
                scr[:, L0A_XB + j * S + c * 512: L0A_XB + j * S + (c + 1) * 512], ALU.add),
                reads=[k.acc_r[j], xb.res], writes=[k.acc_r[j]])
    p.barrier()


BLOCK_EMITTERS["l0_attn"] = emit_l0_attn


def prep_l0_attn(inputs):
    def tile8(w):
        w = np.asarray(w, np.float32)
        return np.ascontiguousarray(
            w.reshape(8, 128, 8, 128).transpose(2, 1, 0, 3).reshape(8, 128, 1024))
    proj = np.asarray(inputs["L0_attn_proj"], np.float32)
    pt = np.ascontiguousarray(proj.T)
    return {
        "l0a_qw": tile8(inputs["L0_attn_q_w"]),
        "l0a_kw": tile8(inputs["L0_attn_k_w"]),
        "l0a_vw": tile8(inputs["L0_attn_v_w"]),
        "l0a_ow": tile8(inputs["L0_attn_out_w"]),
        "l0a_projT2": np.ascontiguousarray(np.vstack([pt, pt])),
        "L0_attn_q_b": np.asarray(inputs["L0_attn_q_b"], np.float32),
        "L0_attn_k_b": np.asarray(inputs["L0_attn_k_b"], np.float32),
        "L0_attn_v_b": np.asarray(inputs["L0_attn_v_b"], np.float32),
        "L0_attn_out_b": np.asarray(inputs["L0_attn_out_b"], np.float32),
        "L0_attn_gamma": np.asarray(inputs["L0_attn_gamma"], np.float32),
        "L0_attn_beta": np.asarray(inputs["L0_attn_beta"], np.float32),
    }


PREPPERS["l0_attn"] = prep_l0_attn



def prep_consts():
    kk = np.arange(128)[:, None]
    cc = np.arange(BAND_W)[None, :]
    oh = np.zeros((8, 1024), dtype=BF)
    for e in range(8):
        oh[e, e * 128:(e + 1) * 128] = 1
    return {
        "c_id_f32": np.eye(128, dtype=np.float32),
        "c_ones_col_bf": np.ones((128, 1), dtype=BF),
        "c_ones_row_f32": np.ones((1, 128), dtype=np.float32),
        "c_band": (kk <= cc - 384).astype(BF),
        "c_onehot8": oh,
    }


def prep_l1_attn(inputs):
    w_qkv = np.asarray(inputs["L1_attn_w_qkv"], dtype=np.float32)
    wt = np.ascontiguousarray(w_qkv.T)
    qkw = np.empty((8, 2, 128, 1024), np.float32)
    for pair in range(8):
        for qki in range(2):
            base = qki * 1024 + pair * 128
            blk = wt[:, base:base + 128]
            qkw[pair, qki] = blk.reshape(8, 128, 128).transpose(1, 0, 2).reshape(128, 1024)
    vw = np.empty((2, 8, 128, 512), np.float32)
    for g in range(2):
        for kt in range(8):
            vw[g, kt] = wt[kt * 128:(kt + 1) * 128, 2048 + g * 512: 2048 + (g + 1) * 512]
    w_out = np.asarray(inputs["L1_attn_w_out"], dtype=np.float32)
    wo = np.ascontiguousarray(w_out.T.reshape(8, 128, 1024))
    return {
        "l1a_qkw": qkw, "l1a_vw": vw, "l1a_wo": wo,
        "L1_attn_ln_gamma": np.asarray(inputs["L1_attn_ln_gamma"], np.float32),
        "L1_attn_ln_beta": np.asarray(inputs["L1_attn_ln_beta"], np.float32),
        "L1_attn_b_out": np.asarray(inputs["L1_attn_b_out"], np.float32),
    }


PREPPERS["l1_attn"] = prep_l1_attn


def run_blocks(blocks, x, inputs, cores=8):
    nc, k = build_program(blocks)
    needed = set(k.inp.keys())
    base = dict(prep_consts())
    cvals = np.zeros((128, 32), np.float32)
    for val, c in k.cval_map.items():
        cvals[:, c] = val
    base["c_cvals"] = cvals
    for b in blocks:
        base.update(PREPPERS[b](inputs))
    in_maps = []
    for c in range(cores):
        m = {"xio": np.ascontiguousarray(np.asarray(x[c], np.float32).T)}
        for name, arr in base.items():
            if name in needed:
                m[name] = arr
        in_maps.append(m)
    res = bass_utils.run_bass_kernel_spmd(nc, in_maps, core_ids=list(range(cores)))
    return np.stack([np.ascontiguousarray(r["yio"].T) for r in res.results])


ALL_BLOCKS = ["l0_attn", "l0_ffn", "l1_attn", "l1_ffn"]
_CACHE = {}


def _get_program(blocks):
    key = tuple(blocks)
    if key not in _CACHE:
        nc, k = build_program(list(blocks))
        cvals = np.zeros((128, 32), np.float32)
        for val, c in k.cval_map.items():
            cvals[:, c] = val
        _CACHE[key] = (nc, k, cvals)
    return _CACHE[key]


def kernel(**inputs):
    x = np.asarray(inputs["L0_attn_x"], np.float32)
    nc, k, cvals = _get_program(tuple(ALL_BLOCKS))
    base = prep_consts()
    base["c_cvals"] = cvals
    for b in ALL_BLOCKS:
        base.update(PREPPERS[b](inputs))
    needed = set(k.inp.keys())
    in_maps = []
    for c in range(8):
        m = {"xio": np.ascontiguousarray(x[c].T)}
        for name, arr in base.items():
            if name in needed:
                m[name] = arr
        in_maps.append(m)
    res = bass_utils.run_bass_kernel_spmd(nc, in_maps, core_ids=list(range(8)))
    return np.stack([np.ascontiguousarray(np.asarray(r["yio"], np.float32).T)
                     for r in res.results])


# version 13
# speedup vs baseline: 1.3157x; 1.3157x over previous
import contextlib
import numpy as np
import ml_dtypes
import concourse.bass as bass
import concourse.mybir as mybir
from concourse import bass_utils

F32 = mybir.dt.float32
BF16 = mybir.dt.bfloat16
AF = mybir.ActivationFunctionType
ALU = mybir.AluOpType
AX = mybir.AxisListType
BF = ml_dtypes.bfloat16

D = 1024
S = 2048
NT = D // 128
NC = 4

ENGS = ("pe", "dve", "act", "sp", "gp")


class Res:
    __slots__ = ("name", "w", "r")

    def __init__(self, name=""):
        self.name = name
        self.w = None
        self.r = []


class Reg:
    __slots__ = ("res", "deps")

    def __init__(self, res, deps):
        self.res = res
        self.deps = list(deps)

    def wdeps(self):
        d, self.deps = self.deps, []
        return d


NDMA = 8


class Op:
    __slots__ = ("eng", "fn", "deps", "dma", "idx", "sem_val", "needed", "slot")


class Prog:
    def __init__(self, nc):
        self.nc = nc
        self.items = []

    def op(self, eng, fn, reads=(), writes=(), deps=(), dma=False):
        idx = len(self.items)
        dd = set(d for d in deps if d is not None)
        for r in reads:
            if r.w is not None:
                dd.add(r.w)
        for w in writes:
            if w.w is not None:
                dd.add(w.w)
            dd.update(w.r)
        o = Op()
        o.eng, o.fn, o.deps, o.dma, o.idx = eng, fn, sorted(dd), dma, idx
        o.sem_val, o.needed = None, False
        self.items.append(o)
        for r in reads:
            r.r.append(idx)
        for w in writes:
            w.w = idx
            w.r = []
        return idx

    def barrier(self):
        last = {}
        for it in self.items:
            last[it.eng] = it.idx
        toks = list(last.values())
        for e in ENGS:
            if e in ("dve", "act", "gp"):
                def fn(eng):
                    eng.drain()
                    return eng.nop()
            else:
                def fn(eng):
                    return eng.nop()
            self.op(e, fn, deps=toks)

    def emit(self, block, sems):
        items = self.items
        for it in items:
            for d in it.deps:
                p = items[d]
                if p.eng != it.eng or p.dma:
                    p.needed = True
        cnt = {e: 0 for e in ENGS}
        dcnt = {e: 0 for e in ENGS}
        for it in items:
            if it.dma:
                i = dcnt[it.eng]
                dcnt[it.eng] += 1
                it.slot = i % NDMA
                it.sem_val = 16 * (i // NDMA + 1)
            elif it.needed:
                cnt[it.eng] += 1
                it.sem_val = cnt[it.eng]

        def run_engine(ename, eng):
            waited = {}
            drained_upto = -1
            own_done = -1
            for it in items:
                if it.eng != ename:
                    continue
                need_drain = False
                for d in it.deps:
                    p = items[d]
                    if p.eng == ename and not p.dma:
                        if ename != "pe" and d > drained_upto:
                            need_drain = True
                    else:
                        if p.dma:
                            key = (p.eng, "d", p.slot)
                        else:
                            key = (p.eng, "c")
                        v = p.sem_val
                        assert v is not None, (ename, it.idx, d, p.eng)
                        if waited.get(key, 0) < v:
                            eng.wait_ge(sems[key], v)
                            waited[key] = v
                if it.dma and it.sem_val > 16:
                    key = (ename, "d", it.slot)
                    v = it.sem_val - 16
                    if waited.get(key, 0) < v:
                        eng.wait_ge(sems[key], v)
                        waited[key] = v
                if need_drain:
                    eng.drain()
                    drained_upto = own_done
                inst = it.fn(eng)
                own_done = it.idx
                if it.sem_val is not None:
                    if inst is None:
                        inst = eng.nop()
                    key = (ename, "d", it.slot) if it.dma else (ename, "c")
                    inst.then_inc(sems[key], 16 if it.dma else 1)

        nc = self.nc

        @block.tensor
        def _(_e):
            run_engine("pe", nc.tensor)

        @block.vector
        def _(_e):
            run_engine("dve", nc.vector)

        @block.scalar
        def _(_e):
            run_engine("act", nc.scalar)

        @block.sync
        def _(_e):
            run_engine("sp", nc.sync)

        @block.gpsimd
        def _(_e):
            run_engine("gp", nc.gpsimd)


class RegTracker:
    def __init__(self, name):
        self.name = name
        self.regs = []

    def carve(self, start, end, tag=""):
        deps = []
        keep = []
        for (s, e, r) in self.regs:
            if s < end and start < e:
                if r.w is not None:
                    deps.append(r.w)
                deps.extend(r.r)
            else:
                keep.append((s, e, r))
        res = Res(self.name + ":" + tag)
        keep.append((start, end, res))
        self.regs = keep
        return Reg(res, deps)


class MMGroup:
    def __init__(self, k, out_ap, bank, n):
        self.k = k
        self.out_ap = out_ap
        self.bank = bank
        self.n = n
        self.i = 0
        self.prev = None

    def part(self, lhsT, rhs, reads, tp=None, extra_deps=(), out_ap=None):
        i, n = self.i, self.n
        assert i < n
        deps = list(extra_deps)
        if i == 0:
            if self.bank.w is not None:
                deps.append(self.bank.w)
            deps += self.bank.r
        else:
            deps.append(self.prev)
        fn = (lambda eng, o=(out_ap if out_ap is not None else self.out_ap),
              l=lhsT, r=rhs, st=(i == 0), sp=(i == n - 1),
              tp=tp: eng.matmul(o, l, r, start=st, stop=sp, tile_position=tp))
        self.prev = self.k.p.op("pe", fn, reads=list(reads), deps=deps)
        self.i += 1
        if self.i == self.n:
            self.bank.w = self.prev
            self.bank.r = []
        return self.prev


def mm(k, out_ap, bank, parts, tp=None):
    g = MMGroup(k, out_ap, bank, len(parts))
    for (l, r, rd) in parts:
        g.part(l, r, rd, tp=tp)
    return g.prev



SCR_COLS = 55296
F32S_COLS = 3072
LNT = 2048
BAND_W = 1280


class K:
    def __init__(self, nc, es):
        self.nc = nc
        self.p = Prog(nc)
        self.es = es
        self.inp = {}
        sb = lambda n, sh, dt: es.enter_context(nc.sbuf_tensor(n, sh, dt))
        pst = lambda n, sh: es.enter_context(nc.psum_tensor(n, sh, F32))
        self.acc = sb("acc", [128, NT * S], F32)
        self.acc_r = [Res("acc%d" % j) for j in range(NT)]
        self.rows = sb("rows", [1, 4096], F32)
        self.scr = sb("scr", [128, SCR_COLS], BF16)
        self.f32s = sb("f32s", [128, F32S_COLS], F32)
        self.id_f32 = sb("id_f32", [128, 128], F32)
        self.ones_col_bf = sb("ones_col_bf", [128, 1], BF16)
        self.ones_row_f32 = sb("ones_row_f32", [1, 128], F32)
        self.band = sb("band", [128, BAND_W], BF16)
        self.onehot8 = sb("onehot8", [8, 1024], BF16)
        self.misc = sb("misc", [128, 128], F32)
        self.PA = pst("PA", [128, 1024])
        self.PB = pst("PB", [128, 1024])
        self.PC = pst("PC", [128, 512])
        self.PD = pst("PD", [128, 512])
        self.PEb = pst("PEb", [128, 512])
        self.PF = pst("PF", [128, 512])
        self.bPA0, self.bPA1 = Res("PA0"), Res("PA1")
        self.bPB0, self.bPB1 = Res("PB0"), Res("PB1")
        self.bPC, self.bPD = Res("PC"), Res("PD")
        self.bPE, self.bPF = Res("PEb"), Res("PF")
        for n in ("id_f32", "ones_col_bf", "ones_row_f32", "band", "misc", "onehot8"):
            setattr(self, n + "_r", Res(n))
        self.scr_t = RegTracker("scr")
        self.f32_t = RegTracker("f32s")
        self.row_t = RegTracker("rows")
        self.misc_used = 0
        self.cvals = sb("cvals", [128, 32], F32)
        self.cvals_r = Res("cvals")
        self.cval_map = {}
        self.cast_rr = 0
        self.cload("c_cvals", [128, 32], self.cvals[:], self.cvals_r)
        self.cload("c_id_f32", [128, 128], self.id_f32[:], self.id_f32_r)
        self.cload("c_ones_col_bf", [128, 1], self.ones_col_bf[:], self.ones_col_bf_r, dt=BF16)
        self.cload("c_ones_row_f32", [1, 128], self.ones_row_f32[:], self.ones_row_f32_r)
        self.cload("c_band", [128, BAND_W], self.band[:], self.band_r, dt=BF16)
        self.cload("c_onehot8", [8, 1024], self.onehot8[:], self.onehot8_r, dt=BF16)

    def dram(self, name, shape, dt=F32, kind="ExternalInput"):
        if name not in self.inp:
            self.inp[name] = self.nc.dram_tensor(name, list(shape), dt, kind=kind).ap()
        return self.inp[name]

    def cv(self, val, p=128):
        val = float(val)
        if val not in self.cval_map:
            self.cval_map[val] = len(self.cval_map)
            assert len(self.cval_map) <= 32
        c = self.cval_map[val]
        return self.cvals[0:p, c:c + 1]

    def cload(self, name, shape, dst_ap, dst_res, dt=F32):
        ap = self.dram(name, shape, dt)
        return self.p.op("sp", lambda eng, o=dst_ap, i=ap: eng.dma_start(out=o, in_=i),
                         writes=[dst_res], dma=True)


def param_cols(k, name, n=D):
    nt = n // 128
    ap = k.dram(name, [n], F32)
    st = k.f32_t.carve(2824, 2952, name)
    k.p.op("sp", lambda eng, i=ap, nt=nt: eng.dma_start(
        out=k.f32s[0:nt, 2824:2824 + 128], in_=i.rearrange("(t p) -> t p", p=128)),
        writes=[st.res], deps=st.wdeps(), dma=True)
    k.p.op("pe", lambda eng, nt=nt: eng.transpose(
        k.PF[0:128, 0:nt], k.f32s[0:nt, 2824:2824 + 128], k.id_f32[0:nt, 0:nt]),
        reads=[st.res, k.id_f32_r], writes=[k.bPF])
    c0 = k.misc_used
    k.misc_used += nt
    assert k.misc_used <= 128
    col_res = Res(name + "_col")
    k.p.op("dve", lambda eng, c0=c0, nt=nt: eng.tensor_copy(
        k.misc[:, c0:c0 + nt], k.PF[0:128, 0:nt]),
        reads=[k.bPF], writes=[col_res])
    return c0, col_res


STAT_XC = 52224
STAT_SQ = 52736


def stats(k, eps):
    p, scr, rows = k.p, k.scr, k.rows
    mu = k.row_t.carve(0, 2048, "mu")
    rstd = k.row_t.carve(2048, 4096, "rstd")
    xc = k.scr_t.carve(STAT_XC, STAT_XC + 512, "statxc")
    sq = k.scr_t.carve(STAT_SQ, STAT_SQ + 512, "statsq")
    for c in range(NC):
        g1 = MMGroup(k, k.PC[0:1, 0:512], k.bPC, NT)
        g2 = MMGroup(k, k.PD[0:1, 0:512], k.bPD, NT)
        for j in range(NT):
            p.op("dve", lambda eng, j=j, c=c: eng.tensor_copy(
                scr[:, STAT_XC:STAT_XC + 512],
                k.acc[:, j * S + c * 512: j * S + (c + 1) * 512]),
                reads=[k.acc_r[j]], writes=[xc.res], deps=xc.wdeps())
            p.op("act", lambda eng: eng.activation(
                scr[:, STAT_SQ:STAT_SQ + 512], scr[:, STAT_XC:STAT_XC + 512], AF.Square),
                reads=[xc.res], writes=[sq.res], deps=sq.wdeps())
            g1.part(k.ones_col_bf[:], scr[:, STAT_XC:STAT_XC + 512], [xc.res, k.ones_col_bf_r])
            g2.part(k.ones_col_bf[:], scr[:, STAT_SQ:STAT_SQ + 512], [sq.res, k.ones_col_bf_r])
        p.op("dve", lambda eng, c=c: eng.tensor_scalar(
            rows[0:1, c * 512:(c + 1) * 512], k.PC[0:1, 0:512], 1.0 / D, None, ALU.mult),
            reads=[k.bPC], writes=[mu.res], deps=mu.wdeps())
        p.op("dve", lambda eng, c=c: eng.tensor_scalar(
            rows[0:1, 2048 + c * 512:2048 + (c + 1) * 512], k.PD[0:1, 0:512], 1.0 / D,
            None, ALU.mult),
            reads=[k.bPD], writes=[rstd.res], deps=rstd.wdeps())
    for c in range(NC):
        p.op("dve", lambda eng, c=c: eng.tensor_tensor(
            k.PC[0:1, 0:512], rows[0:1, c * 512:(c + 1) * 512],
            rows[0:1, c * 512:(c + 1) * 512], ALU.mult),
            reads=[mu.res], writes=[k.bPC])
        p.op("dve", lambda eng, c=c: eng.tensor_tensor(
            rows[0:1, 2048 + c * 512:2048 + (c + 1) * 512],
            rows[0:1, 2048 + c * 512:2048 + (c + 1) * 512], k.PC[0:1, 0:512], ALU.subtract),
            reads=[k.bPC, rstd.res], writes=[rstd.res])
    p.op("act", lambda eng, b=k.cv(eps, 1): eng.activation(
        rows[0:1, 2048:4096], rows[0:1, 2048:4096], AF.Ln, bias=b),
        reads=[rstd.res, k.cvals_r], writes=[rstd.res])
    p.op("act", lambda eng: eng.activation(
        rows[0:1, 2048:4096], rows[0:1, 2048:4096], AF.Exp, scale=-0.5),
        reads=[rstd.res], writes=[rstd.res])
    return mu.res, rstd.res


def ln_apply(k, mu_res, rstd_res, affine=None, out_reg=None, out_region=None):
    p, rows, scr = k.p, k.rows, k.scr
    lnt = k.f32_t.carve(LNT, LNT + 512, "lnt")
    for c in range(NC):
        mm(k, k.PEb[:, 0:512], k.bPE,
           [(k.ones_row_f32[:], rows[0:1, c * 512:(c + 1) * 512], [mu_res, k.ones_row_f32_r])])
        mm(k, k.PF[:, 0:512], k.bPF,
           [(k.ones_row_f32[:], rows[0:1, 2048 + c * 512:2048 + (c + 1) * 512],
             [rstd_res, k.ones_row_f32_r])])
        for j in range(NT):
            a_sl = (slice(None), slice(j * S + c * 512, j * S + (c + 1) * 512))
            p.op("dve", lambda eng, a=a_sl: eng.tensor_tensor(
                k.f32s[:, LNT:LNT + 512], k.acc[a], k.PEb[:, 0:512], ALU.subtract),
                reads=[k.acc_r[j], k.bPE], writes=[lnt.res], deps=lnt.wdeps())
            if affine is None:
                p.op("dve", lambda eng, a=a_sl: eng.tensor_tensor(
                    k.acc[a], k.f32s[:, LNT:LNT + 512], k.PF[:, 0:512], ALU.mult),
                    reads=[lnt.res, k.bPF], writes=[k.acc_r[j]])
            elif out_region is None:
                gc0, g_res, bc0, b_res = affine
                p.op("dve", lambda eng: eng.tensor_tensor(
                    k.f32s[:, LNT:LNT + 512], k.f32s[:, LNT:LNT + 512], k.PF[:, 0:512],
                    ALU.mult),
                    reads=[lnt.res, k.bPF], writes=[lnt.res])
                p.op("dve", lambda eng, a=a_sl, j=j, gc0=gc0, bc0=bc0: eng.tensor_scalar(
                    k.acc[a], k.f32s[:, LNT:LNT + 512], k.misc[:, gc0 + j:gc0 + j + 1],
                    k.misc[:, bc0 + j:bc0 + j + 1], ALU.mult, ALU.add),
                    reads=[lnt.res, g_res, b_res], writes=[k.acc_r[j]])
            else:
                gc0, g_res, bc0, b_res = affine
                p.op("dve", lambda eng: eng.tensor_tensor(
                    k.f32s[:, LNT:LNT + 512], k.f32s[:, LNT:LNT + 512], k.PF[:, 0:512],
                    ALU.mult),
                    reads=[lnt.res, k.bPF], writes=[lnt.res])
                o_sl = (slice(None),
                        slice(out_region + j * S + c * 512, out_region + j * S + (c + 1) * 512))
                p.op("dve", lambda eng, o=o_sl, j=j, gc0=gc0, bc0=bc0: eng.tensor_scalar(
                    scr[o], k.f32s[:, LNT:LNT + 512], k.misc[:, gc0 + j:gc0 + j + 1],
                    k.misc[:, bc0 + j:bc0 + j + 1], ALU.mult, ALU.add),
                    reads=[lnt.res, g_res, b_res], writes=[out_reg.res], deps=out_reg.wdeps())


def stream_cast(k, slot, f32_off, dram_ap, ncols, bf_off, bf_reg, crit=True):
    p = k.p
    p.op("sp", lambda eng, i=dram_ap, fo=f32_off, n=ncols: eng.dma_start(
        out=k.f32s[:, fo:fo + n], in_=i), writes=[slot.res], deps=slot.wdeps(), dma=True)
    if crit:
        eng_name = ("dve", "act")[k.cast_rr % 2]
        k.cast_rr += 1
    else:
        eng_name = "gp"
    if eng_name == "act":
        p.op("act", lambda eng, fo=f32_off, n=ncols, bo=bf_off: eng.activation(
            k.scr[:, bo:bo + n], k.f32s[:, fo:fo + n], AF.Copy),
            reads=[slot.res], writes=[bf_reg.res], deps=bf_reg.wdeps())
    else:
        p.op(eng_name, lambda eng, fo=f32_off, n=ncols, bo=bf_off: eng.tensor_copy(
            k.scr[:, bo:bo + n], k.f32s[:, fo:fo + n]),
            reads=[slot.res], writes=[bf_reg.res], deps=bf_reg.wdeps())


L1A_XN = 0
L1A_VG = 16384
L1A_QK = 24576
L1A_EXP = 52224
L1A_OT = 29696
L1A_VW = 46080
L1A_QKW = 50176


def emit_l1_attn(k):
    p, scr, rows = k.p, k.scr, k.rows
    g_c0, g_res = param_cols(k, "L1_attn_ln_gamma")
    b_c0, b_res = param_cols(k, "L1_attn_ln_beta")
    bo_c0, bo_res = param_cols(k, "L1_attn_b_out")
    qkw = k.dram("l1a_qkw", [8, 2, 128, 1024])
    vw = k.dram("l1a_vw", [2, 8, 128, 512])
    wo = k.dram("l1a_wo", [8, 128, 1024])

    mu_res, rstd_res = stats(k, 1e-5)
    xn = k.scr_t.carve(L1A_XN, L1A_XN + 16384, "xn")
    ln_apply(k, mu_res, rstd_res, affine=(g_c0, g_res, b_c0, b_res),
             out_reg=xn, out_region=L1A_XN)

    vg = k.scr_t.carve(L1A_VG, L1A_VG + 8192, "vg")
    ot = k.scr_t.carve(L1A_OT, L1A_OT + 16384, "ot")
    fslot = [k.f32_t.carve(0, 1024, "f0"), k.f32_t.carve(1024, 2048, "f1")]
    vwb = [k.scr_t.carve(L1A_VW + i * 512, L1A_VW + (i + 1) * 512, "vwb%d" % i)
           for i in range(8)]
    qkwb = [k.scr_t.carve(L1A_QKW + i * 1024, L1A_QKW + (i + 1) * 1024, "qkwb%d" % i)
            for i in range(2)]
    expb = [k.scr_t.carve(L1A_EXP + i * 512, L1A_EXP + (i + 1) * 512, "expb%d" % i)
            for i in range(4)]
    qk = k.scr_t.carve(L1A_QK, L1A_QK + 4096, "qk")

    for pair in range(8):
        g = pair // 4
        if pair % 4 == 0:
            for kt in range(NT):
                stream_cast(k, fslot[kt % 2], (kt % 2) * 1024, vw[g, kt], 512,
                            L1A_VW + kt * 512, vwb[kt])
            for t in range(16):
                gmm = MMGroup(k, k.PB[:, 512:1024], k.bPB1, NT)
                for kt in range(NT):
                    gmm.part(scr[:, L1A_XN + kt * S + t * 128: L1A_XN + kt * S + t * 128 + 128],
                             scr[:, L1A_VW + kt * 512: L1A_VW + (kt + 1) * 512],
                             [xn.res, vwb[kt].res])
                p.op("act", lambda eng, t=t: eng.activation(
                    scr[:, L1A_VG + t * 512: L1A_VG + (t + 1) * 512],
                    k.PB[:, 512:1024], AF.Copy),
                    reads=[k.bPB1], writes=[vg.res], deps=vg.wdeps())
        for qki in range(2):
            stream_cast(k, fslot[qki], qki * 1024, qkw[pair, qki], 1024,
                        L1A_QKW + qki * 1024, qkwb[qki])
            bank_ap = k.PEb[:, 0:512] if qki == 0 else k.PF[:, 0:512]
            bank = k.bPE if qki == 0 else k.bPF
            for c in range(NC):
                gmm = MMGroup(k, bank_ap, bank, NT)
                for kt in range(NT):
                    gmm.part(
                        scr[:, L1A_QKW + qki * 1024 + kt * 128: L1A_QKW + qki * 1024 + (kt + 1) * 128],
                        scr[:, L1A_XN + kt * S + c * 512: L1A_XN + kt * S + (c + 1) * 512],
                        [qkwb[qki].res, xn.res])
                p.op("act", lambda eng, qki=qki, c=c, b=bank_ap: eng.activation(
                    scr[:, L1A_QK + qki * 2048 + c * 512: L1A_QK + qki * 2048 + (c + 1) * 512],
                    b, AF.Copy),
                    reads=[bank], writes=[qk.res], deps=qk.wdeps())
        for qc in range(NC):
            nkb = 4 * qc + 4
            deng = []
            for h in range(2):
                denb = k.bPC if h == 0 else k.bPD
                denp = (k.PC if h == 0 else k.PD)[0:1, 0:512]
                deng.append(MMGroup(k, denp, denb, nkb))
            ot0 = MMGroup(k, k.PB[0:64, 0:512], k.bPB0, nkb)
            ot1 = MMGroup(k, k.PB[64:128, 512:1024], k.bPB1, nkb)

            def sc_bank(kb, h):
                if kb % 2 == 0:
                    return (k.PA[:, h * 512:(h + 1) * 512], k.bPA0 if h == 0 else k.bPA1)
                return ((k.PEb if h == 0 else k.PF)[:, 0:512], k.bPE if h == 0 else k.bPF)

            def emit_scores(kb):
                for h in range(2):
                    ei = (kb * 2 + h) % 4
                    eo = L1A_EXP + ei * 512
                    sb_ap, sb = sc_bank(kb, h)
                    mm(k, sb_ap, sb,
                       [(scr[64 * h:64 * h + 64, L1A_QK + 2048 + kb * 128: L1A_QK + 2048 + (kb + 1) * 128],
                         scr[64 * h:64 * h + 64, L1A_QK + qc * 512: L1A_QK + (qc + 1) * 512],
                         [qk.res])],
                       tp=(64 * h, 0))
                    p.op("act", lambda eng, eo=eo, sp=sb_ap, b=k.cv(-20.0): eng.activation(
                        scr[:, eo:eo + 512], sp, AF.Exp, bias=b, scale=0.125),
                        reads=[sb, k.cvals_r], writes=[expb[ei].res], deps=expb[ei].wdeps())
                    off = qc * 512 - kb * 128
                    if off < 512:
                        cb = off + 384
                        assert 0 <= cb and cb + 512 <= BAND_W
                        p.op("dve", lambda eng, eo=eo, cb=cb: eng.tensor_tensor(
                            scr[:, eo:eo + 512], scr[:, eo:eo + 512],
                            k.band[:, cb:cb + 512], ALU.mult),
                            reads=[expb[ei].res, k.band_r], writes=[expb[ei].res])

            emit_scores(0)
            for kb in range(nkb):
                if kb + 1 < nkb:
                    emit_scores(kb + 1)
                for h in range(2):
                    ei = (kb * 2 + h) % 4
                    eo = L1A_EXP + ei * 512
                    deng[h].part(k.ones_col_bf[:], scr[:, eo:eo + 512],
                                 [expb[ei].res, k.ones_col_bf_r])
                    hw = (pair % 4) * 2 + h
                    (ot0 if h == 0 else ot1).part(
                        scr[:, L1A_VG + kb * 512 + hw * 64: L1A_VG + kb * 512 + hw * 64 + 64],
                        scr[:, eo:eo + 512],
                        [vg.res, expb[ei].res], tp=(0, 64 * h))
            for h in range(2):
                den = k.row_t.carve(h * 512, (h + 1) * 512, "den%d" % h)
                bank = k.bPC if h == 0 else k.bPD
                p.op("dve", lambda eng, h=h, ps=(k.PC if h == 0 else k.PD): eng.tensor_copy(
                    rows[0:1, h * 512:(h + 1) * 512], ps[0:1, 0:512]),
                    reads=[bank], writes=[den.res], deps=den.wdeps())
                p.op("act", lambda eng, h=h: eng.activation(
                    rows[0:1, h * 512:(h + 1) * 512], rows[0:1, h * 512:(h + 1) * 512], AF.Ln),
                    reads=[den.res], writes=[den.res])
                p.op("act", lambda eng, h=h: eng.activation(
                    rows[0:1, h * 512:(h + 1) * 512], rows[0:1, h * 512:(h + 1) * 512],
                    AF.Exp, scale=-1.0),
                    reads=[den.res], writes=[den.res])
                rb_ap = k.PA[64 * h:64 * h + 64, h * 512:(h + 1) * 512]
                rb = k.bPA0 if h == 0 else k.bPA1
                mm(k, rb_ap, rb,
                   [(k.ones_row_f32[0:1, 0:64], rows[0:1, h * 512:(h + 1) * 512],
                     [den.res, k.ones_row_f32_r])], tp=(0, 64 * h))
                p.op("act", lambda eng, h=h: eng.activation(
                    scr[64 * h:64 * h + 64, L1A_EXP + h * 512: L1A_EXP + h * 512 + 512],
                    k.PB[64 * h:64 * h + 64, 512 * h:512 * h + 512], AF.Copy),
                    reads=[k.bPB0 if h == 0 else k.bPB1], writes=[expb[h].res])
                o_col = L1A_OT + pair * 2048 + qc * 512
                p.op("dve", lambda eng, h=h, o=o_col, rba=rb_ap: eng.tensor_tensor(
                    scr[64 * h:64 * h + 64, o:o + 512],
                    scr[64 * h:64 * h + 64, L1A_EXP + h * 512: L1A_EXP + h * 512 + 512],
                    rba, ALU.mult),
                    reads=[expb[h].res, rb], writes=[ot.res], deps=ot.wdeps())

    p.barrier()
    wob = [k.scr_t.carve(L1A_XN + kt * 1024, L1A_XN + (kt + 1) * 1024, "wob%d" % kt)
           for kt in range(NT)]
    for kt in range(NT):
        stream_cast(k, fslot[kt % 2], (kt % 2) * 1024, wo[kt], 1024,
                    L1A_XN + kt * 1024, wob[kt])
    for m in range(NT):
        for c in range(NC):
            par = (m * NC + c) % 2
            bank_ap = k.PA[:, par * 512:(par + 1) * 512]
            bank = k.bPA0 if par == 0 else k.bPA1
            gmm = MMGroup(k, bank_ap, bank, NT)
            for kt in range(NT):
                gmm.part(scr[:, L1A_XN + kt * 1024 + m * 128: L1A_XN + kt * 1024 + (m + 1) * 128],
                         scr[:, L1A_OT + kt * 2048 + c * 512: L1A_OT + kt * 2048 + (c + 1) * 512],
                         [wob[kt].res, ot.res])
            a_sl = (slice(None), slice(m * S + c * 512, m * S + (c + 1) * 512))
            p.op("dve", lambda eng, a=a_sl, m=m, b=bank_ap: eng.scalar_tensor_tensor(
                k.acc[a], b, k.misc[:, bo_c0 + m:bo_c0 + m + 1], k.acc[a], ALU.add, ALU.add),
                reads=[bank, bo_res, k.acc_r[m]], writes=[k.acc_r[m]])
    p.barrier()


def emit_outer_ln(k, eps=1e-5):
    mu_res, rstd_res = stats(k, eps)
    ln_apply(k, mu_res, rstd_res)
    k.p.barrier()



BLOCK_EMITTERS = {}
PREPPERS = {}


def build_program(blocks):
    nc = bass.Bass("TRN2", target_bir_lowering=False)
    with contextlib.ExitStack() as es:
        k = K(nc, es)
        xio = k.dram("xio", [D, S], F32)
        yio = k.dram("yio", [D, S], F32, kind="ExternalOutput")
        for j in range(NT):
            k.p.op("sp", lambda eng, j=j: eng.dma_start(
                out=k.acc[:, j * S:(j + 1) * S], in_=xio[j * 128:(j + 1) * 128, :]),
                writes=[k.acc_r[j]], dma=True)
        for b in blocks:
            BLOCK_EMITTERS[b](k)
            emit_outer_ln(k)
        out_toks = []
        for j in range(NT):
            t = k.p.op("sp", lambda eng, j=j: eng.dma_start(
                out=yio[j * 128:(j + 1) * 128, :], in_=k.acc[:, j * S:(j + 1) * S]),
                reads=[k.acc_r[j]], dma=True)
            out_toks.append(t)
        k.p.op("sp", lambda eng: eng.nop(), deps=out_toks)
        sems = {}
        with contextlib.ExitStack() as es2:
            for e in ENGS:
                sems[(e, "c")] = es2.enter_context(nc.semaphore("s_" + e))
                for i in range(NDMA):
                    sems[(e, "d", i)] = es2.enter_context(
                        nc.semaphore("sd_%s_%d" % (e, i)))
            block = es2.enter_context(nc.Block())
            k.p.emit(block, sems)
    return nc, k


BLOCK_EMITTERS["l1_attn"] = emit_l1_attn





def emit_top2_gate(k, lhsT_tile_fn, lhsT_res_fn, wg_sb, wg_res, bg_row, bg_res,
                   pexp_off, pexp_reg, scale_col_fn=None, scale_col_res=None):
    p, scr, rows = k.p, k.scr, k.rows
    gt = k.f32_t.carve(2560, 2760, "gate")
    GT = 2560
    for t in range(16):
        g = MMGroup(k, k.PC[0:128, 0:8], k.bPC, NT + (1 if bg_row is not None else 0))
        for kt in range(NT):
            g.part(lhsT_tile_fn(kt, t), wg_sb[:, kt * 8:(kt + 1) * 8],
                   [lhsT_res_fn(kt), wg_res])
        if bg_row is not None:
            g.part(k.ones_row_f32[:], bg_row, [bg_res, k.ones_row_f32_r])
        d = gt.wdeps()
        p.op("dve", lambda eng: eng.tensor_copy(k.f32s[:, GT:GT + 8], k.PC[0:128, 0:8]),
             reads=[k.bPC], writes=[gt.res], deps=d)
        if scale_col_fn is not None:
            p.op("dve", lambda eng, sc=scale_col_fn(t): eng.tensor_scalar(
                k.f32s[:, GT:GT + 8], k.f32s[:, GT:GT + 8], sc, None, ALU.mult),
                reads=[gt.res, scale_col_res], writes=[gt.res])
        p.op("dve", lambda eng: eng.tensor_reduce(
            k.f32s[:, GT + 8:GT + 9], k.f32s[:, GT:GT + 8], AX.X, ALU.max),
            reads=[gt.res], writes=[gt.res])
        p.op("dve", lambda eng: eng.tensor_scalar(
            k.f32s[:, GT + 16:GT + 24], k.f32s[:, GT:GT + 8],
            k.f32s[:, GT + 8:GT + 9], None, ALU.is_equal),
            reads=[gt.res], writes=[gt.res])
        p.op("dve", lambda eng: eng.scalar_tensor_tensor(
            k.f32s[:, GT + 32:GT + 40], k.f32s[:, GT + 16:GT + 24], -1.0e30,
            k.f32s[:, GT:GT + 8], ALU.mult, ALU.add),
            reads=[gt.res], writes=[gt.res])
        p.op("dve", lambda eng: eng.tensor_reduce(
            k.f32s[:, GT + 9:GT + 10], k.f32s[:, GT + 32:GT + 40], AX.X, ALU.max),
            reads=[gt.res], writes=[gt.res])
        p.op("dve", lambda eng: eng.tensor_scalar(
            k.f32s[:, GT + 24:GT + 32], k.f32s[:, GT + 32:GT + 40],
            k.f32s[:, GT + 9:GT + 10], None, ALU.is_equal),
            reads=[gt.res], writes=[gt.res])
        p.op("dve", lambda eng: eng.tensor_tensor(
            k.f32s[:, GT + 10:GT + 11], k.f32s[:, GT + 8:GT + 9],
            k.f32s[:, GT + 9:GT + 10], ALU.subtract),
            reads=[gt.res], writes=[gt.res])
        p.op("act", lambda eng: eng.activation(
            k.f32s[:, GT + 11:GT + 12], k.f32s[:, GT + 10:GT + 11], AF.Sigmoid),
            reads=[gt.res], writes=[gt.res])
        p.op("act", lambda eng: eng.activation(
            k.f32s[:, GT + 12:GT + 13], k.f32s[:, GT + 10:GT + 11], AF.Sigmoid,
            scale=-1.0),
            reads=[gt.res], writes=[gt.res])
        p.op("dve", lambda eng: eng.tensor_scalar(
            k.f32s[:, GT + 16:GT + 24], k.f32s[:, GT + 16:GT + 24],
            k.f32s[:, GT + 11:GT + 12], None, ALU.mult),
            reads=[gt.res], writes=[gt.res])
        p.op("dve", lambda eng: eng.tensor_scalar(
            k.f32s[:, GT + 24:GT + 32], k.f32s[:, GT + 24:GT + 32],
            k.f32s[:, GT + 12:GT + 13], None, ALU.mult),
            reads=[gt.res], writes=[gt.res])
        p.op("dve", lambda eng: eng.tensor_tensor(
            k.f32s[:, GT + 40:GT + 48], k.f32s[:, GT + 16:GT + 24],
            k.f32s[:, GT + 24:GT + 32], ALU.add),
            reads=[gt.res], writes=[gt.res])
        p.op("pe", lambda eng: eng.transpose(
            k.PF[0:8, 0:128], k.f32s[:, GT + 40:GT + 48], k.id_f32[:]),
            reads=[gt.res, k.id_f32_r], writes=[k.bPF])
        p.op("dve", lambda eng, t=t: eng.tensor_copy(
            scr[0:8, pexp_off + t * 128: pexp_off + (t + 1) * 128], k.PF[0:8, 0:128]),
            reads=[k.bPF], writes=[pexp_reg.res], deps=pexp_reg.wdeps())


L0F_XB = 0
L0F_XP = 8192
L0F_H = 16384
L0F_PEXP = 49152
L0F_PB = 51200
L0F_W13 = 53248


def emit_l0_ffn(k):
    import os
    dbg_v = os.environ.get("L0F_DEBUG", "0")
    dbg = dbg_v == "1"
    halves = [1] if dbg_v in ("1", "2", "3") else range(2)
    experts = ([7] if dbg_v in ("1", "4") else [6, 7] if dbg_v == "3" else range(8))
    p, scr, rows = k.p, k.scr, k.rows
    w1d = k.dram("l0f_w1", [8, 32, 128, 1024])
    w3d = k.dram("l0f_w3", [8, 32, 128, 1024])
    w2d = k.dram("l0f_w2", [8, 8, 128, 4096])
    wgd = k.dram("l0f_wg", [128, 64])
    bgd = k.dram("L0_ffn_bg", [8])

    wgr = k.f32_t.carve(2760, 2824, "wg")
    p.op("sp", lambda eng: eng.dma_start(out=k.f32s[:, 2760:2824], in_=wgd[:]),
         writes=[wgr.res], deps=wgr.wdeps(), dma=True)
    bgr = k.row_t.carve(3072, 3080, "bg")
    p.op("sp", lambda eng: eng.dma_start(
        out=rows[0:1, 3072:3080], in_=bgd.rearrange("(o e) -> o e", o=1)),
        writes=[bgr.res], deps=bgr.wdeps(), dma=True)

    pexp = k.scr_t.carve(L0F_PEXP, L0F_PEXP + 2048, "pexp")
    emit_top2_gate(k,
                   lambda kt, t: k.acc[:, kt * S + t * 128: kt * S + (t + 1) * 128],
                   lambda kt: k.acc_r[kt],
                   k.f32s[:, 2760:2824], wgr.res,
                   rows[0:1, 3072:3080], bgr.res,
                   L0F_PEXP, pexp)

    pb = k.scr_t.carve(L0F_PB, L0F_PB + 2048, "pb")
    for half in halves:
        hb = half * 1024
        xb = k.scr_t.carve(L0F_XB, L0F_XB + 8192, "xb%d" % half)
        for kt in range(NT):
            p.op("dve", lambda eng, kt=kt, hb=hb: eng.tensor_copy(
                scr[:, L0F_XB + kt * 1024: L0F_XB + (kt + 1) * 1024],
                k.acc[:, kt * S + hb: kt * S + hb + 1024]),
                reads=[k.acc_r[kt]], writes=[xb.res], deps=xb.wdeps())
            p.op("dve", lambda eng, kt=kt, hb=hb: eng.tensor_scalar(
                k.acc[:, kt * S + hb: kt * S + hb + 1024],
                k.acc[:, kt * S + hb: kt * S + hb + 1024], 2.0, None, ALU.mult),
                reads=[xb.res], writes=[k.acc_r[kt]])
        for e in experts:
            for c in range(2):
                mm(k, k.PEb[:, 0:512], k.bPE,
                   [(k.onehot8[0:8, e * 128:(e + 1) * 128],
                     scr[0:8, L0F_PEXP + hb + c * 512: L0F_PEXP + hb + (c + 1) * 512],
                     [pexp.res, k.onehot8_r])])
                p.op("dve", lambda eng, c=c: eng.tensor_copy(
                    scr[:, L0F_PB + c * 512: L0F_PB + (c + 1) * 512], k.PEb[:, 0:512]),
                    reads=[k.bPE], writes=[pb.res], deps=pb.wdeps())
            xp = [k.scr_t.carve(L0F_XP + kt * 1024, L0F_XP + (kt + 1) * 1024,
                                "xp%d_%d_%d" % (half, e, kt)) for kt in range(NT)]
            for kt in range(NT):
                p.op("dve", lambda eng, kt=kt: eng.tensor_tensor(
                    scr[:, L0F_XP + kt * 1024: L0F_XP + (kt + 1) * 1024],
                    scr[:, L0F_XB + kt * 1024: L0F_XB + (kt + 1) * 1024],
                    scr[:, L0F_PB: L0F_PB + 1024], ALU.mult),
                    reads=[xb.res, pb.res], writes=[xp[kt].res], deps=xp[kt].wdeps())
            w1r = [k.scr_t.carve(L0F_W13 + i * 512, L0F_W13 + (i + 1) * 512, "w1r%d" % i)
                   for i in range(2)]
            w3r = [k.scr_t.carve(L0F_W13 + 1024 + i * 512, L0F_W13 + 1024 + (i + 1) * 512,
                                 "w3r%d" % i) for i in range(2)]
            fs6 = [k.f32_t.carve(i * 512, (i + 1) * 512, "fs%d" % i) for i in range(6)]
            sctr = [0, 0]
            hreg = k.scr_t.carve(L0F_H, L0F_H + 32768, "h%d_%d" % (half, e))

            def l0f_streams(ht):
                for hc in range(2):
                    s1 = sctr[0] % 3
                    sctr[0] += 1
                    stream_cast(k, fs6[s1], s1 * 512,
                                w1d[e, ht, :, hc * 512:(hc + 1) * 512],
                                512, L0F_W13 + hc * 512, w1r[hc])
                    s3 = 3 + sctr[1] % 3
                    sctr[1] += 1
                    stream_cast(k, fs6[s3], s3 * 512,
                                w3d[e, ht, :, hc * 512:(hc + 1) * 512],
                                512, L0F_W13 + 1024 + hc * 512, w3r[hc])

            l0f_streams(0)
            for ht in range(32):
                par = ht % 2
                if par == 0:
                    ub = [(k.PA[:, 0:512], k.bPA0), (k.PA[:, 512:1024], k.bPA1)]
                    vb = [(k.PB[:, 0:512], k.bPB0), (k.PB[:, 512:1024], k.bPB1)]
                else:
                    ub = [(k.PC[0:128, 0:512], k.bPC), (k.PD[0:128, 0:512], k.bPD)]
                    vb = [(k.PEb[0:128, 0:512], k.bPE), (k.PF[0:128, 0:512], k.bPF)]
                for c in range(2):
                    gu = MMGroup(k, ub[c][0], ub[c][1], NT)
                    for kt in range(NT):
                        gu.part(scr[:, L0F_W13 + kt * 128: L0F_W13 + (kt + 1) * 128],
                                scr[:, L0F_XP + kt * 1024 + c * 512: L0F_XP + kt * 1024 + (c + 1) * 512],
                                [w1r[kt // 4].res, xp[kt].res])
                    gv = MMGroup(k, vb[c][0], vb[c][1], NT)
                    for kt in range(NT):
                        gv.part(scr[:, L0F_W13 + 1024 + kt * 128: L0F_W13 + 1024 + (kt + 1) * 128],
                                scr[:, L0F_XP + kt * 1024 + c * 512: L0F_XP + kt * 1024 + (c + 1) * 512],
                                [w3r[kt // 4].res, xp[kt].res])
                if ht + 1 < 32:
                    l0f_streams(ht + 1)
                hcol = L0F_H + ht * 1024
                for c in range(2):
                    p.op("act", lambda eng, hcol=hcol, c=c, ua=ub[c][0]: eng.activation(
                        scr[:, hcol + c * 512:hcol + (c + 1) * 512], ua, AF.Silu),
                        reads=[ub[c][1]], writes=[hreg.res], deps=hreg.wdeps())
                    p.op("dve", lambda eng, hcol=hcol, c=c, va=vb[c][0]: eng.tensor_tensor(
                        scr[:, hcol + c * 512:hcol + (c + 1) * 512],
                        scr[:, hcol + c * 512:hcol + (c + 1) * 512], va, ALU.mult),
                        reads=[vb[c][1], hreg.res], writes=[hreg.res])
            if dbg:
                continue
            w2b = [k.scr_t.carve(L0F_XP + i * 1024, L0F_XP + (i + 1) * 1024, "w2b%d" % i)
                   for i in range(4)]
            w2f = [k.f32_t.carve(i * 1024, (i + 1) * 1024, "w2f%d" % i) for i in range(2)]
            for d2 in range(NT):
                for q in range(4):
                    stream_cast(k, w2f[q % 2], (q % 2) * 1024,
                                w2d[e, d2, :, q * 1024:(q + 1) * 1024],
                                1024, L0F_XP + q * 1024, w2b[q], crit=False)
                if d2 % 2 == 0:
                    yb = [(k.PA[:, 0:512], k.bPA0), (k.PA[:, 512:1024], k.bPA1)]
                else:
                    yb = [(k.PB[:, 0:512], k.bPB0), (k.PB[:, 512:1024], k.bPB1)]
                for c in range(2):
                    gy = MMGroup(k, yb[c][0], yb[c][1], 32)
                    for kt in range(32):
                        gy.part(scr[:, L0F_XP + kt * 128: L0F_XP + (kt + 1) * 128],
                                scr[:, L0F_H + kt * 1024 + c * 512: L0F_H + kt * 1024 + (c + 1) * 512],
                                [w2b[kt // 8].res, hreg.res])
                    p.op("dve", lambda eng, d2=d2, c=c, hb=hb, ps=yb[c][0]:
                         eng.tensor_tensor(
                        k.acc[:, d2 * S + hb + c * 512: d2 * S + hb + (c + 1) * 512],
                        k.acc[:, d2 * S + hb + c * 512: d2 * S + hb + (c + 1) * 512],
                        ps, ALU.add),
                        reads=[yb[c][1], k.acc_r[d2]],
                        writes=[k.acc_r[d2]])
    p.barrier()


BLOCK_EMITTERS["l0_ffn"] = emit_l0_ffn


def prep_l0_ffn(inputs):
    W1 = np.asarray(inputs["L0_ffn_W1"], np.float32)
    W3 = np.asarray(inputs["L0_ffn_W3"], np.float32)
    W2 = np.asarray(inputs["L0_ffn_W2"], np.float32)
    w1t = np.ascontiguousarray(
        W1.reshape(8, 8, 128, 32, 128).transpose(0, 3, 2, 1, 4).reshape(8, 32, 128, 1024))
    w3t = np.ascontiguousarray(
        W3.reshape(8, 8, 128, 32, 128).transpose(0, 3, 2, 1, 4).reshape(8, 32, 128, 1024))
    w2t = np.ascontiguousarray(
        W2.reshape(8, 32, 128, 8, 128).transpose(0, 3, 2, 1, 4).reshape(8, 8, 128, 4096))
    Wg = np.asarray(inputs["L0_ffn_Wg"], np.float32)
    wgt = np.ascontiguousarray(Wg.reshape(8, 128, 8).transpose(1, 0, 2).reshape(128, 64))
    return {"l0f_w1": w1t, "l0f_w3": w3t, "l0f_w2": w2t, "l0f_wg": wgt,
            "L0_ffn_bg": np.asarray(inputs["L0_ffn_bg"], np.float32)}


PREPPERS["l0_ffn"] = prep_l0_ffn


def prep_l1_ffn(inputs):
    wd = np.asarray(inputs["L1_ffn_w_down"], np.float32)
    wq = np.asarray(inputs["L1_ffn_wq"], np.float32)
    wk = np.asarray(inputs["L1_ffn_wk"], np.float32)
    wv = np.asarray(inputs["L1_ffn_wv"], np.float32)
    wu = np.asarray(inputs["L1_ffn_w_up"], np.float32)
    rw = np.asarray(inputs["L1_ffn_router_w"], np.float32)
    w1 = np.asarray(inputs["L1_ffn_w1"], np.float32)
    w3 = np.asarray(inputs["L1_ffn_w3"], np.float32)
    w2 = np.asarray(inputs["L1_ffn_w2"], np.float32)
    perm = np.empty(512, np.int64)
    for h in range(16):
        for f in range(16):
            perm[16 * h + f] = 32 * h + 2 * f
            perm[256 + 16 * h + f] = 32 * h + 2 * f + 1
    wq_p = wq[:, perm]
    wk_p = wk[:, perm]
    tile4 = lambda w, M: np.ascontiguousarray(
        w.reshape(w.shape[0] // 128, 128, M, 128).transpose(2, 1, 0, 3)
        .reshape(M, 128, (w.shape[0] // 128) * 128))
    out = {
        "l1f_wd": tile4(wd, 4),
        "l1f_wq": tile4(wq_p, 4),
        "l1f_wk": tile4(wk_p, 4),
        "L1_ffn_wv": wv,
        "l1f_wu": tile4(wu, 8),
        "l1f_rw": np.ascontiguousarray(rw.reshape(8, 128, 8).transpose(1, 0, 2).reshape(128, 64)),
        "l1f_w1": np.ascontiguousarray(
            w1.reshape(8, 8, 128, 16, 128).transpose(0, 3, 2, 1, 4).reshape(8, 16, 128, 1024)),
        "l1f_w3": np.ascontiguousarray(
            w3.reshape(8, 8, 128, 16, 128).transpose(0, 3, 2, 1, 4).reshape(8, 16, 128, 1024)),
        "l1f_w2": np.ascontiguousarray(
            w2.reshape(8, 16, 128, 8, 128).transpose(0, 3, 2, 1, 4).reshape(8, 8, 128, 2048)),
        "L1_ffn_g1": np.asarray(inputs["L1_ffn_g1"], np.float32),
        "L1_ffn_g2": np.asarray(inputs["L1_ffn_g2"], np.float32),
    }
    f = np.arange(16)
    inv = 10000.0 ** (-(2.0 * f) / 32.0)
    ang = np.arange(2048)[None, :] * np.tile(inv, 8)[:, None]
    out["l1f_cos"] = np.cos(ang).astype(BF)
    out["l1f_sin"] = np.sin(ang).astype(BF)
    sel = np.zeros((128, 512), BF)
    for par in range(2):
        for j in range(4):
            for t in range(16):
                sel[64 * par + 16 * j + t, par * 128 + 32 * j + t] = 1
                sel[64 * par + 16 * j + t, (2 + par) * 128 + 32 * j + 16 + t] = 1
    out["l1f_sel"] = sel
    return out


PREPPERS["l1_ffn"] = prep_l1_ffn



L1F_H1 = 0
L1F_CT = 16384
L1F_Q = 24576
L1F_K = 32768
L1F_V = 40960
L1F_OT = 16384
L1F_COS = 0
L1F_SIN = 2048
L1F_SEL = 4096
L1F_RTMP = 4608
L1F_RR = 6656
L1F_WST = 53248


def emit_rms_rows(k, eps, src_fn=None, src_res=None, bf_src=False):
    p, scr, rows = k.p, k.scr, k.rows
    rstd = k.row_t.carve(2048, 4096, "rmsr")
    xc = k.scr_t.carve(STAT_XC, STAT_XC + 512, "rmsxc")
    sq = k.scr_t.carve(STAT_SQ, STAT_SQ + 512, "rmssq")
    for c in range(NC):
        g2 = MMGroup(k, k.PD[0:1, 0:512], k.bPD, NT)
        for j in range(NT):
            if not bf_src:
                p.op("dve", lambda eng, j=j, c=c: eng.tensor_copy(
                    scr[:, STAT_XC:STAT_XC + 512],
                    src_fn(j)[:, c * 512:(c + 1) * 512]),
                    reads=[src_res] if src_res else [k.acc_r[j]], writes=[xc.res],
                    deps=xc.wdeps())
                sqsrc = scr[:, STAT_XC:STAT_XC + 512]
                sqread = xc.res
            else:
                sqsrc = src_fn(j)[:, c * 512:(c + 1) * 512]
                sqread = src_res
            p.op("act", lambda eng, s=sqsrc: eng.activation(
                scr[:, STAT_SQ:STAT_SQ + 512], s, AF.Square),
                reads=[sqread], writes=[sq.res], deps=sq.wdeps())
            g2.part(k.ones_col_bf[:], scr[:, STAT_SQ:STAT_SQ + 512],
                    [sq.res, k.ones_col_bf_r])
        p.op("dve", lambda eng, c=c: eng.tensor_scalar(
            rows[0:1, 2048 + c * 512:2048 + (c + 1) * 512], k.PD[0:1, 0:512],
            1.0 / D, None, ALU.mult),
            reads=[k.bPD], writes=[rstd.res], deps=rstd.wdeps())
    p.op("act", lambda eng, b=k.cv(eps, 1): eng.activation(
        rows[0:1, 2048:4096], rows[0:1, 2048:4096], AF.Ln, bias=b),
        reads=[rstd.res, k.cvals_r], writes=[rstd.res])
    p.op("act", lambda eng: eng.activation(
        rows[0:1, 2048:4096], rows[0:1, 2048:4096], AF.Exp, scale=-0.5),
        reads=[rstd.res], writes=[rstd.res])
    return rstd


def emit_l1_ffn(k):
    import os
    stop = os.environ.get("L1F_STOP", "")
    p, scr, rows = k.p, k.scr, k.rows
    g1_c0, g1_res = param_cols(k, "L1_ffn_g1")
    g2_c0, g2_res = param_cols(k, "L1_ffn_g2")
    wdd = k.dram("l1f_wd", [4, 128, 1024])
    wqd = k.dram("l1f_wq", [4, 128, 512])
    wkd = k.dram("l1f_wk", [4, 128, 512])
    wvd = k.dram("L1_ffn_wv", [512, 512])
    wud = k.dram("l1f_wu", [8, 128, 512])
    rwd = k.dram("l1f_rw", [128, 64])
    w1d = k.dram("l1f_w1", [8, 16, 128, 1024])
    w3d = k.dram("l1f_w3", [8, 16, 128, 1024])
    w2d = k.dram("l1f_w2", [8, 8, 128, 2048])
    cosd = k.dram("l1f_cos", [128, 2048], BF16)
    sind = k.dram("l1f_sin", [128, 2048], BF16)
    seld = k.dram("l1f_sel", [128, 512], BF16)

    fsl = [k.f32_t.carve(0, 1024, "lf0"), k.f32_t.carve(1024, 2048, "lf1")]

    rstd = emit_rms_rows(k, 1e-5, src_fn=lambda j: k.acc[:, j * S:(j + 1) * S])
    h1 = k.scr_t.carve(L1F_H1, L1F_H1 + 16384, "h1")
    for c in range(NC):
        mm(k, k.PF[:, 0:512], k.bPF,
           [(k.ones_row_f32[:], rows[0:1, 2048 + c * 512:2048 + (c + 1) * 512],
             [rstd.res, k.ones_row_f32_r])])
        for j in range(NT):
            p.op("dve", lambda eng, j=j, c=c, gc=g1_c0: eng.scalar_tensor_tensor(
                scr[:, L1F_H1 + j * S + c * 512: L1F_H1 + j * S + (c + 1) * 512],
                k.acc[:, j * S + c * 512: j * S + (c + 1) * 512],
                k.misc[:, gc + j:gc + j + 1], k.PF[:, 0:512], ALU.mult, ALU.mult),
                reads=[k.acc_r[j], g1_res, k.bPF], writes=[h1.res], deps=h1.wdeps())

    ct = k.scr_t.carve(L1F_CT, L1F_CT + 4096, "ct")
    wst = k.scr_t.carve(L1F_WST, L1F_WST + 1024, "wst")
    for m in range(4):
        stream_cast(k, fsl[m % 2], (m % 2) * 1024, wdd[m], 1024, L1F_WST, wst)
        for c in range(NC):
            g = MMGroup(k, k.PA[:, 0:512], k.bPA0, NT)
            for kt in range(NT):
                g.part(scr[:, L1F_WST + kt * 128: L1F_WST + (kt + 1) * 128],
                       scr[:, L1F_H1 + kt * S + c * 512: L1F_H1 + kt * S + (c + 1) * 512],
                       [wst.res, h1.res])
            p.op("act", lambda eng, m=m, c=c: eng.activation(
                scr[:, L1F_CT + m * S + c * 512: L1F_CT + m * S + (c + 1) * 512],
                k.PA[:, 0:512], AF.Copy),
                reads=[k.bPA0], writes=[ct.res], deps=ct.wdeps())

    qraw = k.scr_t.carve(L1F_Q, L1F_Q + 8192, "qraw")
    kraw = k.scr_t.carve(L1F_K, L1F_K + 8192, "kraw")
    for qk, (dram_w, reg, base) in enumerate([(wqd, qraw, L1F_Q), (wkd, kraw, L1F_K)]):
        for m in range(4):
            wst = k.scr_t.carve(L1F_WST, L1F_WST + 512, "wqk%d_%d" % (qk, m))
            stream_cast(k, fsl[m % 2], (m % 2) * 1024, dram_w[m], 512, L1F_WST, wst)
            for c in range(NC):
                g = MMGroup(k, k.PA[:, 0:512], k.bPA0, 4)
                for kt in range(4):
                    g.part(scr[:, L1F_WST + kt * 128: L1F_WST + (kt + 1) * 128],
                           scr[:, L1F_CT + kt * S + c * 512: L1F_CT + kt * S + (c + 1) * 512],
                           [wst.res, ct.res])
                p.op("act", lambda eng, b=base, m=m, c=c: eng.activation(
                    scr[:, b + m * S + c * 512: b + m * S + (c + 1) * 512],
                    k.PA[:, 0:512], AF.Copy),
                    reads=[k.bPA0], writes=[reg.res], deps=reg.wdeps())
    vreg = k.scr_t.carve(L1F_V, L1F_V + 8192, "v")
    wstv = [k.scr_t.carve(L1F_WST + i * 512, L1F_WST + (i + 1) * 512, "wv%d" % i)
            for i in range(4)]
    for kt in range(4):
        stream_cast(k, fsl[kt % 2], (kt % 2) * 1024, wvd[kt * 128:(kt + 1) * 128, :],
                    512, L1F_WST + kt * 512, wstv[kt])
    for t in range(16):
        g = MMGroup(k, k.PB[:, 0:512], k.bPB0, 4)
        for kt in range(4):
            g.part(scr[:, L1F_CT + kt * S + t * 128: L1F_CT + kt * S + t * 128 + 128],
                   scr[:, L1F_WST + kt * 512: L1F_WST + (kt + 1) * 512],
                   [ct.res, wstv[kt].res])
        p.op("act", lambda eng, t=t: eng.activation(
            scr[:, L1F_V + t * 512: L1F_V + (t + 1) * 512], k.PB[:, 0:512], AF.Copy),
            reads=[k.bPB0], writes=[vreg.res], deps=vreg.wdeps())

    if stop == "p3":
        return
    cos = k.scr_t.carve(L1F_COS, L1F_COS + 2048, "cos")
    sin = k.scr_t.carve(L1F_SIN, L1F_SIN + 2048, "sin")
    sel = k.scr_t.carve(L1F_SEL, L1F_SEL + 512, "sel")
    p.op("sp", lambda eng: eng.dma_start(out=scr[:, L1F_COS:L1F_COS + 2048], in_=cosd[:]),
         writes=[cos.res], deps=cos.wdeps(), dma=True)
    p.op("sp", lambda eng: eng.dma_start(out=scr[:, L1F_SIN:L1F_SIN + 2048], in_=sind[:]),
         writes=[sin.res], deps=sin.wdeps(), dma=True)
    p.op("sp", lambda eng: eng.dma_start(out=scr[:, L1F_SEL:L1F_SEL + 512], in_=seld[:]),
         writes=[sel.res], deps=sel.wdeps(), dma=True)
    rtmp = k.scr_t.carve(L1F_RTMP, L1F_RTMP + 2048, "rtmp")
    hc_regs = []
    for qk, (raw_base, raw_reg) in enumerate([(L1F_Q, qraw), (L1F_K, kraw)]):
        rr1 = [k.scr_t.carve(L1F_RR + i * S, L1F_RR + (i + 1) * S, "rr1_%d_%d" % (qk, i))
               for i in range(2)]
        rr2 = [k.scr_t.carve(L1F_RR + (2 + i) * S, L1F_RR + (3 + i) * S, "rr2_%d_%d" % (qk, i))
               for i in range(2)]
        rtm0 = k.scr_t.carve(L1F_RTMP, L1F_RTMP + S, "rtm%d" % qk)
        rtm = [rtm0, rtm0]
        for i in range(2):
            q1 = (slice(None), slice(raw_base + i * S, raw_base + (i + 1) * S))
            q2 = (slice(None), slice(raw_base + (2 + i) * S, raw_base + (3 + i) * S))
            r1 = (slice(None), slice(L1F_RR + i * S, L1F_RR + (i + 1) * S))
            r2 = (slice(None), slice(L1F_RR + (2 + i) * S, L1F_RR + (3 + i) * S))
            tm = (slice(None), slice(L1F_RTMP, L1F_RTMP + S))
            p.op("dve", lambda eng, a=q1, o=r1: eng.tensor_tensor(
                scr[o], scr[a], scr[:, L1F_COS:L1F_COS + 2048], ALU.mult),
                reads=[raw_reg.res, cos.res], writes=[rr1[i].res], deps=rr1[i].wdeps())
            p.op("dve", lambda eng, a=q1, o=tm: eng.tensor_tensor(
                scr[o], scr[a], scr[:, L1F_SIN:L1F_SIN + 2048], ALU.mult),
                reads=[raw_reg.res, sin.res], writes=[rtm[i].res], deps=rtm[i].wdeps())
            p.op("dve", lambda eng, a=q2, o=r2: eng.tensor_tensor(
                scr[o], scr[a], scr[:, L1F_SIN:L1F_SIN + 2048], ALU.mult),
                reads=[raw_reg.res, sin.res], writes=[rr2[i].res], deps=rr2[i].wdeps())
            p.op("dve", lambda eng, o=r1, b=r2: eng.tensor_tensor(
                scr[o], scr[o], scr[b], ALU.subtract),
                reads=[rr1[i].res, rr2[i].res], writes=[rr1[i].res])
            p.op("dve", lambda eng, a=q2, o=r2: eng.tensor_tensor(
                scr[o], scr[a], scr[:, L1F_COS:L1F_COS + 2048], ALU.mult),
                reads=[raw_reg.res, rr2[i].res], writes=[rr2[i].res])
            p.op("dve", lambda eng, o=r2, b=tm: eng.tensor_tensor(
                scr[o], scr[o], scr[b], ALU.add),
                reads=[rr2[i].res, rtm[i].res], writes=[rr2[i].res])
        hc = k.scr_t.carve(raw_base, raw_base + 8192, "hc%d" % qk)
        for i in range(4):
            par = i % 2
            for c in range(NC):
                g = MMGroup(k, k.PA[:, 0:512], k.bPA0, 2)
                g.part(scr[:, L1F_SEL + par * 128: L1F_SEL + (par + 1) * 128],
                       scr[:, L1F_RR + (i // 2) * S + c * 512: L1F_RR + (i // 2) * S + (c + 1) * 512],
                       [sel.res, rr1[i // 2].res])
                g.part(scr[:, L1F_SEL + (2 + par) * 128: L1F_SEL + (3 + par) * 128],
                       scr[:, L1F_RR + (2 + i // 2) * S + c * 512: L1F_RR + (2 + i // 2) * S + (c + 1) * 512],
                       [sel.res, rr2[i // 2].res])
                p.op("act", lambda eng, b=raw_base, i=i, c=c: eng.activation(
                    scr[:, b + i * S + c * 512: b + i * S + (c + 1) * 512],
                    k.PA[:, 0:512], AF.Copy),
                    reads=[k.bPA0], writes=[hc.res], deps=hc.wdeps())
        hc_regs.append(hc)
    qr, kr = hc_regs

    ot = k.scr_t.carve(L1F_OT, L1F_OT + 8192, "ot")
    expb = [k.scr_t.carve(L1F_RR + i * 512, L1F_RR + (i + 1) * 512, "l1fexp%d" % i)
            for i in range(4)]
    SC32 = float(1.0 / np.sqrt(32.0))
    for pr in range(8):
        tile = pr // 2
        for qc in range(NC):
            nkb = 4 * qc + 4
            deng = []
            otg = []
            for h in range(2):
                H = 2 * pr + h
                base = 32 * (H % 4)
                denb = k.bPC if h == 0 else k.bPD
                deng.append(MMGroup(k, (k.PC if h == 0 else k.PD)[0:1, 0:512],
                                    denb, nkb))
                og = MMGroup(k, (k.PEb if h == 0 else k.PF)[base:base + 32, 0:512],
                             k.bPE if h == 0 else k.bPF, nkb)
                otg.append(og)
            def sc_bank2(kb, h):
                if kb % 2 == 0:
                    return (k.PA[:, h * 512:(h + 1) * 512], k.bPA0 if h == 0 else k.bPA1)
                return (k.PB[:, h * 512:(h + 1) * 512], k.bPB0 if h == 0 else k.bPB1)

            def emit_scores2(kb):
                for h in range(2):
                    H = 2 * pr + h
                    base = 32 * (H % 4)
                    ei = (kb * 2 + h) % 4
                    eo = L1F_RR + ei * 512
                    sb_ap, sb = sc_bank2(kb, h)
                    mm(k, sb_ap, sb,
                       [(scr[base:base + 32, L1F_K + tile * S + kb * 128: L1F_K + tile * S + (kb + 1) * 128],
                         scr[base:base + 32, L1F_Q + tile * S + qc * 512: L1F_Q + tile * S + (qc + 1) * 512],
                         [qr.res, kr.res])],
                       tp=(base, 0))
                    p.op("act", lambda eng, eo=eo, sp_=sb_ap, b=k.cv(-12.0): eng.activation(
                        scr[:, eo:eo + 512], sp_, AF.Exp, bias=b, scale=SC32),
                        reads=[sb, k.cvals_r], writes=[expb[ei].res], deps=expb[ei].wdeps())
                    off = qc * 512 - kb * 128
                    if off < 512:
                        cb = off + 384
                        p.op("dve", lambda eng, eo=eo, cb=cb: eng.tensor_tensor(
                            scr[:, eo:eo + 512], scr[:, eo:eo + 512],
                            k.band[:, cb:cb + 512], ALU.mult),
                            reads=[expb[ei].res, k.band_r], writes=[expb[ei].res])

            emit_scores2(0)
            for kb in range(nkb):
                if kb + 1 < nkb:
                    emit_scores2(kb + 1)
                for h in range(2):
                    H = 2 * pr + h
                    base = 32 * (H % 4)
                    ei = (kb * 2 + h) % 4
                    eo = L1F_RR + ei * 512
                    deng[h].part(k.ones_col_bf[:], scr[:, eo:eo + 512],
                                 [expb[ei].res, k.ones_col_bf_r])
                    otg[h].part(
                        scr[:, L1F_V + kb * 512 + 32 * H: L1F_V + kb * 512 + 32 * H + 32],
                        scr[:, eo:eo + 512],
                        [vreg.res, expb[ei].res], tp=(0, base))
            for h in range(2):
                H = 2 * pr + h
                base = 32 * (H % 4)
                den = k.row_t.carve(h * 512, (h + 1) * 512, "l1fden%d" % h)
                bank = k.bPC if h == 0 else k.bPD
                p.op("dve", lambda eng, h=h, ps=(k.PC if h == 0 else k.PD): eng.tensor_copy(
                    rows[0:1, h * 512:(h + 1) * 512], ps[0:1, 0:512]),
                    reads=[bank], writes=[den.res], deps=den.wdeps())
                p.op("act", lambda eng, h=h: eng.activation(
                    rows[0:1, h * 512:(h + 1) * 512], rows[0:1, h * 512:(h + 1) * 512],
                    AF.Ln),
                    reads=[den.res], writes=[den.res])
                p.op("act", lambda eng, h=h: eng.activation(
                    rows[0:1, h * 512:(h + 1) * 512], rows[0:1, h * 512:(h + 1) * 512],
                    AF.Exp, scale=-1.0),
                    reads=[den.res], writes=[den.res])
                rb_ap = k.PB[base:base + 32, 0:512] if h == 0 else k.PB[base:base + 32, 512:1024]
                rb = k.bPB0 if h == 0 else k.bPB1
                mm(k, rb_ap, rb,
                   [(k.ones_row_f32[0:1, 0:32], rows[0:1, h * 512:(h + 1) * 512],
                     [den.res, k.ones_row_f32_r])], tp=(0, base))
                p.op("act", lambda eng, h=h, base=base,
                     ps=(k.PEb if h == 0 else k.PF): eng.activation(
                    scr[base:base + 32, L1F_RR + h * 512: L1F_RR + h * 512 + 512],
                    ps[base:base + 32, 0:512], AF.Copy),
                    reads=[k.bPE if h == 0 else k.bPF], writes=[expb[h].res])
                o_col = L1F_OT + tile * S + qc * 512
                p.op("dve", lambda eng, h=h, base=base, o=o_col, rba=rb_ap: eng.tensor_tensor(
                    scr[base:base + 32, o:o + 512],
                    scr[base:base + 32, L1F_RR + h * 512: L1F_RR + h * 512 + 512],
                    rba, ALU.mult),
                    reads=[expb[h].res, rb], writes=[ot.res], deps=ot.wdeps())

    if stop == "p5":
        return
    p.barrier()
    resbf = k.scr_t.carve(L1F_Q, L1F_Q + 16384, "resbf")
    for m in range(NT):
        wst = k.scr_t.carve(L1F_WST, L1F_WST + 512, "wu%d" % m)
        stream_cast(k, fsl[m % 2], (m % 2) * 1024, wud[m], 512, L1F_WST, wst)
        for c in range(NC):
            par = c % 2
            bank_ap = k.PA[:, par * 512:(par + 1) * 512]
            bank = k.bPA0 if par == 0 else k.bPA1
            g = MMGroup(k, bank_ap, bank, 4)
            for kt in range(4):
                g.part(scr[:, L1F_WST + kt * 128: L1F_WST + (kt + 1) * 128],
                       scr[:, L1F_OT + kt * S + c * 512: L1F_OT + kt * S + (c + 1) * 512],
                       [wst.res, ot.res])
            a_sl = (slice(None), slice(m * S + c * 512, m * S + (c + 1) * 512))
            r_sl = (slice(None), slice(L1F_Q + m * S + c * 512, L1F_Q + m * S + (c + 1) * 512))
            p.op("dve", lambda eng, a=a_sl, r=r_sl, b=bank_ap: eng.tensor_tensor(
                scr[r], k.acc[a], b, ALU.add),
                reads=[k.acc_r[m], bank], writes=[resbf.res], deps=resbf.wdeps())
            p.op("dve", lambda eng, a=a_sl, b=bank_ap: eng.scalar_tensor_tensor(
                k.acc[a], k.acc[a], 2.0, b, ALU.mult, ALU.add),
                reads=[bank, k.acc_r[m]], writes=[k.acc_r[m]])

    rstd2 = emit_rms_rows(
        k, 1e-5, src_fn=lambda j: scr[:, L1F_Q + j * S: L1F_Q + (j + 1) * S],
        src_res=resbf.res, bf_src=True)
    rc0 = k.misc_used
    k.misc_used += 16
    assert k.misc_used <= 128
    rcol_res = Res("rstdcols")
    for t in range(16):
        p.op("pe", lambda eng, t=t: eng.transpose(
            k.PF[0:128, 0:1], rows[0:1, 2048 + t * 128: 2048 + (t + 1) * 128],
            k.id_f32[0:1, 0:1]),
            reads=[rstd2.res, k.id_f32_r], writes=[k.bPF])
        p.op("dve", lambda eng, t=t, rc0=rc0: eng.tensor_copy(
            k.misc[:, rc0 + t:rc0 + t + 1], k.PF[0:128, 0:1]),
            reads=[k.bPF], writes=[rcol_res])
    rwf = k.f32_t.carve(2760, 2824, "rw")
    p.op("sp", lambda eng: eng.dma_start(out=k.f32s[:, 2760:2824], in_=rwd[:]),
         writes=[rwf.res], deps=rwf.wdeps(), dma=True)
    RWP = 43008
    rwp = k.scr_t.carve(RWP, RWP + 64, "rwp")
    for kt in range(NT):
        p.op("dve", lambda eng, kt=kt, gc=g2_c0: eng.tensor_scalar(
            scr[:, RWP + kt * 8: RWP + (kt + 1) * 8],
            k.f32s[:, 2760 + kt * 8: 2760 + (kt + 1) * 8],
            k.misc[:, gc + kt:gc + kt + 1], None, ALU.mult),
            reads=[rwf.res, g2_res], writes=[rwp.res], deps=rwp.wdeps())
    PEXP2 = 40960
    pexp = k.scr_t.carve(PEXP2, PEXP2 + 2048, "l1fpexp")
    emit_top2_gate(k,
                   lambda kt, t: scr[:, L1F_Q + kt * S + t * 128: L1F_Q + kt * S + (t + 1) * 128],
                   lambda kt: resbf.res,
                   scr[:, RWP:RWP + 64], rwp.res, None, None,
                   PEXP2, pexp,
                   scale_col_fn=lambda t: k.misc[:, rc0 + t:rc0 + t + 1],
                   scale_col_res=rcol_res)

    treg = k.scr_t.carve(0, 16384, "T")
    for c in range(NC):
        mm(k, k.PF[:, 0:512], k.bPF,
           [(k.ones_row_f32[:], rows[0:1, 2048 + c * 512:2048 + (c + 1) * 512],
             [rstd2.res, k.ones_row_f32_r])])
        for j in range(NT):
            p.op("dve", lambda eng, j=j, c=c, gc=g2_c0: eng.scalar_tensor_tensor(
                scr[:, j * S + c * 512: j * S + (c + 1) * 512],
                scr[:, L1F_Q + j * S + c * 512: L1F_Q + j * S + (c + 1) * 512],
                k.misc[:, gc + j:gc + j + 1], k.PF[:, 0:512], ALU.mult, ALU.mult),
                reads=[resbf.res, g2_res, k.bPF], writes=[treg.res], deps=treg.wdeps())
    TG, HID, PBR, W13R, W2BR = 16384, 24576, 43072, 45120, 47168
    pb = k.scr_t.carve(PBR, PBR + 2048, "l1fpb")
    for half in range(2):
        hb = half * 1024
        for e in range(8):
            for c in range(2):
                mm(k, k.PEb[:, 0:512], k.bPE,
                   [(k.onehot8[0:8, e * 128:(e + 1) * 128],
                     scr[0:8, PEXP2 + hb + c * 512: PEXP2 + hb + (c + 1) * 512],
                     [pexp.res, k.onehot8_r])])
                p.op("dve", lambda eng, c=c: eng.tensor_copy(
                    scr[:, PBR + c * 512: PBR + (c + 1) * 512], k.PEb[:, 0:512]),
                    reads=[k.bPE], writes=[pb.res], deps=pb.wdeps())
            tg = [k.scr_t.carve(TG + kt * 1024, TG + (kt + 1) * 1024,
                                "tg%d_%d_%d" % (half, e, kt)) for kt in range(NT)]
            for kt in range(NT):
                p.op("dve", lambda eng, kt=kt, hb=hb: eng.tensor_tensor(
                    scr[:, TG + kt * 1024: TG + (kt + 1) * 1024],
                    scr[:, kt * S + hb: kt * S + hb + 1024],
                    scr[:, PBR: PBR + 1024], ALU.mult),
                    reads=[treg.res, pb.res], writes=[tg[kt].res], deps=tg[kt].wdeps())
            w1r = [k.scr_t.carve(W13R + i * 512, W13R + (i + 1) * 512, "l1fw1r%d" % i)
                   for i in range(2)]
            w3r = [k.scr_t.carve(W13R + 1024 + i * 512, W13R + 1024 + (i + 1) * 512,
                                 "l1fw3r%d" % i) for i in range(2)]
            fs6 = [k.f32_t.carve(i * 512, (i + 1) * 512, "l1fs%d" % i) for i in range(6)]
            sctr = [0, 0]
            hreg = k.scr_t.carve(HID, HID + 16384, "l1fh%d_%d" % (half, e))

            def l1f_streams(ht):
                for hc in range(2):
                    s1 = sctr[0] % 3
                    sctr[0] += 1
                    stream_cast(k, fs6[s1], s1 * 512,
                                w1d[e, ht, :, hc * 512:(hc + 1) * 512],
                                512, W13R + hc * 512, w1r[hc])
                    s3 = 3 + sctr[1] % 3
                    sctr[1] += 1
                    stream_cast(k, fs6[s3], s3 * 512,
                                w3d[e, ht, :, hc * 512:(hc + 1) * 512],
                                512, W13R + 1024 + hc * 512, w3r[hc])

            l1f_streams(0)
            for ht in range(16):
                par = ht % 2
                if par == 0:
                    ub = [(k.PA[:, 0:512], k.bPA0), (k.PA[:, 512:1024], k.bPA1)]
                    vb = [(k.PB[:, 0:512], k.bPB0), (k.PB[:, 512:1024], k.bPB1)]
                else:
                    ub = [(k.PC[0:128, 0:512], k.bPC), (k.PD[0:128, 0:512], k.bPD)]
                    vb = [(k.PEb[0:128, 0:512], k.bPE), (k.PF[0:128, 0:512], k.bPF)]
                for c in range(2):
                    gu = MMGroup(k, ub[c][0], ub[c][1], NT)
                    for kt in range(NT):
                        gu.part(scr[:, W13R + kt * 128: W13R + (kt + 1) * 128],
                                scr[:, kt * S + hb + c * 512: kt * S + hb + (c + 1) * 512],
                                [w1r[kt // 4].res, treg.res])
                    gv = MMGroup(k, vb[c][0], vb[c][1], NT)
                    for kt in range(NT):
                        gv.part(scr[:, W13R + 1024 + kt * 128: W13R + 1024 + (kt + 1) * 128],
                                scr[:, TG + kt * 1024 + c * 512: TG + kt * 1024 + (c + 1) * 512],
                                [w3r[kt // 4].res, tg[kt].res])
                if ht + 1 < 16:
                    l1f_streams(ht + 1)
                hcol = HID + ht * 1024
                for c in range(2):
                    p.op("act", lambda eng, hcol=hcol, c=c, ua=ub[c][0]: eng.activation(
                        scr[:, hcol + c * 512:hcol + (c + 1) * 512], ua, AF.Silu),
                        reads=[ub[c][1]], writes=[hreg.res], deps=hreg.wdeps())
                    p.op("dve", lambda eng, hcol=hcol, c=c, va=vb[c][0]: eng.tensor_tensor(
                        scr[:, hcol + c * 512:hcol + (c + 1) * 512],
                        scr[:, hcol + c * 512:hcol + (c + 1) * 512], va, ALU.mult),
                        reads=[vb[c][1], hreg.res], writes=[hreg.res])
            w2b = [k.scr_t.carve(W2BR + i * 1024, W2BR + (i + 1) * 1024, "l1fw2b%d" % i)
                   for i in range(2)]
            w2f = [k.f32_t.carve(i * 1024, (i + 1) * 1024, "lw2f%d" % i) for i in range(2)]
            for d2 in range(NT):
                for q in range(2):
                    stream_cast(k, w2f[q], q * 1024,
                                w2d[e, d2, :, q * 1024:(q + 1) * 1024],
                                1024, W2BR + q * 1024, w2b[q], crit=False)
                if d2 % 2 == 0:
                    yb = [(k.PA[:, 0:512], k.bPA0), (k.PA[:, 512:1024], k.bPA1)]
                else:
                    yb = [(k.PB[:, 0:512], k.bPB0), (k.PB[:, 512:1024], k.bPB1)]
                for c in range(2):
                    gy = MMGroup(k, yb[c][0], yb[c][1], 16)
                    for kt in range(16):
                        gy.part(scr[:, W2BR + kt * 128: W2BR + (kt + 1) * 128],
                                scr[:, HID + kt * 1024 + c * 512: HID + kt * 1024 + (c + 1) * 512],
                                [w2b[kt // 8].res, hreg.res])
                    p.op("dve", lambda eng, d2=d2, c=c, hb=hb, ps=yb[c][0]:
                         eng.tensor_tensor(
                        k.acc[:, d2 * S + hb + c * 512: d2 * S + hb + (c + 1) * 512],
                        k.acc[:, d2 * S + hb + c * 512: d2 * S + hb + (c + 1) * 512],
                        ps, ALU.add),
                        reads=[yb[c][1], k.acc_r[d2]],
                        writes=[k.acc_r[d2]])
    p.barrier()


BLOCK_EMITTERS["l1_ffn"] = emit_l1_ffn


L0A_XB = 0
L0A_OT = 16384
L0A_VP = 32768
L0A_QP = 34816
L0A_KP = 36864
L0A_QPT = 38912
L0A_KPT = 43008
L0A_KTOK = 47104
L0A_KVS = 51200
L0A_NUM = 51456
L0A_WST = 53504
L0A_ASB = 53504
L0A_PROJ = 54528
L0A_KSP = 54784
LN16 = float(np.log(16.0))


def emit_l0_attn(k):
    import os
    dbg = os.environ.get("L0A_DEBUG", "")
    pairs = [7] if dbg else range(8)
    heads = [0] if dbg == "1" else range(2)
    p, scr, rows = k.p, k.scr, k.rows
    qw = k.dram("l0a_qw", [8, 128, 1024])
    kw = k.dram("l0a_kw", [8, 128, 1024])
    vw = k.dram("l0a_vw", [8, 128, 1024])
    ow = k.dram("l0a_ow", [8, 128, 1024])
    projd = k.dram("l0a_projT2", [128, 256])
    vbd = k.dram("L0_attn_v_b", [D])
    qb_c0, qb_res = param_cols(k, "L0_attn_q_b")
    kb_c0, kb_res = param_cols(k, "L0_attn_k_b")
    ga_c0, ga_res = param_cols(k, "L0_attn_gamma")
    be_c0, be_res = param_cols(k, "L0_attn_beta")
    qb8_c0 = k.misc_used
    k.misc_used += 8
    kb8_c0 = k.misc_used
    k.misc_used += 8
    assert k.misc_used <= 128
    qb8_res, kb8_res = Res("qb8"), Res("kb8")
    p.op("dve", lambda eng: eng.tensor_scalar(
        k.misc[:, qb8_c0:qb8_c0 + 8], k.misc[:, qb_c0:qb_c0 + 8], 0.125, None, ALU.mult),
        reads=[qb_res], writes=[qb8_res])
    p.op("dve", lambda eng: eng.tensor_scalar(
        k.misc[:, kb8_c0:kb8_c0 + 8], k.misc[:, kb_c0:kb_c0 + 8], 0.125, None, ALU.mult),
        reads=[kb_res], writes=[kb8_res])
    vb = k.row_t.carve(2560, 3584, "l0avb")
    p.op("sp", lambda eng: eng.dma_start(
        out=rows[0:1, 2560:3584], in_=vbd.rearrange("(o e) -> o e", o=1)),
        writes=[vb.res], deps=vb.wdeps(), dma=True)
    prj = k.scr_t.carve(L0A_PROJ, L0A_PROJ + 256, "prj")
    pf32 = k.f32_t.carve(2760, 3016, "prjf")
    p.op("sp", lambda eng: eng.dma_start(out=k.f32s[:, 2760:3016], in_=projd[:]),
         writes=[pf32.res], deps=pf32.wdeps(), dma=True)
    p.op("dve", lambda eng: eng.tensor_copy(
        scr[:, L0A_PROJ:L0A_PROJ + 256], k.f32s[:, 2760:3016]),
        reads=[pf32.res], writes=[prj.res], deps=prj.wdeps())
    xb = k.scr_t.carve(L0A_XB, L0A_XB + 16384, "l0axb")
    for j in range(NT):
        p.op("dve", lambda eng, j=j: eng.tensor_copy(
            scr[:, L0A_XB + j * S: L0A_XB + (j + 1) * S], k.acc[:, j * S:(j + 1) * S]),
            reads=[k.acc_r[j]], writes=[xb.res], deps=xb.wdeps())

    ot = k.scr_t.carve(L0A_OT, L0A_OT + 16384, "l0aot")
    fsl = [k.f32_t.carve(0, 1024, "l0af0"), k.f32_t.carve(1024, 2048, "l0af1")]
    kvs = k.scr_t.carve(L0A_KVS, L0A_KVS + 256, "kvs")
    num = k.scr_t.carve(L0A_NUM, L0A_NUM + 2048, "num")

    for pr in pairs:
        qp_ = k.scr_t.carve(L0A_QP, L0A_QP + 2048, "qp%d" % pr)
        kp_ = k.scr_t.carve(L0A_KP, L0A_KP + 2048, "kp%d" % pr)
        vp_ = k.scr_t.carve(L0A_VP, L0A_VP + 2048, "vp%d" % pr)
        for qki, (wd_, reg, base, b8c0, b8res) in enumerate(
                [(qw, qp_, L0A_QP, qb8_c0, qb8_res), (kw, kp_, L0A_KP, kb8_c0, kb8_res)]):
            wst = k.scr_t.carve(L0A_WST, L0A_WST + 1024, "l0awq%d" % qki)
            stream_cast(k, fsl[qki], qki * 1024, wd_[pr], 1024, L0A_WST, wst)
            for c in range(NC):
                g = MMGroup(k, k.PA[:, 0:512], k.bPA0, NT)
                for kt in range(NT):
                    g.part(scr[:, L0A_WST + kt * 128: L0A_WST + (kt + 1) * 128],
                           scr[:, L0A_XB + kt * S + c * 512: L0A_XB + kt * S + (c + 1) * 512],
                           [wst.res, xb.res])
                p.op("dve", lambda eng, b=base, c=c, bc=(b8c0, pr): eng.tensor_scalar(
                    scr[:, b + c * 512: b + (c + 1) * 512], k.PA[:, 0:512], 0.125,
                    k.misc[:, bc[0] + bc[1]:bc[0] + bc[1] + 1], ALU.mult, ALU.add),
                    reads=[k.bPA0, b8res], writes=[reg.res], deps=reg.wdeps())
        wst = k.scr_t.carve(L0A_WST, L0A_WST + 1024, "l0awv")
        stream_cast(k, fsl[0], 0, vw[pr], 1024, L0A_WST, wst)
        for t in range(16):
            g = MMGroup(k, k.PA[0:128, 0:128], k.bPA0, NT + 1)
            for kt in range(NT):
                g.part(scr[:, L0A_XB + kt * S + t * 128: L0A_XB + kt * S + t * 128 + 128],
                       scr[:, L0A_WST + kt * 128: L0A_WST + (kt + 1) * 128],
                       [xb.res, wst.res])
            g.part(k.ones_row_f32[:], rows[0:1, 2560 + pr * 128: 2560 + (pr + 1) * 128],
                   [vb.res, k.ones_row_f32_r])
            p.op("act", lambda eng, t=t: eng.activation(
                scr[:, L0A_VP + t * 128: L0A_VP + (t + 1) * 128],
                k.PA[0:128, 0:128], AF.Copy),
                reads=[k.bPA0], writes=[vp_.res], deps=vp_.wdeps())
        for h01 in heads:
            hb = 64 * h01
            skip_scan = (dbg in ("3", "4", "5", "6", "7") and h01 == 1)
            qpt = k.scr_t.carve(L0A_QPT, L0A_QPT + 4096, "qpt")
            kpt = k.scr_t.carve(L0A_KPT, L0A_KPT + 4096, "kpt")
            ktok = k.scr_t.carve(L0A_KTOK, L0A_KTOK + 4096, "ktok")
            feat_list = [(L0A_QP, qp_), (L0A_KP, kp_)]
            if skip_scan and dbg in ("4", "6", "7"):
                feat_list = feat_list[:1]
            for qki, (base, reg) in enumerate(feat_list):
                crow = k.row_t.carve(0, 2048, "corr%d" % qki)
                skip_corr = skip_scan and dbg == "7"
                skip_map = skip_scan and dbg == "6"
                for c in range(NC if not skip_corr else 0):
                    sq = k.scr_t.carve(STAT_XC, STAT_XC + 512, "l0asq")
                    p.op("act", lambda eng, b=base, c=c, hb=hb: eng.activation(
                        scr[hb:hb + 64, STAT_XC:STAT_XC + 512],
                        scr[hb:hb + 64, b + c * 512: b + (c + 1) * 512], AF.Square),
                        reads=[reg.res], writes=[sq.res], deps=sq.wdeps())
                    mm(k, k.PC[0:1, 0:512], k.bPC,
                       [(k.ones_col_bf[hb:hb + 64, :], scr[hb:hb + 64, STAT_XC:STAT_XC + 512],
                         [sq.res, k.ones_col_bf_r])], tp=(hb, 0))
                    p.op("dve", lambda eng, c=c: eng.tensor_scalar(
                        rows[0:1, c * 512: (c + 1) * 512],
                        k.PC[0:1, 0:512], -0.5, -LN16, ALU.mult, ALU.add),
                        reads=[k.bPC], writes=[crow.res], deps=crow.wdeps())
                dst = L0A_QPT if qki == 0 else L0A_KPT
                dreg = qpt if qki == 0 else kpt
                for mt in range(2 if not skip_map else 0):
                    for c in range(NC):
                        ba = k.PB[:, 0:512]
                        bb = k.bPB0
                        g = MMGroup(k, ba, bb, 2)
                        g.part(scr[hb:hb + 64, L0A_PROJ + mt * 128: L0A_PROJ + (mt + 1) * 128],
                               scr[hb:hb + 64, base + c * 512: base + (c + 1) * 512],
                               [prj.res, reg.res], tp=(hb, 0))
                        g.part(k.ones_row_f32[:],
                               rows[0:1, c * 512: (c + 1) * 512],
                               [crow.res, k.ones_row_f32_r])
                        p.op("act", lambda eng, d=dst, mt=mt, c=c, ba=ba: eng.activation(
                            scr[:, d + mt * S + c * 512: d + mt * S + (c + 1) * 512],
                            ba, AF.Exp),
                            reads=[bb], writes=[dreg.res], deps=dreg.wdeps())
                if qki == 1 and dbg == "5" and skip_scan:
                    continue
                if qki == 1:
                    for t in range(16):
                        ba = k.PB[:, 512:768]
                        bb = k.bPB1
                        g = MMGroup(k, ba, bb, 3)
                        g.part(scr[hb:hb + 64, L0A_KP + t * 128: L0A_KP + (t + 1) * 128],
                               scr[hb:hb + 64, L0A_PROJ:L0A_PROJ + 256],
                               [kp_.res, prj.res], tp=(hb, 0))
                        g.part(rows[0:1, t * 128: (t + 1) * 128],
                               k.ones_row_f32[:], [crow.res, k.ones_row_f32_r],
                               out_ap=ba[:, 0:128])
                        g.part(rows[0:1, t * 128: (t + 1) * 128],
                               k.ones_row_f32[:], [crow.res, k.ones_row_f32_r],
                               out_ap=ba[:, 128:256])
                        p.op("act", lambda eng, t=t, ba=ba: eng.activation(
                            scr[:, L0A_KTOK + t * 256: L0A_KTOK + (t + 1) * 256],
                            ba, AF.Exp),
                            reads=[bb], writes=[ktok.res], deps=ktok.wdeps())
                    gks = MMGroup(k, k.PC[0:128, 0:16], k.bPC, 32)
                    for ch8 in range(8):
                        for mt in range(2):
                            for tt in range(2):
                                gks.part(
                                    scr[:, L0A_KTOK + (2 * ch8 + tt) * 256 + mt * 128:
                                        L0A_KTOK + (2 * ch8 + tt) * 256 + (mt + 1) * 128],
                                    k.ones_col_bf[:], [ktok.res, k.ones_col_bf_r],
                                    out_ap=k.PC[0:128, mt * 8 + ch8: mt * 8 + ch8 + 1])
                    ksf = k.f32_t.carve(2560, 2608, "ksf")
                    SS, PP, QQ = 2560, 2576, 2592
                    p.op("dve", lambda eng: eng.tensor_copy(
                        k.f32s[:, SS:SS + 16], k.PC[0:128, 0:16]),
                        reads=[k.bPC], writes=[ksf.res], deps=ksf.wdeps())
                    p.op("dve", lambda eng: eng.tensor_copy(
                        k.f32s[:, PP:PP + 16], k.f32s[:, SS:SS + 16]),
                        reads=[ksf.res], writes=[ksf.res])
                    cur, oth = PP, QQ
                    for sh in (1, 2, 4):
                        for mt in range(2):
                            a, b = cur + mt * 8, oth + mt * 8
                            p.op("dve", lambda eng, a=a, b=b, sh=sh: eng.tensor_copy(
                                k.f32s[:, b:b + sh], k.f32s[:, a:a + sh]),
                                reads=[ksf.res], writes=[ksf.res])
                            p.op("dve", lambda eng, a=a, b=b, sh=sh: eng.tensor_tensor(
                                k.f32s[:, b + sh:b + 8], k.f32s[:, a + sh:a + 8],
                                k.f32s[:, a:a + 8 - sh], ALU.add),
                                reads=[ksf.res], writes=[ksf.res])
                        cur, oth = oth, cur
                    ksp = k.scr_t.carve(L0A_KSP, L0A_KSP + 16, "ksp")
                    p.op("dve", lambda eng, a=cur: eng.tensor_tensor(
                        k.f32s[:, a:a + 16], k.f32s[:, a:a + 16],
                        k.f32s[:, SS:SS + 16], ALU.subtract),
                        reads=[ksf.res], writes=[ksf.res])
                    p.op("dve", lambda eng, a=cur: eng.tensor_copy(
                        scr[:, L0A_KSP:L0A_KSP + 16], k.f32s[:, a:a + 16]),
                        reads=[ksf.res], writes=[ksp.res], deps=ksp.wdeps())
            dreg_den = k.row_t.carve(0, 2048, "l0aden")
            asbs = [k.scr_t.carve(L0A_ASB, L0A_ASB + 512, "asb0"),
                    k.scr_t.carve(L0A_ASB + 512, L0A_ASB + 1024, "asb1")]

            def emit_at2(ch):
                bank_ap = k.PA[:, 0:512] if ch % 2 == 0 else k.PA[:, 512:1024]
                bank = k.bPA0 if ch % 2 == 0 else k.bPA1
                for ks in range(2):
                    g = MMGroup(k, bank_ap[:, ks * 256:(ks + 1) * 256], bank, 2)
                    for mt in range(2):
                        g.part(
                            scr[:, L0A_KPT + mt * S + ch * 256 + ks * 128:
                                L0A_KPT + mt * S + ch * 256 + (ks + 1) * 128],
                            scr[:, L0A_QPT + mt * S + ch * 256: L0A_QPT + mt * S + (ch + 1) * 256],
                            [kpt.res, qpt.res])
                ao = L0A_ASB + (ch % 2) * 512
                for ks in range(2):
                    cb = 384 - 128 * ks
                    p.op("dve", lambda eng, b=bank_ap, ao=ao, ks=ks, cb=cb: eng.tensor_tensor(
                        scr[:, ao + ks * 256: ao + (ks + 1) * 256],
                        b[:, ks * 256:(ks + 1) * 256],
                        k.band[:, cb:cb + 256], ALU.mult),
                        reads=[bank, k.band_r], writes=[asbs[ch % 2].res],
                        deps=asbs[ch % 2].wdeps())

            emit_at2(0)
            for ch in range(8):
                if ch + 1 < 8:
                    emit_at2(ch + 1)
                ao = L0A_ASB + (ch % 2) * 512
                ob_ap = (k.PB[:, 0:512] if ch % 2 == 0 else k.PB[:, 512:1024])[hb:hb + 64, 0:256]
                ob_bank = k.bPB0 if ch % 2 == 0 else k.bPB1
                nparts = 2 if ch == 0 else 4
                g = MMGroup(k, ob_ap, ob_bank, nparts)
                for ks in range(2):
                    g.part(scr[:, L0A_VP + (2 * ch + ks) * 128 + hb:
                               L0A_VP + (2 * ch + ks) * 128 + hb + 64],
                           scr[:, ao + ks * 256: ao + (ks + 1) * 256],
                           [vp_.res, asbs[ch % 2].res], tp=(0, hb))
                if ch > 0:
                    for mt in range(2):
                        g.part(scr[:, L0A_KVS + mt * 128: L0A_KVS + mt * 128 + 64],
                               scr[:, L0A_QPT + mt * S + ch * 256: L0A_QPT + mt * S + (ch + 1) * 256],
                               [kvs.res, qpt.res], tp=(0, hb))
                dn_ap = (k.PC if ch % 2 == 0 else k.PF)[0:1, 0:256]
                dn_bank = k.bPC if ch % 2 == 0 else k.bPF
                gd = MMGroup(k, dn_ap, dn_bank, nparts)
                for ks in range(2):
                    gd.part(k.ones_col_bf[:], scr[:, ao + ks * 256: ao + (ks + 1) * 256],
                            [asbs[ch % 2].res, k.ones_col_bf_r])
                if ch > 0:
                    for mt in range(2):
                        gd.part(scr[:, L0A_KSP + mt * 8 + ch: L0A_KSP + mt * 8 + ch + 1],
                                scr[:, L0A_QPT + mt * S + ch * 256: L0A_QPT + mt * S + (ch + 1) * 256],
                                [ksp.res, qpt.res])
                p.op("act", lambda eng, ch=ch, hb=hb, b=ob_ap: eng.activation(
                    scr[hb:hb + 64, L0A_NUM + ch * 256: L0A_NUM + (ch + 1) * 256],
                    b, AF.Copy),
                    reads=[ob_bank], writes=[num.res], deps=num.wdeps())
                p.op("dve", lambda eng, ch=ch, b=dn_ap: eng.tensor_copy(
                    rows[0:1, ch * 256:(ch + 1) * 256], b),
                    reads=[dn_bank], writes=[dreg_den.res], deps=dreg_den.wdeps())
                for mt in range(2):
                    kvp = k.PD if mt == 0 else k.PEb
                    kvb = k.bPD if mt == 0 else k.bPE
                    for tt in range(2):
                        p.op("pe", lambda eng, mt=mt, ch=ch, tt=tt, hb=hb, kvp=kvp:
                             eng.matmul(
                            kvp[0:128, 0:64],
                            scr[:, L0A_KTOK + (2 * ch + tt) * 256 + mt * 128:
                                L0A_KTOK + (2 * ch + tt) * 256 + (mt + 1) * 128],
                            scr[:, L0A_VP + (2 * ch + tt) * 128 + hb:
                                L0A_VP + (2 * ch + tt) * 128 + hb + 64],
                            start=(ch == 0 and tt == 0), stop=True,
                            skip_group_check=True),
                            reads=[ktok.res, vp_.res],
                            deps=[kvb.w] + kvb.r)
                        kvb.w = p.items[-1].idx
                        kvb.r = []
                if ch < 7:
                    for mt in range(2):
                        kvp = k.PD if mt == 0 else k.PEb
                        kvb = k.bPD if mt == 0 else k.bPE
                        p.op("dve", lambda eng, mt=mt, kvp=kvp: eng.tensor_copy(
                            scr[:, L0A_KVS + mt * 128: L0A_KVS + mt * 128 + 64],
                            kvp[0:128, 0:64]),
                            reads=[kvb], writes=[kvs.res], deps=kvs.wdeps())
            p.op("act", lambda eng: eng.activation(
                rows[0:1, 0:2048], rows[0:1, 0:2048], AF.Ln),
                reads=[dreg_den.res], writes=[dreg_den.res])
            p.op("act", lambda eng: eng.activation(
                rows[0:1, 0:2048], rows[0:1, 0:2048], AF.Exp, scale=-1.0),
                reads=[dreg_den.res], writes=[dreg_den.res])
            for c in range(NC):
                mm(k, k.PF[hb:hb + 64, 0:512], k.bPF,
                   [(k.ones_row_f32[0:1, 0:64], rows[0:1, c * 512:(c + 1) * 512],
                     [dreg_den.res, k.ones_row_f32_r])], tp=(0, hb))
                o_col = L0A_OT + pr * 2048 + c * 512
                p.op("dve", lambda eng, c=c, o=o_col, hb=hb: eng.tensor_tensor(
                    scr[hb:hb + 64, o:o + 512],
                    scr[hb:hb + 64, L0A_NUM + c * 512: L0A_NUM + (c + 1) * 512],
                    k.PF[hb:hb + 64, 0:512], ALU.mult),
                    reads=[num.res, k.bPF], writes=[ot.res], deps=ot.wdeps())

    if dbg:
        return
    p.barrier()
    ob_c0, ob_res = param_cols(k, "L0_attn_out_b")
    for m in range(NT):
        wst = k.scr_t.carve(L0A_WST, L0A_WST + 1024, "l0awo%d" % m)
        stream_cast(k, fsl[m % 2], (m % 2) * 1024, ow[m], 1024, L0A_WST, wst)
        for c in range(NC):
            par = c % 2
            bank_ap = k.PA[:, par * 512:(par + 1) * 512]
            bank = k.bPA0 if par == 0 else k.bPA1
            g = MMGroup(k, bank_ap, bank, NT)
            for kt in range(NT):
                g.part(scr[:, L0A_WST + kt * 128: L0A_WST + (kt + 1) * 128],
                       scr[:, L0A_OT + kt * S + c * 512: L0A_OT + kt * S + (c + 1) * 512],
                       [wst.res, ot.res])
            a_sl = (slice(None), slice(m * S + c * 512, m * S + (c + 1) * 512))
            p.op("dve", lambda eng, a=a_sl, m=m, b=bank_ap: eng.scalar_tensor_tensor(
                k.acc[a], b, k.misc[:, ob_c0 + m:ob_c0 + m + 1], k.acc[a],
                ALU.add, ALU.add),
                reads=[bank, ob_res, k.acc_r[m]], writes=[k.acc_r[m]])
    mu_res, rstd_res = stats(k, 1e-6)
    ln_apply(k, mu_res, rstd_res, affine=(ga_c0, ga_res, be_c0, be_res))
    for j in range(NT):
        for c in range(NC):
            a_sl = (slice(None), slice(j * S + c * 512, j * S + (c + 1) * 512))
            p.op("dve", lambda eng, a=a_sl, j=j, c=c: eng.tensor_tensor(
                k.acc[a], k.acc[a],
                scr[:, L0A_XB + j * S + c * 512: L0A_XB + j * S + (c + 1) * 512], ALU.add),
                reads=[k.acc_r[j], xb.res], writes=[k.acc_r[j]])
    p.barrier()


BLOCK_EMITTERS["l0_attn"] = emit_l0_attn


def prep_l0_attn(inputs):
    def tile8(w):
        w = np.asarray(w, np.float32)
        return np.ascontiguousarray(
            w.reshape(8, 128, 8, 128).transpose(2, 1, 0, 3).reshape(8, 128, 1024))
    proj = np.asarray(inputs["L0_attn_proj"], np.float32)
    pt = np.ascontiguousarray(proj.T)
    return {
        "l0a_qw": tile8(inputs["L0_attn_q_w"]),
        "l0a_kw": tile8(inputs["L0_attn_k_w"]),
        "l0a_vw": tile8(inputs["L0_attn_v_w"]),
        "l0a_ow": tile8(inputs["L0_attn_out_w"]),
        "l0a_projT2": np.ascontiguousarray(np.vstack([pt, pt])),
        "L0_attn_q_b": np.asarray(inputs["L0_attn_q_b"], np.float32),
        "L0_attn_k_b": np.asarray(inputs["L0_attn_k_b"], np.float32),
        "L0_attn_v_b": np.asarray(inputs["L0_attn_v_b"], np.float32),
        "L0_attn_out_b": np.asarray(inputs["L0_attn_out_b"], np.float32),
        "L0_attn_gamma": np.asarray(inputs["L0_attn_gamma"], np.float32),
        "L0_attn_beta": np.asarray(inputs["L0_attn_beta"], np.float32),
    }


PREPPERS["l0_attn"] = prep_l0_attn



def prep_consts():
    kk = np.arange(128)[:, None]
    cc = np.arange(BAND_W)[None, :]
    oh = np.zeros((8, 1024), dtype=BF)
    for e in range(8):
        oh[e, e * 128:(e + 1) * 128] = 1
    return {
        "c_id_f32": np.eye(128, dtype=np.float32),
        "c_ones_col_bf": np.ones((128, 1), dtype=BF),
        "c_ones_row_f32": np.ones((1, 128), dtype=np.float32),
        "c_band": (kk <= cc - 384).astype(BF),
        "c_onehot8": oh,
    }


def prep_l1_attn(inputs):
    w_qkv = np.asarray(inputs["L1_attn_w_qkv"], dtype=np.float32)
    wt = np.ascontiguousarray(w_qkv.T)
    qkw = np.empty((8, 2, 128, 1024), np.float32)
    for pair in range(8):
        for qki in range(2):
            base = qki * 1024 + pair * 128
            blk = wt[:, base:base + 128]
            qkw[pair, qki] = blk.reshape(8, 128, 128).transpose(1, 0, 2).reshape(128, 1024)
    vw = np.empty((2, 8, 128, 512), np.float32)
    for g in range(2):
        for kt in range(8):
            vw[g, kt] = wt[kt * 128:(kt + 1) * 128, 2048 + g * 512: 2048 + (g + 1) * 512]
    w_out = np.asarray(inputs["L1_attn_w_out"], dtype=np.float32)
    wo = np.ascontiguousarray(w_out.T.reshape(8, 128, 1024))
    return {
        "l1a_qkw": qkw, "l1a_vw": vw, "l1a_wo": wo,
        "L1_attn_ln_gamma": np.asarray(inputs["L1_attn_ln_gamma"], np.float32),
        "L1_attn_ln_beta": np.asarray(inputs["L1_attn_ln_beta"], np.float32),
        "L1_attn_b_out": np.asarray(inputs["L1_attn_b_out"], np.float32),
    }


PREPPERS["l1_attn"] = prep_l1_attn


def run_blocks(blocks, x, inputs, cores=8):
    nc, k = build_program(blocks)
    needed = set(k.inp.keys())
    base = dict(prep_consts())
    cvals = np.zeros((128, 32), np.float32)
    for val, c in k.cval_map.items():
        cvals[:, c] = val
    base["c_cvals"] = cvals
    for b in blocks:
        base.update(PREPPERS[b](inputs))
    in_maps = []
    for c in range(cores):
        m = {"xio": np.ascontiguousarray(np.asarray(x[c], np.float32).T)}
        for name, arr in base.items():
            if name in needed:
                m[name] = arr
        in_maps.append(m)
    res = bass_utils.run_bass_kernel_spmd(nc, in_maps, core_ids=list(range(cores)))
    return np.stack([np.ascontiguousarray(r["yio"].T) for r in res.results])


ALL_BLOCKS = ["l0_attn", "l0_ffn", "l1_attn", "l1_ffn"]
_CACHE = {}


def _get_program(blocks):
    key = tuple(blocks)
    if key not in _CACHE:
        nc, k = build_program(list(blocks))
        cvals = np.zeros((128, 32), np.float32)
        for val, c in k.cval_map.items():
            cvals[:, c] = val
        _CACHE[key] = (nc, k, cvals)
    return _CACHE[key]


def kernel(**inputs):
    x = np.asarray(inputs["L0_attn_x"], np.float32)
    nc, k, cvals = _get_program(tuple(ALL_BLOCKS))
    base = prep_consts()
    base["c_cvals"] = cvals
    for b in ALL_BLOCKS:
        base.update(PREPPERS[b](inputs))
    needed = set(k.inp.keys())
    in_maps = []
    for c in range(8):
        m = {"xio": np.ascontiguousarray(x[c].T)}
        for name, arr in base.items():
            if name in needed:
                m[name] = arr
        in_maps.append(m)
    res = bass_utils.run_bass_kernel_spmd(nc, in_maps, core_ids=list(range(8)))
    return np.stack([np.ascontiguousarray(np.asarray(r["yio"], np.float32).T)
                     for r in res.results])


# version 14 (final)
# speedup vs baseline: 1.3281x; 1.0095x over previous
import contextlib
import numpy as np
import ml_dtypes
import concourse.bass as bass
import concourse.mybir as mybir
from concourse import bass_utils

F32 = mybir.dt.float32
BF16 = mybir.dt.bfloat16
AF = mybir.ActivationFunctionType
ALU = mybir.AluOpType
AX = mybir.AxisListType
BF = ml_dtypes.bfloat16

D = 1024
S = 2048
NT = D // 128
NC = 4

ENGS = ("pe", "dve", "act", "sp", "gp")


class Res:
    __slots__ = ("name", "w", "r")

    def __init__(self, name=""):
        self.name = name
        self.w = None
        self.r = []


class Reg:
    __slots__ = ("res", "deps")

    def __init__(self, res, deps):
        self.res = res
        self.deps = list(deps)

    def wdeps(self):
        d, self.deps = self.deps, []
        return d


NDMA = 8


class Op:
    __slots__ = ("eng", "fn", "deps", "dma", "idx", "sem_val", "needed", "slot")


class Prog:
    def __init__(self, nc):
        self.nc = nc
        self.items = []

    def op(self, eng, fn, reads=(), writes=(), deps=(), dma=False):
        idx = len(self.items)
        dd = set(d for d in deps if d is not None)
        for r in reads:
            if r.w is not None:
                dd.add(r.w)
        for w in writes:
            if w.w is not None:
                dd.add(w.w)
            dd.update(w.r)
        o = Op()
        o.eng, o.fn, o.deps, o.dma, o.idx = eng, fn, sorted(dd), dma, idx
        o.sem_val, o.needed = None, False
        self.items.append(o)
        for r in reads:
            r.r.append(idx)
        for w in writes:
            w.w = idx
            w.r = []
        return idx

    def barrier(self):
        last = {}
        for it in self.items:
            last[it.eng] = it.idx
        toks = list(last.values())
        for e in ENGS:
            if e in ("dve", "act", "gp"):
                def fn(eng):
                    eng.drain()
                    return eng.nop()
            else:
                def fn(eng):
                    return eng.nop()
            self.op(e, fn, deps=toks)

    def emit(self, block, sems):
        items = self.items
        for it in items:
            for d in it.deps:
                p = items[d]
                if p.eng != it.eng or p.dma:
                    p.needed = True
        cnt = {e: 0 for e in ENGS}
        dcnt = {e: 0 for e in ENGS}
        for it in items:
            if it.dma:
                i = dcnt[it.eng]
                dcnt[it.eng] += 1
                it.slot = i % NDMA
                it.sem_val = 16 * (i // NDMA + 1)
            elif it.needed:
                cnt[it.eng] += 1
                it.sem_val = cnt[it.eng]

        def run_engine(ename, eng):
            waited = {}
            drained_upto = -1
            own_done = -1
            for it in items:
                if it.eng != ename:
                    continue
                need_drain = False
                for d in it.deps:
                    p = items[d]
                    if p.eng == ename and not p.dma:
                        if ename != "pe" and d > drained_upto:
                            need_drain = True
                    else:
                        if p.dma:
                            key = (p.eng, "d", p.slot)
                        else:
                            key = (p.eng, "c")
                        v = p.sem_val
                        assert v is not None, (ename, it.idx, d, p.eng)
                        if waited.get(key, 0) < v:
                            eng.wait_ge(sems[key], v)
                            waited[key] = v
                if it.dma and it.sem_val > 16:
                    key = (ename, "d", it.slot)
                    v = it.sem_val - 16
                    if waited.get(key, 0) < v:
                        eng.wait_ge(sems[key], v)
                        waited[key] = v
                if need_drain:
                    eng.drain()
                    drained_upto = own_done
                inst = it.fn(eng)
                own_done = it.idx
                if it.sem_val is not None:
                    if inst is None:
                        inst = eng.nop()
                    key = (ename, "d", it.slot) if it.dma else (ename, "c")
                    inst.then_inc(sems[key], 16 if it.dma else 1)

        nc = self.nc

        @block.tensor
        def _(_e):
            run_engine("pe", nc.tensor)

        @block.vector
        def _(_e):
            run_engine("dve", nc.vector)

        @block.scalar
        def _(_e):
            run_engine("act", nc.scalar)

        @block.sync
        def _(_e):
            run_engine("sp", nc.sync)

        @block.gpsimd
        def _(_e):
            run_engine("gp", nc.gpsimd)


class RegTracker:
    def __init__(self, name):
        self.name = name
        self.regs = []

    def carve(self, start, end, tag=""):
        deps = []
        keep = []
        for (s, e, r) in self.regs:
            if s < end and start < e:
                if r.w is not None:
                    deps.append(r.w)
                deps.extend(r.r)
                if s < start:
                    keep.append((s, start, r))
                if end < e:
                    keep.append((end, e, r))
            else:
                keep.append((s, e, r))
        res = Res(self.name + ":" + tag)
        keep.append((start, end, res))
        self.regs = keep
        return Reg(res, deps)


class MMGroup:
    def __init__(self, k, out_ap, bank, n):
        self.k = k
        self.out_ap = out_ap
        self.bank = bank
        self.n = n
        self.i = 0
        self.prev = None

    def part(self, lhsT, rhs, reads, tp=None, extra_deps=(), out_ap=None):
        i, n = self.i, self.n
        assert i < n
        deps = list(extra_deps)
        if i == 0:
            if self.bank.w is not None:
                deps.append(self.bank.w)
            deps += self.bank.r
        else:
            deps.append(self.prev)
        fn = (lambda eng, o=(out_ap if out_ap is not None else self.out_ap),
              l=lhsT, r=rhs, st=(i == 0), sp=(i == n - 1),
              tp=tp: eng.matmul(o, l, r, start=st, stop=sp, tile_position=tp))
        self.prev = self.k.p.op("pe", fn, reads=list(reads), deps=deps)
        self.i += 1
        if self.i == self.n:
            self.bank.w = self.prev
            self.bank.r = []
        return self.prev


def mm(k, out_ap, bank, parts, tp=None):
    g = MMGroup(k, out_ap, bank, len(parts))
    for (l, r, rd) in parts:
        g.part(l, r, rd, tp=tp)
    return g.prev



SCR_COLS = 55296
F32S_COLS = 3072
LNT = 2048
BAND_W = 1280


class K:
    def __init__(self, nc, es):
        self.nc = nc
        self.p = Prog(nc)
        self.es = es
        self.inp = {}
        sb = lambda n, sh, dt: es.enter_context(nc.sbuf_tensor(n, sh, dt))
        pst = lambda n, sh: es.enter_context(nc.psum_tensor(n, sh, F32))
        self.acc = sb("acc", [128, NT * S], F32)
        self.acc_r = [Res("acc%d" % j) for j in range(NT)]
        self.rows = sb("rows", [1, 4096], F32)
        self.scr = sb("scr", [128, SCR_COLS], BF16)
        self.f32s = sb("f32s", [128, F32S_COLS], F32)
        self.id_f32 = sb("id_f32", [128, 128], F32)
        self.ones_col_bf = sb("ones_col_bf", [128, 1], BF16)
        self.ones_row_f32 = sb("ones_row_f32", [1, 128], F32)
        self.band = sb("band", [128, BAND_W], BF16)
        self.onehot8 = sb("onehot8", [8, 1024], BF16)
        self.misc = sb("misc", [128, 128], F32)
        self.PA = pst("PA", [128, 1024])
        self.PB = pst("PB", [128, 1024])
        self.PC = pst("PC", [128, 512])
        self.PD = pst("PD", [128, 512])
        self.PEb = pst("PEb", [128, 512])
        self.PF = pst("PF", [128, 512])
        self.bPA0, self.bPA1 = Res("PA0"), Res("PA1")
        self.bPB0, self.bPB1 = Res("PB0"), Res("PB1")
        self.bPC, self.bPD = Res("PC"), Res("PD")
        self.bPE, self.bPF = Res("PEb"), Res("PF")
        for n in ("id_f32", "ones_col_bf", "ones_row_f32", "band", "misc", "onehot8"):
            setattr(self, n + "_r", Res(n))
        self.scr_t = RegTracker("scr")
        self.f32_t = RegTracker("f32s")
        self.row_t = RegTracker("rows")
        self.misc_used = 0
        self.cvals = sb("cvals", [128, 32], F32)
        self.cvals_r = Res("cvals")
        self.cval_map = {}
        self.cast_rr = 0
        self.cload("c_cvals", [128, 32], self.cvals[:], self.cvals_r)
        self.cload("c_id_f32", [128, 128], self.id_f32[:], self.id_f32_r)
        self.cload("c_ones_col_bf", [128, 1], self.ones_col_bf[:], self.ones_col_bf_r, dt=BF16)
        self.cload("c_ones_row_f32", [1, 128], self.ones_row_f32[:], self.ones_row_f32_r)
        self.cload("c_band", [128, BAND_W], self.band[:], self.band_r, dt=BF16)
        self.cload("c_onehot8", [8, 1024], self.onehot8[:], self.onehot8_r, dt=BF16)

    def dram(self, name, shape, dt=F32, kind="ExternalInput"):
        if name not in self.inp:
            self.inp[name] = self.nc.dram_tensor(name, list(shape), dt, kind=kind).ap()
        return self.inp[name]

    def cv(self, val, p=128):
        val = float(val)
        if val not in self.cval_map:
            self.cval_map[val] = len(self.cval_map)
            assert len(self.cval_map) <= 32
        c = self.cval_map[val]
        return self.cvals[0:p, c:c + 1]

    def cload(self, name, shape, dst_ap, dst_res, dt=F32):
        ap = self.dram(name, shape, dt)
        return self.p.op("sp", lambda eng, o=dst_ap, i=ap: eng.dma_start(out=o, in_=i),
                         writes=[dst_res], dma=True)


def param_cols(k, name, n=D):
    nt = n // 128
    ap = k.dram(name, [n], F32)
    st = k.f32_t.carve(2824, 2952, name)
    k.p.op("sp", lambda eng, i=ap, nt=nt: eng.dma_start(
        out=k.f32s[0:nt, 2824:2824 + 128], in_=i.rearrange("(t p) -> t p", p=128)),
        writes=[st.res], deps=st.wdeps(), dma=True)
    k.p.op("pe", lambda eng, nt=nt: eng.transpose(
        k.PF[0:128, 0:nt], k.f32s[0:nt, 2824:2824 + 128], k.id_f32[0:nt, 0:nt]),
        reads=[st.res, k.id_f32_r], writes=[k.bPF])
    c0 = k.misc_used
    k.misc_used += nt
    assert k.misc_used <= 128
    col_res = Res(name + "_col")
    k.p.op("dve", lambda eng, c0=c0, nt=nt: eng.tensor_copy(
        k.misc[:, c0:c0 + nt], k.PF[0:128, 0:nt]),
        reads=[k.bPF], writes=[col_res])
    return c0, col_res


STAT_XC = 52224
STAT_SQ = 52736


def stats(k, eps):
    p, scr, rows = k.p, k.scr, k.rows
    mu = k.row_t.carve(0, 2048, "mu")
    rstd = k.row_t.carve(2048, 4096, "rstd")
    xc = k.scr_t.carve(STAT_XC, STAT_XC + 512, "statxc")
    sq = k.scr_t.carve(STAT_SQ, STAT_SQ + 512, "statsq")
    for c in range(NC):
        g1 = MMGroup(k, k.PC[0:1, 0:512], k.bPC, NT)
        g2 = MMGroup(k, k.PD[0:1, 0:512], k.bPD, NT)
        for j in range(NT):
            p.op("dve", lambda eng, j=j, c=c: eng.tensor_copy(
                scr[:, STAT_XC:STAT_XC + 512],
                k.acc[:, j * S + c * 512: j * S + (c + 1) * 512]),
                reads=[k.acc_r[j]], writes=[xc.res], deps=xc.wdeps())
            p.op("act", lambda eng: eng.activation(
                scr[:, STAT_SQ:STAT_SQ + 512], scr[:, STAT_XC:STAT_XC + 512], AF.Square),
                reads=[xc.res], writes=[sq.res], deps=sq.wdeps())
            g1.part(k.ones_col_bf[:], scr[:, STAT_XC:STAT_XC + 512], [xc.res, k.ones_col_bf_r])
            g2.part(k.ones_col_bf[:], scr[:, STAT_SQ:STAT_SQ + 512], [sq.res, k.ones_col_bf_r])
        p.op("dve", lambda eng, c=c: eng.tensor_scalar(
            rows[0:1, c * 512:(c + 1) * 512], k.PC[0:1, 0:512], 1.0 / D, None, ALU.mult),
            reads=[k.bPC], writes=[mu.res], deps=mu.wdeps())
        p.op("dve", lambda eng, c=c: eng.tensor_scalar(
            rows[0:1, 2048 + c * 512:2048 + (c + 1) * 512], k.PD[0:1, 0:512], 1.0 / D,
            None, ALU.mult),
            reads=[k.bPD], writes=[rstd.res], deps=rstd.wdeps())
    for c in range(NC):
        p.op("dve", lambda eng, c=c: eng.tensor_tensor(
            k.PC[0:1, 0:512], rows[0:1, c * 512:(c + 1) * 512],
            rows[0:1, c * 512:(c + 1) * 512], ALU.mult),
            reads=[mu.res], writes=[k.bPC])
        p.op("dve", lambda eng, c=c: eng.tensor_tensor(
            rows[0:1, 2048 + c * 512:2048 + (c + 1) * 512],
            rows[0:1, 2048 + c * 512:2048 + (c + 1) * 512], k.PC[0:1, 0:512], ALU.subtract),
            reads=[k.bPC, rstd.res], writes=[rstd.res])
    p.op("act", lambda eng, b=k.cv(eps, 1): eng.activation(
        rows[0:1, 2048:4096], rows[0:1, 2048:4096], AF.Ln, bias=b),
        reads=[rstd.res, k.cvals_r], writes=[rstd.res])
    p.op("act", lambda eng: eng.activation(
        rows[0:1, 2048:4096], rows[0:1, 2048:4096], AF.Exp, scale=-0.5),
        reads=[rstd.res], writes=[rstd.res])
    return mu.res, rstd.res


def ln_apply(k, mu_res, rstd_res, affine=None, out_reg=None, out_region=None):
    p, rows, scr = k.p, k.rows, k.scr
    lnt = k.f32_t.carve(LNT, LNT + 512, "lnt")
    for c in range(NC):
        mm(k, k.PEb[:, 0:512], k.bPE,
           [(k.ones_row_f32[:], rows[0:1, c * 512:(c + 1) * 512], [mu_res, k.ones_row_f32_r])])
        mm(k, k.PF[:, 0:512], k.bPF,
           [(k.ones_row_f32[:], rows[0:1, 2048 + c * 512:2048 + (c + 1) * 512],
             [rstd_res, k.ones_row_f32_r])])
        for j in range(NT):
            a_sl = (slice(None), slice(j * S + c * 512, j * S + (c + 1) * 512))
            p.op("dve", lambda eng, a=a_sl: eng.tensor_tensor(
                k.f32s[:, LNT:LNT + 512], k.acc[a], k.PEb[:, 0:512], ALU.subtract),
                reads=[k.acc_r[j], k.bPE], writes=[lnt.res], deps=lnt.wdeps())
            if affine is None:
                p.op("dve", lambda eng, a=a_sl: eng.tensor_tensor(
                    k.acc[a], k.f32s[:, LNT:LNT + 512], k.PF[:, 0:512], ALU.mult),
                    reads=[lnt.res, k.bPF], writes=[k.acc_r[j]])
            elif out_region is None:
                gc0, g_res, bc0, b_res = affine
                p.op("dve", lambda eng: eng.tensor_tensor(
                    k.f32s[:, LNT:LNT + 512], k.f32s[:, LNT:LNT + 512], k.PF[:, 0:512],
                    ALU.mult),
                    reads=[lnt.res, k.bPF], writes=[lnt.res])
                p.op("dve", lambda eng, a=a_sl, j=j, gc0=gc0, bc0=bc0: eng.tensor_scalar(
                    k.acc[a], k.f32s[:, LNT:LNT + 512], k.misc[:, gc0 + j:gc0 + j + 1],
                    k.misc[:, bc0 + j:bc0 + j + 1], ALU.mult, ALU.add),
                    reads=[lnt.res, g_res, b_res], writes=[k.acc_r[j]])
            else:
                gc0, g_res, bc0, b_res = affine
                p.op("dve", lambda eng: eng.tensor_tensor(
                    k.f32s[:, LNT:LNT + 512], k.f32s[:, LNT:LNT + 512], k.PF[:, 0:512],
                    ALU.mult),
                    reads=[lnt.res, k.bPF], writes=[lnt.res])
                o_sl = (slice(None),
                        slice(out_region + j * S + c * 512, out_region + j * S + (c + 1) * 512))
                p.op("dve", lambda eng, o=o_sl, j=j, gc0=gc0, bc0=bc0: eng.tensor_scalar(
                    scr[o], k.f32s[:, LNT:LNT + 512], k.misc[:, gc0 + j:gc0 + j + 1],
                    k.misc[:, bc0 + j:bc0 + j + 1], ALU.mult, ALU.add),
                    reads=[lnt.res, g_res, b_res], writes=[out_reg.res], deps=out_reg.wdeps())


def stream_cast(k, slot, f32_off, dram_ap, ncols, bf_off, bf_reg, crit=True):
    p = k.p
    p.op("sp", lambda eng, i=dram_ap, fo=f32_off, n=ncols: eng.dma_start(
        out=k.f32s[:, fo:fo + n], in_=i), writes=[slot.res], deps=slot.wdeps(), dma=True)
    if crit:
        eng_name = ("dve", "act")[k.cast_rr % 2]
        k.cast_rr += 1
    else:
        eng_name = "gp"
    if eng_name == "act":
        p.op("act", lambda eng, fo=f32_off, n=ncols, bo=bf_off: eng.activation(
            k.scr[:, bo:bo + n], k.f32s[:, fo:fo + n], AF.Copy),
            reads=[slot.res], writes=[bf_reg.res], deps=bf_reg.wdeps())
    else:
        p.op(eng_name, lambda eng, fo=f32_off, n=ncols, bo=bf_off: eng.tensor_copy(
            k.scr[:, bo:bo + n], k.f32s[:, fo:fo + n]),
            reads=[slot.res], writes=[bf_reg.res], deps=bf_reg.wdeps())


L1A_XN = 0
L1A_VG = 16384
L1A_QK = 24576
L1A_EXP = 52224
L1A_OT = 29696
L1A_VW = 46080
L1A_QKW = 50176


def emit_l1_attn(k):
    p, scr, rows = k.p, k.scr, k.rows
    g_c0, g_res = param_cols(k, "L1_attn_ln_gamma")
    b_c0, b_res = param_cols(k, "L1_attn_ln_beta")
    bo_c0, bo_res = param_cols(k, "L1_attn_b_out")
    qkw = k.dram("l1a_qkw", [8, 2, 128, 1024])
    vw = k.dram("l1a_vw", [2, 8, 128, 512])
    wo = k.dram("l1a_wo", [8, 128, 1024])

    mu_res, rstd_res = stats(k, 1e-5)
    xn = k.scr_t.carve(L1A_XN, L1A_XN + 16384, "xn")
    ln_apply(k, mu_res, rstd_res, affine=(g_c0, g_res, b_c0, b_res),
             out_reg=xn, out_region=L1A_XN)

    vg = k.scr_t.carve(L1A_VG, L1A_VG + 8192, "vg")
    ot = k.scr_t.carve(L1A_OT, L1A_OT + 16384, "ot")
    fslot = [k.f32_t.carve(0, 1024, "f0"), k.f32_t.carve(1024, 2048, "f1")]
    vwb = [k.scr_t.carve(L1A_VW + i * 512, L1A_VW + (i + 1) * 512, "vwb%d" % i)
           for i in range(8)]
    qkwb = [k.scr_t.carve(L1A_QKW + i * 1024, L1A_QKW + (i + 1) * 1024, "qkwb%d" % i)
            for i in range(2)]
    expb = [k.scr_t.carve(L1A_EXP + i * 512, L1A_EXP + (i + 1) * 512, "expb%d" % i)
            for i in range(4)]
    qk = k.scr_t.carve(L1A_QK, L1A_QK + 4096, "qk")

    for pair in range(8):
        g = pair // 4
        if pair % 4 == 0:
            for kt in range(NT):
                stream_cast(k, fslot[kt % 2], (kt % 2) * 1024, vw[g, kt], 512,
                            L1A_VW + kt * 512, vwb[kt])
            for t in range(16):
                gmm = MMGroup(k, k.PB[:, 512:1024], k.bPB1, NT)
                for kt in range(NT):
                    gmm.part(scr[:, L1A_XN + kt * S + t * 128: L1A_XN + kt * S + t * 128 + 128],
                             scr[:, L1A_VW + kt * 512: L1A_VW + (kt + 1) * 512],
                             [xn.res, vwb[kt].res])
                p.op("act", lambda eng, t=t: eng.activation(
                    scr[:, L1A_VG + t * 512: L1A_VG + (t + 1) * 512],
                    k.PB[:, 512:1024], AF.Copy),
                    reads=[k.bPB1], writes=[vg.res], deps=vg.wdeps())
        for qki in range(2):
            stream_cast(k, fslot[qki], qki * 1024, qkw[pair, qki], 1024,
                        L1A_QKW + qki * 1024, qkwb[qki])
            bank_ap = k.PEb[:, 0:512] if qki == 0 else k.PF[:, 0:512]
            bank = k.bPE if qki == 0 else k.bPF
            for c in range(NC):
                gmm = MMGroup(k, bank_ap, bank, NT)
                for kt in range(NT):
                    gmm.part(
                        scr[:, L1A_QKW + qki * 1024 + kt * 128: L1A_QKW + qki * 1024 + (kt + 1) * 128],
                        scr[:, L1A_XN + kt * S + c * 512: L1A_XN + kt * S + (c + 1) * 512],
                        [qkwb[qki].res, xn.res])
                p.op("act", lambda eng, qki=qki, c=c, b=bank_ap: eng.activation(
                    scr[:, L1A_QK + qki * 2048 + c * 512: L1A_QK + qki * 2048 + (c + 1) * 512],
                    b, AF.Copy),
                    reads=[bank], writes=[qk.res], deps=qk.wdeps())
        for qc in range(NC):
            nkb = 4 * qc + 4
            deng = []
            for h in range(2):
                denb = k.bPC if h == 0 else k.bPD
                denp = (k.PC if h == 0 else k.PD)[0:1, 0:512]
                deng.append(MMGroup(k, denp, denb, nkb))
            ot0 = MMGroup(k, k.PB[0:64, 0:512], k.bPB0, nkb)
            ot1 = MMGroup(k, k.PB[64:128, 512:1024], k.bPB1, nkb)

            def sc_bank(kb, h):
                if kb % 2 == 0:
                    return (k.PA[:, h * 512:(h + 1) * 512], k.bPA0 if h == 0 else k.bPA1)
                return ((k.PEb if h == 0 else k.PF)[:, 0:512], k.bPE if h == 0 else k.bPF)

            def emit_scores(kb):
                for h in range(2):
                    ei = (kb * 2 + h) % 4
                    eo = L1A_EXP + ei * 512
                    sb_ap, sb = sc_bank(kb, h)
                    mm(k, sb_ap, sb,
                       [(scr[64 * h:64 * h + 64, L1A_QK + 2048 + kb * 128: L1A_QK + 2048 + (kb + 1) * 128],
                         scr[64 * h:64 * h + 64, L1A_QK + qc * 512: L1A_QK + (qc + 1) * 512],
                         [qk.res])],
                       tp=(64 * h, 0))
                    p.op("act", lambda eng, eo=eo, sp=sb_ap, b=k.cv(-20.0): eng.activation(
                        scr[:, eo:eo + 512], sp, AF.Exp, bias=b, scale=0.125),
                        reads=[sb, k.cvals_r], writes=[expb[ei].res], deps=expb[ei].wdeps())
                    off = qc * 512 - kb * 128
                    if off < 512:
                        cb = off + 384
                        assert 0 <= cb and cb + 512 <= BAND_W
                        p.op("dve", lambda eng, eo=eo, cb=cb: eng.tensor_tensor(
                            scr[:, eo:eo + 512], scr[:, eo:eo + 512],
                            k.band[:, cb:cb + 512], ALU.mult),
                            reads=[expb[ei].res, k.band_r], writes=[expb[ei].res])

            emit_scores(0)
            for kb in range(nkb):
                if kb + 1 < nkb:
                    emit_scores(kb + 1)
                for h in range(2):
                    ei = (kb * 2 + h) % 4
                    eo = L1A_EXP + ei * 512
                    deng[h].part(k.ones_col_bf[:], scr[:, eo:eo + 512],
                                 [expb[ei].res, k.ones_col_bf_r])
                    hw = (pair % 4) * 2 + h
                    (ot0 if h == 0 else ot1).part(
                        scr[:, L1A_VG + kb * 512 + hw * 64: L1A_VG + kb * 512 + hw * 64 + 64],
                        scr[:, eo:eo + 512],
                        [vg.res, expb[ei].res], tp=(0, 64 * h))
            for h in range(2):
                den = k.row_t.carve(h * 512, (h + 1) * 512, "den%d" % h)
                bank = k.bPC if h == 0 else k.bPD
                p.op("dve", lambda eng, h=h, ps=(k.PC if h == 0 else k.PD): eng.tensor_copy(
                    rows[0:1, h * 512:(h + 1) * 512], ps[0:1, 0:512]),
                    reads=[bank], writes=[den.res], deps=den.wdeps())
                p.op("act", lambda eng, h=h: eng.activation(
                    rows[0:1, h * 512:(h + 1) * 512], rows[0:1, h * 512:(h + 1) * 512], AF.Ln),
                    reads=[den.res], writes=[den.res])
                p.op("act", lambda eng, h=h: eng.activation(
                    rows[0:1, h * 512:(h + 1) * 512], rows[0:1, h * 512:(h + 1) * 512],
                    AF.Exp, scale=-1.0),
                    reads=[den.res], writes=[den.res])
                rb_ap = k.PA[64 * h:64 * h + 64, h * 512:(h + 1) * 512]
                rb = k.bPA0 if h == 0 else k.bPA1
                mm(k, rb_ap, rb,
                   [(k.ones_row_f32[0:1, 0:64], rows[0:1, h * 512:(h + 1) * 512],
                     [den.res, k.ones_row_f32_r])], tp=(0, 64 * h))
                p.op("act", lambda eng, h=h: eng.activation(
                    scr[64 * h:64 * h + 64, L1A_EXP + h * 512: L1A_EXP + h * 512 + 512],
                    k.PB[64 * h:64 * h + 64, 512 * h:512 * h + 512], AF.Copy),
                    reads=[k.bPB0 if h == 0 else k.bPB1], writes=[expb[h].res])
                o_col = L1A_OT + pair * 2048 + qc * 512
                p.op("dve", lambda eng, h=h, o=o_col, rba=rb_ap: eng.tensor_tensor(
                    scr[64 * h:64 * h + 64, o:o + 512],
                    scr[64 * h:64 * h + 64, L1A_EXP + h * 512: L1A_EXP + h * 512 + 512],
                    rba, ALU.mult),
                    reads=[expb[h].res, rb], writes=[ot.res], deps=ot.wdeps())

    p.barrier()
    wob = [k.scr_t.carve(L1A_XN + kt * 1024, L1A_XN + (kt + 1) * 1024, "wob%d" % kt)
           for kt in range(NT)]
    for kt in range(NT):
        stream_cast(k, fslot[kt % 2], (kt % 2) * 1024, wo[kt], 1024,
                    L1A_XN + kt * 1024, wob[kt])
    for m in range(NT):
        for c in range(NC):
            par = (m * NC + c) % 2
            bank_ap = k.PA[:, par * 512:(par + 1) * 512]
            bank = k.bPA0 if par == 0 else k.bPA1
            gmm = MMGroup(k, bank_ap, bank, NT)
            for kt in range(NT):
                gmm.part(scr[:, L1A_XN + kt * 1024 + m * 128: L1A_XN + kt * 1024 + (m + 1) * 128],
                         scr[:, L1A_OT + kt * 2048 + c * 512: L1A_OT + kt * 2048 + (c + 1) * 512],
                         [wob[kt].res, ot.res])
            a_sl = (slice(None), slice(m * S + c * 512, m * S + (c + 1) * 512))
            p.op("dve", lambda eng, a=a_sl, m=m, b=bank_ap: eng.scalar_tensor_tensor(
                k.acc[a], b, k.misc[:, bo_c0 + m:bo_c0 + m + 1], k.acc[a], ALU.add, ALU.add),
                reads=[bank, bo_res, k.acc_r[m]], writes=[k.acc_r[m]])
    p.barrier()


def emit_outer_ln(k, eps=1e-5):
    mu_res, rstd_res = stats(k, eps)
    ln_apply(k, mu_res, rstd_res)
    k.p.barrier()



BLOCK_EMITTERS = {}
PREPPERS = {}


def build_program(blocks):
    nc = bass.Bass("TRN2", target_bir_lowering=False)
    with contextlib.ExitStack() as es:
        k = K(nc, es)
        xio = k.dram("xio", [D, S], F32)
        yio = k.dram("yio", [D, S], F32, kind="ExternalOutput")
        for j in range(NT):
            k.p.op("sp", lambda eng, j=j: eng.dma_start(
                out=k.acc[:, j * S:(j + 1) * S], in_=xio[j * 128:(j + 1) * 128, :]),
                writes=[k.acc_r[j]], dma=True)
        for b in blocks:
            BLOCK_EMITTERS[b](k)
            emit_outer_ln(k)
        out_toks = []
        for j in range(NT):
            t = k.p.op("sp", lambda eng, j=j: eng.dma_start(
                out=yio[j * 128:(j + 1) * 128, :], in_=k.acc[:, j * S:(j + 1) * S]),
                reads=[k.acc_r[j]], dma=True)
            out_toks.append(t)
        k.p.op("sp", lambda eng: eng.nop(), deps=out_toks)
        sems = {}
        with contextlib.ExitStack() as es2:
            for e in ENGS:
                sems[(e, "c")] = es2.enter_context(nc.semaphore("s_" + e))
                for i in range(NDMA):
                    sems[(e, "d", i)] = es2.enter_context(
                        nc.semaphore("sd_%s_%d" % (e, i)))
            block = es2.enter_context(nc.Block())
            k.p.emit(block, sems)
    return nc, k


BLOCK_EMITTERS["l1_attn"] = emit_l1_attn





def emit_top2_gate(k, lhsT_tile_fn, lhsT_res_fn, wg_sb, wg_res, bg_row, bg_res,
                   pexp_off, pexp_reg, scale_col_fn=None, scale_col_res=None):
    p, scr, rows = k.p, k.scr, k.rows
    gt = k.f32_t.carve(2560, 2760, "gate")
    GT = 2560
    for t in range(16):
        g = MMGroup(k, k.PC[0:128, 0:8], k.bPC, NT + (1 if bg_row is not None else 0))
        for kt in range(NT):
            g.part(lhsT_tile_fn(kt, t), wg_sb[:, kt * 8:(kt + 1) * 8],
                   [lhsT_res_fn(kt), wg_res])
        if bg_row is not None:
            g.part(k.ones_row_f32[:], bg_row, [bg_res, k.ones_row_f32_r])
        d = gt.wdeps()
        p.op("dve", lambda eng: eng.tensor_copy(k.f32s[:, GT:GT + 8], k.PC[0:128, 0:8]),
             reads=[k.bPC], writes=[gt.res], deps=d)
        if scale_col_fn is not None:
            p.op("dve", lambda eng, sc=scale_col_fn(t): eng.tensor_scalar(
                k.f32s[:, GT:GT + 8], k.f32s[:, GT:GT + 8], sc, None, ALU.mult),
                reads=[gt.res, scale_col_res], writes=[gt.res])
        p.op("dve", lambda eng: eng.tensor_reduce(
            k.f32s[:, GT + 8:GT + 9], k.f32s[:, GT:GT + 8], AX.X, ALU.max),
            reads=[gt.res], writes=[gt.res])
        p.op("dve", lambda eng: eng.tensor_scalar(
            k.f32s[:, GT + 16:GT + 24], k.f32s[:, GT:GT + 8],
            k.f32s[:, GT + 8:GT + 9], None, ALU.is_equal),
            reads=[gt.res], writes=[gt.res])
        p.op("dve", lambda eng: eng.scalar_tensor_tensor(
            k.f32s[:, GT + 32:GT + 40], k.f32s[:, GT + 16:GT + 24], -1.0e30,
            k.f32s[:, GT:GT + 8], ALU.mult, ALU.add),
            reads=[gt.res], writes=[gt.res])
        p.op("dve", lambda eng: eng.tensor_reduce(
            k.f32s[:, GT + 9:GT + 10], k.f32s[:, GT + 32:GT + 40], AX.X, ALU.max),
            reads=[gt.res], writes=[gt.res])
        p.op("dve", lambda eng: eng.tensor_scalar(
            k.f32s[:, GT + 24:GT + 32], k.f32s[:, GT + 32:GT + 40],
            k.f32s[:, GT + 9:GT + 10], None, ALU.is_equal),
            reads=[gt.res], writes=[gt.res])
        p.op("dve", lambda eng: eng.tensor_tensor(
            k.f32s[:, GT + 10:GT + 11], k.f32s[:, GT + 8:GT + 9],
            k.f32s[:, GT + 9:GT + 10], ALU.subtract),
            reads=[gt.res], writes=[gt.res])
        p.op("act", lambda eng: eng.activation(
            k.f32s[:, GT + 11:GT + 12], k.f32s[:, GT + 10:GT + 11], AF.Sigmoid),
            reads=[gt.res], writes=[gt.res])
        p.op("act", lambda eng: eng.activation(
            k.f32s[:, GT + 12:GT + 13], k.f32s[:, GT + 10:GT + 11], AF.Sigmoid,
            scale=-1.0),
            reads=[gt.res], writes=[gt.res])
        p.op("dve", lambda eng: eng.tensor_scalar(
            k.f32s[:, GT + 16:GT + 24], k.f32s[:, GT + 16:GT + 24],
            k.f32s[:, GT + 11:GT + 12], None, ALU.mult),
            reads=[gt.res], writes=[gt.res])
        p.op("dve", lambda eng: eng.tensor_scalar(
            k.f32s[:, GT + 24:GT + 32], k.f32s[:, GT + 24:GT + 32],
            k.f32s[:, GT + 12:GT + 13], None, ALU.mult),
            reads=[gt.res], writes=[gt.res])
        p.op("dve", lambda eng: eng.tensor_tensor(
            k.f32s[:, GT + 40:GT + 48], k.f32s[:, GT + 16:GT + 24],
            k.f32s[:, GT + 24:GT + 32], ALU.add),
            reads=[gt.res], writes=[gt.res])
        p.op("pe", lambda eng: eng.transpose(
            k.PF[0:8, 0:128], k.f32s[:, GT + 40:GT + 48], k.id_f32[:]),
            reads=[gt.res, k.id_f32_r], writes=[k.bPF])
        p.op("dve", lambda eng, t=t: eng.tensor_copy(
            scr[0:8, pexp_off + t * 128: pexp_off + (t + 1) * 128], k.PF[0:8, 0:128]),
            reads=[k.bPF], writes=[pexp_reg.res], deps=pexp_reg.wdeps())


L0F_XB = 0
L0F_XP = 8192
L0F_H = 16384
L0F_PEXP = 49152
L0F_PB = 51200
L0F_W13 = 53248


def emit_l0_ffn(k):
    import os
    dbg_v = os.environ.get("L0F_DEBUG", "0")
    dbg = dbg_v == "1"
    halves = [1] if dbg_v in ("1", "2", "3") else range(2)
    experts = ([7] if dbg_v in ("1", "4") else [6, 7] if dbg_v == "3" else range(8))
    p, scr, rows = k.p, k.scr, k.rows
    w1d = k.dram("l0f_w1", [8, 32, 128, 1024])
    w3d = k.dram("l0f_w3", [8, 32, 128, 1024])
    w2d = k.dram("l0f_w2", [8, 8, 128, 4096])
    wgd = k.dram("l0f_wg", [128, 64])
    bgd = k.dram("L0_ffn_bg", [8])

    wgr = k.f32_t.carve(2760, 2824, "wg")
    p.op("sp", lambda eng: eng.dma_start(out=k.f32s[:, 2760:2824], in_=wgd[:]),
         writes=[wgr.res], deps=wgr.wdeps(), dma=True)
    bgr = k.row_t.carve(3072, 3080, "bg")
    p.op("sp", lambda eng: eng.dma_start(
        out=rows[0:1, 3072:3080], in_=bgd.rearrange("(o e) -> o e", o=1)),
        writes=[bgr.res], deps=bgr.wdeps(), dma=True)

    pexp = k.scr_t.carve(L0F_PEXP, L0F_PEXP + 2048, "pexp")
    emit_top2_gate(k,
                   lambda kt, t: k.acc[:, kt * S + t * 128: kt * S + (t + 1) * 128],
                   lambda kt: k.acc_r[kt],
                   k.f32s[:, 2760:2824], wgr.res,
                   rows[0:1, 3072:3080], bgr.res,
                   L0F_PEXP, pexp)

    pb = k.scr_t.carve(L0F_PB, L0F_PB + 2048, "pb")
    for half in halves:
        hb = half * 1024
        xb = k.scr_t.carve(L0F_XB, L0F_XB + 8192, "xb%d" % half)
        for kt in range(NT):
            p.op("dve", lambda eng, kt=kt, hb=hb: eng.tensor_copy(
                scr[:, L0F_XB + kt * 1024: L0F_XB + (kt + 1) * 1024],
                k.acc[:, kt * S + hb: kt * S + hb + 1024]),
                reads=[k.acc_r[kt]], writes=[xb.res], deps=xb.wdeps())
            p.op("dve", lambda eng, kt=kt, hb=hb: eng.tensor_scalar(
                k.acc[:, kt * S + hb: kt * S + hb + 1024],
                k.acc[:, kt * S + hb: kt * S + hb + 1024], 2.0, None, ALU.mult),
                reads=[xb.res], writes=[k.acc_r[kt]])
        for e in experts:
            for c in range(2):
                mm(k, k.PEb[:, 0:512], k.bPE,
                   [(k.onehot8[0:8, e * 128:(e + 1) * 128],
                     scr[0:8, L0F_PEXP + hb + c * 512: L0F_PEXP + hb + (c + 1) * 512],
                     [pexp.res, k.onehot8_r])])
                p.op("dve", lambda eng, c=c: eng.tensor_copy(
                    scr[:, L0F_PB + c * 512: L0F_PB + (c + 1) * 512], k.PEb[:, 0:512]),
                    reads=[k.bPE], writes=[pb.res], deps=pb.wdeps())
            xp = [k.scr_t.carve(L0F_XP + kt * 1024, L0F_XP + (kt + 1) * 1024,
                                "xp%d_%d_%d" % (half, e, kt)) for kt in range(NT)]
            for kt in range(NT):
                p.op("dve", lambda eng, kt=kt: eng.tensor_tensor(
                    scr[:, L0F_XP + kt * 1024: L0F_XP + (kt + 1) * 1024],
                    scr[:, L0F_XB + kt * 1024: L0F_XB + (kt + 1) * 1024],
                    scr[:, L0F_PB: L0F_PB + 1024], ALU.mult),
                    reads=[xb.res, pb.res], writes=[xp[kt].res], deps=xp[kt].wdeps())
            w1r = [k.scr_t.carve(L0F_W13 + i * 512, L0F_W13 + (i + 1) * 512, "w1r%d" % i)
                   for i in range(2)]
            w3r = [k.scr_t.carve(L0F_W13 + 1024 + i * 512, L0F_W13 + 1024 + (i + 1) * 512,
                                 "w3r%d" % i) for i in range(2)]
            fs6 = [k.f32_t.carve(i * 512, (i + 1) * 512, "fs%d" % i) for i in range(6)]
            sctr = [0, 0]
            hreg = k.scr_t.carve(L0F_H, L0F_H + 32768, "h%d_%d" % (half, e))

            def l0f_streams(ht):
                for hc in range(2):
                    s1 = sctr[0] % 3
                    sctr[0] += 1
                    stream_cast(k, fs6[s1], s1 * 512,
                                w1d[e, ht, :, hc * 512:(hc + 1) * 512],
                                512, L0F_W13 + hc * 512, w1r[hc])
                    s3 = 3 + sctr[1] % 3
                    sctr[1] += 1
                    stream_cast(k, fs6[s3], s3 * 512,
                                w3d[e, ht, :, hc * 512:(hc + 1) * 512],
                                512, L0F_W13 + 1024 + hc * 512, w3r[hc])

            l0f_streams(0)
            for ht in range(32):
                par = ht % 2
                if par == 0:
                    ub = [(k.PA[:, 0:512], k.bPA0), (k.PA[:, 512:1024], k.bPA1)]
                    vb = [(k.PB[:, 0:512], k.bPB0), (k.PB[:, 512:1024], k.bPB1)]
                else:
                    ub = [(k.PC[0:128, 0:512], k.bPC), (k.PD[0:128, 0:512], k.bPD)]
                    vb = [(k.PEb[0:128, 0:512], k.bPE), (k.PF[0:128, 0:512], k.bPF)]
                for c in range(2):
                    gu = MMGroup(k, ub[c][0], ub[c][1], NT)
                    for kt in range(NT):
                        gu.part(scr[:, L0F_W13 + kt * 128: L0F_W13 + (kt + 1) * 128],
                                scr[:, L0F_XP + kt * 1024 + c * 512: L0F_XP + kt * 1024 + (c + 1) * 512],
                                [w1r[kt // 4].res, xp[kt].res])
                    gv = MMGroup(k, vb[c][0], vb[c][1], NT)
                    for kt in range(NT):
                        gv.part(scr[:, L0F_W13 + 1024 + kt * 128: L0F_W13 + 1024 + (kt + 1) * 128],
                                scr[:, L0F_XP + kt * 1024 + c * 512: L0F_XP + kt * 1024 + (c + 1) * 512],
                                [w3r[kt // 4].res, xp[kt].res])
                if ht + 1 < 32:
                    l0f_streams(ht + 1)
                hcol = L0F_H + ht * 1024
                for c in range(2):
                    p.op("act", lambda eng, hcol=hcol, c=c, ua=ub[c][0]: eng.activation(
                        scr[:, hcol + c * 512:hcol + (c + 1) * 512], ua, AF.Silu),
                        reads=[ub[c][1]], writes=[hreg.res], deps=hreg.wdeps())
                    p.op("dve", lambda eng, hcol=hcol, c=c, va=vb[c][0]: eng.tensor_tensor(
                        scr[:, hcol + c * 512:hcol + (c + 1) * 512],
                        scr[:, hcol + c * 512:hcol + (c + 1) * 512], va, ALU.mult),
                        reads=[vb[c][1], hreg.res], writes=[hreg.res])
            if dbg:
                continue
            w2b = [k.scr_t.carve(L0F_XP + i * 1024, L0F_XP + (i + 1) * 1024, "w2b%d" % i)
                   for i in range(4)]
            w2f = [k.f32_t.carve(i * 1024, (i + 1) * 1024, "w2f%d" % i) for i in range(2)]
            for d2 in range(NT):
                for q in range(4):
                    stream_cast(k, w2f[q % 2], (q % 2) * 1024,
                                w2d[e, d2, :, q * 1024:(q + 1) * 1024],
                                1024, L0F_XP + q * 1024, w2b[q], crit=False)
                if d2 % 2 == 0:
                    yb = [(k.PA[:, 0:512], k.bPA0), (k.PA[:, 512:1024], k.bPA1)]
                else:
                    yb = [(k.PB[:, 0:512], k.bPB0), (k.PB[:, 512:1024], k.bPB1)]
                for c in range(2):
                    gy = MMGroup(k, yb[c][0], yb[c][1], 32)
                    for kt in range(32):
                        gy.part(scr[:, L0F_XP + kt * 128: L0F_XP + (kt + 1) * 128],
                                scr[:, L0F_H + kt * 1024 + c * 512: L0F_H + kt * 1024 + (c + 1) * 512],
                                [w2b[kt // 8].res, hreg.res])
                    p.op("dve", lambda eng, d2=d2, c=c, hb=hb, ps=yb[c][0]:
                         eng.tensor_tensor(
                        k.acc[:, d2 * S + hb + c * 512: d2 * S + hb + (c + 1) * 512],
                        k.acc[:, d2 * S + hb + c * 512: d2 * S + hb + (c + 1) * 512],
                        ps, ALU.add),
                        reads=[yb[c][1], k.acc_r[d2]],
                        writes=[k.acc_r[d2]])
    p.barrier()


BLOCK_EMITTERS["l0_ffn"] = emit_l0_ffn


def prep_l0_ffn(inputs):
    W1 = np.asarray(inputs["L0_ffn_W1"], np.float32)
    W3 = np.asarray(inputs["L0_ffn_W3"], np.float32)
    W2 = np.asarray(inputs["L0_ffn_W2"], np.float32)
    w1t = np.ascontiguousarray(
        W1.reshape(8, 8, 128, 32, 128).transpose(0, 3, 2, 1, 4).reshape(8, 32, 128, 1024))
    w3t = np.ascontiguousarray(
        W3.reshape(8, 8, 128, 32, 128).transpose(0, 3, 2, 1, 4).reshape(8, 32, 128, 1024))
    w2t = np.ascontiguousarray(
        W2.reshape(8, 32, 128, 8, 128).transpose(0, 3, 2, 1, 4).reshape(8, 8, 128, 4096))
    Wg = np.asarray(inputs["L0_ffn_Wg"], np.float32)
    wgt = np.ascontiguousarray(Wg.reshape(8, 128, 8).transpose(1, 0, 2).reshape(128, 64))
    return {"l0f_w1": w1t, "l0f_w3": w3t, "l0f_w2": w2t, "l0f_wg": wgt,
            "L0_ffn_bg": np.asarray(inputs["L0_ffn_bg"], np.float32)}


PREPPERS["l0_ffn"] = prep_l0_ffn


def prep_l1_ffn(inputs):
    wd = np.asarray(inputs["L1_ffn_w_down"], np.float32)
    wq = np.asarray(inputs["L1_ffn_wq"], np.float32)
    wk = np.asarray(inputs["L1_ffn_wk"], np.float32)
    wv = np.asarray(inputs["L1_ffn_wv"], np.float32)
    wu = np.asarray(inputs["L1_ffn_w_up"], np.float32)
    rw = np.asarray(inputs["L1_ffn_router_w"], np.float32)
    w1 = np.asarray(inputs["L1_ffn_w1"], np.float32)
    w3 = np.asarray(inputs["L1_ffn_w3"], np.float32)
    w2 = np.asarray(inputs["L1_ffn_w2"], np.float32)
    perm = np.empty(512, np.int64)
    for h in range(16):
        for f in range(16):
            perm[16 * h + f] = 32 * h + 2 * f
            perm[256 + 16 * h + f] = 32 * h + 2 * f + 1
    wq_p = wq[:, perm]
    wk_p = wk[:, perm]
    tile4 = lambda w, M: np.ascontiguousarray(
        w.reshape(w.shape[0] // 128, 128, M, 128).transpose(2, 1, 0, 3)
        .reshape(M, 128, (w.shape[0] // 128) * 128))
    out = {
        "l1f_wd": tile4(wd, 4),
        "l1f_wq": tile4(wq_p, 4),
        "l1f_wk": tile4(wk_p, 4),
        "L1_ffn_wv": wv,
        "l1f_wu": tile4(wu, 8),
        "l1f_rw": np.ascontiguousarray(rw.reshape(8, 128, 8).transpose(1, 0, 2).reshape(128, 64)),
        "l1f_w1": np.ascontiguousarray(
            w1.reshape(8, 8, 128, 16, 128).transpose(0, 3, 2, 1, 4).reshape(8, 16, 128, 1024)),
        "l1f_w3": np.ascontiguousarray(
            w3.reshape(8, 8, 128, 16, 128).transpose(0, 3, 2, 1, 4).reshape(8, 16, 128, 1024)),
        "l1f_w2": np.ascontiguousarray(
            w2.reshape(8, 16, 128, 8, 128).transpose(0, 3, 2, 1, 4).reshape(8, 8, 128, 2048)),
        "L1_ffn_g1": np.asarray(inputs["L1_ffn_g1"], np.float32),
        "L1_ffn_g2": np.asarray(inputs["L1_ffn_g2"], np.float32),
    }
    f = np.arange(16)
    inv = 10000.0 ** (-(2.0 * f) / 32.0)
    ang = np.arange(2048)[None, :] * np.tile(inv, 8)[:, None]
    out["l1f_cos"] = np.cos(ang).astype(BF)
    out["l1f_sin"] = np.sin(ang).astype(BF)
    sel = np.zeros((128, 512), BF)
    for par in range(2):
        for j in range(4):
            for t in range(16):
                sel[64 * par + 16 * j + t, par * 128 + 32 * j + t] = 1
                sel[64 * par + 16 * j + t, (2 + par) * 128 + 32 * j + 16 + t] = 1
    out["l1f_sel"] = sel
    return out


PREPPERS["l1_ffn"] = prep_l1_ffn



L1F_H1 = 0
L1F_CT = 16384
L1F_Q = 24576
L1F_K = 32768
L1F_V = 40960
L1F_OT = 16384
L1F_COS = 0
L1F_SIN = 2048
L1F_SEL = 4096
L1F_RTMP = 4608
L1F_RR = 6656
L1F_WST = 53248


def emit_rms_rows(k, eps, src_fn=None, src_res=None, bf_src=False):
    p, scr, rows = k.p, k.scr, k.rows
    rstd = k.row_t.carve(2048, 4096, "rmsr")
    xc = k.scr_t.carve(STAT_XC, STAT_XC + 512, "rmsxc")
    sq = k.scr_t.carve(STAT_SQ, STAT_SQ + 512, "rmssq")
    for c in range(NC):
        g2 = MMGroup(k, k.PD[0:1, 0:512], k.bPD, NT)
        for j in range(NT):
            if not bf_src:
                p.op("dve", lambda eng, j=j, c=c: eng.tensor_copy(
                    scr[:, STAT_XC:STAT_XC + 512],
                    src_fn(j)[:, c * 512:(c + 1) * 512]),
                    reads=[src_res] if src_res else [k.acc_r[j]], writes=[xc.res],
                    deps=xc.wdeps())
                sqsrc = scr[:, STAT_XC:STAT_XC + 512]
                sqread = xc.res
            else:
                sqsrc = src_fn(j)[:, c * 512:(c + 1) * 512]
                sqread = src_res
            p.op("act", lambda eng, s=sqsrc: eng.activation(
                scr[:, STAT_SQ:STAT_SQ + 512], s, AF.Square),
                reads=[sqread], writes=[sq.res], deps=sq.wdeps())
            g2.part(k.ones_col_bf[:], scr[:, STAT_SQ:STAT_SQ + 512],
                    [sq.res, k.ones_col_bf_r])
        p.op("dve", lambda eng, c=c: eng.tensor_scalar(
            rows[0:1, 2048 + c * 512:2048 + (c + 1) * 512], k.PD[0:1, 0:512],
            1.0 / D, None, ALU.mult),
            reads=[k.bPD], writes=[rstd.res], deps=rstd.wdeps())
    p.op("act", lambda eng, b=k.cv(eps, 1): eng.activation(
        rows[0:1, 2048:4096], rows[0:1, 2048:4096], AF.Ln, bias=b),
        reads=[rstd.res, k.cvals_r], writes=[rstd.res])
    p.op("act", lambda eng: eng.activation(
        rows[0:1, 2048:4096], rows[0:1, 2048:4096], AF.Exp, scale=-0.5),
        reads=[rstd.res], writes=[rstd.res])
    return rstd


def emit_l1_ffn(k):
    import os
    stop = os.environ.get("L1F_STOP", "")
    p, scr, rows = k.p, k.scr, k.rows
    g1_c0, g1_res = param_cols(k, "L1_ffn_g1")
    g2_c0, g2_res = param_cols(k, "L1_ffn_g2")
    wdd = k.dram("l1f_wd", [4, 128, 1024])
    wqd = k.dram("l1f_wq", [4, 128, 512])
    wkd = k.dram("l1f_wk", [4, 128, 512])
    wvd = k.dram("L1_ffn_wv", [512, 512])
    wud = k.dram("l1f_wu", [8, 128, 512])
    rwd = k.dram("l1f_rw", [128, 64])
    w1d = k.dram("l1f_w1", [8, 16, 128, 1024])
    w3d = k.dram("l1f_w3", [8, 16, 128, 1024])
    w2d = k.dram("l1f_w2", [8, 8, 128, 2048])
    cosd = k.dram("l1f_cos", [128, 2048], BF16)
    sind = k.dram("l1f_sin", [128, 2048], BF16)
    seld = k.dram("l1f_sel", [128, 512], BF16)

    fsl = [k.f32_t.carve(0, 1024, "lf0"), k.f32_t.carve(1024, 2048, "lf1")]

    rstd = emit_rms_rows(k, 1e-5, src_fn=lambda j: k.acc[:, j * S:(j + 1) * S])
    h1 = k.scr_t.carve(L1F_H1, L1F_H1 + 16384, "h1")
    for c in range(NC):
        mm(k, k.PF[:, 0:512], k.bPF,
           [(k.ones_row_f32[:], rows[0:1, 2048 + c * 512:2048 + (c + 1) * 512],
             [rstd.res, k.ones_row_f32_r])])
        for j in range(NT):
            p.op("dve", lambda eng, j=j, c=c, gc=g1_c0: eng.scalar_tensor_tensor(
                scr[:, L1F_H1 + j * S + c * 512: L1F_H1 + j * S + (c + 1) * 512],
                k.acc[:, j * S + c * 512: j * S + (c + 1) * 512],
                k.misc[:, gc + j:gc + j + 1], k.PF[:, 0:512], ALU.mult, ALU.mult),
                reads=[k.acc_r[j], g1_res, k.bPF], writes=[h1.res], deps=h1.wdeps())

    ct = k.scr_t.carve(L1F_CT, L1F_CT + 4096, "ct")
    wst = k.scr_t.carve(L1F_WST, L1F_WST + 1024, "wst")
    for m in range(4):
        stream_cast(k, fsl[m % 2], (m % 2) * 1024, wdd[m], 1024, L1F_WST, wst)
        for c in range(NC):
            g = MMGroup(k, k.PA[:, 0:512], k.bPA0, NT)
            for kt in range(NT):
                g.part(scr[:, L1F_WST + kt * 128: L1F_WST + (kt + 1) * 128],
                       scr[:, L1F_H1 + kt * S + c * 512: L1F_H1 + kt * S + (c + 1) * 512],
                       [wst.res, h1.res])
            p.op("act", lambda eng, m=m, c=c: eng.activation(
                scr[:, L1F_CT + m * S + c * 512: L1F_CT + m * S + (c + 1) * 512],
                k.PA[:, 0:512], AF.Copy),
                reads=[k.bPA0], writes=[ct.res], deps=ct.wdeps())

    qraw = k.scr_t.carve(L1F_Q, L1F_Q + 8192, "qraw")
    kraw = k.scr_t.carve(L1F_K, L1F_K + 8192, "kraw")
    for qk, (dram_w, reg, base) in enumerate([(wqd, qraw, L1F_Q), (wkd, kraw, L1F_K)]):
        for m in range(4):
            wst = k.scr_t.carve(L1F_WST, L1F_WST + 512, "wqk%d_%d" % (qk, m))
            stream_cast(k, fsl[m % 2], (m % 2) * 1024, dram_w[m], 512, L1F_WST, wst)
            for c in range(NC):
                g = MMGroup(k, k.PA[:, 0:512], k.bPA0, 4)
                for kt in range(4):
                    g.part(scr[:, L1F_WST + kt * 128: L1F_WST + (kt + 1) * 128],
                           scr[:, L1F_CT + kt * S + c * 512: L1F_CT + kt * S + (c + 1) * 512],
                           [wst.res, ct.res])
                p.op("act", lambda eng, b=base, m=m, c=c: eng.activation(
                    scr[:, b + m * S + c * 512: b + m * S + (c + 1) * 512],
                    k.PA[:, 0:512], AF.Copy),
                    reads=[k.bPA0], writes=[reg.res], deps=reg.wdeps())
    vreg = k.scr_t.carve(L1F_V, L1F_V + 8192, "v")
    wstv = [k.scr_t.carve(L1F_WST + i * 512, L1F_WST + (i + 1) * 512, "wv%d" % i)
            for i in range(4)]
    for kt in range(4):
        stream_cast(k, fsl[kt % 2], (kt % 2) * 1024, wvd[kt * 128:(kt + 1) * 128, :],
                    512, L1F_WST + kt * 512, wstv[kt])
    for t in range(16):
        g = MMGroup(k, k.PB[:, 0:512], k.bPB0, 4)
        for kt in range(4):
            g.part(scr[:, L1F_CT + kt * S + t * 128: L1F_CT + kt * S + t * 128 + 128],
                   scr[:, L1F_WST + kt * 512: L1F_WST + (kt + 1) * 512],
                   [ct.res, wstv[kt].res])
        p.op("act", lambda eng, t=t: eng.activation(
            scr[:, L1F_V + t * 512: L1F_V + (t + 1) * 512], k.PB[:, 0:512], AF.Copy),
            reads=[k.bPB0], writes=[vreg.res], deps=vreg.wdeps())

    if stop == "p3":
        return
    cos = k.scr_t.carve(L1F_COS, L1F_COS + 2048, "cos")
    sin = k.scr_t.carve(L1F_SIN, L1F_SIN + 2048, "sin")
    sel = k.scr_t.carve(L1F_SEL, L1F_SEL + 512, "sel")
    p.op("sp", lambda eng: eng.dma_start(out=scr[:, L1F_COS:L1F_COS + 2048], in_=cosd[:]),
         writes=[cos.res], deps=cos.wdeps(), dma=True)
    p.op("sp", lambda eng: eng.dma_start(out=scr[:, L1F_SIN:L1F_SIN + 2048], in_=sind[:]),
         writes=[sin.res], deps=sin.wdeps(), dma=True)
    p.op("sp", lambda eng: eng.dma_start(out=scr[:, L1F_SEL:L1F_SEL + 512], in_=seld[:]),
         writes=[sel.res], deps=sel.wdeps(), dma=True)
    rtmp = k.scr_t.carve(L1F_RTMP, L1F_RTMP + 2048, "rtmp")
    hc_regs = []
    for qk, (raw_base, raw_reg) in enumerate([(L1F_Q, qraw), (L1F_K, kraw)]):
        rr1 = [k.scr_t.carve(L1F_RR + i * S, L1F_RR + (i + 1) * S, "rr1_%d_%d" % (qk, i))
               for i in range(2)]
        rr2 = [k.scr_t.carve(L1F_RR + (2 + i) * S, L1F_RR + (3 + i) * S, "rr2_%d_%d" % (qk, i))
               for i in range(2)]
        rtm0 = k.scr_t.carve(L1F_RTMP, L1F_RTMP + S, "rtm%d" % qk)
        rtm = [rtm0, rtm0]
        for i in range(2):
            q1 = (slice(None), slice(raw_base + i * S, raw_base + (i + 1) * S))
            q2 = (slice(None), slice(raw_base + (2 + i) * S, raw_base + (3 + i) * S))
            r1 = (slice(None), slice(L1F_RR + i * S, L1F_RR + (i + 1) * S))
            r2 = (slice(None), slice(L1F_RR + (2 + i) * S, L1F_RR + (3 + i) * S))
            tm = (slice(None), slice(L1F_RTMP, L1F_RTMP + S))
            p.op("dve", lambda eng, a=q1, o=r1: eng.tensor_tensor(
                scr[o], scr[a], scr[:, L1F_COS:L1F_COS + 2048], ALU.mult),
                reads=[raw_reg.res, cos.res], writes=[rr1[i].res], deps=rr1[i].wdeps())
            p.op("dve", lambda eng, a=q1, o=tm: eng.tensor_tensor(
                scr[o], scr[a], scr[:, L1F_SIN:L1F_SIN + 2048], ALU.mult),
                reads=[raw_reg.res, sin.res], writes=[rtm[i].res], deps=rtm[i].wdeps())
            p.op("dve", lambda eng, a=q2, o=r2: eng.tensor_tensor(
                scr[o], scr[a], scr[:, L1F_SIN:L1F_SIN + 2048], ALU.mult),
                reads=[raw_reg.res, sin.res], writes=[rr2[i].res], deps=rr2[i].wdeps())
            p.op("dve", lambda eng, o=r1, b=r2: eng.tensor_tensor(
                scr[o], scr[o], scr[b], ALU.subtract),
                reads=[rr1[i].res, rr2[i].res], writes=[rr1[i].res])
            p.op("dve", lambda eng, a=q2, o=r2: eng.tensor_tensor(
                scr[o], scr[a], scr[:, L1F_COS:L1F_COS + 2048], ALU.mult),
                reads=[raw_reg.res, rr2[i].res], writes=[rr2[i].res])
            p.op("dve", lambda eng, o=r2, b=tm: eng.tensor_tensor(
                scr[o], scr[o], scr[b], ALU.add),
                reads=[rr2[i].res, rtm[i].res], writes=[rr2[i].res])
        hc = k.scr_t.carve(raw_base, raw_base + 8192, "hc%d" % qk)
        for i in range(4):
            par = i % 2
            for c in range(NC):
                g = MMGroup(k, k.PA[:, 0:512], k.bPA0, 2)
                g.part(scr[:, L1F_SEL + par * 128: L1F_SEL + (par + 1) * 128],
                       scr[:, L1F_RR + (i // 2) * S + c * 512: L1F_RR + (i // 2) * S + (c + 1) * 512],
                       [sel.res, rr1[i // 2].res])
                g.part(scr[:, L1F_SEL + (2 + par) * 128: L1F_SEL + (3 + par) * 128],
                       scr[:, L1F_RR + (2 + i // 2) * S + c * 512: L1F_RR + (2 + i // 2) * S + (c + 1) * 512],
                       [sel.res, rr2[i // 2].res])
                p.op("act", lambda eng, b=raw_base, i=i, c=c: eng.activation(
                    scr[:, b + i * S + c * 512: b + i * S + (c + 1) * 512],
                    k.PA[:, 0:512], AF.Copy),
                    reads=[k.bPA0], writes=[hc.res], deps=hc.wdeps())
        hc_regs.append(hc)
    qr, kr = hc_regs

    ot = k.scr_t.carve(L1F_OT, L1F_OT + 8192, "ot")
    expb = [k.scr_t.carve(L1F_RR + i * 512, L1F_RR + (i + 1) * 512, "l1fexp%d" % i)
            for i in range(4)]
    SC32 = float(1.0 / np.sqrt(32.0))
    for pr in range(8):
        tile = pr // 2
        for qc in range(NC):
            nkb = 4 * qc + 4
            deng = []
            otg = []
            for h in range(2):
                H = 2 * pr + h
                base = 32 * (H % 4)
                denb = k.bPC if h == 0 else k.bPD
                deng.append(MMGroup(k, (k.PC if h == 0 else k.PD)[0:1, 0:512],
                                    denb, nkb))
                og = MMGroup(k, (k.PEb if h == 0 else k.PF)[base:base + 32, 0:512],
                             k.bPE if h == 0 else k.bPF, nkb)
                otg.append(og)
            def sc_bank2(kb, h):
                if kb % 2 == 0:
                    return (k.PA[:, h * 512:(h + 1) * 512], k.bPA0 if h == 0 else k.bPA1)
                return (k.PB[:, h * 512:(h + 1) * 512], k.bPB0 if h == 0 else k.bPB1)

            def emit_scores2(kb):
                for h in range(2):
                    H = 2 * pr + h
                    base = 32 * (H % 4)
                    ei = (kb * 2 + h) % 4
                    eo = L1F_RR + ei * 512
                    sb_ap, sb = sc_bank2(kb, h)
                    mm(k, sb_ap, sb,
                       [(scr[base:base + 32, L1F_K + tile * S + kb * 128: L1F_K + tile * S + (kb + 1) * 128],
                         scr[base:base + 32, L1F_Q + tile * S + qc * 512: L1F_Q + tile * S + (qc + 1) * 512],
                         [qr.res, kr.res])],
                       tp=(base, 0))
                    p.op("act", lambda eng, eo=eo, sp_=sb_ap, b=k.cv(-12.0): eng.activation(
                        scr[:, eo:eo + 512], sp_, AF.Exp, bias=b, scale=SC32),
                        reads=[sb, k.cvals_r], writes=[expb[ei].res], deps=expb[ei].wdeps())
                    off = qc * 512 - kb * 128
                    if off < 512:
                        cb = off + 384
                        p.op("dve", lambda eng, eo=eo, cb=cb: eng.tensor_tensor(
                            scr[:, eo:eo + 512], scr[:, eo:eo + 512],
                            k.band[:, cb:cb + 512], ALU.mult),
                            reads=[expb[ei].res, k.band_r], writes=[expb[ei].res])

            emit_scores2(0)
            for kb in range(nkb):
                if kb + 1 < nkb:
                    emit_scores2(kb + 1)
                for h in range(2):
                    H = 2 * pr + h
                    base = 32 * (H % 4)
                    ei = (kb * 2 + h) % 4
                    eo = L1F_RR + ei * 512
                    deng[h].part(k.ones_col_bf[:], scr[:, eo:eo + 512],
                                 [expb[ei].res, k.ones_col_bf_r])
                    otg[h].part(
                        scr[:, L1F_V + kb * 512 + 32 * H: L1F_V + kb * 512 + 32 * H + 32],
                        scr[:, eo:eo + 512],
                        [vreg.res, expb[ei].res], tp=(0, base))
            for h in range(2):
                H = 2 * pr + h
                base = 32 * (H % 4)
                den = k.row_t.carve(h * 512, (h + 1) * 512, "l1fden%d" % h)
                bank = k.bPC if h == 0 else k.bPD
                p.op("dve", lambda eng, h=h, ps=(k.PC if h == 0 else k.PD): eng.tensor_copy(
                    rows[0:1, h * 512:(h + 1) * 512], ps[0:1, 0:512]),
                    reads=[bank], writes=[den.res], deps=den.wdeps())
                p.op("act", lambda eng, h=h: eng.activation(
                    rows[0:1, h * 512:(h + 1) * 512], rows[0:1, h * 512:(h + 1) * 512],
                    AF.Ln),
                    reads=[den.res], writes=[den.res])
                p.op("act", lambda eng, h=h: eng.activation(
                    rows[0:1, h * 512:(h + 1) * 512], rows[0:1, h * 512:(h + 1) * 512],
                    AF.Exp, scale=-1.0),
                    reads=[den.res], writes=[den.res])
                rb_ap = k.PB[base:base + 32, 0:512] if h == 0 else k.PB[base:base + 32, 512:1024]
                rb = k.bPB0 if h == 0 else k.bPB1
                mm(k, rb_ap, rb,
                   [(k.ones_row_f32[0:1, 0:32], rows[0:1, h * 512:(h + 1) * 512],
                     [den.res, k.ones_row_f32_r])], tp=(0, base))
                p.op("act", lambda eng, h=h, base=base,
                     ps=(k.PEb if h == 0 else k.PF): eng.activation(
                    scr[base:base + 32, L1F_RR + h * 512: L1F_RR + h * 512 + 512],
                    ps[base:base + 32, 0:512], AF.Copy),
                    reads=[k.bPE if h == 0 else k.bPF], writes=[expb[h].res])
                o_col = L1F_OT + tile * S + qc * 512
                p.op("dve", lambda eng, h=h, base=base, o=o_col, rba=rb_ap: eng.tensor_tensor(
                    scr[base:base + 32, o:o + 512],
                    scr[base:base + 32, L1F_RR + h * 512: L1F_RR + h * 512 + 512],
                    rba, ALU.mult),
                    reads=[expb[h].res, rb], writes=[ot.res], deps=ot.wdeps())

    if stop == "p5":
        return
    p.barrier()
    resbf = k.scr_t.carve(L1F_Q, L1F_Q + 16384, "resbf")
    for m in range(NT):
        wst = k.scr_t.carve(L1F_WST, L1F_WST + 512, "wu%d" % m)
        stream_cast(k, fsl[m % 2], (m % 2) * 1024, wud[m], 512, L1F_WST, wst)
        for c in range(NC):
            par = c % 2
            bank_ap = k.PA[:, par * 512:(par + 1) * 512]
            bank = k.bPA0 if par == 0 else k.bPA1
            g = MMGroup(k, bank_ap, bank, 4)
            for kt in range(4):
                g.part(scr[:, L1F_WST + kt * 128: L1F_WST + (kt + 1) * 128],
                       scr[:, L1F_OT + kt * S + c * 512: L1F_OT + kt * S + (c + 1) * 512],
                       [wst.res, ot.res])
            a_sl = (slice(None), slice(m * S + c * 512, m * S + (c + 1) * 512))
            r_sl = (slice(None), slice(L1F_Q + m * S + c * 512, L1F_Q + m * S + (c + 1) * 512))
            p.op("dve", lambda eng, a=a_sl, r=r_sl, b=bank_ap: eng.tensor_tensor(
                scr[r], k.acc[a], b, ALU.add),
                reads=[k.acc_r[m], bank], writes=[resbf.res], deps=resbf.wdeps())
            p.op("dve", lambda eng, a=a_sl, b=bank_ap: eng.scalar_tensor_tensor(
                k.acc[a], k.acc[a], 2.0, b, ALU.mult, ALU.add),
                reads=[bank, k.acc_r[m]], writes=[k.acc_r[m]])

    rstd2 = emit_rms_rows(
        k, 1e-5, src_fn=lambda j: scr[:, L1F_Q + j * S: L1F_Q + (j + 1) * S],
        src_res=resbf.res, bf_src=True)
    rc0 = k.misc_used
    k.misc_used += 16
    assert k.misc_used <= 128
    rcol_res = Res("rstdcols")
    for t in range(16):
        p.op("pe", lambda eng, t=t: eng.transpose(
            k.PF[0:128, 0:1], rows[0:1, 2048 + t * 128: 2048 + (t + 1) * 128],
            k.id_f32[0:1, 0:1]),
            reads=[rstd2.res, k.id_f32_r], writes=[k.bPF])
        p.op("dve", lambda eng, t=t, rc0=rc0: eng.tensor_copy(
            k.misc[:, rc0 + t:rc0 + t + 1], k.PF[0:128, 0:1]),
            reads=[k.bPF], writes=[rcol_res])
    rwf = k.f32_t.carve(2760, 2824, "rw")
    p.op("sp", lambda eng: eng.dma_start(out=k.f32s[:, 2760:2824], in_=rwd[:]),
         writes=[rwf.res], deps=rwf.wdeps(), dma=True)
    RWP = 43008
    rwp = k.scr_t.carve(RWP, RWP + 64, "rwp")
    for kt in range(NT):
        p.op("dve", lambda eng, kt=kt, gc=g2_c0: eng.tensor_scalar(
            scr[:, RWP + kt * 8: RWP + (kt + 1) * 8],
            k.f32s[:, 2760 + kt * 8: 2760 + (kt + 1) * 8],
            k.misc[:, gc + kt:gc + kt + 1], None, ALU.mult),
            reads=[rwf.res, g2_res], writes=[rwp.res], deps=rwp.wdeps())
    PEXP2 = 40960
    pexp = k.scr_t.carve(PEXP2, PEXP2 + 2048, "l1fpexp")
    emit_top2_gate(k,
                   lambda kt, t: scr[:, L1F_Q + kt * S + t * 128: L1F_Q + kt * S + (t + 1) * 128],
                   lambda kt: resbf.res,
                   scr[:, RWP:RWP + 64], rwp.res, None, None,
                   PEXP2, pexp,
                   scale_col_fn=lambda t: k.misc[:, rc0 + t:rc0 + t + 1],
                   scale_col_res=rcol_res)

    treg = k.scr_t.carve(0, 16384, "T")
    for c in range(NC):
        mm(k, k.PF[:, 0:512], k.bPF,
           [(k.ones_row_f32[:], rows[0:1, 2048 + c * 512:2048 + (c + 1) * 512],
             [rstd2.res, k.ones_row_f32_r])])
        for j in range(NT):
            p.op("dve", lambda eng, j=j, c=c, gc=g2_c0: eng.scalar_tensor_tensor(
                scr[:, j * S + c * 512: j * S + (c + 1) * 512],
                scr[:, L1F_Q + j * S + c * 512: L1F_Q + j * S + (c + 1) * 512],
                k.misc[:, gc + j:gc + j + 1], k.PF[:, 0:512], ALU.mult, ALU.mult),
                reads=[resbf.res, g2_res, k.bPF], writes=[treg.res], deps=treg.wdeps())
    TG, HID, PBR, W13R, W2BR = 16384, 24576, 43072, 45120, 47168
    pb = k.scr_t.carve(PBR, PBR + 2048, "l1fpb")
    for half in range(2):
        hb = half * 1024
        for e in range(8):
            for c in range(2):
                mm(k, k.PEb[:, 0:512], k.bPE,
                   [(k.onehot8[0:8, e * 128:(e + 1) * 128],
                     scr[0:8, PEXP2 + hb + c * 512: PEXP2 + hb + (c + 1) * 512],
                     [pexp.res, k.onehot8_r])])
                p.op("dve", lambda eng, c=c: eng.tensor_copy(
                    scr[:, PBR + c * 512: PBR + (c + 1) * 512], k.PEb[:, 0:512]),
                    reads=[k.bPE], writes=[pb.res], deps=pb.wdeps())
            tg = [k.scr_t.carve(TG + kt * 1024, TG + (kt + 1) * 1024,
                                "tg%d_%d_%d" % (half, e, kt)) for kt in range(NT)]
            for kt in range(NT):
                p.op("dve", lambda eng, kt=kt, hb=hb: eng.tensor_tensor(
                    scr[:, TG + kt * 1024: TG + (kt + 1) * 1024],
                    scr[:, kt * S + hb: kt * S + hb + 1024],
                    scr[:, PBR: PBR + 1024], ALU.mult),
                    reads=[treg.res, pb.res], writes=[tg[kt].res], deps=tg[kt].wdeps())
            w1r = [k.scr_t.carve(W13R + i * 512, W13R + (i + 1) * 512, "l1fw1r%d" % i)
                   for i in range(2)]
            w3r = [k.scr_t.carve(W13R + 1024 + i * 512, W13R + 1024 + (i + 1) * 512,
                                 "l1fw3r%d" % i) for i in range(2)]
            fs6 = [k.f32_t.carve(i * 512, (i + 1) * 512, "l1fs%d" % i) for i in range(6)]
            sctr = [0, 0]
            hreg = k.scr_t.carve(HID, HID + 16384, "l1fh%d_%d" % (half, e))

            def l1f_streams(ht):
                for hc in range(2):
                    s1 = sctr[0] % 3
                    sctr[0] += 1
                    stream_cast(k, fs6[s1], s1 * 512,
                                w1d[e, ht, :, hc * 512:(hc + 1) * 512],
                                512, W13R + hc * 512, w1r[hc])
                    s3 = 3 + sctr[1] % 3
                    sctr[1] += 1
                    stream_cast(k, fs6[s3], s3 * 512,
                                w3d[e, ht, :, hc * 512:(hc + 1) * 512],
                                512, W13R + 1024 + hc * 512, w3r[hc])

            l1f_streams(0)
            for ht in range(16):
                par = ht % 2
                if par == 0:
                    ub = [(k.PA[:, 0:512], k.bPA0), (k.PA[:, 512:1024], k.bPA1)]
                    vb = [(k.PB[:, 0:512], k.bPB0), (k.PB[:, 512:1024], k.bPB1)]
                else:
                    ub = [(k.PC[0:128, 0:512], k.bPC), (k.PD[0:128, 0:512], k.bPD)]
                    vb = [(k.PEb[0:128, 0:512], k.bPE), (k.PF[0:128, 0:512], k.bPF)]
                for c in range(2):
                    gu = MMGroup(k, ub[c][0], ub[c][1], NT)
                    for kt in range(NT):
                        gu.part(scr[:, W13R + kt * 128: W13R + (kt + 1) * 128],
                                scr[:, kt * S + hb + c * 512: kt * S + hb + (c + 1) * 512],
                                [w1r[kt // 4].res, treg.res])
                    gv = MMGroup(k, vb[c][0], vb[c][1], NT)
                    for kt in range(NT):
                        gv.part(scr[:, W13R + 1024 + kt * 128: W13R + 1024 + (kt + 1) * 128],
                                scr[:, TG + kt * 1024 + c * 512: TG + kt * 1024 + (c + 1) * 512],
                                [w3r[kt // 4].res, tg[kt].res])
                if ht + 1 < 16:
                    l1f_streams(ht + 1)
                hcol = HID + ht * 1024
                for c in range(2):
                    p.op("act", lambda eng, hcol=hcol, c=c, ua=ub[c][0]: eng.activation(
                        scr[:, hcol + c * 512:hcol + (c + 1) * 512], ua, AF.Silu),
                        reads=[ub[c][1]], writes=[hreg.res], deps=hreg.wdeps())
                    p.op("dve", lambda eng, hcol=hcol, c=c, va=vb[c][0]: eng.tensor_tensor(
                        scr[:, hcol + c * 512:hcol + (c + 1) * 512],
                        scr[:, hcol + c * 512:hcol + (c + 1) * 512], va, ALU.mult),
                        reads=[vb[c][1], hreg.res], writes=[hreg.res])
            w2b = [k.scr_t.carve(W2BR + i * 1024, W2BR + (i + 1) * 1024, "l1fw2b%d" % i)
                   for i in range(2)]
            w2f = [k.f32_t.carve(i * 1024, (i + 1) * 1024, "lw2f%d" % i) for i in range(2)]
            for d2 in range(NT):
                for q in range(2):
                    stream_cast(k, w2f[q], q * 1024,
                                w2d[e, d2, :, q * 1024:(q + 1) * 1024],
                                1024, W2BR + q * 1024, w2b[q], crit=False)
                if d2 % 2 == 0:
                    yb = [(k.PA[:, 0:512], k.bPA0), (k.PA[:, 512:1024], k.bPA1)]
                else:
                    yb = [(k.PB[:, 0:512], k.bPB0), (k.PB[:, 512:1024], k.bPB1)]
                for c in range(2):
                    gy = MMGroup(k, yb[c][0], yb[c][1], 16)
                    for kt in range(16):
                        gy.part(scr[:, W2BR + kt * 128: W2BR + (kt + 1) * 128],
                                scr[:, HID + kt * 1024 + c * 512: HID + kt * 1024 + (c + 1) * 512],
                                [w2b[kt // 8].res, hreg.res])
                    p.op("dve", lambda eng, d2=d2, c=c, hb=hb, ps=yb[c][0]:
                         eng.tensor_tensor(
                        k.acc[:, d2 * S + hb + c * 512: d2 * S + hb + (c + 1) * 512],
                        k.acc[:, d2 * S + hb + c * 512: d2 * S + hb + (c + 1) * 512],
                        ps, ALU.add),
                        reads=[yb[c][1], k.acc_r[d2]],
                        writes=[k.acc_r[d2]])
    p.barrier()


BLOCK_EMITTERS["l1_ffn"] = emit_l1_ffn


L0A_XB = 0
L0A_OT = 16384
L0A_VP = 32768
L0A_QP = 34816
L0A_KP = 36864
L0A_QPT = 38912
L0A_KPT = 43008
L0A_KTOK = 47104
L0A_KVS = 51200
L0A_NUM = 51456
L0A_WST = 53504
L0A_ASB = 53504
L0A_PROJ = 54528
L0A_KSP = 54784
LN16 = float(np.log(16.0))


def emit_l0_attn(k):
    import os
    dbg = os.environ.get("L0A_DEBUG", "")
    pairs = [7] if dbg else range(8)
    heads = [0] if dbg == "1" else range(2)
    p, scr, rows = k.p, k.scr, k.rows
    qw = k.dram("l0a_qw", [8, 128, 1024])
    kw = k.dram("l0a_kw", [8, 128, 1024])
    vw = k.dram("l0a_vw", [8, 128, 1024])
    ow = k.dram("l0a_ow", [8, 128, 1024])
    projd = k.dram("l0a_projT2", [128, 256])
    vbd = k.dram("L0_attn_v_b", [D])
    qb_c0, qb_res = param_cols(k, "L0_attn_q_b")
    kb_c0, kb_res = param_cols(k, "L0_attn_k_b")
    ga_c0, ga_res = param_cols(k, "L0_attn_gamma")
    be_c0, be_res = param_cols(k, "L0_attn_beta")
    qb8_c0 = k.misc_used
    k.misc_used += 8
    kb8_c0 = k.misc_used
    k.misc_used += 8
    assert k.misc_used <= 128
    qb8_res, kb8_res = Res("qb8"), Res("kb8")
    p.op("dve", lambda eng: eng.tensor_scalar(
        k.misc[:, qb8_c0:qb8_c0 + 8], k.misc[:, qb_c0:qb_c0 + 8], 0.125, None, ALU.mult),
        reads=[qb_res], writes=[qb8_res])
    p.op("dve", lambda eng: eng.tensor_scalar(
        k.misc[:, kb8_c0:kb8_c0 + 8], k.misc[:, kb_c0:kb_c0 + 8], 0.125, None, ALU.mult),
        reads=[kb_res], writes=[kb8_res])
    vb = k.row_t.carve(2560, 3584, "l0avb")
    p.op("sp", lambda eng: eng.dma_start(
        out=rows[0:1, 2560:3584], in_=vbd.rearrange("(o e) -> o e", o=1)),
        writes=[vb.res], deps=vb.wdeps(), dma=True)
    prj = k.scr_t.carve(L0A_PROJ, L0A_PROJ + 256, "prj")
    pf32 = k.f32_t.carve(2760, 3016, "prjf")
    p.op("sp", lambda eng: eng.dma_start(out=k.f32s[:, 2760:3016], in_=projd[:]),
         writes=[pf32.res], deps=pf32.wdeps(), dma=True)
    p.op("dve", lambda eng: eng.tensor_copy(
        scr[:, L0A_PROJ:L0A_PROJ + 256], k.f32s[:, 2760:3016]),
        reads=[pf32.res], writes=[prj.res], deps=prj.wdeps())
    xb = k.scr_t.carve(L0A_XB, L0A_XB + 16384, "l0axb")
    for j in range(NT):
        p.op("dve", lambda eng, j=j: eng.tensor_copy(
            scr[:, L0A_XB + j * S: L0A_XB + (j + 1) * S], k.acc[:, j * S:(j + 1) * S]),
            reads=[k.acc_r[j]], writes=[xb.res], deps=xb.wdeps())

    ot = k.scr_t.carve(L0A_OT, L0A_OT + 16384, "l0aot")
    fsl = [k.f32_t.carve(0, 1024, "l0af0"), k.f32_t.carve(1024, 2048, "l0af1")]
    kvs = k.scr_t.carve(L0A_KVS, L0A_KVS + 256, "kvs")
    num = k.scr_t.carve(L0A_NUM, L0A_NUM + 2048, "num")

    for pr in pairs:
        qp_ = k.scr_t.carve(L0A_QP, L0A_QP + 2048, "qp%d" % pr)
        kp_ = k.scr_t.carve(L0A_KP, L0A_KP + 2048, "kp%d" % pr)
        vp_ = k.scr_t.carve(L0A_VP, L0A_VP + 2048, "vp%d" % pr)
        for qki, (wd_, reg, base, b8c0, b8res) in enumerate(
                [(qw, qp_, L0A_QP, qb8_c0, qb8_res), (kw, kp_, L0A_KP, kb8_c0, kb8_res)]):
            wst = k.scr_t.carve(L0A_WST, L0A_WST + 1024, "l0awq%d" % qki)
            stream_cast(k, fsl[qki], qki * 1024, wd_[pr], 1024, L0A_WST, wst)
            for c in range(NC):
                g = MMGroup(k, k.PA[:, 0:512], k.bPA0, NT)
                for kt in range(NT):
                    g.part(scr[:, L0A_WST + kt * 128: L0A_WST + (kt + 1) * 128],
                           scr[:, L0A_XB + kt * S + c * 512: L0A_XB + kt * S + (c + 1) * 512],
                           [wst.res, xb.res])
                p.op("dve", lambda eng, b=base, c=c, bc=(b8c0, pr): eng.tensor_scalar(
                    scr[:, b + c * 512: b + (c + 1) * 512], k.PA[:, 0:512], 0.125,
                    k.misc[:, bc[0] + bc[1]:bc[0] + bc[1] + 1], ALU.mult, ALU.add),
                    reads=[k.bPA0, b8res], writes=[reg.res], deps=reg.wdeps())
        wst = k.scr_t.carve(L0A_WST, L0A_WST + 1024, "l0awv")
        stream_cast(k, fsl[0], 0, vw[pr], 1024, L0A_WST, wst)
        for t in range(16):
            g = MMGroup(k, k.PA[0:128, 0:128], k.bPA0, NT + 1)
            for kt in range(NT):
                g.part(scr[:, L0A_XB + kt * S + t * 128: L0A_XB + kt * S + t * 128 + 128],
                       scr[:, L0A_WST + kt * 128: L0A_WST + (kt + 1) * 128],
                       [xb.res, wst.res])
            g.part(k.ones_row_f32[:], rows[0:1, 2560 + pr * 128: 2560 + (pr + 1) * 128],
                   [vb.res, k.ones_row_f32_r])
            p.op("act", lambda eng, t=t: eng.activation(
                scr[:, L0A_VP + t * 128: L0A_VP + (t + 1) * 128],
                k.PA[0:128, 0:128], AF.Copy),
                reads=[k.bPA0], writes=[vp_.res], deps=vp_.wdeps())
        for h01 in heads:
            hb = 64 * h01
            skip_scan = (dbg in ("3", "4", "5", "6", "7") and h01 == 1)
            qpt = k.scr_t.carve(L0A_QPT, L0A_QPT + 4096, "qpt")
            kpt = k.scr_t.carve(L0A_KPT, L0A_KPT + 4096, "kpt")
            ktok = k.scr_t.carve(L0A_KTOK, L0A_KTOK + 4096, "ktok")
            feat_list = [(L0A_QP, qp_), (L0A_KP, kp_)]
            if skip_scan and dbg in ("4", "6", "7"):
                feat_list = feat_list[:1]
            for qki, (base, reg) in enumerate(feat_list):
                crow = k.row_t.carve(0, 2048, "corr%d" % qki)
                skip_corr = skip_scan and dbg == "7"
                skip_map = skip_scan and dbg == "6"
                for c in range(NC if not skip_corr else 0):
                    sq = k.scr_t.carve(STAT_XC, STAT_XC + 512, "l0asq")
                    p.op("act", lambda eng, b=base, c=c, hb=hb: eng.activation(
                        scr[hb:hb + 64, STAT_XC:STAT_XC + 512],
                        scr[hb:hb + 64, b + c * 512: b + (c + 1) * 512], AF.Square),
                        reads=[reg.res], writes=[sq.res], deps=sq.wdeps())
                    mm(k, k.PC[0:1, 0:512], k.bPC,
                       [(k.ones_col_bf[hb:hb + 64, :], scr[hb:hb + 64, STAT_XC:STAT_XC + 512],
                         [sq.res, k.ones_col_bf_r])], tp=(hb, 0))
                    p.op("dve", lambda eng, c=c: eng.tensor_scalar(
                        rows[0:1, c * 512: (c + 1) * 512],
                        k.PC[0:1, 0:512], -0.5, -LN16, ALU.mult, ALU.add),
                        reads=[k.bPC], writes=[crow.res], deps=crow.wdeps())
                dst = L0A_QPT if qki == 0 else L0A_KPT
                dreg = qpt if qki == 0 else kpt
                for mt in range(2 if not skip_map else 0):
                    for c in range(NC):
                        ba = k.PB[:, 0:512]
                        bb = k.bPB0
                        g = MMGroup(k, ba, bb, 2)
                        g.part(scr[hb:hb + 64, L0A_PROJ + mt * 128: L0A_PROJ + (mt + 1) * 128],
                               scr[hb:hb + 64, base + c * 512: base + (c + 1) * 512],
                               [prj.res, reg.res], tp=(hb, 0))
                        g.part(k.ones_row_f32[:],
                               rows[0:1, c * 512: (c + 1) * 512],
                               [crow.res, k.ones_row_f32_r])
                        p.op("act", lambda eng, d=dst, mt=mt, c=c, ba=ba: eng.activation(
                            scr[:, d + mt * S + c * 512: d + mt * S + (c + 1) * 512],
                            ba, AF.Exp),
                            reads=[bb], writes=[dreg.res], deps=dreg.wdeps())
                if qki == 1 and dbg == "5" and skip_scan:
                    continue
                if qki == 1:
                    for t in range(16):
                        ba = k.PB[:, 512:768]
                        bb = k.bPB1
                        g = MMGroup(k, ba, bb, 3)
                        g.part(scr[hb:hb + 64, L0A_KP + t * 128: L0A_KP + (t + 1) * 128],
                               scr[hb:hb + 64, L0A_PROJ:L0A_PROJ + 256],
                               [kp_.res, prj.res], tp=(hb, 0))
                        g.part(rows[0:1, t * 128: (t + 1) * 128],
                               k.ones_row_f32[:], [crow.res, k.ones_row_f32_r],
                               out_ap=ba[:, 0:128])
                        g.part(rows[0:1, t * 128: (t + 1) * 128],
                               k.ones_row_f32[:], [crow.res, k.ones_row_f32_r],
                               out_ap=ba[:, 128:256])
                        p.op("act", lambda eng, t=t, ba=ba: eng.activation(
                            scr[:, L0A_KTOK + t * 256: L0A_KTOK + (t + 1) * 256],
                            ba, AF.Exp),
                            reads=[bb], writes=[ktok.res], deps=ktok.wdeps())
                    gks = MMGroup(k, k.PC[0:128, 0:16], k.bPC, 32)
                    for ch8 in range(8):
                        for mt in range(2):
                            for tt in range(2):
                                gks.part(
                                    scr[:, L0A_KTOK + (2 * ch8 + tt) * 256 + mt * 128:
                                        L0A_KTOK + (2 * ch8 + tt) * 256 + (mt + 1) * 128],
                                    k.ones_col_bf[:], [ktok.res, k.ones_col_bf_r],
                                    out_ap=k.PC[0:128, mt * 8 + ch8: mt * 8 + ch8 + 1])
                    ksf = k.f32_t.carve(2560, 2608, "ksf")
                    SS, PP, QQ = 2560, 2576, 2592
                    p.op("dve", lambda eng: eng.tensor_copy(
                        k.f32s[:, SS:SS + 16], k.PC[0:128, 0:16]),
                        reads=[k.bPC], writes=[ksf.res], deps=ksf.wdeps())
                    p.op("dve", lambda eng: eng.tensor_copy(
                        k.f32s[:, PP:PP + 16], k.f32s[:, SS:SS + 16]),
                        reads=[ksf.res], writes=[ksf.res])
                    cur, oth = PP, QQ
                    for sh in (1, 2, 4):
                        for mt in range(2):
                            a, b = cur + mt * 8, oth + mt * 8
                            p.op("dve", lambda eng, a=a, b=b, sh=sh: eng.tensor_copy(
                                k.f32s[:, b:b + sh], k.f32s[:, a:a + sh]),
                                reads=[ksf.res], writes=[ksf.res])
                            p.op("dve", lambda eng, a=a, b=b, sh=sh: eng.tensor_tensor(
                                k.f32s[:, b + sh:b + 8], k.f32s[:, a + sh:a + 8],
                                k.f32s[:, a:a + 8 - sh], ALU.add),
                                reads=[ksf.res], writes=[ksf.res])
                        cur, oth = oth, cur
                    ksp = k.scr_t.carve(L0A_KSP, L0A_KSP + 16, "ksp")
                    p.op("dve", lambda eng, a=cur: eng.tensor_tensor(
                        k.f32s[:, a:a + 16], k.f32s[:, a:a + 16],
                        k.f32s[:, SS:SS + 16], ALU.subtract),
                        reads=[ksf.res], writes=[ksf.res])
                    p.op("dve", lambda eng, a=cur: eng.tensor_copy(
                        scr[:, L0A_KSP:L0A_KSP + 16], k.f32s[:, a:a + 16]),
                        reads=[ksf.res], writes=[ksp.res], deps=ksp.wdeps())
            dreg_den = k.row_t.carve(0, 2048, "l0aden")
            asbs = [k.scr_t.carve(L0A_ASB, L0A_ASB + 512, "asb0"),
                    k.scr_t.carve(L0A_ASB + 512, L0A_ASB + 1024, "asb1")]

            def emit_at2(ch):
                bank_ap = k.PA[:, 0:512] if ch % 2 == 0 else k.PA[:, 512:1024]
                bank = k.bPA0 if ch % 2 == 0 else k.bPA1
                for ks in range(2):
                    g = MMGroup(k, bank_ap[:, ks * 256:(ks + 1) * 256], bank, 2)
                    for mt in range(2):
                        g.part(
                            scr[:, L0A_KPT + mt * S + ch * 256 + ks * 128:
                                L0A_KPT + mt * S + ch * 256 + (ks + 1) * 128],
                            scr[:, L0A_QPT + mt * S + ch * 256: L0A_QPT + mt * S + (ch + 1) * 256],
                            [kpt.res, qpt.res])
                ao = L0A_ASB + (ch % 2) * 512
                for ks in range(2):
                    cb = 384 - 128 * ks
                    p.op("dve", lambda eng, b=bank_ap, ao=ao, ks=ks, cb=cb: eng.tensor_tensor(
                        scr[:, ao + ks * 256: ao + (ks + 1) * 256],
                        b[:, ks * 256:(ks + 1) * 256],
                        k.band[:, cb:cb + 256], ALU.mult),
                        reads=[bank, k.band_r], writes=[asbs[ch % 2].res],
                        deps=asbs[ch % 2].wdeps())

            emit_at2(0)
            for ch in range(8):
                if ch + 1 < 8:
                    emit_at2(ch + 1)
                ao = L0A_ASB + (ch % 2) * 512
                ob_ap = (k.PB[:, 0:512] if ch % 2 == 0 else k.PB[:, 512:1024])[hb:hb + 64, 0:256]
                ob_bank = k.bPB0 if ch % 2 == 0 else k.bPB1
                nparts = 2 if ch == 0 else 4
                g = MMGroup(k, ob_ap, ob_bank, nparts)
                for ks in range(2):
                    g.part(scr[:, L0A_VP + (2 * ch + ks) * 128 + hb:
                               L0A_VP + (2 * ch + ks) * 128 + hb + 64],
                           scr[:, ao + ks * 256: ao + (ks + 1) * 256],
                           [vp_.res, asbs[ch % 2].res], tp=(0, hb))
                if ch > 0:
                    for mt in range(2):
                        g.part(scr[:, L0A_KVS + mt * 128: L0A_KVS + mt * 128 + 64],
                               scr[:, L0A_QPT + mt * S + ch * 256: L0A_QPT + mt * S + (ch + 1) * 256],
                               [kvs.res, qpt.res], tp=(0, hb))
                dn_ap = (k.PC if ch % 2 == 0 else k.PF)[0:1, 0:256]
                dn_bank = k.bPC if ch % 2 == 0 else k.bPF
                gd = MMGroup(k, dn_ap, dn_bank, nparts)
                for ks in range(2):
                    gd.part(k.ones_col_bf[:], scr[:, ao + ks * 256: ao + (ks + 1) * 256],
                            [asbs[ch % 2].res, k.ones_col_bf_r])
                if ch > 0:
                    for mt in range(2):
                        gd.part(scr[:, L0A_KSP + mt * 8 + ch: L0A_KSP + mt * 8 + ch + 1],
                                scr[:, L0A_QPT + mt * S + ch * 256: L0A_QPT + mt * S + (ch + 1) * 256],
                                [ksp.res, qpt.res])
                p.op("act", lambda eng, ch=ch, hb=hb, b=ob_ap: eng.activation(
                    scr[hb:hb + 64, L0A_NUM + ch * 256: L0A_NUM + (ch + 1) * 256],
                    b, AF.Copy),
                    reads=[ob_bank], writes=[num.res], deps=num.wdeps())
                p.op("dve", lambda eng, ch=ch, b=dn_ap: eng.tensor_copy(
                    rows[0:1, ch * 256:(ch + 1) * 256], b),
                    reads=[dn_bank], writes=[dreg_den.res], deps=dreg_den.wdeps())
                for mt in range(2):
                    kvp = k.PD if mt == 0 else k.PEb
                    kvb = k.bPD if mt == 0 else k.bPE
                    for tt in range(2):
                        p.op("pe", lambda eng, mt=mt, ch=ch, tt=tt, hb=hb, kvp=kvp:
                             eng.matmul(
                            kvp[0:128, 0:64],
                            scr[:, L0A_KTOK + (2 * ch + tt) * 256 + mt * 128:
                                L0A_KTOK + (2 * ch + tt) * 256 + (mt + 1) * 128],
                            scr[:, L0A_VP + (2 * ch + tt) * 128 + hb:
                                L0A_VP + (2 * ch + tt) * 128 + hb + 64],
                            start=(ch == 0 and tt == 0), stop=True,
                            skip_group_check=True),
                            reads=[ktok.res, vp_.res],
                            deps=[kvb.w] + kvb.r)
                        kvb.w = p.items[-1].idx
                        kvb.r = []
                if ch < 7:
                    for mt in range(2):
                        kvp = k.PD if mt == 0 else k.PEb
                        kvb = k.bPD if mt == 0 else k.bPE
                        p.op("dve", lambda eng, mt=mt, kvp=kvp: eng.tensor_copy(
                            scr[:, L0A_KVS + mt * 128: L0A_KVS + mt * 128 + 64],
                            kvp[0:128, 0:64]),
                            reads=[kvb], writes=[kvs.res], deps=kvs.wdeps())
            p.op("act", lambda eng: eng.activation(
                rows[0:1, 0:2048], rows[0:1, 0:2048], AF.Ln),
                reads=[dreg_den.res], writes=[dreg_den.res])
            p.op("act", lambda eng: eng.activation(
                rows[0:1, 0:2048], rows[0:1, 0:2048], AF.Exp, scale=-1.0),
                reads=[dreg_den.res], writes=[dreg_den.res])
            for c in range(NC):
                mm(k, k.PF[hb:hb + 64, 0:512], k.bPF,
                   [(k.ones_row_f32[0:1, 0:64], rows[0:1, c * 512:(c + 1) * 512],
                     [dreg_den.res, k.ones_row_f32_r])], tp=(0, hb))
                o_col = L0A_OT + pr * 2048 + c * 512
                p.op("dve", lambda eng, c=c, o=o_col, hb=hb: eng.tensor_tensor(
                    scr[hb:hb + 64, o:o + 512],
                    scr[hb:hb + 64, L0A_NUM + c * 512: L0A_NUM + (c + 1) * 512],
                    k.PF[hb:hb + 64, 0:512], ALU.mult),
                    reads=[num.res, k.bPF], writes=[ot.res], deps=ot.wdeps())

    if dbg:
        return
    p.barrier()
    ob_c0, ob_res = param_cols(k, "L0_attn_out_b")
    for m in range(NT):
        wst = k.scr_t.carve(L0A_WST, L0A_WST + 1024, "l0awo%d" % m)
        stream_cast(k, fsl[m % 2], (m % 2) * 1024, ow[m], 1024, L0A_WST, wst)
        for c in range(NC):
            par = c % 2
            bank_ap = k.PA[:, par * 512:(par + 1) * 512]
            bank = k.bPA0 if par == 0 else k.bPA1
            g = MMGroup(k, bank_ap, bank, NT)
            for kt in range(NT):
                g.part(scr[:, L0A_WST + kt * 128: L0A_WST + (kt + 1) * 128],
                       scr[:, L0A_OT + kt * S + c * 512: L0A_OT + kt * S + (c + 1) * 512],
                       [wst.res, ot.res])
            a_sl = (slice(None), slice(m * S + c * 512, m * S + (c + 1) * 512))
            p.op("dve", lambda eng, a=a_sl, m=m, b=bank_ap: eng.scalar_tensor_tensor(
                k.acc[a], b, k.misc[:, ob_c0 + m:ob_c0 + m + 1], k.acc[a],
                ALU.add, ALU.add),
                reads=[bank, ob_res, k.acc_r[m]], writes=[k.acc_r[m]])
    mu_res, rstd_res = stats(k, 1e-6)
    ln_apply(k, mu_res, rstd_res, affine=(ga_c0, ga_res, be_c0, be_res))
    for j in range(NT):
        for c in range(NC):
            a_sl = (slice(None), slice(j * S + c * 512, j * S + (c + 1) * 512))
            p.op("dve", lambda eng, a=a_sl, j=j, c=c: eng.tensor_tensor(
                k.acc[a], k.acc[a],
                scr[:, L0A_XB + j * S + c * 512: L0A_XB + j * S + (c + 1) * 512], ALU.add),
                reads=[k.acc_r[j], xb.res], writes=[k.acc_r[j]])
    p.barrier()


BLOCK_EMITTERS["l0_attn"] = emit_l0_attn


def prep_l0_attn(inputs):
    def tile8(w):
        w = np.asarray(w, np.float32)
        return np.ascontiguousarray(
            w.reshape(8, 128, 8, 128).transpose(2, 1, 0, 3).reshape(8, 128, 1024))
    proj = np.asarray(inputs["L0_attn_proj"], np.float32)
    pt = np.ascontiguousarray(proj.T)
    return {
        "l0a_qw": tile8(inputs["L0_attn_q_w"]),
        "l0a_kw": tile8(inputs["L0_attn_k_w"]),
        "l0a_vw": tile8(inputs["L0_attn_v_w"]),
        "l0a_ow": tile8(inputs["L0_attn_out_w"]),
        "l0a_projT2": np.ascontiguousarray(np.vstack([pt, pt])),
        "L0_attn_q_b": np.asarray(inputs["L0_attn_q_b"], np.float32),
        "L0_attn_k_b": np.asarray(inputs["L0_attn_k_b"], np.float32),
        "L0_attn_v_b": np.asarray(inputs["L0_attn_v_b"], np.float32),
        "L0_attn_out_b": np.asarray(inputs["L0_attn_out_b"], np.float32),
        "L0_attn_gamma": np.asarray(inputs["L0_attn_gamma"], np.float32),
        "L0_attn_beta": np.asarray(inputs["L0_attn_beta"], np.float32),
    }


PREPPERS["l0_attn"] = prep_l0_attn



def prep_consts():
    kk = np.arange(128)[:, None]
    cc = np.arange(BAND_W)[None, :]
    oh = np.zeros((8, 1024), dtype=BF)
    for e in range(8):
        oh[e, e * 128:(e + 1) * 128] = 1
    return {
        "c_id_f32": np.eye(128, dtype=np.float32),
        "c_ones_col_bf": np.ones((128, 1), dtype=BF),
        "c_ones_row_f32": np.ones((1, 128), dtype=np.float32),
        "c_band": (kk <= cc - 384).astype(BF),
        "c_onehot8": oh,
    }


def prep_l1_attn(inputs):
    w_qkv = np.asarray(inputs["L1_attn_w_qkv"], dtype=np.float32)
    wt = np.ascontiguousarray(w_qkv.T)
    qkw = np.empty((8, 2, 128, 1024), np.float32)
    for pair in range(8):
        for qki in range(2):
            base = qki * 1024 + pair * 128
            blk = wt[:, base:base + 128]
            qkw[pair, qki] = blk.reshape(8, 128, 128).transpose(1, 0, 2).reshape(128, 1024)
    vw = np.empty((2, 8, 128, 512), np.float32)
    for g in range(2):
        for kt in range(8):
            vw[g, kt] = wt[kt * 128:(kt + 1) * 128, 2048 + g * 512: 2048 + (g + 1) * 512]
    w_out = np.asarray(inputs["L1_attn_w_out"], dtype=np.float32)
    wo = np.ascontiguousarray(w_out.T.reshape(8, 128, 1024))
    return {
        "l1a_qkw": qkw, "l1a_vw": vw, "l1a_wo": wo,
        "L1_attn_ln_gamma": np.asarray(inputs["L1_attn_ln_gamma"], np.float32),
        "L1_attn_ln_beta": np.asarray(inputs["L1_attn_ln_beta"], np.float32),
        "L1_attn_b_out": np.asarray(inputs["L1_attn_b_out"], np.float32),
    }


PREPPERS["l1_attn"] = prep_l1_attn


def run_blocks(blocks, x, inputs, cores=8):
    nc, k = build_program(blocks)
    needed = set(k.inp.keys())
    base = dict(prep_consts())
    cvals = np.zeros((128, 32), np.float32)
    for val, c in k.cval_map.items():
        cvals[:, c] = val
    base["c_cvals"] = cvals
    for b in blocks:
        base.update(PREPPERS[b](inputs))
    in_maps = []
    for c in range(cores):
        m = {"xio": np.ascontiguousarray(np.asarray(x[c], np.float32).T)}
        for name, arr in base.items():
            if name in needed:
                m[name] = arr
        in_maps.append(m)
    res = bass_utils.run_bass_kernel_spmd(nc, in_maps, core_ids=list(range(cores)))
    return np.stack([np.ascontiguousarray(r["yio"].T) for r in res.results])


ALL_BLOCKS = ["l0_attn", "l0_ffn", "l1_attn", "l1_ffn"]
_CACHE = {}


def _get_program(blocks):
    key = tuple(blocks)
    if key not in _CACHE:
        nc, k = build_program(list(blocks))
        cvals = np.zeros((128, 32), np.float32)
        for val, c in k.cval_map.items():
            cvals[:, c] = val
        _CACHE[key] = (nc, k, cvals)
    return _CACHE[key]


def kernel(**inputs):
    x = np.asarray(inputs["L0_attn_x"], np.float32)
    nc, k, cvals = _get_program(tuple(ALL_BLOCKS))
    base = prep_consts()
    base["c_cvals"] = cvals
    for b in ALL_BLOCKS:
        base.update(PREPPERS[b](inputs))
    needed = set(k.inp.keys())
    in_maps = []
    for c in range(8):
        m = {"xio": np.ascontiguousarray(x[c].T)}
        for name, arr in base.items():
            if name in needed:
                m[name] = arr
        in_maps.append(m)
    res = bass_utils.run_bass_kernel_spmd(nc, in_maps, core_ids=list(range(8)))
    return np.stack([np.ascontiguousarray(np.asarray(r["yio"], np.float32).T)
                     for r in res.results])


# version 15
# speedup vs baseline: 1.3447x; 1.0125x over previous
import contextlib
import numpy as np
import ml_dtypes
import concourse.bass as bass
import concourse.mybir as mybir
from concourse import bass_utils

F32 = mybir.dt.float32
BF16 = mybir.dt.bfloat16
AF = mybir.ActivationFunctionType
ALU = mybir.AluOpType
AX = mybir.AxisListType
BF = ml_dtypes.bfloat16

D = 1024
S = 2048
NT = D // 128
NC = 4

ENGS = ("pe", "dve", "act", "sp", "gp")


class Res:
    __slots__ = ("name", "w", "r")

    def __init__(self, name=""):
        self.name = name
        self.w = None
        self.r = []


class Reg:
    __slots__ = ("res", "deps")

    def __init__(self, res, deps):
        self.res = res
        self.deps = list(deps)

    def wdeps(self):
        d, self.deps = self.deps, []
        return d


NDMA = 8


class Op:
    __slots__ = ("eng", "fn", "deps", "dma", "idx", "sem_val", "needed", "slot")


class Prog:
    def __init__(self, nc):
        self.nc = nc
        self.items = []

    def op(self, eng, fn, reads=(), writes=(), deps=(), dma=False):
        idx = len(self.items)
        dd = set(d for d in deps if d is not None)
        for r in reads:
            if r.w is not None:
                dd.add(r.w)
        for w in writes:
            if w.w is not None:
                dd.add(w.w)
            dd.update(w.r)
        o = Op()
        o.eng, o.fn, o.deps, o.dma, o.idx = eng, fn, sorted(dd), dma, idx
        o.sem_val, o.needed = None, False
        self.items.append(o)
        for r in reads:
            r.r.append(idx)
        for w in writes:
            w.w = idx
            w.r = []
        return idx

    def barrier(self):
        last = {}
        for it in self.items:
            last[it.eng] = it.idx
        toks = list(last.values())
        for e in ENGS:
            if e in ("dve", "act", "gp"):
                def fn(eng):
                    eng.drain()
                    return eng.nop()
            else:
                def fn(eng):
                    return eng.nop()
            self.op(e, fn, deps=toks)

    def emit(self, block, sems):
        items = self.items
        for it in items:
            for d in it.deps:
                p = items[d]
                if p.eng != it.eng or p.dma:
                    p.needed = True
        cnt = {e: 0 for e in ENGS}
        dcnt = {e: 0 for e in ENGS}
        for it in items:
            if it.dma:
                i = dcnt[it.eng]
                dcnt[it.eng] += 1
                it.slot = i % NDMA
                it.sem_val = 16 * (i // NDMA + 1)
            elif it.needed:
                cnt[it.eng] += 1
                it.sem_val = cnt[it.eng]

        def run_engine(ename, eng):
            waited = {}
            drained_upto = -1
            own_done = -1
            for it in items:
                if it.eng != ename:
                    continue
                need_drain = False
                for d in it.deps:
                    p = items[d]
                    if p.eng == ename and not p.dma:
                        if ename != "pe" and d > drained_upto:
                            need_drain = True
                    else:
                        if p.dma:
                            key = (p.eng, "d", p.slot)
                        else:
                            key = (p.eng, "c")
                        v = p.sem_val
                        assert v is not None, (ename, it.idx, d, p.eng)
                        if waited.get(key, 0) < v:
                            eng.wait_ge(sems[key], v)
                            waited[key] = v
                if it.dma and it.sem_val > 16:
                    key = (ename, "d", it.slot)
                    v = it.sem_val - 16
                    if waited.get(key, 0) < v:
                        eng.wait_ge(sems[key], v)
                        waited[key] = v
                if need_drain:
                    eng.drain()
                    drained_upto = own_done
                inst = it.fn(eng)
                own_done = it.idx
                if it.sem_val is not None:
                    if inst is None:
                        inst = eng.nop()
                    key = (ename, "d", it.slot) if it.dma else (ename, "c")
                    inst.then_inc(sems[key], 16 if it.dma else 1)

        nc = self.nc

        @block.tensor
        def _(_e):
            run_engine("pe", nc.tensor)

        @block.vector
        def _(_e):
            run_engine("dve", nc.vector)

        @block.scalar
        def _(_e):
            run_engine("act", nc.scalar)

        @block.sync
        def _(_e):
            run_engine("sp", nc.sync)

        @block.gpsimd
        def _(_e):
            run_engine("gp", nc.gpsimd)


class RegTracker:
    def __init__(self, name):
        self.name = name
        self.regs = []

    def carve(self, start, end, tag=""):
        deps = []
        keep = []
        for (s, e, r) in self.regs:
            if s < end and start < e:
                if r.w is not None:
                    deps.append(r.w)
                deps.extend(r.r)
                if s < start:
                    keep.append((s, start, r))
                if end < e:
                    keep.append((end, e, r))
            else:
                keep.append((s, e, r))
        res = Res(self.name + ":" + tag)
        keep.append((start, end, res))
        self.regs = keep
        return Reg(res, deps)


class MMGroup:
    def __init__(self, k, out_ap, bank, n):
        self.k = k
        self.out_ap = out_ap
        self.bank = bank
        self.n = n
        self.i = 0
        self.prev = None

    def part(self, lhsT, rhs, reads, tp=None, extra_deps=(), out_ap=None):
        i, n = self.i, self.n
        assert i < n
        deps = list(extra_deps)
        if i == 0:
            if self.bank.w is not None:
                deps.append(self.bank.w)
            deps += self.bank.r
        else:
            deps.append(self.prev)
        fn = (lambda eng, o=(out_ap if out_ap is not None else self.out_ap),
              l=lhsT, r=rhs, st=(i == 0), sp=(i == n - 1),
              tp=tp: eng.matmul(o, l, r, start=st, stop=sp, tile_position=tp))
        self.prev = self.k.p.op("pe", fn, reads=list(reads), deps=deps)
        self.i += 1
        if self.i == self.n:
            self.bank.w = self.prev
            self.bank.r = []
        return self.prev


def mm(k, out_ap, bank, parts, tp=None):
    g = MMGroup(k, out_ap, bank, len(parts))
    for (l, r, rd) in parts:
        g.part(l, r, rd, tp=tp)
    return g.prev



SCR_COLS = 55296
F32S_COLS = 3072
LNT = 2048
BAND_W = 1280


class K:
    def __init__(self, nc, es):
        self.nc = nc
        self.p = Prog(nc)
        self.es = es
        self.inp = {}
        sb = lambda n, sh, dt: es.enter_context(nc.sbuf_tensor(n, sh, dt))
        pst = lambda n, sh: es.enter_context(nc.psum_tensor(n, sh, F32))
        self.acc = sb("acc", [128, NT * S], F32)
        self.acc_r = [Res("acc%d" % j) for j in range(NT)]
        self.rows = sb("rows", [1, 4096], F32)
        self.scr = sb("scr", [128, SCR_COLS], BF16)
        self.f32s = sb("f32s", [128, F32S_COLS], F32)
        self.id_f32 = sb("id_f32", [128, 128], F32)
        self.ones_col_bf = sb("ones_col_bf", [128, 1], BF16)
        self.ones_row_f32 = sb("ones_row_f32", [1, 128], F32)
        self.band = sb("band", [128, BAND_W], BF16)
        self.onehot8 = sb("onehot8", [8, 1024], BF16)
        self.misc = sb("misc", [128, 128], F32)
        self.PA = pst("PA", [128, 1024])
        self.PB = pst("PB", [128, 1024])
        self.PC = pst("PC", [128, 512])
        self.PD = pst("PD", [128, 512])
        self.PEb = pst("PEb", [128, 512])
        self.PF = pst("PF", [128, 512])
        self.bPA0, self.bPA1 = Res("PA0"), Res("PA1")
        self.bPB0, self.bPB1 = Res("PB0"), Res("PB1")
        self.bPC, self.bPD = Res("PC"), Res("PD")
        self.bPE, self.bPF = Res("PEb"), Res("PF")
        for n in ("id_f32", "ones_col_bf", "ones_row_f32", "band", "misc", "onehot8"):
            setattr(self, n + "_r", Res(n))
        self.scr_t = RegTracker("scr")
        self.f32_t = RegTracker("f32s")
        self.row_t = RegTracker("rows")
        self.misc_used = 0
        self.cvals = sb("cvals", [128, 32], F32)
        self.cvals_r = Res("cvals")
        self.cval_map = {}
        self.cast_rr = 0
        self.cload("c_cvals", [128, 32], self.cvals[:], self.cvals_r)
        self.cload("c_id_f32", [128, 128], self.id_f32[:], self.id_f32_r)
        self.cload("c_ones_col_bf", [128, 1], self.ones_col_bf[:], self.ones_col_bf_r, dt=BF16)
        self.cload("c_ones_row_f32", [1, 128], self.ones_row_f32[:], self.ones_row_f32_r)
        self.cload("c_band", [128, BAND_W], self.band[:], self.band_r, dt=BF16)
        self.cload("c_onehot8", [8, 1024], self.onehot8[:], self.onehot8_r, dt=BF16)

    def dram(self, name, shape, dt=F32, kind="ExternalInput"):
        if name not in self.inp:
            self.inp[name] = self.nc.dram_tensor(name, list(shape), dt, kind=kind).ap()
        return self.inp[name]

    def cv(self, val, p=128):
        val = float(val)
        if val not in self.cval_map:
            self.cval_map[val] = len(self.cval_map)
            assert len(self.cval_map) <= 32
        c = self.cval_map[val]
        return self.cvals[0:p, c:c + 1]

    def cload(self, name, shape, dst_ap, dst_res, dt=F32):
        ap = self.dram(name, shape, dt)
        return self.p.op("sp", lambda eng, o=dst_ap, i=ap: eng.dma_start(out=o, in_=i),
                         writes=[dst_res], dma=True)


def param_cols(k, name, n=D):
    nt = n // 128
    ap = k.dram(name, [n], F32)
    st = k.f32_t.carve(2824, 2952, name)
    k.p.op("sp", lambda eng, i=ap, nt=nt: eng.dma_start(
        out=k.f32s[0:nt, 2824:2824 + 128], in_=i.rearrange("(t p) -> t p", p=128)),
        writes=[st.res], deps=st.wdeps(), dma=True)
    k.p.op("pe", lambda eng, nt=nt: eng.transpose(
        k.PF[0:128, 0:nt], k.f32s[0:nt, 2824:2824 + 128], k.id_f32[0:nt, 0:nt]),
        reads=[st.res, k.id_f32_r], writes=[k.bPF])
    c0 = k.misc_used
    k.misc_used += nt
    assert k.misc_used <= 128
    col_res = Res(name + "_col")
    k.p.op("dve", lambda eng, c0=c0, nt=nt: eng.tensor_copy(
        k.misc[:, c0:c0 + nt], k.PF[0:128, 0:nt]),
        reads=[k.bPF], writes=[col_res])
    return c0, col_res


STAT_XC = 52224
STAT_SQ = 52736


def stats(k, eps):
    p, scr, rows = k.p, k.scr, k.rows
    mu = k.row_t.carve(0, 2048, "mu")
    rstd = k.row_t.carve(2048, 4096, "rstd")
    xc = k.scr_t.carve(STAT_XC, STAT_XC + 512, "statxc")
    sq = k.scr_t.carve(STAT_SQ, STAT_SQ + 512, "statsq")
    for c in range(NC):
        g1 = MMGroup(k, k.PC[0:1, 0:512], k.bPC, NT)
        g2 = MMGroup(k, k.PD[0:1, 0:512], k.bPD, NT)
        for j in range(NT):
            p.op("dve", lambda eng, j=j, c=c: eng.tensor_copy(
                scr[:, STAT_XC:STAT_XC + 512],
                k.acc[:, j * S + c * 512: j * S + (c + 1) * 512]),
                reads=[k.acc_r[j]], writes=[xc.res], deps=xc.wdeps())
            p.op("act", lambda eng: eng.activation(
                scr[:, STAT_SQ:STAT_SQ + 512], scr[:, STAT_XC:STAT_XC + 512], AF.Square),
                reads=[xc.res], writes=[sq.res], deps=sq.wdeps())
            g1.part(k.ones_col_bf[:], scr[:, STAT_XC:STAT_XC + 512], [xc.res, k.ones_col_bf_r])
            g2.part(k.ones_col_bf[:], scr[:, STAT_SQ:STAT_SQ + 512], [sq.res, k.ones_col_bf_r])
        p.op("dve", lambda eng, c=c: eng.tensor_scalar(
            rows[0:1, c * 512:(c + 1) * 512], k.PC[0:1, 0:512], 1.0 / D, None, ALU.mult),
            reads=[k.bPC], writes=[mu.res], deps=mu.wdeps())
        p.op("dve", lambda eng, c=c: eng.tensor_scalar(
            rows[0:1, 2048 + c * 512:2048 + (c + 1) * 512], k.PD[0:1, 0:512], 1.0 / D,
            None, ALU.mult),
            reads=[k.bPD], writes=[rstd.res], deps=rstd.wdeps())
    for c in range(NC):
        p.op("dve", lambda eng, c=c: eng.tensor_tensor(
            k.PC[0:1, 0:512], rows[0:1, c * 512:(c + 1) * 512],
            rows[0:1, c * 512:(c + 1) * 512], ALU.mult),
            reads=[mu.res], writes=[k.bPC])
        p.op("dve", lambda eng, c=c: eng.tensor_tensor(
            rows[0:1, 2048 + c * 512:2048 + (c + 1) * 512],
            rows[0:1, 2048 + c * 512:2048 + (c + 1) * 512], k.PC[0:1, 0:512], ALU.subtract),
            reads=[k.bPC, rstd.res], writes=[rstd.res])
    p.op("act", lambda eng, b=k.cv(eps, 1): eng.activation(
        rows[0:1, 2048:4096], rows[0:1, 2048:4096], AF.Ln, bias=b),
        reads=[rstd.res, k.cvals_r], writes=[rstd.res])
    p.op("act", lambda eng: eng.activation(
        rows[0:1, 2048:4096], rows[0:1, 2048:4096], AF.Exp, scale=-0.5),
        reads=[rstd.res], writes=[rstd.res])
    return mu.res, rstd.res


def ln_apply(k, mu_res, rstd_res, affine=None, out_reg=None, out_region=None):
    p, rows, scr = k.p, k.rows, k.scr
    lnt = k.f32_t.carve(LNT, LNT + 512, "lnt")
    for c in range(NC):
        mm(k, k.PEb[:, 0:512], k.bPE,
           [(k.ones_row_f32[:], rows[0:1, c * 512:(c + 1) * 512], [mu_res, k.ones_row_f32_r])])
        mm(k, k.PF[:, 0:512], k.bPF,
           [(k.ones_row_f32[:], rows[0:1, 2048 + c * 512:2048 + (c + 1) * 512],
             [rstd_res, k.ones_row_f32_r])])
        for j in range(NT):
            a_sl = (slice(None), slice(j * S + c * 512, j * S + (c + 1) * 512))
            p.op("dve", lambda eng, a=a_sl: eng.tensor_tensor(
                k.f32s[:, LNT:LNT + 512], k.acc[a], k.PEb[:, 0:512], ALU.subtract),
                reads=[k.acc_r[j], k.bPE], writes=[lnt.res], deps=lnt.wdeps())
            if affine is None:
                p.op("dve", lambda eng, a=a_sl: eng.tensor_tensor(
                    k.acc[a], k.f32s[:, LNT:LNT + 512], k.PF[:, 0:512], ALU.mult),
                    reads=[lnt.res, k.bPF], writes=[k.acc_r[j]])
            elif out_region is None:
                gc0, g_res, bc0, b_res = affine
                p.op("dve", lambda eng: eng.tensor_tensor(
                    k.f32s[:, LNT:LNT + 512], k.f32s[:, LNT:LNT + 512], k.PF[:, 0:512],
                    ALU.mult),
                    reads=[lnt.res, k.bPF], writes=[lnt.res])
                p.op("dve", lambda eng, a=a_sl, j=j, gc0=gc0, bc0=bc0: eng.tensor_scalar(
                    k.acc[a], k.f32s[:, LNT:LNT + 512], k.misc[:, gc0 + j:gc0 + j + 1],
                    k.misc[:, bc0 + j:bc0 + j + 1], ALU.mult, ALU.add),
                    reads=[lnt.res, g_res, b_res], writes=[k.acc_r[j]])
            else:
                gc0, g_res, bc0, b_res = affine
                p.op("dve", lambda eng: eng.tensor_tensor(
                    k.f32s[:, LNT:LNT + 512], k.f32s[:, LNT:LNT + 512], k.PF[:, 0:512],
                    ALU.mult),
                    reads=[lnt.res, k.bPF], writes=[lnt.res])
                o_sl = (slice(None),
                        slice(out_region + j * S + c * 512, out_region + j * S + (c + 1) * 512))
                p.op("dve", lambda eng, o=o_sl, j=j, gc0=gc0, bc0=bc0: eng.tensor_scalar(
                    scr[o], k.f32s[:, LNT:LNT + 512], k.misc[:, gc0 + j:gc0 + j + 1],
                    k.misc[:, bc0 + j:bc0 + j + 1], ALU.mult, ALU.add),
                    reads=[lnt.res, g_res, b_res], writes=[out_reg.res], deps=out_reg.wdeps())


def stream_cast(k, slot, f32_off, dram_ap, ncols, bf_off, bf_reg, crit=True):
    p = k.p
    p.op("sp", lambda eng, i=dram_ap, fo=f32_off, n=ncols: eng.dma_start(
        out=k.f32s[:, fo:fo + n], in_=i), writes=[slot.res], deps=slot.wdeps(), dma=True)
    if crit:
        eng_name = ("dve", "act")[k.cast_rr % 2]
        k.cast_rr += 1
    else:
        eng_name = "gp"
    if eng_name == "act":
        p.op("act", lambda eng, fo=f32_off, n=ncols, bo=bf_off: eng.activation(
            k.scr[:, bo:bo + n], k.f32s[:, fo:fo + n], AF.Copy),
            reads=[slot.res], writes=[bf_reg.res], deps=bf_reg.wdeps())
    else:
        p.op(eng_name, lambda eng, fo=f32_off, n=ncols, bo=bf_off: eng.tensor_copy(
            k.scr[:, bo:bo + n], k.f32s[:, fo:fo + n]),
            reads=[slot.res], writes=[bf_reg.res], deps=bf_reg.wdeps())


L1A_XN = 0
L1A_VG = 16384
L1A_QK = 24576
L1A_EXP = 52224
L1A_OT = 29696
L1A_VW = 46080
L1A_QKW = 50176


def emit_l1_attn(k):
    p, scr, rows = k.p, k.scr, k.rows
    g_c0, g_res = param_cols(k, "L1_attn_ln_gamma")
    b_c0, b_res = param_cols(k, "L1_attn_ln_beta")
    bo_c0, bo_res = param_cols(k, "L1_attn_b_out")
    qkw = k.dram("l1a_qkw", [8, 2, 128, 1024])
    vw = k.dram("l1a_vw", [2, 8, 128, 512])
    wo = k.dram("l1a_wo", [8, 128, 1024])

    mu_res, rstd_res = stats(k, 1e-5)
    xn = k.scr_t.carve(L1A_XN, L1A_XN + 16384, "xn")
    ln_apply(k, mu_res, rstd_res, affine=(g_c0, g_res, b_c0, b_res),
             out_reg=xn, out_region=L1A_XN)

    vg = k.scr_t.carve(L1A_VG, L1A_VG + 8192, "vg")
    ot = k.scr_t.carve(L1A_OT, L1A_OT + 16384, "ot")
    fslot = [k.f32_t.carve(0, 1024, "f0"), k.f32_t.carve(1024, 2048, "f1")]
    vwb = [k.scr_t.carve(L1A_VW + i * 512, L1A_VW + (i + 1) * 512, "vwb%d" % i)
           for i in range(8)]
    qkwb = [k.scr_t.carve(L1A_QKW + i * 1024, L1A_QKW + (i + 1) * 1024, "qkwb%d" % i)
            for i in range(2)]
    expb = [k.scr_t.carve(L1A_EXP + i * 512, L1A_EXP + (i + 1) * 512, "expb%d" % i)
            for i in range(4)]
    qk = k.scr_t.carve(L1A_QK, L1A_QK + 4096, "qk")

    for pair in range(8):
        g = pair // 4
        if pair % 4 == 0:
            for kt in range(NT):
                stream_cast(k, fslot[kt % 2], (kt % 2) * 1024, vw[g, kt], 512,
                            L1A_VW + kt * 512, vwb[kt])
            for t in range(16):
                gmm = MMGroup(k, k.PB[:, 512:1024], k.bPB1, NT)
                for kt in range(NT):
                    gmm.part(scr[:, L1A_XN + kt * S + t * 128: L1A_XN + kt * S + t * 128 + 128],
                             scr[:, L1A_VW + kt * 512: L1A_VW + (kt + 1) * 512],
                             [xn.res, vwb[kt].res])
                p.op("act", lambda eng, t=t: eng.activation(
                    scr[:, L1A_VG + t * 512: L1A_VG + (t + 1) * 512],
                    k.PB[:, 512:1024], AF.Copy),
                    reads=[k.bPB1], writes=[vg.res], deps=vg.wdeps())
        for qki in range(2):
            stream_cast(k, fslot[qki], qki * 1024, qkw[pair, qki], 1024,
                        L1A_QKW + qki * 1024, qkwb[qki])
            bank_ap = k.PEb[:, 0:512] if qki == 0 else k.PF[:, 0:512]
            bank = k.bPE if qki == 0 else k.bPF
            for c in range(NC):
                gmm = MMGroup(k, bank_ap, bank, NT)
                for kt in range(NT):
                    gmm.part(
                        scr[:, L1A_QKW + qki * 1024 + kt * 128: L1A_QKW + qki * 1024 + (kt + 1) * 128],
                        scr[:, L1A_XN + kt * S + c * 512: L1A_XN + kt * S + (c + 1) * 512],
                        [qkwb[qki].res, xn.res])
                p.op("act", lambda eng, qki=qki, c=c, b=bank_ap: eng.activation(
                    scr[:, L1A_QK + qki * 2048 + c * 512: L1A_QK + qki * 2048 + (c + 1) * 512],
                    b, AF.Copy),
                    reads=[bank], writes=[qk.res], deps=qk.wdeps())
        for qc in range(NC):
            nkb = 4 * qc + 4
            deng = []
            for h in range(2):
                denb = k.bPC if h == 0 else k.bPD
                denp = (k.PC if h == 0 else k.PD)[0:1, 0:512]
                deng.append(MMGroup(k, denp, denb, nkb))
            ot0 = MMGroup(k, k.PB[0:64, 0:512], k.bPB0, nkb)
            ot1 = MMGroup(k, k.PB[64:128, 512:1024], k.bPB1, nkb)

            def sc_bank(kb, h):
                if kb % 2 == 0:
                    return (k.PA[:, h * 512:(h + 1) * 512], k.bPA0 if h == 0 else k.bPA1)
                return ((k.PEb if h == 0 else k.PF)[:, 0:512], k.bPE if h == 0 else k.bPF)

            def emit_scores(kb):
                for h in range(2):
                    ei = (kb * 2 + h) % 4
                    eo = L1A_EXP + ei * 512
                    sb_ap, sb = sc_bank(kb, h)
                    mm(k, sb_ap, sb,
                       [(scr[64 * h:64 * h + 64, L1A_QK + 2048 + kb * 128: L1A_QK + 2048 + (kb + 1) * 128],
                         scr[64 * h:64 * h + 64, L1A_QK + qc * 512: L1A_QK + (qc + 1) * 512],
                         [qk.res])],
                       tp=(64 * h, 0))
                    p.op("act", lambda eng, eo=eo, sp=sb_ap, b=k.cv(-20.0): eng.activation(
                        scr[:, eo:eo + 512], sp, AF.Exp, bias=b, scale=0.125),
                        reads=[sb, k.cvals_r], writes=[expb[ei].res], deps=expb[ei].wdeps())
                    off = qc * 512 - kb * 128
                    if off < 512:
                        cb = off + 384
                        assert 0 <= cb and cb + 512 <= BAND_W
                        p.op("dve", lambda eng, eo=eo, cb=cb: eng.tensor_tensor(
                            scr[:, eo:eo + 512], scr[:, eo:eo + 512],
                            k.band[:, cb:cb + 512], ALU.mult),
                            reads=[expb[ei].res, k.band_r], writes=[expb[ei].res])

            emit_scores(0)
            for kb in range(nkb):
                if kb + 1 < nkb:
                    emit_scores(kb + 1)
                for h in range(2):
                    ei = (kb * 2 + h) % 4
                    eo = L1A_EXP + ei * 512
                    deng[h].part(k.ones_col_bf[:], scr[:, eo:eo + 512],
                                 [expb[ei].res, k.ones_col_bf_r])
                    hw = (pair % 4) * 2 + h
                    (ot0 if h == 0 else ot1).part(
                        scr[:, L1A_VG + kb * 512 + hw * 64: L1A_VG + kb * 512 + hw * 64 + 64],
                        scr[:, eo:eo + 512],
                        [vg.res, expb[ei].res], tp=(0, 64 * h))
            for h in range(2):
                den = k.row_t.carve(h * 512, (h + 1) * 512, "den%d" % h)
                bank = k.bPC if h == 0 else k.bPD
                p.op("dve", lambda eng, h=h, ps=(k.PC if h == 0 else k.PD): eng.tensor_copy(
                    rows[0:1, h * 512:(h + 1) * 512], ps[0:1, 0:512]),
                    reads=[bank], writes=[den.res], deps=den.wdeps())
                p.op("act", lambda eng, h=h: eng.activation(
                    rows[0:1, h * 512:(h + 1) * 512], rows[0:1, h * 512:(h + 1) * 512], AF.Ln),
                    reads=[den.res], writes=[den.res])
                p.op("act", lambda eng, h=h: eng.activation(
                    rows[0:1, h * 512:(h + 1) * 512], rows[0:1, h * 512:(h + 1) * 512],
                    AF.Exp, scale=-1.0),
                    reads=[den.res], writes=[den.res])
                rb_ap = k.PA[64 * h:64 * h + 64, h * 512:(h + 1) * 512]
                rb = k.bPA0 if h == 0 else k.bPA1
                mm(k, rb_ap, rb,
                   [(k.ones_row_f32[0:1, 0:64], rows[0:1, h * 512:(h + 1) * 512],
                     [den.res, k.ones_row_f32_r])], tp=(0, 64 * h))
                p.op("act", lambda eng, h=h: eng.activation(
                    scr[64 * h:64 * h + 64, L1A_EXP + h * 512: L1A_EXP + h * 512 + 512],
                    k.PB[64 * h:64 * h + 64, 512 * h:512 * h + 512], AF.Copy),
                    reads=[k.bPB0 if h == 0 else k.bPB1], writes=[expb[h].res])
                o_col = L1A_OT + pair * 2048 + qc * 512
                p.op("dve", lambda eng, h=h, o=o_col, rba=rb_ap: eng.tensor_tensor(
                    scr[64 * h:64 * h + 64, o:o + 512],
                    scr[64 * h:64 * h + 64, L1A_EXP + h * 512: L1A_EXP + h * 512 + 512],
                    rba, ALU.mult),
                    reads=[expb[h].res, rb], writes=[ot.res], deps=ot.wdeps())

    p.barrier()
    wob = [k.scr_t.carve(L1A_XN + kt * 1024, L1A_XN + (kt + 1) * 1024, "wob%d" % kt)
           for kt in range(NT)]
    for kt in range(NT):
        stream_cast(k, fslot[kt % 2], (kt % 2) * 1024, wo[kt], 1024,
                    L1A_XN + kt * 1024, wob[kt])
    for m in range(NT):
        for c in range(NC):
            par = (m * NC + c) % 2
            bank_ap = k.PA[:, par * 512:(par + 1) * 512]
            bank = k.bPA0 if par == 0 else k.bPA1
            gmm = MMGroup(k, bank_ap, bank, NT)
            for kt in range(NT):
                gmm.part(scr[:, L1A_XN + kt * 1024 + m * 128: L1A_XN + kt * 1024 + (m + 1) * 128],
                         scr[:, L1A_OT + kt * 2048 + c * 512: L1A_OT + kt * 2048 + (c + 1) * 512],
                         [wob[kt].res, ot.res])
            a_sl = (slice(None), slice(m * S + c * 512, m * S + (c + 1) * 512))
            p.op("dve", lambda eng, a=a_sl, m=m, b=bank_ap: eng.scalar_tensor_tensor(
                k.acc[a], b, k.misc[:, bo_c0 + m:bo_c0 + m + 1], k.acc[a], ALU.add, ALU.add),
                reads=[bank, bo_res, k.acc_r[m]], writes=[k.acc_r[m]])
    p.barrier()


def emit_outer_ln(k, eps=1e-5):
    mu_res, rstd_res = stats(k, eps)
    ln_apply(k, mu_res, rstd_res)
    k.p.barrier()



BLOCK_EMITTERS = {}
PREPPERS = {}


def build_program(blocks):
    nc = bass.Bass("TRN2", target_bir_lowering=False)
    with contextlib.ExitStack() as es:
        k = K(nc, es)
        xio = k.dram("xio", [D, S], F32)
        yio = k.dram("yio", [D, S], F32, kind="ExternalOutput")
        for j in range(NT):
            k.p.op("sp", lambda eng, j=j: eng.dma_start(
                out=k.acc[:, j * S:(j + 1) * S], in_=xio[j * 128:(j + 1) * 128, :]),
                writes=[k.acc_r[j]], dma=True)
        for b in blocks:
            BLOCK_EMITTERS[b](k)
            emit_outer_ln(k)
        out_toks = []
        for j in range(NT):
            t = k.p.op("sp", lambda eng, j=j: eng.dma_start(
                out=yio[j * 128:(j + 1) * 128, :], in_=k.acc[:, j * S:(j + 1) * S]),
                reads=[k.acc_r[j]], dma=True)
            out_toks.append(t)
        k.p.op("sp", lambda eng: eng.nop(), deps=out_toks)
        sems = {}
        with contextlib.ExitStack() as es2:
            for e in ENGS:
                sems[(e, "c")] = es2.enter_context(nc.semaphore("s_" + e))
                for i in range(NDMA):
                    sems[(e, "d", i)] = es2.enter_context(
                        nc.semaphore("sd_%s_%d" % (e, i)))
            block = es2.enter_context(nc.Block())
            k.p.emit(block, sems)
    return nc, k


BLOCK_EMITTERS["l1_attn"] = emit_l1_attn





def emit_top2_gate(k, lhsT_tile_fn, lhsT_res_fn, wg_sb, wg_res, bg_row, bg_res,
                   pexp_off, pexp_reg, scale_col_fn=None, scale_col_res=None):
    p, scr, rows = k.p, k.scr, k.rows
    gt = k.f32_t.carve(2560, 2760, "gate")
    GT = 2560
    for t in range(16):
        g = MMGroup(k, k.PC[0:128, 0:8], k.bPC, NT + (1 if bg_row is not None else 0))
        for kt in range(NT):
            g.part(lhsT_tile_fn(kt, t), wg_sb[:, kt * 8:(kt + 1) * 8],
                   [lhsT_res_fn(kt), wg_res])
        if bg_row is not None:
            g.part(k.ones_row_f32[:], bg_row, [bg_res, k.ones_row_f32_r])
        d = gt.wdeps()
        p.op("dve", lambda eng: eng.tensor_copy(k.f32s[:, GT:GT + 8], k.PC[0:128, 0:8]),
             reads=[k.bPC], writes=[gt.res], deps=d)
        if scale_col_fn is not None:
            p.op("dve", lambda eng, sc=scale_col_fn(t): eng.tensor_scalar(
                k.f32s[:, GT:GT + 8], k.f32s[:, GT:GT + 8], sc, None, ALU.mult),
                reads=[gt.res, scale_col_res], writes=[gt.res])
        p.op("dve", lambda eng: eng.tensor_reduce(
            k.f32s[:, GT + 8:GT + 9], k.f32s[:, GT:GT + 8], AX.X, ALU.max),
            reads=[gt.res], writes=[gt.res])
        p.op("dve", lambda eng: eng.tensor_scalar(
            k.f32s[:, GT + 16:GT + 24], k.f32s[:, GT:GT + 8],
            k.f32s[:, GT + 8:GT + 9], None, ALU.is_equal),
            reads=[gt.res], writes=[gt.res])
        p.op("dve", lambda eng: eng.scalar_tensor_tensor(
            k.f32s[:, GT + 32:GT + 40], k.f32s[:, GT + 16:GT + 24], -1.0e30,
            k.f32s[:, GT:GT + 8], ALU.mult, ALU.add),
            reads=[gt.res], writes=[gt.res])
        p.op("dve", lambda eng: eng.tensor_reduce(
            k.f32s[:, GT + 9:GT + 10], k.f32s[:, GT + 32:GT + 40], AX.X, ALU.max),
            reads=[gt.res], writes=[gt.res])
        p.op("dve", lambda eng: eng.tensor_scalar(
            k.f32s[:, GT + 24:GT + 32], k.f32s[:, GT + 32:GT + 40],
            k.f32s[:, GT + 9:GT + 10], None, ALU.is_equal),
            reads=[gt.res], writes=[gt.res])
        p.op("dve", lambda eng: eng.tensor_tensor(
            k.f32s[:, GT + 10:GT + 11], k.f32s[:, GT + 8:GT + 9],
            k.f32s[:, GT + 9:GT + 10], ALU.subtract),
            reads=[gt.res], writes=[gt.res])
        p.op("act", lambda eng: eng.activation(
            k.f32s[:, GT + 11:GT + 12], k.f32s[:, GT + 10:GT + 11], AF.Sigmoid),
            reads=[gt.res], writes=[gt.res])
        p.op("act", lambda eng: eng.activation(
            k.f32s[:, GT + 12:GT + 13], k.f32s[:, GT + 10:GT + 11], AF.Sigmoid,
            scale=-1.0),
            reads=[gt.res], writes=[gt.res])
        p.op("dve", lambda eng: eng.tensor_scalar(
            k.f32s[:, GT + 16:GT + 24], k.f32s[:, GT + 16:GT + 24],
            k.f32s[:, GT + 11:GT + 12], None, ALU.mult),
            reads=[gt.res], writes=[gt.res])
        p.op("dve", lambda eng: eng.tensor_scalar(
            k.f32s[:, GT + 24:GT + 32], k.f32s[:, GT + 24:GT + 32],
            k.f32s[:, GT + 12:GT + 13], None, ALU.mult),
            reads=[gt.res], writes=[gt.res])
        p.op("dve", lambda eng: eng.tensor_tensor(
            k.f32s[:, GT + 40:GT + 48], k.f32s[:, GT + 16:GT + 24],
            k.f32s[:, GT + 24:GT + 32], ALU.add),
            reads=[gt.res], writes=[gt.res])
        p.op("pe", lambda eng: eng.transpose(
            k.PF[0:8, 0:128], k.f32s[:, GT + 40:GT + 48], k.id_f32[:]),
            reads=[gt.res, k.id_f32_r], writes=[k.bPF])
        p.op("dve", lambda eng, t=t: eng.tensor_copy(
            scr[0:8, pexp_off + t * 128: pexp_off + (t + 1) * 128], k.PF[0:8, 0:128]),
            reads=[k.bPF], writes=[pexp_reg.res], deps=pexp_reg.wdeps())


L0F_XB = 0
L0F_XP = 8192
L0F_H = 16384
L0F_PEXP = 49152
L0F_PB = 51200
L0F_W13 = 53248


def emit_l0_ffn(k):
    import os
    dbg_v = os.environ.get("L0F_DEBUG", "0")
    dbg = dbg_v == "1"
    halves = [1] if dbg_v in ("1", "2", "3") else range(2)
    experts = ([7] if dbg_v in ("1", "4") else [6, 7] if dbg_v == "3" else range(8))
    p, scr, rows = k.p, k.scr, k.rows
    w1d = k.dram("l0f_w1", [8, 32, 128, 1024])
    w3d = k.dram("l0f_w3", [8, 32, 128, 1024])
    w2d = k.dram("l0f_w2", [8, 8, 128, 4096])
    wgd = k.dram("l0f_wg", [128, 64])
    bgd = k.dram("L0_ffn_bg", [8])

    wgr = k.f32_t.carve(2760, 2824, "wg")
    p.op("sp", lambda eng: eng.dma_start(out=k.f32s[:, 2760:2824], in_=wgd[:]),
         writes=[wgr.res], deps=wgr.wdeps(), dma=True)
    bgr = k.row_t.carve(3072, 3080, "bg")
    p.op("sp", lambda eng: eng.dma_start(
        out=rows[0:1, 3072:3080], in_=bgd.rearrange("(o e) -> o e", o=1)),
        writes=[bgr.res], deps=bgr.wdeps(), dma=True)

    pexp = k.scr_t.carve(L0F_PEXP, L0F_PEXP + 2048, "pexp")
    emit_top2_gate(k,
                   lambda kt, t: k.acc[:, kt * S + t * 128: kt * S + (t + 1) * 128],
                   lambda kt: k.acc_r[kt],
                   k.f32s[:, 2760:2824], wgr.res,
                   rows[0:1, 3072:3080], bgr.res,
                   L0F_PEXP, pexp)

    pb = k.scr_t.carve(L0F_PB, L0F_PB + 2048, "pb")
    for half in halves:
        hb = half * 1024
        xb = k.scr_t.carve(L0F_XB, L0F_XB + 8192, "xb%d" % half)
        for kt in range(NT):
            p.op("dve", lambda eng, kt=kt, hb=hb: eng.tensor_copy(
                scr[:, L0F_XB + kt * 1024: L0F_XB + (kt + 1) * 1024],
                k.acc[:, kt * S + hb: kt * S + hb + 1024]),
                reads=[k.acc_r[kt]], writes=[xb.res], deps=xb.wdeps())
            p.op("dve", lambda eng, kt=kt, hb=hb: eng.tensor_scalar(
                k.acc[:, kt * S + hb: kt * S + hb + 1024],
                k.acc[:, kt * S + hb: kt * S + hb + 1024], 2.0, None, ALU.mult),
                reads=[xb.res], writes=[k.acc_r[kt]])
        for e in experts:
            for c in range(2):
                mm(k, k.PEb[:, 0:512], k.bPE,
                   [(k.onehot8[0:8, e * 128:(e + 1) * 128],
                     scr[0:8, L0F_PEXP + hb + c * 512: L0F_PEXP + hb + (c + 1) * 512],
                     [pexp.res, k.onehot8_r])])
                p.op("dve", lambda eng, c=c: eng.tensor_copy(
                    scr[:, L0F_PB + c * 512: L0F_PB + (c + 1) * 512], k.PEb[:, 0:512]),
                    reads=[k.bPE], writes=[pb.res], deps=pb.wdeps())
            xp = [k.scr_t.carve(L0F_XP + kt * 1024, L0F_XP + (kt + 1) * 1024,
                                "xp%d_%d_%d" % (half, e, kt)) for kt in range(NT)]
            for kt in range(NT):
                p.op("dve", lambda eng, kt=kt: eng.tensor_tensor(
                    scr[:, L0F_XP + kt * 1024: L0F_XP + (kt + 1) * 1024],
                    scr[:, L0F_XB + kt * 1024: L0F_XB + (kt + 1) * 1024],
                    scr[:, L0F_PB: L0F_PB + 1024], ALU.mult),
                    reads=[xb.res, pb.res], writes=[xp[kt].res], deps=xp[kt].wdeps())
            w1r = [k.scr_t.carve(L0F_W13 + i * 512, L0F_W13 + (i + 1) * 512, "w1r%d" % i)
                   for i in range(2)]
            w3r = [k.scr_t.carve(L0F_W13 + 1024 + i * 512, L0F_W13 + 1024 + (i + 1) * 512,
                                 "w3r%d" % i) for i in range(2)]
            fs6 = [k.f32_t.carve(i * 512, (i + 1) * 512, "fs%d" % i) for i in range(6)]
            sctr = [0, 0]
            hreg = k.scr_t.carve(L0F_H, L0F_H + 32768, "h%d_%d" % (half, e))

            def l0f_streams(ht):
                for hc in range(2):
                    s1 = sctr[0] % 3
                    sctr[0] += 1
                    stream_cast(k, fs6[s1], s1 * 512,
                                w1d[e, ht, :, hc * 512:(hc + 1) * 512],
                                512, L0F_W13 + hc * 512, w1r[hc])
                    s3 = 3 + sctr[1] % 3
                    sctr[1] += 1
                    stream_cast(k, fs6[s3], s3 * 512,
                                w3d[e, ht, :, hc * 512:(hc + 1) * 512],
                                512, L0F_W13 + 1024 + hc * 512, w3r[hc])

            l0f_streams(0)
            for ht in range(32):
                par = ht % 2
                if par == 0:
                    ub = [(k.PA[:, 0:512], k.bPA0), (k.PA[:, 512:1024], k.bPA1)]
                    vb = [(k.PB[:, 0:512], k.bPB0), (k.PB[:, 512:1024], k.bPB1)]
                else:
                    ub = [(k.PC[0:128, 0:512], k.bPC), (k.PD[0:128, 0:512], k.bPD)]
                    vb = [(k.PEb[0:128, 0:512], k.bPE), (k.PF[0:128, 0:512], k.bPF)]
                for c in range(2):
                    gu = MMGroup(k, ub[c][0], ub[c][1], NT)
                    for kt in range(NT):
                        gu.part(scr[:, L0F_W13 + kt * 128: L0F_W13 + (kt + 1) * 128],
                                scr[:, L0F_XP + kt * 1024 + c * 512: L0F_XP + kt * 1024 + (c + 1) * 512],
                                [w1r[kt // 4].res, xp[kt].res])
                    gv = MMGroup(k, vb[c][0], vb[c][1], NT)
                    for kt in range(NT):
                        gv.part(scr[:, L0F_W13 + 1024 + kt * 128: L0F_W13 + 1024 + (kt + 1) * 128],
                                scr[:, L0F_XP + kt * 1024 + c * 512: L0F_XP + kt * 1024 + (c + 1) * 512],
                                [w3r[kt // 4].res, xp[kt].res])
                if ht + 1 < 32:
                    l0f_streams(ht + 1)
                hcol = L0F_H + ht * 1024
                for c in range(2):
                    p.op("act", lambda eng, hcol=hcol, c=c, ua=ub[c][0]: eng.activation(
                        scr[:, hcol + c * 512:hcol + (c + 1) * 512], ua, AF.Silu),
                        reads=[ub[c][1]], writes=[hreg.res], deps=hreg.wdeps())
                    p.op("dve", lambda eng, hcol=hcol, c=c, va=vb[c][0]: eng.tensor_tensor(
                        scr[:, hcol + c * 512:hcol + (c + 1) * 512],
                        scr[:, hcol + c * 512:hcol + (c + 1) * 512], va, ALU.mult),
                        reads=[vb[c][1], hreg.res], writes=[hreg.res])
            if dbg:
                continue
            w2b = [k.scr_t.carve(L0F_XP + i * 1024, L0F_XP + (i + 1) * 1024, "w2b%d" % i)
                   for i in range(4)]
            w2f = [k.f32_t.carve(i * 1024, (i + 1) * 1024, "w2f%d" % i) for i in range(2)]
            for d2 in range(NT):
                for q in range(4):
                    stream_cast(k, w2f[q % 2], (q % 2) * 1024,
                                w2d[e, d2, :, q * 1024:(q + 1) * 1024],
                                1024, L0F_XP + q * 1024, w2b[q], crit=False)
                if d2 % 2 == 0:
                    yb = [(k.PA[:, 0:512], k.bPA0), (k.PA[:, 512:1024], k.bPA1)]
                else:
                    yb = [(k.PB[:, 0:512], k.bPB0), (k.PB[:, 512:1024], k.bPB1)]
                for c in range(2):
                    gy = MMGroup(k, yb[c][0], yb[c][1], 32)
                    for kt in range(32):
                        gy.part(scr[:, L0F_XP + kt * 128: L0F_XP + (kt + 1) * 128],
                                scr[:, L0F_H + kt * 1024 + c * 512: L0F_H + kt * 1024 + (c + 1) * 512],
                                [w2b[kt // 8].res, hreg.res])
                    p.op("dve", lambda eng, d2=d2, c=c, hb=hb, ps=yb[c][0]:
                         eng.tensor_tensor(
                        k.acc[:, d2 * S + hb + c * 512: d2 * S + hb + (c + 1) * 512],
                        k.acc[:, d2 * S + hb + c * 512: d2 * S + hb + (c + 1) * 512],
                        ps, ALU.add),
                        reads=[yb[c][1], k.acc_r[d2]],
                        writes=[k.acc_r[d2]])
    p.barrier()


BLOCK_EMITTERS["l0_ffn"] = emit_l0_ffn


def prep_l0_ffn(inputs):
    W1 = np.asarray(inputs["L0_ffn_W1"], np.float32)
    W3 = np.asarray(inputs["L0_ffn_W3"], np.float32)
    W2 = np.asarray(inputs["L0_ffn_W2"], np.float32)
    w1t = np.ascontiguousarray(
        W1.reshape(8, 8, 128, 32, 128).transpose(0, 3, 2, 1, 4).reshape(8, 32, 128, 1024))
    w3t = np.ascontiguousarray(
        W3.reshape(8, 8, 128, 32, 128).transpose(0, 3, 2, 1, 4).reshape(8, 32, 128, 1024))
    w2t = np.ascontiguousarray(
        W2.reshape(8, 32, 128, 8, 128).transpose(0, 3, 2, 1, 4).reshape(8, 8, 128, 4096))
    Wg = np.asarray(inputs["L0_ffn_Wg"], np.float32)
    wgt = np.ascontiguousarray(Wg.reshape(8, 128, 8).transpose(1, 0, 2).reshape(128, 64))
    return {"l0f_w1": w1t, "l0f_w3": w3t, "l0f_w2": w2t, "l0f_wg": wgt,
            "L0_ffn_bg": np.asarray(inputs["L0_ffn_bg"], np.float32)}


PREPPERS["l0_ffn"] = prep_l0_ffn


def prep_l1_ffn(inputs):
    wd = np.asarray(inputs["L1_ffn_w_down"], np.float32)
    wq = np.asarray(inputs["L1_ffn_wq"], np.float32)
    wk = np.asarray(inputs["L1_ffn_wk"], np.float32)
    wv = np.asarray(inputs["L1_ffn_wv"], np.float32)
    wu = np.asarray(inputs["L1_ffn_w_up"], np.float32)
    rw = np.asarray(inputs["L1_ffn_router_w"], np.float32)
    w1 = np.asarray(inputs["L1_ffn_w1"], np.float32)
    w3 = np.asarray(inputs["L1_ffn_w3"], np.float32)
    w2 = np.asarray(inputs["L1_ffn_w2"], np.float32)
    perm = np.empty(512, np.int64)
    for h in range(16):
        for f in range(16):
            perm[16 * h + f] = 32 * h + 2 * f
            perm[256 + 16 * h + f] = 32 * h + 2 * f + 1
    wq_p = wq[:, perm]
    wk_p = wk[:, perm]
    tile4 = lambda w, M: np.ascontiguousarray(
        w.reshape(w.shape[0] // 128, 128, M, 128).transpose(2, 1, 0, 3)
        .reshape(M, 128, (w.shape[0] // 128) * 128))
    out = {
        "l1f_wd": tile4(wd, 4),
        "l1f_wq": tile4(wq_p, 4),
        "l1f_wk": tile4(wk_p, 4),
        "L1_ffn_wv": wv,
        "l1f_wu": tile4(wu, 8),
        "l1f_rw": np.ascontiguousarray(rw.reshape(8, 128, 8).transpose(1, 0, 2).reshape(128, 64)),
        "l1f_w1": np.ascontiguousarray(
            w1.reshape(8, 8, 128, 16, 128).transpose(0, 3, 2, 1, 4).reshape(8, 16, 128, 1024)),
        "l1f_w3": np.ascontiguousarray(
            w3.reshape(8, 8, 128, 16, 128).transpose(0, 3, 2, 1, 4).reshape(8, 16, 128, 1024)),
        "l1f_w2": np.ascontiguousarray(
            w2.reshape(8, 16, 128, 8, 128).transpose(0, 3, 2, 1, 4).reshape(8, 8, 128, 2048)),
        "L1_ffn_g1": np.asarray(inputs["L1_ffn_g1"], np.float32),
        "L1_ffn_g2": np.asarray(inputs["L1_ffn_g2"], np.float32),
    }
    f = np.arange(16)
    inv = 10000.0 ** (-(2.0 * f) / 32.0)
    ang = np.arange(2048)[None, :] * np.tile(inv, 8)[:, None]
    out["l1f_cos"] = np.cos(ang).astype(BF)
    out["l1f_sin"] = np.sin(ang).astype(BF)
    sel = np.zeros((128, 512), BF)
    for par in range(2):
        for j in range(4):
            for t in range(16):
                sel[64 * par + 16 * j + t, par * 128 + 32 * j + t] = 1
                sel[64 * par + 16 * j + t, (2 + par) * 128 + 32 * j + 16 + t] = 1
    out["l1f_sel"] = sel
    return out


PREPPERS["l1_ffn"] = prep_l1_ffn



L1F_H1 = 0
L1F_CT = 16384
L1F_Q = 24576
L1F_K = 32768
L1F_V = 40960
L1F_OT = 16384
L1F_COS = 0
L1F_SIN = 2048
L1F_SEL = 4096
L1F_RTMP = 4608
L1F_RR = 6656
L1F_WST = 53248


def emit_rms_rows(k, eps, src_fn=None, src_res=None, bf_src=False):
    p, scr, rows = k.p, k.scr, k.rows
    rstd = k.row_t.carve(2048, 4096, "rmsr")
    xc = k.scr_t.carve(STAT_XC, STAT_XC + 512, "rmsxc")
    sq = k.scr_t.carve(STAT_SQ, STAT_SQ + 512, "rmssq")
    for c in range(NC):
        g2 = MMGroup(k, k.PD[0:1, 0:512], k.bPD, NT)
        for j in range(NT):
            if not bf_src:
                p.op("dve", lambda eng, j=j, c=c: eng.tensor_copy(
                    scr[:, STAT_XC:STAT_XC + 512],
                    src_fn(j)[:, c * 512:(c + 1) * 512]),
                    reads=[src_res] if src_res else [k.acc_r[j]], writes=[xc.res],
                    deps=xc.wdeps())
                sqsrc = scr[:, STAT_XC:STAT_XC + 512]
                sqread = xc.res
            else:
                sqsrc = src_fn(j)[:, c * 512:(c + 1) * 512]
                sqread = src_res
            p.op("act", lambda eng, s=sqsrc: eng.activation(
                scr[:, STAT_SQ:STAT_SQ + 512], s, AF.Square),
                reads=[sqread], writes=[sq.res], deps=sq.wdeps())
            g2.part(k.ones_col_bf[:], scr[:, STAT_SQ:STAT_SQ + 512],
                    [sq.res, k.ones_col_bf_r])
        p.op("dve", lambda eng, c=c: eng.tensor_scalar(
            rows[0:1, 2048 + c * 512:2048 + (c + 1) * 512], k.PD[0:1, 0:512],
            1.0 / D, None, ALU.mult),
            reads=[k.bPD], writes=[rstd.res], deps=rstd.wdeps())
    p.op("act", lambda eng, b=k.cv(eps, 1): eng.activation(
        rows[0:1, 2048:4096], rows[0:1, 2048:4096], AF.Ln, bias=b),
        reads=[rstd.res, k.cvals_r], writes=[rstd.res])
    p.op("act", lambda eng: eng.activation(
        rows[0:1, 2048:4096], rows[0:1, 2048:4096], AF.Exp, scale=-0.5),
        reads=[rstd.res], writes=[rstd.res])
    return rstd


def emit_l1_ffn(k):
    import os
    stop = os.environ.get("L1F_STOP", "")
    p, scr, rows = k.p, k.scr, k.rows
    g1_c0, g1_res = param_cols(k, "L1_ffn_g1")
    g2_c0, g2_res = param_cols(k, "L1_ffn_g2")
    wdd = k.dram("l1f_wd", [4, 128, 1024])
    wqd = k.dram("l1f_wq", [4, 128, 512])
    wkd = k.dram("l1f_wk", [4, 128, 512])
    wvd = k.dram("L1_ffn_wv", [512, 512])
    wud = k.dram("l1f_wu", [8, 128, 512])
    rwd = k.dram("l1f_rw", [128, 64])
    w1d = k.dram("l1f_w1", [8, 16, 128, 1024])
    w3d = k.dram("l1f_w3", [8, 16, 128, 1024])
    w2d = k.dram("l1f_w2", [8, 8, 128, 2048])
    cosd = k.dram("l1f_cos", [128, 2048], BF16)
    sind = k.dram("l1f_sin", [128, 2048], BF16)
    seld = k.dram("l1f_sel", [128, 512], BF16)

    fsl = [k.f32_t.carve(0, 1024, "lf0"), k.f32_t.carve(1024, 2048, "lf1")]

    rstd = emit_rms_rows(k, 1e-5, src_fn=lambda j: k.acc[:, j * S:(j + 1) * S])
    h1 = k.scr_t.carve(L1F_H1, L1F_H1 + 16384, "h1")
    for c in range(NC):
        mm(k, k.PF[:, 0:512], k.bPF,
           [(k.ones_row_f32[:], rows[0:1, 2048 + c * 512:2048 + (c + 1) * 512],
             [rstd.res, k.ones_row_f32_r])])
        for j in range(NT):
            p.op("dve", lambda eng, j=j, c=c, gc=g1_c0: eng.scalar_tensor_tensor(
                scr[:, L1F_H1 + j * S + c * 512: L1F_H1 + j * S + (c + 1) * 512],
                k.acc[:, j * S + c * 512: j * S + (c + 1) * 512],
                k.misc[:, gc + j:gc + j + 1], k.PF[:, 0:512], ALU.mult, ALU.mult),
                reads=[k.acc_r[j], g1_res, k.bPF], writes=[h1.res], deps=h1.wdeps())

    ct = k.scr_t.carve(L1F_CT, L1F_CT + 4096, "ct")
    wst = k.scr_t.carve(L1F_WST, L1F_WST + 1024, "wst")
    for m in range(4):
        stream_cast(k, fsl[m % 2], (m % 2) * 1024, wdd[m], 1024, L1F_WST, wst)
        for c in range(NC):
            ba = k.PA[:, 0:512] if c % 2 == 0 else k.PA[:, 512:1024]
            bb = k.bPA0 if c % 2 == 0 else k.bPA1
            g = MMGroup(k, ba, bb, NT)
            for kt in range(NT):
                g.part(scr[:, L1F_WST + kt * 128: L1F_WST + (kt + 1) * 128],
                       scr[:, L1F_H1 + kt * S + c * 512: L1F_H1 + kt * S + (c + 1) * 512],
                       [wst.res, h1.res])
            p.op("act", lambda eng, m=m, c=c, ba=ba: eng.activation(
                scr[:, L1F_CT + m * S + c * 512: L1F_CT + m * S + (c + 1) * 512],
                ba, AF.Copy),
                reads=[bb], writes=[ct.res], deps=ct.wdeps())

    qraw = k.scr_t.carve(L1F_Q, L1F_Q + 8192, "qraw")
    kraw = k.scr_t.carve(L1F_K, L1F_K + 8192, "kraw")
    for qk, (dram_w, reg, base) in enumerate([(wqd, qraw, L1F_Q), (wkd, kraw, L1F_K)]):
        for m in range(4):
            wst = k.scr_t.carve(L1F_WST, L1F_WST + 512, "wqk%d_%d" % (qk, m))
            stream_cast(k, fsl[m % 2], (m % 2) * 1024, dram_w[m], 512, L1F_WST, wst)
            for c in range(NC):
                ba = k.PA[:, 0:512] if c % 2 == 0 else k.PA[:, 512:1024]
                bb = k.bPA0 if c % 2 == 0 else k.bPA1
                g = MMGroup(k, ba, bb, 4)
                for kt in range(4):
                    g.part(scr[:, L1F_WST + kt * 128: L1F_WST + (kt + 1) * 128],
                           scr[:, L1F_CT + kt * S + c * 512: L1F_CT + kt * S + (c + 1) * 512],
                           [wst.res, ct.res])
                p.op("act", lambda eng, b=base, m=m, c=c, ba=ba: eng.activation(
                    scr[:, b + m * S + c * 512: b + m * S + (c + 1) * 512],
                    ba, AF.Copy),
                    reads=[bb], writes=[reg.res], deps=reg.wdeps())
    vreg = k.scr_t.carve(L1F_V, L1F_V + 8192, "v")
    wstv = [k.scr_t.carve(L1F_WST + i * 512, L1F_WST + (i + 1) * 512, "wv%d" % i)
            for i in range(4)]
    for kt in range(4):
        stream_cast(k, fsl[kt % 2], (kt % 2) * 1024, wvd[kt * 128:(kt + 1) * 128, :],
                    512, L1F_WST + kt * 512, wstv[kt])
    for t in range(16):
        ba = k.PB[:, 0:512] if t % 2 == 0 else k.PB[:, 512:1024]
        bb = k.bPB0 if t % 2 == 0 else k.bPB1
        g = MMGroup(k, ba, bb, 4)
        for kt in range(4):
            g.part(scr[:, L1F_CT + kt * S + t * 128: L1F_CT + kt * S + t * 128 + 128],
                   scr[:, L1F_WST + kt * 512: L1F_WST + (kt + 1) * 512],
                   [ct.res, wstv[kt].res])
        p.op("act", lambda eng, t=t, ba=ba: eng.activation(
            scr[:, L1F_V + t * 512: L1F_V + (t + 1) * 512], ba, AF.Copy),
            reads=[bb], writes=[vreg.res], deps=vreg.wdeps())

    if stop == "p3":
        return
    cos = k.scr_t.carve(L1F_COS, L1F_COS + 2048, "cos")
    sin = k.scr_t.carve(L1F_SIN, L1F_SIN + 2048, "sin")
    sel = k.scr_t.carve(L1F_SEL, L1F_SEL + 512, "sel")
    p.op("sp", lambda eng: eng.dma_start(out=scr[:, L1F_COS:L1F_COS + 2048], in_=cosd[:]),
         writes=[cos.res], deps=cos.wdeps(), dma=True)
    p.op("sp", lambda eng: eng.dma_start(out=scr[:, L1F_SIN:L1F_SIN + 2048], in_=sind[:]),
         writes=[sin.res], deps=sin.wdeps(), dma=True)
    p.op("sp", lambda eng: eng.dma_start(out=scr[:, L1F_SEL:L1F_SEL + 512], in_=seld[:]),
         writes=[sel.res], deps=sel.wdeps(), dma=True)
    rtmp = k.scr_t.carve(L1F_RTMP, L1F_RTMP + 2048, "rtmp")
    hc_regs = []
    for qk, (raw_base, raw_reg) in enumerate([(L1F_Q, qraw), (L1F_K, kraw)]):
        rr1 = [k.scr_t.carve(L1F_RR + i * S, L1F_RR + (i + 1) * S, "rr1_%d_%d" % (qk, i))
               for i in range(2)]
        rr2 = [k.scr_t.carve(L1F_RR + (2 + i) * S, L1F_RR + (3 + i) * S, "rr2_%d_%d" % (qk, i))
               for i in range(2)]
        rtm0 = k.scr_t.carve(L1F_RTMP, L1F_RTMP + S, "rtm%d" % qk)
        rtm = [rtm0, rtm0]
        for i in range(2):
            q1 = (slice(None), slice(raw_base + i * S, raw_base + (i + 1) * S))
            q2 = (slice(None), slice(raw_base + (2 + i) * S, raw_base + (3 + i) * S))
            r1 = (slice(None), slice(L1F_RR + i * S, L1F_RR + (i + 1) * S))
            r2 = (slice(None), slice(L1F_RR + (2 + i) * S, L1F_RR + (3 + i) * S))
            tm = (slice(None), slice(L1F_RTMP, L1F_RTMP + S))
            p.op("dve", lambda eng, a=q1, o=r1: eng.tensor_tensor(
                scr[o], scr[a], scr[:, L1F_COS:L1F_COS + 2048], ALU.mult),
                reads=[raw_reg.res, cos.res], writes=[rr1[i].res], deps=rr1[i].wdeps())
            p.op("dve", lambda eng, a=q1, o=tm: eng.tensor_tensor(
                scr[o], scr[a], scr[:, L1F_SIN:L1F_SIN + 2048], ALU.mult),
                reads=[raw_reg.res, sin.res], writes=[rtm[i].res], deps=rtm[i].wdeps())
            p.op("dve", lambda eng, a=q2, o=r2: eng.tensor_tensor(
                scr[o], scr[a], scr[:, L1F_SIN:L1F_SIN + 2048], ALU.mult),
                reads=[raw_reg.res, sin.res], writes=[rr2[i].res], deps=rr2[i].wdeps())
            p.op("dve", lambda eng, o=r1, b=r2: eng.tensor_tensor(
                scr[o], scr[o], scr[b], ALU.subtract),
                reads=[rr1[i].res, rr2[i].res], writes=[rr1[i].res])
            p.op("dve", lambda eng, a=q2, o=r2: eng.tensor_tensor(
                scr[o], scr[a], scr[:, L1F_COS:L1F_COS + 2048], ALU.mult),
                reads=[raw_reg.res, rr2[i].res], writes=[rr2[i].res])
            p.op("dve", lambda eng, o=r2, b=tm: eng.tensor_tensor(
                scr[o], scr[o], scr[b], ALU.add),
                reads=[rr2[i].res, rtm[i].res], writes=[rr2[i].res])
        hc = k.scr_t.carve(raw_base, raw_base + 8192, "hc%d" % qk)
        for i in range(4):
            par = i % 2
            for c in range(NC):
                g = MMGroup(k, k.PA[:, 0:512], k.bPA0, 2)
                g.part(scr[:, L1F_SEL + par * 128: L1F_SEL + (par + 1) * 128],
                       scr[:, L1F_RR + (i // 2) * S + c * 512: L1F_RR + (i // 2) * S + (c + 1) * 512],
                       [sel.res, rr1[i // 2].res])
                g.part(scr[:, L1F_SEL + (2 + par) * 128: L1F_SEL + (3 + par) * 128],
                       scr[:, L1F_RR + (2 + i // 2) * S + c * 512: L1F_RR + (2 + i // 2) * S + (c + 1) * 512],
                       [sel.res, rr2[i // 2].res])
                p.op("act", lambda eng, b=raw_base, i=i, c=c: eng.activation(
                    scr[:, b + i * S + c * 512: b + i * S + (c + 1) * 512],
                    k.PA[:, 0:512], AF.Copy),
                    reads=[k.bPA0], writes=[hc.res], deps=hc.wdeps())
        hc_regs.append(hc)
    qr, kr = hc_regs

    ot = k.scr_t.carve(L1F_OT, L1F_OT + 8192, "ot")
    expb = [k.scr_t.carve(L1F_RR + i * 512, L1F_RR + (i + 1) * 512, "l1fexp%d" % i)
            for i in range(4)]
    SC32 = float(1.0 / np.sqrt(32.0))
    for pr in range(8):
        tile = pr // 2
        for qc in range(NC):
            nkb = 4 * qc + 4
            deng = []
            otg = []
            for h in range(2):
                H = 2 * pr + h
                base = 32 * (H % 4)
                denb = k.bPC if h == 0 else k.bPD
                deng.append(MMGroup(k, (k.PC if h == 0 else k.PD)[0:1, 0:512],
                                    denb, nkb))
                og = MMGroup(k, (k.PEb if h == 0 else k.PF)[base:base + 32, 0:512],
                             k.bPE if h == 0 else k.bPF, nkb)
                otg.append(og)
            def sc_bank2(kb, h):
                if kb % 2 == 0:
                    return (k.PA[:, h * 512:(h + 1) * 512], k.bPA0 if h == 0 else k.bPA1)
                return (k.PB[:, h * 512:(h + 1) * 512], k.bPB0 if h == 0 else k.bPB1)

            def emit_scores2(kb):
                for h in range(2):
                    H = 2 * pr + h
                    base = 32 * (H % 4)
                    ei = (kb * 2 + h) % 4
                    eo = L1F_RR + ei * 512
                    sb_ap, sb = sc_bank2(kb, h)
                    mm(k, sb_ap, sb,
                       [(scr[base:base + 32, L1F_K + tile * S + kb * 128: L1F_K + tile * S + (kb + 1) * 128],
                         scr[base:base + 32, L1F_Q + tile * S + qc * 512: L1F_Q + tile * S + (qc + 1) * 512],
                         [qr.res, kr.res])],
                       tp=(base, 0))
                    p.op("act", lambda eng, eo=eo, sp_=sb_ap, b=k.cv(-12.0): eng.activation(
                        scr[:, eo:eo + 512], sp_, AF.Exp, bias=b, scale=SC32),
                        reads=[sb, k.cvals_r], writes=[expb[ei].res], deps=expb[ei].wdeps())
                    off = qc * 512 - kb * 128
                    if off < 512:
                        cb = off + 384
                        p.op("dve", lambda eng, eo=eo, cb=cb: eng.tensor_tensor(
                            scr[:, eo:eo + 512], scr[:, eo:eo + 512],
                            k.band[:, cb:cb + 512], ALU.mult),
                            reads=[expb[ei].res, k.band_r], writes=[expb[ei].res])

            emit_scores2(0)
            for kb in range(nkb):
                if kb + 1 < nkb:
                    emit_scores2(kb + 1)
                for h in range(2):
                    H = 2 * pr + h
                    base = 32 * (H % 4)
                    ei = (kb * 2 + h) % 4
                    eo = L1F_RR + ei * 512
                    deng[h].part(k.ones_col_bf[:], scr[:, eo:eo + 512],
                                 [expb[ei].res, k.ones_col_bf_r])
                    otg[h].part(
                        scr[:, L1F_V + kb * 512 + 32 * H: L1F_V + kb * 512 + 32 * H + 32],
                        scr[:, eo:eo + 512],
                        [vreg.res, expb[ei].res], tp=(0, base))
            for h in range(2):
                H = 2 * pr + h
                base = 32 * (H % 4)
                den = k.row_t.carve(h * 512, (h + 1) * 512, "l1fden%d" % h)
                bank = k.bPC if h == 0 else k.bPD
                p.op("dve", lambda eng, h=h, ps=(k.PC if h == 0 else k.PD): eng.tensor_copy(
                    rows[0:1, h * 512:(h + 1) * 512], ps[0:1, 0:512]),
                    reads=[bank], writes=[den.res], deps=den.wdeps())
                p.op("act", lambda eng, h=h: eng.activation(
                    rows[0:1, h * 512:(h + 1) * 512], rows[0:1, h * 512:(h + 1) * 512],
                    AF.Ln),
                    reads=[den.res], writes=[den.res])
                p.op("act", lambda eng, h=h: eng.activation(
                    rows[0:1, h * 512:(h + 1) * 512], rows[0:1, h * 512:(h + 1) * 512],
                    AF.Exp, scale=-1.0),
                    reads=[den.res], writes=[den.res])
                rb_ap = k.PB[base:base + 32, 0:512] if h == 0 else k.PB[base:base + 32, 512:1024]
                rb = k.bPB0 if h == 0 else k.bPB1
                mm(k, rb_ap, rb,
                   [(k.ones_row_f32[0:1, 0:32], rows[0:1, h * 512:(h + 1) * 512],
                     [den.res, k.ones_row_f32_r])], tp=(0, base))
                p.op("act", lambda eng, h=h, base=base,
                     ps=(k.PEb if h == 0 else k.PF): eng.activation(
                    scr[base:base + 32, L1F_RR + h * 512: L1F_RR + h * 512 + 512],
                    ps[base:base + 32, 0:512], AF.Copy),
                    reads=[k.bPE if h == 0 else k.bPF], writes=[expb[h].res])
                o_col = L1F_OT + tile * S + qc * 512
                p.op("dve", lambda eng, h=h, base=base, o=o_col, rba=rb_ap: eng.tensor_tensor(
                    scr[base:base + 32, o:o + 512],
                    scr[base:base + 32, L1F_RR + h * 512: L1F_RR + h * 512 + 512],
                    rba, ALU.mult),
                    reads=[expb[h].res, rb], writes=[ot.res], deps=ot.wdeps())

    if stop == "p5":
        return
    p.barrier()
    resbf = k.scr_t.carve(L1F_Q, L1F_Q + 16384, "resbf")
    for m in range(NT):
        wst = k.scr_t.carve(L1F_WST, L1F_WST + 512, "wu%d" % m)
        stream_cast(k, fsl[m % 2], (m % 2) * 1024, wud[m], 512, L1F_WST, wst)
        for c in range(NC):
            par = c % 2
            bank_ap = k.PA[:, par * 512:(par + 1) * 512]
            bank = k.bPA0 if par == 0 else k.bPA1
            g = MMGroup(k, bank_ap, bank, 4)
            for kt in range(4):
                g.part(scr[:, L1F_WST + kt * 128: L1F_WST + (kt + 1) * 128],
                       scr[:, L1F_OT + kt * S + c * 512: L1F_OT + kt * S + (c + 1) * 512],
                       [wst.res, ot.res])
            a_sl = (slice(None), slice(m * S + c * 512, m * S + (c + 1) * 512))
            r_sl = (slice(None), slice(L1F_Q + m * S + c * 512, L1F_Q + m * S + (c + 1) * 512))
            p.op("dve", lambda eng, a=a_sl, r=r_sl, b=bank_ap: eng.tensor_tensor(
                scr[r], k.acc[a], b, ALU.add),
                reads=[k.acc_r[m], bank], writes=[resbf.res], deps=resbf.wdeps())
            p.op("dve", lambda eng, a=a_sl, b=bank_ap: eng.scalar_tensor_tensor(
                k.acc[a], k.acc[a], 2.0, b, ALU.mult, ALU.add),
                reads=[bank, k.acc_r[m]], writes=[k.acc_r[m]])

    rstd2 = emit_rms_rows(
        k, 1e-5, src_fn=lambda j: scr[:, L1F_Q + j * S: L1F_Q + (j + 1) * S],
        src_res=resbf.res, bf_src=True)
    rc0 = k.misc_used
    k.misc_used += 16
    assert k.misc_used <= 128
    rcol_res = Res("rstdcols")
    for t in range(16):
        p.op("pe", lambda eng, t=t: eng.transpose(
            k.PF[0:128, 0:1], rows[0:1, 2048 + t * 128: 2048 + (t + 1) * 128],
            k.id_f32[0:1, 0:1]),
            reads=[rstd2.res, k.id_f32_r], writes=[k.bPF])
        p.op("dve", lambda eng, t=t, rc0=rc0: eng.tensor_copy(
            k.misc[:, rc0 + t:rc0 + t + 1], k.PF[0:128, 0:1]),
            reads=[k.bPF], writes=[rcol_res])
    rwf = k.f32_t.carve(2760, 2824, "rw")
    p.op("sp", lambda eng: eng.dma_start(out=k.f32s[:, 2760:2824], in_=rwd[:]),
         writes=[rwf.res], deps=rwf.wdeps(), dma=True)
    RWP = 43008
    rwp = k.scr_t.carve(RWP, RWP + 64, "rwp")
    for kt in range(NT):
        p.op("dve", lambda eng, kt=kt, gc=g2_c0: eng.tensor_scalar(
            scr[:, RWP + kt * 8: RWP + (kt + 1) * 8],
            k.f32s[:, 2760 + kt * 8: 2760 + (kt + 1) * 8],
            k.misc[:, gc + kt:gc + kt + 1], None, ALU.mult),
            reads=[rwf.res, g2_res], writes=[rwp.res], deps=rwp.wdeps())
    PEXP2 = 40960
    pexp = k.scr_t.carve(PEXP2, PEXP2 + 2048, "l1fpexp")
    emit_top2_gate(k,
                   lambda kt, t: scr[:, L1F_Q + kt * S + t * 128: L1F_Q + kt * S + (t + 1) * 128],
                   lambda kt: resbf.res,
                   scr[:, RWP:RWP + 64], rwp.res, None, None,
                   PEXP2, pexp,
                   scale_col_fn=lambda t: k.misc[:, rc0 + t:rc0 + t + 1],
                   scale_col_res=rcol_res)

    treg = k.scr_t.carve(0, 16384, "T")
    for c in range(NC):
        mm(k, k.PF[:, 0:512], k.bPF,
           [(k.ones_row_f32[:], rows[0:1, 2048 + c * 512:2048 + (c + 1) * 512],
             [rstd2.res, k.ones_row_f32_r])])
        for j in range(NT):
            p.op("dve", lambda eng, j=j, c=c, gc=g2_c0: eng.scalar_tensor_tensor(
                scr[:, j * S + c * 512: j * S + (c + 1) * 512],
                scr[:, L1F_Q + j * S + c * 512: L1F_Q + j * S + (c + 1) * 512],
                k.misc[:, gc + j:gc + j + 1], k.PF[:, 0:512], ALU.mult, ALU.mult),
                reads=[resbf.res, g2_res, k.bPF], writes=[treg.res], deps=treg.wdeps())
    TG, HID, PBR, W13R, W2BR = 16384, 24576, 43072, 45120, 47168
    pb = k.scr_t.carve(PBR, PBR + 2048, "l1fpb")
    for half in range(2):
        hb = half * 1024
        for e in range(8):
            for c in range(2):
                mm(k, k.PEb[:, 0:512], k.bPE,
                   [(k.onehot8[0:8, e * 128:(e + 1) * 128],
                     scr[0:8, PEXP2 + hb + c * 512: PEXP2 + hb + (c + 1) * 512],
                     [pexp.res, k.onehot8_r])])
                p.op("dve", lambda eng, c=c: eng.tensor_copy(
                    scr[:, PBR + c * 512: PBR + (c + 1) * 512], k.PEb[:, 0:512]),
                    reads=[k.bPE], writes=[pb.res], deps=pb.wdeps())
            tg = [k.scr_t.carve(TG + kt * 1024, TG + (kt + 1) * 1024,
                                "tg%d_%d_%d" % (half, e, kt)) for kt in range(NT)]
            for kt in range(NT):
                p.op("dve", lambda eng, kt=kt, hb=hb: eng.tensor_tensor(
                    scr[:, TG + kt * 1024: TG + (kt + 1) * 1024],
                    scr[:, kt * S + hb: kt * S + hb + 1024],
                    scr[:, PBR: PBR + 1024], ALU.mult),
                    reads=[treg.res, pb.res], writes=[tg[kt].res], deps=tg[kt].wdeps())
            w1r = [k.scr_t.carve(W13R + i * 512, W13R + (i + 1) * 512, "l1fw1r%d" % i)
                   for i in range(2)]
            w3r = [k.scr_t.carve(W13R + 1024 + i * 512, W13R + 1024 + (i + 1) * 512,
                                 "l1fw3r%d" % i) for i in range(2)]
            fs6 = [k.f32_t.carve(i * 512, (i + 1) * 512, "l1fs%d" % i) for i in range(6)]
            sctr = [0, 0]
            hreg = k.scr_t.carve(HID, HID + 16384, "l1fh%d_%d" % (half, e))

            def l1f_streams(ht):
                for hc in range(2):
                    s1 = sctr[0] % 3
                    sctr[0] += 1
                    stream_cast(k, fs6[s1], s1 * 512,
                                w1d[e, ht, :, hc * 512:(hc + 1) * 512],
                                512, W13R + hc * 512, w1r[hc])
                    s3 = 3 + sctr[1] % 3
                    sctr[1] += 1
                    stream_cast(k, fs6[s3], s3 * 512,
                                w3d[e, ht, :, hc * 512:(hc + 1) * 512],
                                512, W13R + 1024 + hc * 512, w3r[hc])

            l1f_streams(0)
            for ht in range(16):
                par = ht % 2
                if par == 0:
                    ub = [(k.PA[:, 0:512], k.bPA0), (k.PA[:, 512:1024], k.bPA1)]
                    vb = [(k.PB[:, 0:512], k.bPB0), (k.PB[:, 512:1024], k.bPB1)]
                else:
                    ub = [(k.PC[0:128, 0:512], k.bPC), (k.PD[0:128, 0:512], k.bPD)]
                    vb = [(k.PEb[0:128, 0:512], k.bPE), (k.PF[0:128, 0:512], k.bPF)]
                for c in range(2):
                    gu = MMGroup(k, ub[c][0], ub[c][1], NT)
                    for kt in range(NT):
                        gu.part(scr[:, W13R + kt * 128: W13R + (kt + 1) * 128],
                                scr[:, kt * S + hb + c * 512: kt * S + hb + (c + 1) * 512],
                                [w1r[kt // 4].res, treg.res])
                    gv = MMGroup(k, vb[c][0], vb[c][1], NT)
                    for kt in range(NT):
                        gv.part(scr[:, W13R + 1024 + kt * 128: W13R + 1024 + (kt + 1) * 128],
                                scr[:, TG + kt * 1024 + c * 512: TG + kt * 1024 + (c + 1) * 512],
                                [w3r[kt // 4].res, tg[kt].res])
                if ht + 1 < 16:
                    l1f_streams(ht + 1)
                hcol = HID + ht * 1024
                for c in range(2):
                    p.op("act", lambda eng, hcol=hcol, c=c, ua=ub[c][0]: eng.activation(
                        scr[:, hcol + c * 512:hcol + (c + 1) * 512], ua, AF.Silu),
                        reads=[ub[c][1]], writes=[hreg.res], deps=hreg.wdeps())
                    p.op("dve", lambda eng, hcol=hcol, c=c, va=vb[c][0]: eng.tensor_tensor(
                        scr[:, hcol + c * 512:hcol + (c + 1) * 512],
                        scr[:, hcol + c * 512:hcol + (c + 1) * 512], va, ALU.mult),
                        reads=[vb[c][1], hreg.res], writes=[hreg.res])
            w2b = [k.scr_t.carve(W2BR + i * 1024, W2BR + (i + 1) * 1024, "l1fw2b%d" % i)
                   for i in range(2)]
            w2f = [k.f32_t.carve(i * 1024, (i + 1) * 1024, "lw2f%d" % i) for i in range(2)]
            for d2 in range(NT):
                for q in range(2):
                    stream_cast(k, w2f[q], q * 1024,
                                w2d[e, d2, :, q * 1024:(q + 1) * 1024],
                                1024, W2BR + q * 1024, w2b[q], crit=False)
                if d2 % 2 == 0:
                    yb = [(k.PA[:, 0:512], k.bPA0), (k.PA[:, 512:1024], k.bPA1)]
                else:
                    yb = [(k.PB[:, 0:512], k.bPB0), (k.PB[:, 512:1024], k.bPB1)]
                for c in range(2):
                    gy = MMGroup(k, yb[c][0], yb[c][1], 16)
                    for kt in range(16):
                        gy.part(scr[:, W2BR + kt * 128: W2BR + (kt + 1) * 128],
                                scr[:, HID + kt * 1024 + c * 512: HID + kt * 1024 + (c + 1) * 512],
                                [w2b[kt // 8].res, hreg.res])
                    p.op("dve", lambda eng, d2=d2, c=c, hb=hb, ps=yb[c][0]:
                         eng.tensor_tensor(
                        k.acc[:, d2 * S + hb + c * 512: d2 * S + hb + (c + 1) * 512],
                        k.acc[:, d2 * S + hb + c * 512: d2 * S + hb + (c + 1) * 512],
                        ps, ALU.add),
                        reads=[yb[c][1], k.acc_r[d2]],
                        writes=[k.acc_r[d2]])
    p.barrier()


BLOCK_EMITTERS["l1_ffn"] = emit_l1_ffn


L0A_XB = 0
L0A_OT = 16384
L0A_VP = 32768
L0A_QP = 34816
L0A_KP = 36864
L0A_QPT = 38912
L0A_KPT = 43008
L0A_KTOK = 47104
L0A_KVS = 51200
L0A_NUM = 51456
L0A_WST = 53504
L0A_ASB = 53504
L0A_PROJ = 54528
L0A_KSP = 54784
LN16 = float(np.log(16.0))


def emit_l0_attn(k):
    import os
    dbg = os.environ.get("L0A_DEBUG", "")
    pairs = [7] if dbg else range(8)
    heads = [0] if dbg == "1" else range(2)
    p, scr, rows = k.p, k.scr, k.rows
    qw = k.dram("l0a_qw", [8, 128, 1024])
    kw = k.dram("l0a_kw", [8, 128, 1024])
    vw = k.dram("l0a_vw", [8, 128, 1024])
    ow = k.dram("l0a_ow", [8, 128, 1024])
    projd = k.dram("l0a_projT2", [128, 256])
    vbd = k.dram("L0_attn_v_b", [D])
    qb_c0, qb_res = param_cols(k, "L0_attn_q_b")
    kb_c0, kb_res = param_cols(k, "L0_attn_k_b")
    ga_c0, ga_res = param_cols(k, "L0_attn_gamma")
    be_c0, be_res = param_cols(k, "L0_attn_beta")
    qb8_c0 = k.misc_used
    k.misc_used += 8
    kb8_c0 = k.misc_used
    k.misc_used += 8
    assert k.misc_used <= 128
    qb8_res, kb8_res = Res("qb8"), Res("kb8")
    p.op("dve", lambda eng: eng.tensor_scalar(
        k.misc[:, qb8_c0:qb8_c0 + 8], k.misc[:, qb_c0:qb_c0 + 8], 0.125, None, ALU.mult),
        reads=[qb_res], writes=[qb8_res])
    p.op("dve", lambda eng: eng.tensor_scalar(
        k.misc[:, kb8_c0:kb8_c0 + 8], k.misc[:, kb_c0:kb_c0 + 8], 0.125, None, ALU.mult),
        reads=[kb_res], writes=[kb8_res])
    vb = k.row_t.carve(2560, 3584, "l0avb")
    p.op("sp", lambda eng: eng.dma_start(
        out=rows[0:1, 2560:3584], in_=vbd.rearrange("(o e) -> o e", o=1)),
        writes=[vb.res], deps=vb.wdeps(), dma=True)
    prj = k.scr_t.carve(L0A_PROJ, L0A_PROJ + 256, "prj")
    pf32 = k.f32_t.carve(2760, 3016, "prjf")
    p.op("sp", lambda eng: eng.dma_start(out=k.f32s[:, 2760:3016], in_=projd[:]),
         writes=[pf32.res], deps=pf32.wdeps(), dma=True)
    p.op("dve", lambda eng: eng.tensor_copy(
        scr[:, L0A_PROJ:L0A_PROJ + 256], k.f32s[:, 2760:3016]),
        reads=[pf32.res], writes=[prj.res], deps=prj.wdeps())
    xb = k.scr_t.carve(L0A_XB, L0A_XB + 16384, "l0axb")
    for j in range(NT):
        p.op("dve", lambda eng, j=j: eng.tensor_copy(
            scr[:, L0A_XB + j * S: L0A_XB + (j + 1) * S], k.acc[:, j * S:(j + 1) * S]),
            reads=[k.acc_r[j]], writes=[xb.res], deps=xb.wdeps())

    ot = k.scr_t.carve(L0A_OT, L0A_OT + 16384, "l0aot")
    fsl = [k.f32_t.carve(0, 1024, "l0af0"), k.f32_t.carve(1024, 2048, "l0af1")]
    kvs = k.scr_t.carve(L0A_KVS, L0A_KVS + 256, "kvs")
    num = k.scr_t.carve(L0A_NUM, L0A_NUM + 2048, "num")

    for pr in pairs:
        qp_ = k.scr_t.carve(L0A_QP, L0A_QP + 2048, "qp%d" % pr)
        kp_ = k.scr_t.carve(L0A_KP, L0A_KP + 2048, "kp%d" % pr)
        vp_ = k.scr_t.carve(L0A_VP, L0A_VP + 2048, "vp%d" % pr)
        for qki, (wd_, reg, base, b8c0, b8res) in enumerate(
                [(qw, qp_, L0A_QP, qb8_c0, qb8_res), (kw, kp_, L0A_KP, kb8_c0, kb8_res)]):
            wst = k.scr_t.carve(L0A_WST, L0A_WST + 1024, "l0awq%d" % qki)
            stream_cast(k, fsl[qki], qki * 1024, wd_[pr], 1024, L0A_WST, wst)
            for c in range(NC):
                ba = k.PA[:, 0:512] if c % 2 == 0 else k.PA[:, 512:1024]
                bb = k.bPA0 if c % 2 == 0 else k.bPA1
                g = MMGroup(k, ba, bb, NT)
                for kt in range(NT):
                    g.part(scr[:, L0A_WST + kt * 128: L0A_WST + (kt + 1) * 128],
                           scr[:, L0A_XB + kt * S + c * 512: L0A_XB + kt * S + (c + 1) * 512],
                           [wst.res, xb.res])
                p.op("dve", lambda eng, b=base, c=c, ba=ba, bc=(b8c0, pr): eng.tensor_scalar(
                    scr[:, b + c * 512: b + (c + 1) * 512], ba, 0.125,
                    k.misc[:, bc[0] + bc[1]:bc[0] + bc[1] + 1], ALU.mult, ALU.add),
                    reads=[bb, b8res], writes=[reg.res], deps=reg.wdeps())
        wst = k.scr_t.carve(L0A_WST, L0A_WST + 1024, "l0awv")
        stream_cast(k, fsl[0], 0, vw[pr], 1024, L0A_WST, wst)
        for t in range(16):
            ba = k.PA[0:128, 0:128] if t % 2 == 0 else k.PA[0:128, 512:640]
            bb = k.bPA0 if t % 2 == 0 else k.bPA1
            g = MMGroup(k, ba, bb, NT + 1)
            for kt in range(NT):
                g.part(scr[:, L0A_XB + kt * S + t * 128: L0A_XB + kt * S + t * 128 + 128],
                       scr[:, L0A_WST + kt * 128: L0A_WST + (kt + 1) * 128],
                       [xb.res, wst.res])
            g.part(k.ones_row_f32[:], rows[0:1, 2560 + pr * 128: 2560 + (pr + 1) * 128],
                   [vb.res, k.ones_row_f32_r])
            p.op("act", lambda eng, t=t, ba=ba: eng.activation(
                scr[:, L0A_VP + t * 128: L0A_VP + (t + 1) * 128],
                ba, AF.Copy),
                reads=[bb], writes=[vp_.res], deps=vp_.wdeps())
        for h01 in heads:
            hb = 64 * h01
            skip_scan = (dbg in ("3", "4", "5", "6", "7") and h01 == 1)
            qpt = k.scr_t.carve(L0A_QPT, L0A_QPT + 4096, "qpt")
            kpt = k.scr_t.carve(L0A_KPT, L0A_KPT + 4096, "kpt")
            ktok = k.scr_t.carve(L0A_KTOK, L0A_KTOK + 4096, "ktok")
            feat_list = [(L0A_QP, qp_), (L0A_KP, kp_)]
            if skip_scan and dbg in ("4", "6", "7"):
                feat_list = feat_list[:1]
            for qki, (base, reg) in enumerate(feat_list):
                crow = k.row_t.carve(0, 2048, "corr%d" % qki)
                skip_corr = skip_scan and dbg == "7"
                skip_map = skip_scan and dbg == "6"
                for c in range(NC if not skip_corr else 0):
                    sq = k.scr_t.carve(STAT_XC, STAT_XC + 512, "l0asq")
                    p.op("act", lambda eng, b=base, c=c, hb=hb: eng.activation(
                        scr[hb:hb + 64, STAT_XC:STAT_XC + 512],
                        scr[hb:hb + 64, b + c * 512: b + (c + 1) * 512], AF.Square),
                        reads=[reg.res], writes=[sq.res], deps=sq.wdeps())
                    mm(k, k.PC[0:1, 0:512], k.bPC,
                       [(k.ones_col_bf[hb:hb + 64, :], scr[hb:hb + 64, STAT_XC:STAT_XC + 512],
                         [sq.res, k.ones_col_bf_r])], tp=(hb, 0))
                    p.op("dve", lambda eng, c=c: eng.tensor_scalar(
                        rows[0:1, c * 512: (c + 1) * 512],
                        k.PC[0:1, 0:512], -0.5, -LN16, ALU.mult, ALU.add),
                        reads=[k.bPC], writes=[crow.res], deps=crow.wdeps())
                dst = L0A_QPT if qki == 0 else L0A_KPT
                dreg = qpt if qki == 0 else kpt
                for mt in range(2 if not skip_map else 0):
                    for c in range(NC):
                        ba = k.PB[:, 0:512]
                        bb = k.bPB0
                        g = MMGroup(k, ba, bb, 2)
                        g.part(scr[hb:hb + 64, L0A_PROJ + mt * 128: L0A_PROJ + (mt + 1) * 128],
                               scr[hb:hb + 64, base + c * 512: base + (c + 1) * 512],
                               [prj.res, reg.res], tp=(hb, 0))
                        g.part(k.ones_row_f32[:],
                               rows[0:1, c * 512: (c + 1) * 512],
                               [crow.res, k.ones_row_f32_r])
                        p.op("act", lambda eng, d=dst, mt=mt, c=c, ba=ba: eng.activation(
                            scr[:, d + mt * S + c * 512: d + mt * S + (c + 1) * 512],
                            ba, AF.Exp),
                            reads=[bb], writes=[dreg.res], deps=dreg.wdeps())
                if qki == 1 and dbg == "5" and skip_scan:
                    continue
                if qki == 1:
                    for t in range(16):
                        ba = k.PB[:, 512:768]
                        bb = k.bPB1
                        g = MMGroup(k, ba, bb, 3)
                        g.part(scr[hb:hb + 64, L0A_KP + t * 128: L0A_KP + (t + 1) * 128],
                               scr[hb:hb + 64, L0A_PROJ:L0A_PROJ + 256],
                               [kp_.res, prj.res], tp=(hb, 0))
                        g.part(rows[0:1, t * 128: (t + 1) * 128],
                               k.ones_row_f32[:], [crow.res, k.ones_row_f32_r],
                               out_ap=ba[:, 0:128])
                        g.part(rows[0:1, t * 128: (t + 1) * 128],
                               k.ones_row_f32[:], [crow.res, k.ones_row_f32_r],
                               out_ap=ba[:, 128:256])
                        p.op("act", lambda eng, t=t, ba=ba: eng.activation(
                            scr[:, L0A_KTOK + t * 256: L0A_KTOK + (t + 1) * 256],
                            ba, AF.Exp),
                            reads=[bb], writes=[ktok.res], deps=ktok.wdeps())
                    gks = MMGroup(k, k.PC[0:128, 0:16], k.bPC, 32)
                    for ch8 in range(8):
                        for mt in range(2):
                            for tt in range(2):
                                gks.part(
                                    scr[:, L0A_KTOK + (2 * ch8 + tt) * 256 + mt * 128:
                                        L0A_KTOK + (2 * ch8 + tt) * 256 + (mt + 1) * 128],
                                    k.ones_col_bf[:], [ktok.res, k.ones_col_bf_r],
                                    out_ap=k.PC[0:128, mt * 8 + ch8: mt * 8 + ch8 + 1])
                    ksf = k.f32_t.carve(2560, 2608, "ksf")
                    SS, PP, QQ = 2560, 2576, 2592
                    p.op("dve", lambda eng: eng.tensor_copy(
                        k.f32s[:, SS:SS + 16], k.PC[0:128, 0:16]),
                        reads=[k.bPC], writes=[ksf.res], deps=ksf.wdeps())
                    p.op("dve", lambda eng: eng.tensor_copy(
                        k.f32s[:, PP:PP + 16], k.f32s[:, SS:SS + 16]),
                        reads=[ksf.res], writes=[ksf.res])
                    cur, oth = PP, QQ
                    for sh in (1, 2, 4):
                        for mt in range(2):
                            a, b = cur + mt * 8, oth + mt * 8
                            p.op("dve", lambda eng, a=a, b=b, sh=sh: eng.tensor_copy(
                                k.f32s[:, b:b + sh], k.f32s[:, a:a + sh]),
                                reads=[ksf.res], writes=[ksf.res])
                            p.op("dve", lambda eng, a=a, b=b, sh=sh: eng.tensor_tensor(
                                k.f32s[:, b + sh:b + 8], k.f32s[:, a + sh:a + 8],
                                k.f32s[:, a:a + 8 - sh], ALU.add),
                                reads=[ksf.res], writes=[ksf.res])
                        cur, oth = oth, cur
                    ksp = k.scr_t.carve(L0A_KSP, L0A_KSP + 16, "ksp")
                    p.op("dve", lambda eng, a=cur: eng.tensor_tensor(
                        k.f32s[:, a:a + 16], k.f32s[:, a:a + 16],
                        k.f32s[:, SS:SS + 16], ALU.subtract),
                        reads=[ksf.res], writes=[ksf.res])
                    p.op("dve", lambda eng, a=cur: eng.tensor_copy(
                        scr[:, L0A_KSP:L0A_KSP + 16], k.f32s[:, a:a + 16]),
                        reads=[ksf.res], writes=[ksp.res], deps=ksp.wdeps())
            dreg_den = k.row_t.carve(0, 2048, "l0aden")
            asbs = [k.scr_t.carve(L0A_ASB, L0A_ASB + 512, "asb0"),
                    k.scr_t.carve(L0A_ASB + 512, L0A_ASB + 1024, "asb1")]

            def emit_at2(ch):
                bank_ap = k.PA[:, 0:512] if ch % 2 == 0 else k.PA[:, 512:1024]
                bank = k.bPA0 if ch % 2 == 0 else k.bPA1
                for ks in range(2):
                    g = MMGroup(k, bank_ap[:, ks * 256:(ks + 1) * 256], bank, 2)
                    for mt in range(2):
                        g.part(
                            scr[:, L0A_KPT + mt * S + ch * 256 + ks * 128:
                                L0A_KPT + mt * S + ch * 256 + (ks + 1) * 128],
                            scr[:, L0A_QPT + mt * S + ch * 256: L0A_QPT + mt * S + (ch + 1) * 256],
                            [kpt.res, qpt.res])
                ao = L0A_ASB + (ch % 2) * 512
                for ks in range(2):
                    cb = 384 - 128 * ks
                    p.op("dve", lambda eng, b=bank_ap, ao=ao, ks=ks, cb=cb: eng.tensor_tensor(
                        scr[:, ao + ks * 256: ao + (ks + 1) * 256],
                        b[:, ks * 256:(ks + 1) * 256],
                        k.band[:, cb:cb + 256], ALU.mult),
                        reads=[bank, k.band_r], writes=[asbs[ch % 2].res],
                        deps=asbs[ch % 2].wdeps())

            emit_at2(0)
            for ch in range(8):
                if ch + 1 < 8:
                    emit_at2(ch + 1)
                ao = L0A_ASB + (ch % 2) * 512
                ob_ap = (k.PB[:, 0:512] if ch % 2 == 0 else k.PB[:, 512:1024])[hb:hb + 64, 0:256]
                ob_bank = k.bPB0 if ch % 2 == 0 else k.bPB1
                nparts = 2 if ch == 0 else 4
                g = MMGroup(k, ob_ap, ob_bank, nparts)
                for ks in range(2):
                    g.part(scr[:, L0A_VP + (2 * ch + ks) * 128 + hb:
                               L0A_VP + (2 * ch + ks) * 128 + hb + 64],
                           scr[:, ao + ks * 256: ao + (ks + 1) * 256],
                           [vp_.res, asbs[ch % 2].res], tp=(0, hb))
                if ch > 0:
                    for mt in range(2):
                        g.part(scr[:, L0A_KVS + mt * 128: L0A_KVS + mt * 128 + 64],
                               scr[:, L0A_QPT + mt * S + ch * 256: L0A_QPT + mt * S + (ch + 1) * 256],
                               [kvs.res, qpt.res], tp=(0, hb))
                dn_ap = (k.PC if ch % 2 == 0 else k.PF)[0:1, 0:256]
                dn_bank = k.bPC if ch % 2 == 0 else k.bPF
                gd = MMGroup(k, dn_ap, dn_bank, nparts)
                for ks in range(2):
                    gd.part(k.ones_col_bf[:], scr[:, ao + ks * 256: ao + (ks + 1) * 256],
                            [asbs[ch % 2].res, k.ones_col_bf_r])
                if ch > 0:
                    for mt in range(2):
                        gd.part(scr[:, L0A_KSP + mt * 8 + ch: L0A_KSP + mt * 8 + ch + 1],
                                scr[:, L0A_QPT + mt * S + ch * 256: L0A_QPT + mt * S + (ch + 1) * 256],
                                [ksp.res, qpt.res])
                p.op("act", lambda eng, ch=ch, hb=hb, b=ob_ap: eng.activation(
                    scr[hb:hb + 64, L0A_NUM + ch * 256: L0A_NUM + (ch + 1) * 256],
                    b, AF.Copy),
                    reads=[ob_bank], writes=[num.res], deps=num.wdeps())
                p.op("dve", lambda eng, ch=ch, b=dn_ap: eng.tensor_copy(
                    rows[0:1, ch * 256:(ch + 1) * 256], b),
                    reads=[dn_bank], writes=[dreg_den.res], deps=dreg_den.wdeps())
                for mt in range(2):
                    kvp = k.PD if mt == 0 else k.PEb
                    kvb = k.bPD if mt == 0 else k.bPE
                    for tt in range(2):
                        p.op("pe", lambda eng, mt=mt, ch=ch, tt=tt, hb=hb, kvp=kvp:
                             eng.matmul(
                            kvp[0:128, 0:64],
                            scr[:, L0A_KTOK + (2 * ch + tt) * 256 + mt * 128:
                                L0A_KTOK + (2 * ch + tt) * 256 + (mt + 1) * 128],
                            scr[:, L0A_VP + (2 * ch + tt) * 128 + hb:
                                L0A_VP + (2 * ch + tt) * 128 + hb + 64],
                            start=(ch == 0 and tt == 0), stop=True,
                            skip_group_check=True),
                            reads=[ktok.res, vp_.res],
                            deps=[kvb.w] + kvb.r)
                        kvb.w = p.items[-1].idx
                        kvb.r = []
                if ch < 7:
                    for mt in range(2):
                        kvp = k.PD if mt == 0 else k.PEb
                        kvb = k.bPD if mt == 0 else k.bPE
                        p.op("dve", lambda eng, mt=mt, kvp=kvp: eng.tensor_copy(
                            scr[:, L0A_KVS + mt * 128: L0A_KVS + mt * 128 + 64],
                            kvp[0:128, 0:64]),
                            reads=[kvb], writes=[kvs.res], deps=kvs.wdeps())
            p.op("act", lambda eng: eng.activation(
                rows[0:1, 0:2048], rows[0:1, 0:2048], AF.Ln),
                reads=[dreg_den.res], writes=[dreg_den.res])
            p.op("act", lambda eng: eng.activation(
                rows[0:1, 0:2048], rows[0:1, 0:2048], AF.Exp, scale=-1.0),
                reads=[dreg_den.res], writes=[dreg_den.res])
            for c in range(NC):
                mm(k, k.PF[hb:hb + 64, 0:512], k.bPF,
                   [(k.ones_row_f32[0:1, 0:64], rows[0:1, c * 512:(c + 1) * 512],
                     [dreg_den.res, k.ones_row_f32_r])], tp=(0, hb))
                o_col = L0A_OT + pr * 2048 + c * 512
                p.op("dve", lambda eng, c=c, o=o_col, hb=hb: eng.tensor_tensor(
                    scr[hb:hb + 64, o:o + 512],
                    scr[hb:hb + 64, L0A_NUM + c * 512: L0A_NUM + (c + 1) * 512],
                    k.PF[hb:hb + 64, 0:512], ALU.mult),
                    reads=[num.res, k.bPF], writes=[ot.res], deps=ot.wdeps())

    if dbg:
        return
    p.barrier()
    ob_c0, ob_res = param_cols(k, "L0_attn_out_b")
    for m in range(NT):
        wst = k.scr_t.carve(L0A_WST, L0A_WST + 1024, "l0awo%d" % m)
        stream_cast(k, fsl[m % 2], (m % 2) * 1024, ow[m], 1024, L0A_WST, wst)
        for c in range(NC):
            par = c % 2
            bank_ap = k.PA[:, par * 512:(par + 1) * 512]
            bank = k.bPA0 if par == 0 else k.bPA1
            g = MMGroup(k, bank_ap, bank, NT)
            for kt in range(NT):
                g.part(scr[:, L0A_WST + kt * 128: L0A_WST + (kt + 1) * 128],
                       scr[:, L0A_OT + kt * S + c * 512: L0A_OT + kt * S + (c + 1) * 512],
                       [wst.res, ot.res])
            a_sl = (slice(None), slice(m * S + c * 512, m * S + (c + 1) * 512))
            p.op("dve", lambda eng, a=a_sl, m=m, b=bank_ap: eng.scalar_tensor_tensor(
                k.acc[a], b, k.misc[:, ob_c0 + m:ob_c0 + m + 1], k.acc[a],
                ALU.add, ALU.add),
                reads=[bank, ob_res, k.acc_r[m]], writes=[k.acc_r[m]])
    mu_res, rstd_res = stats(k, 1e-6)
    ln_apply(k, mu_res, rstd_res, affine=(ga_c0, ga_res, be_c0, be_res))
    for j in range(NT):
        for c in range(NC):
            a_sl = (slice(None), slice(j * S + c * 512, j * S + (c + 1) * 512))
            p.op("dve", lambda eng, a=a_sl, j=j, c=c: eng.tensor_tensor(
                k.acc[a], k.acc[a],
                scr[:, L0A_XB + j * S + c * 512: L0A_XB + j * S + (c + 1) * 512], ALU.add),
                reads=[k.acc_r[j], xb.res], writes=[k.acc_r[j]])
    p.barrier()


BLOCK_EMITTERS["l0_attn"] = emit_l0_attn


def prep_l0_attn(inputs):
    def tile8(w):
        w = np.asarray(w, np.float32)
        return np.ascontiguousarray(
            w.reshape(8, 128, 8, 128).transpose(2, 1, 0, 3).reshape(8, 128, 1024))
    proj = np.asarray(inputs["L0_attn_proj"], np.float32)
    pt = np.ascontiguousarray(proj.T)
    return {
        "l0a_qw": tile8(inputs["L0_attn_q_w"]),
        "l0a_kw": tile8(inputs["L0_attn_k_w"]),
        "l0a_vw": tile8(inputs["L0_attn_v_w"]),
        "l0a_ow": tile8(inputs["L0_attn_out_w"]),
        "l0a_projT2": np.ascontiguousarray(np.vstack([pt, pt])),
        "L0_attn_q_b": np.asarray(inputs["L0_attn_q_b"], np.float32),
        "L0_attn_k_b": np.asarray(inputs["L0_attn_k_b"], np.float32),
        "L0_attn_v_b": np.asarray(inputs["L0_attn_v_b"], np.float32),
        "L0_attn_out_b": np.asarray(inputs["L0_attn_out_b"], np.float32),
        "L0_attn_gamma": np.asarray(inputs["L0_attn_gamma"], np.float32),
        "L0_attn_beta": np.asarray(inputs["L0_attn_beta"], np.float32),
    }


PREPPERS["l0_attn"] = prep_l0_attn



def prep_consts():
    kk = np.arange(128)[:, None]
    cc = np.arange(BAND_W)[None, :]
    oh = np.zeros((8, 1024), dtype=BF)
    for e in range(8):
        oh[e, e * 128:(e + 1) * 128] = 1
    return {
        "c_id_f32": np.eye(128, dtype=np.float32),
        "c_ones_col_bf": np.ones((128, 1), dtype=BF),
        "c_ones_row_f32": np.ones((1, 128), dtype=np.float32),
        "c_band": (kk <= cc - 384).astype(BF),
        "c_onehot8": oh,
    }


def prep_l1_attn(inputs):
    w_qkv = np.asarray(inputs["L1_attn_w_qkv"], dtype=np.float32)
    wt = np.ascontiguousarray(w_qkv.T)
    qkw = np.empty((8, 2, 128, 1024), np.float32)
    for pair in range(8):
        for qki in range(2):
            base = qki * 1024 + pair * 128
            blk = wt[:, base:base + 128]
            qkw[pair, qki] = blk.reshape(8, 128, 128).transpose(1, 0, 2).reshape(128, 1024)
    vw = np.empty((2, 8, 128, 512), np.float32)
    for g in range(2):
        for kt in range(8):
            vw[g, kt] = wt[kt * 128:(kt + 1) * 128, 2048 + g * 512: 2048 + (g + 1) * 512]
    w_out = np.asarray(inputs["L1_attn_w_out"], dtype=np.float32)
    wo = np.ascontiguousarray(w_out.T.reshape(8, 128, 1024))
    return {
        "l1a_qkw": qkw, "l1a_vw": vw, "l1a_wo": wo,
        "L1_attn_ln_gamma": np.asarray(inputs["L1_attn_ln_gamma"], np.float32),
        "L1_attn_ln_beta": np.asarray(inputs["L1_attn_ln_beta"], np.float32),
        "L1_attn_b_out": np.asarray(inputs["L1_attn_b_out"], np.float32),
    }


PREPPERS["l1_attn"] = prep_l1_attn


def run_blocks(blocks, x, inputs, cores=8):
    nc, k = build_program(blocks)
    needed = set(k.inp.keys())
    base = dict(prep_consts())
    cvals = np.zeros((128, 32), np.float32)
    for val, c in k.cval_map.items():
        cvals[:, c] = val
    base["c_cvals"] = cvals
    for b in blocks:
        base.update(PREPPERS[b](inputs))
    in_maps = []
    for c in range(cores):
        m = {"xio": np.ascontiguousarray(np.asarray(x[c], np.float32).T)}
        for name, arr in base.items():
            if name in needed:
                m[name] = arr
        in_maps.append(m)
    res = bass_utils.run_bass_kernel_spmd(nc, in_maps, core_ids=list(range(cores)))
    return np.stack([np.ascontiguousarray(r["yio"].T) for r in res.results])


ALL_BLOCKS = ["l0_attn", "l0_ffn", "l1_attn", "l1_ffn"]
_CACHE = {}


def _get_program(blocks):
    key = tuple(blocks)
    if key not in _CACHE:
        nc, k = build_program(list(blocks))
        cvals = np.zeros((128, 32), np.float32)
        for val, c in k.cval_map.items():
            cvals[:, c] = val
        _CACHE[key] = (nc, k, cvals)
    return _CACHE[key]


def kernel(**inputs):
    x = np.asarray(inputs["L0_attn_x"], np.float32)
    nc, k, cvals = _get_program(tuple(ALL_BLOCKS))
    base = prep_consts()
    base["c_cvals"] = cvals
    for b in ALL_BLOCKS:
        base.update(PREPPERS[b](inputs))
    needed = set(k.inp.keys())
    in_maps = []
    for c in range(8):
        m = {"xio": np.ascontiguousarray(x[c].T)}
        for name, arr in base.items():
            if name in needed:
                m[name] = arr
        in_maps.append(m)
    res = bass_utils.run_bass_kernel_spmd(nc, in_maps, core_ids=list(range(8)))
    return np.stack([np.ascontiguousarray(np.asarray(r["yio"], np.float32).T)
                     for r in res.results])
